# version 01
# baseline (speedup 1.0000x reference)
import contextlib
import numpy as np
import concourse.bass as bass
import concourse.mybir as mybir

F32 = mybir.dt.float32
BF16 = mybir.dt.bfloat16
U8 = mybir.dt.uint8
AF = mybir.ActivationFunctionType
ALU = mybir.AluOpType
AX = mybir.AxisListType

S = 2048
E = 2048
NCH = 16
NH = 8
HD = 128
DEX = 8192
ECH = 64
LAM_INIT = 0.8 - 0.6 * float(np.exp(-0.3 * 12))
SCALE = HD ** -0.5
EPS_LN = 1e-5
EPS_Q = 1e-8
MASKVAL = -1e30

SROW_MOD = 0
SROW_SC = 6 * E
SLOT_NEGLAM = 0
SLOT_SUP = 1
SLOT_SGD = 2
SLOT_THR_UP = 3
SLOT_THR_GATE = 4
SLOT_THR_DOWN = 5
NSLOT = 8


class Evt:
    __slots__ = ("sem", "v")

    def __init__(self, sem, v):
        self.sem = sem
        self.v = v


class ESem:

    def __init__(self, nc, es, name, eng, flush):
        self.sem = es.enter_context(nc.semaphore(name))
        self.n = 0
        self.eng = eng
        self.flush = flush

    def tag(self, ins):
        ins.then_inc(self.sem, 1)
        self.n += 1
        if self.flush:
            self.eng.wait_ge(self.sem, self.n)
        return Evt(self.sem, self.n)

    def bump(self):
        if self.n > 0:
            self.eng.wait_ge(self.sem, self.n)
        self.eng.nop().then_inc(self.sem, 1)
        self.n += 1
        return Evt(self.sem, self.n)

    def evt(self):
        return Evt(self.sem, self.n)


class Slot:

    def __init__(self, nc, es, name):
        self.sem = es.enter_context(nc.semaphore(name))
        self.n = 0

    def evt(self):
        return Evt(self.sem, self.n)


def wt(eng, *evts):
    for e in evts:
        if e is not None and e.v > 0:
            eng.wait_ge(e.sem, e.v)


class Builder:
    def __init__(self, dbg=(), upto=99):
        self.dbg_names = set(dbg)
        self.upto = upto
        nc = self.nc = bass.Bass("TRN2", target_bir_lowering=False)
        self.es = contextlib.ExitStack()

        def din(name, shape):
            return nc.dram_tensor(name, shape, F32, kind="ExternalInput").ap()

        self.xin = din("xin", [S, E])
        self.cvec = din("cvec", [E])
        self.wq = din("wq", [E, E])
        self.wk = din("wk", [E, E])
        self.wv = din("wv", [E, E])
        self.wo = din("wo", [E, E])
        self.wmod = din("wmod", [E, 6 * E])
        self.wupT = din("wupT", [E, DEX])
        self.wgateT = din("wgateT", [E, DEX])
        self.wdownT = din("wdownT", [DEX, E])
        self.lq1 = din("lq1", [HD])
        self.lk1 = din("lk1", [HD])
        self.lq2 = din("lq2", [HD])
        self.lk2 = din("lk2", [HD])
        self.subw = din("subw", [2 * HD])
        self.out = nc.dram_tensor("out", [S, E], F32, kind="ExternalOutput").ap()

        def scr(name, shape, dt=BF16):
            kind = "ExternalOutput" if ("spill" in self.dbg_names) else "Internal"
            return nc.dram_tensor(name, shape, dt, kind=kind).ap()

        self.srow = scr("srow", [SROW_SC + NSLOT], F32)
        self.sq_hi = scr("sq_hi", [NCH, 128, S])
        self.sq_lo = scr("sq_lo", [NCH, 128, S])
        self.sk_hi = scr("sk_hi", [NCH, 128, S])
        self.sk_lo = scr("sk_lo", [NCH, 128, S])
        self.sv_hi = scr("sv_hi", [NH, NCH, 128, 256])
        self.sv_lo = scr("sv_lo", [NH, NCH, 128, 256])
        self.sa_hi = scr("sa_hi", [NCH, NCH, 128, 128])
        self.sa_lo = scr("sa_lo", [NCH, NCH, 128, 128])
        self.shid_hi = scr("shid_hi", [ECH, 128, S])
        self.shid_lo = scr("shid_lo", [ECH, 128, S])
        self.sx1 = scr("sx1", [NCH, 128, E], F32)

        self.arenaA = nc.alloc_sbuf_tensor("arenaA", [128, 65536], U8)
        self.arenaB = nc.alloc_sbuf_tensor("arenaB", [128, 65536], U8)
        self.arenaW = nc.alloc_sbuf_tensor("arenaW", [128, 57344], U8)
        self.arenaD = nc.alloc_sbuf_tensor("arenaD", [128, 22528], U8)

        self.hfm_hi = self.arenaA.bitcast(BF16)
        self.hfm_lo = self.arenaB.bitcast(BF16)
        D = self.arenaD.bitcast(F32)
        Db = self.arenaD.bitcast(BF16)
        self.bc0 = D[:, 0:2048]
        self.bc1 = D[:, 2048:4096]
        self.sw2 = D[:, 4096:4352]
        self.mask = D[:, 4352:4480]
        self.ident = Db[:, 8960:9088]
        self.cols = D[:, 4544:4608]
        self.stats = D[:, 4608:4864]

        self.psb = [nc.alloc_psum_tensor(f"psb{i}", [128, 512], F32)
                    for i in range(4)]
        self.pst = [nc.alloc_psum_tensor(f"pst{i}", [128, 1024], BF16)
                    for i in range(2)]
        self.ps6 = nc.alloc_psum_tensor("ps6", [128, 512], F32)
        self.ps7 = nc.alloc_psum_tensor("ps7", [128, 512], F32)

        es = self.es
        self.TE = ESem(nc, es, "sTE", nc.tensor, flush=False)
        self.VE = ESem(nc, es, "sVE", nc.vector, flush=True)
        self.AE = ESem(nc, es, "sAE", nc.scalar, flush=True)
        self.GE = ESem(nc, es, "sGE", nc.gpsimd, flush=True)
        self.SE = ESem(nc, es, "sSE", nc.sync, flush=False)
        self.engines = [(nc.tensor, self.TE), (nc.vector, self.VE),
                        (nc.scalar, self.AE), (nc.gpsimd, self.GE),
                        (nc.sync, self.SE)]
        self._slots = {}
        self.dbg_tensors = {}

    def slot(self, name):
        if name not in self._slots:
            self._slots[name] = Slot(self.nc, self.es, "sl_" + name)
        return self._slots[name]

    def dma(self, eng, slotname, out, in_, **kw):
        sl = self.slot(slotname)
        eng.dma_start(out=out, in_=in_, **kw).then_inc(sl.sem, 16)
        sl.n += 16
        return Evt(sl.sem, sl.n)

    def barrier(self):
        for eng, es_ in self.engines:
            es_.bump()
        evs = [es_.evt() for _, es_ in self.engines]
        evs += [sl.evt() for sl in self._slots.values()]
        for eng, _ in self.engines:
            wt(eng, *evs)

    def want(self, name):
        return name in self.dbg_names

    def dbg_dram(self, name, shape, dt=F32):
        t = self.nc.dram_tensor("dbg_" + name, list(shape), dt,
                                kind="ExternalOutput").ap()
        self.dbg_tensors[name] = t
        return t

    def wview(self, dt, cols, off_bytes):
        a = self.arenaW.bitcast(dt)
        isz = {F32: 4, BF16: 2, U8: 1}[dt]
        assert off_bytes % isz == 0
        c0 = off_bytes // isz
        assert (c0 + cols) * isz <= 57344, (off_bytes, cols)
        return a[:, c0:c0 + cols]

    def wap3(self, dt, off_bytes, dims):
        h = self.arenaW.bitcast(dt)
        isz = {F32: 4, BF16: 2, U8: 1}[dt]
        width = 57344 // isz
        assert off_bytes % isz == 0
        return bass.AP(h, off_bytes // isz, [[width, 128]] + list(dims))

    def build(self):
        nc = self.nc
        with self.es, nc.Block():
            self.phase0()
            if self.upto >= 1:
                self.phase1()
            if self.upto >= 2:
                self.phase2()
            if self.upto >= 3:
                self.phase3()
            if self.upto >= 4:
                self.phase4()
            if self.upto >= 5:
                self.phase5()
            if self.upto >= 6:
                self.phase6()
            if self.upto >= 7:
                self.phase7()
            self.finish()
        return nc

    def phase0(self):
        nc = self.nc
        TE, VE, AE, GE = self.TE, self.VE, self.AE, self.GE

        cfm = self.wview(F32, 16, 0)
        lv = self.wview(F32, 512, 256)
        lt = self.wview(F32, 16, 2304)
        mrow = [self.wview(F32, 512, 3584), self.wview(F32, 512, 5632)]
        wmb = [self.wview(F32, 6144, 8192), self.wview(F32, 6144, 32768)]

        GE.tag(nc.gpsimd.memset(self.stats[:, :], 0.0))
        GE.tag(nc.gpsimd.memset(self.cols[:, :], 0.0))
        GE.tag(nc.gpsimd.memset(self.cols[:, 60:64], 1.0))
        GE.tag(nc.gpsimd.memset(self.ident[:, :], 0.0))
        GE.tag(nc.gpsimd.affine_select(
            out=self.ident[:, :], in_=self.ident[:, :],
            compare_op=ALU.not_equal, fill=1.0, base=0,
            pattern=[[-1, 128]], channel_multiplier=1))
        GE.tag(nc.gpsimd.memset(self.mask[:, :], 0.0))
        self.e_consts = GE.tag(nc.gpsimd.affine_select(
            out=self.mask[:, :], in_=self.mask[:, :],
            compare_op=ALU.is_ge, fill=MASKVAL, base=0,
            pattern=[[-1, 128]], channel_multiplier=1))

        self.dma(nc.sync, "lam", lv[0:1, 0:128], self.lq1[None, :])
        self.dma(nc.sync, "lam", lv[0:1, 128:256], self.lk1[None, :])
        self.dma(nc.sync, "lam", lv[0:1, 256:384], self.lq2[None, :])
        e_l = self.dma(nc.sync, "lam", lv[0:1, 384:512], self.lk2[None, :])
        e_c = self.dma(nc.sync, "cfm", cfm[:, 0:16],
                       bass.AP(self.cvec.tensor, 0, [[1, 128], [128, 16]]),
                       allow_slow_non_contiguous=True)
        e_swb = self.dma(nc.sync, "swb", self.sw2[:, :],
                         bass.AP(self.subw.tensor, 0, [[0, 128], [1, 256]]))
        wt(nc.vector, e_swb)
        self.e_sw2 = VE.tag(nc.vector.tensor_scalar(
            self.sw2[:, :], self.sw2[:, :], 1.0 - LAM_INIT, None, ALU.mult))

        wt(nc.scalar, e_c)
        e_sc = AE.tag(nc.scalar.activation(cfm[:, 0:16], cfm[:, 0:16], AF.Silu))

        wt(nc.vector, e_l)
        VE.tag(nc.vector.tensor_tensor(lv[0:1, 0:128], lv[0:1, 0:128],
                                       lv[0:1, 128:256], ALU.mult))
        VE.tag(nc.vector.tensor_tensor(lv[0:1, 256:384], lv[0:1, 256:384],
                                       lv[0:1, 384:512], ALU.mult))
        nc.vector.reduce_sum(lt[0:1, 0:1], lv[0:1, 0:128], axis=AX.X)
        e_vr = VE.tag(nc.vector.reduce_sum(lt[0:1, 1:2], lv[0:1, 256:384],
                                           axis=AX.X))
        wt(nc.scalar, e_vr)
        e_ae = AE.tag(nc.scalar.activation(lt[0:1, 2:4], lt[0:1, 0:2], AF.Exp))
        wt(nc.vector, e_ae)
        VE.tag(nc.vector.tensor_tensor(lt[0:1, 4:5], lt[0:1, 2:3],
                                       lt[0:1, 3:4], ALU.subtract))
        e_nl = VE.tag(nc.vector.tensor_scalar(
            lt[0:1, 5:6], lt[0:1, 4:5], LAM_INIT, -1.0, ALU.add, ALU.mult))
        wt(nc.gpsimd, e_nl)
        self.dma(nc.gpsimd, "srow_sc",
                 bass.AP(self.srow.tensor, SROW_SC + SLOT_NEGLAM,
                         [[1, 1], [1, 1]]),
                 lt[0:1, 5:6])

        wmt = self.wmod.tensor
        psm = [self.ps6, self.ps7]
        ev_te = [None, None]
        ev_cp = [None, None]
        ev_wr = [None, None]
        WMW = 384
        for t in range(32):
            par = t % 2
            wt(nc.sync, ev_te[par])
            w3 = self.wap3(F32, 8192 + par * 24576, [[WMW, 16], [1, WMW]])
            e_ld = self.dma(nc.sync, f"wm{par}", w3,
                            bass.AP(wmt, t * WMW,
                                    [[6 * E, 128], [6 * E * 128, 16],
                                     [1, WMW]]))
            wt(nc.tensor, e_ld, e_sc, ev_cp[par])
            ins = None
            for j in range(16):
                rj = wmb[par][:, j * WMW:(j + 1) * WMW]
                ins = nc.tensor.matmul(
                    psm[par][0:1, 0:WMW], cfm[:, j:j + 1], rj,
                    start=(j == 0), stop=(j == 15))
            ev_te[par] = TE.tag(ins)
            wt(nc.vector, ev_te[par], ev_wr[par])
            e_cp = VE.tag(nc.vector.tensor_copy(mrow[par][0:1, 0:WMW],
                                                psm[par][0:1, 0:WMW]))
            ev_cp[par] = e_cp
            wt(nc.gpsimd, e_cp)
            ev_wr[par] = self.dma(
                nc.gpsimd, f"mrow{par}",
                bass.AP(self.srow.tensor, SROW_MOD + t * WMW,
                        [[1, 1], [1, WMW]]),
                mrow[par][0:1, 0:WMW])
        self.e_modrow = [ev_wr[0], ev_wr[1]]

        if self.want("mod"):
            t = self.dbg_dram("mod", [6 * E])
            wt(nc.gpsimd, *self.e_modrow)
            self.dma(nc.gpsimd, "dbg", t[None, :],
                     bass.AP(self.srow.tensor, 0, [[1, 1], [1, 6 * E]]))

        wt(nc.sync, *self.e_modrow)
        e_b0 = self.dma(nc.sync, "bc0", self.bc0[:, :],
                        bass.AP(self.srow.tensor, SROW_MOD + E,
                                [[0, 128], [1, E]]))
        e_b1 = self.dma(nc.sync, "bc1", self.bc1[:, :],
                        bass.AP(self.srow.tensor, SROW_MOD + 0,
                                [[0, 128], [1, E]]))
        wt(nc.vector, e_b0)
        self.e_bc = VE.tag(nc.vector.tensor_scalar(
            self.bc0[:, :], self.bc0[:, :], 1.0, None, ALU.add))
        self.e_bc_ld = e_b1

    def emit_stats(self, xb, mcol, rcol, pre_evts, eps=EPS_LN, n=E, par=0):
        nc = self.nc
        VE, AE = self.VE, self.AE
        stt = self.stats
        dump = self.wview(F32, 2048, 16384)
        wt(nc.scalar, *pre_evts)
        e_sq = AE.tag(nc.scalar.activation(dump[:, 0:n], xb, AF.Square,
                                           accum_out=stt[:, 12 + par:13 + par]))
        wt(nc.vector, *pre_evts)
        VE.tag(nc.vector.reduce_sum(stt[:, 1:2], xb, axis=AX.X))
        VE.tag(nc.vector.tensor_scalar(mcol, stt[:, 1:2], 1.0 / n, None,
                                       ALU.mult))
        nc.vector.tensor_tensor(stt[:, 3:4], mcol, mcol, ALU.mult)
        wt(nc.vector, e_sq)
        VE.tag(nc.vector.tensor_scalar(stt[:, 4:5], stt[:, 12 + par:13 + par],
                                       1.0 / n, None, ALU.mult))
        VE.tag(nc.vector.tensor_tensor(stt[:, 5:6], stt[:, 4:5], stt[:, 3:4],
                                       ALU.subtract))
        e_u = VE.tag(nc.vector.tensor_scalar(stt[:, 5:6], stt[:, 5:6], eps,
                                             None, ALU.add))
        wt(nc.scalar, e_u)
        e_sq2 = AE.tag(nc.scalar.activation(stt[:, 6:7], stt[:, 5:6], AF.Sqrt))
        wt(nc.vector, e_sq2)
        VE.tag(nc.vector.reciprocal(stt[:, 7:8], stt[:, 6:7]))
        VE.tag(nc.vector.tensor_tensor(stt[:, 8:9], stt[:, 7:8], stt[:, 7:8],
                                       ALU.mult))
        VE.tag(nc.vector.tensor_tensor(stt[:, 9:10], stt[:, 8:9], stt[:, 5:6],
                                       ALU.mult))
        VE.tag(nc.vector.tensor_scalar(stt[:, 9:10], stt[:, 9:10], -0.5, 1.5,
                                       ALU.mult, ALU.add))
        e_r = VE.tag(nc.vector.tensor_tensor(rcol, stt[:, 7:8], stt[:, 9:10],
                                             ALU.mult))
        return e_r

    def emit_fm_store(self, src_bf16, dest_handle, sc, e_src, w=2048,
                      dest_width=32768, dest_col0=0, sc_stride=2048):
        nc = self.nc
        TE, VE = self.TE, self.VE
        nj = w // 128
        e_ev = None
        e_t = None
        for g in range(0, nj, 8):
            gb = min(8, nj - g)
            bi = (g // 8) % 2
            pbank = self.pst[bi]
            wt(nc.tensor, self.pst_guard[bi], e_src)
            ins = None
            for jj in range(gb):
                j = g + jj
                ins = nc.tensor.transpose(
                    pbank[:, jj * 128:(jj + 1) * 128],
                    src_bf16[:, j * 128:(j + 1) * 128], self.ident[:, :])
            e_t = TE.tag(ins)
            wt(nc.vector, e_t)
            outap = bass.AP(dest_handle,
                            dest_col0 + g * sc_stride + sc * 128,
                            [[dest_width, 128], [sc_stride, gb], [1, 128]])
            inap = bass.AP(pbank, 0, [[1024, 128], [128, gb], [1, 128]])
            e_ev = VE.tag(nc.vector.tensor_copy(outap, inap))
            self.pst_guard[bi] = e_ev
        return e_t, e_ev

    def phase1(self):
        nc = self.nc
        VE, AE, GE = self.VE, self.AE, self.GE
        self.barrier()
        self.pst_guard = [None, None]
        xb = [self.wview(F32, 2048, 0), self.wview(F32, 2048, 8192)]
        zb = self.wview(F32, 2048, 24576)
        hfp = self.wview(F32, 2048, 32768)
        hhi = self.wview(BF16, 2048, 40960)
        hlo = self.wview(BF16, 2048, 45056)
        stt = self.stats
        e_used = [None, None]
        e_hi_prev = e_lo_prev = None
        e_te_hi = e_te_lo = None
        for sc in range(NCH):
            par = sc % 2
            wt(nc.sync, e_used[par])
            e_x = self.dma(nc.sync, f"x{par}", xb[par][:, :],
                           self.xin[sc * 128:(sc + 1) * 128, :])
            e_r = self.emit_stats(xb[par], stt[:, 16:17], stt[:, 17:18],
                                  [e_x], par=par)
            e_z = VE.tag(nc.vector.tensor_scalar(
                zb[:, :], xb[par][:, :], stt[:, 16:17],
                stt[:, 17:18], ALU.subtract, ALU.mult))
            e_used[par] = e_z
            wt(nc.vector, self.e_bc, self.e_bc_ld, e_hi_prev, e_lo_prev)
            VE.tag(nc.vector.tensor_tensor(hfp[:, :], zb[:, :],
                                           self.bc0[:, :], ALU.mult))
            e_h = VE.tag(nc.vector.tensor_tensor(hfp[:, :], hfp[:, :],
                                                 self.bc1[:, :], ALU.add))
            wt(nc.scalar, e_h, e_te_hi)
            e_hi = AE.tag(nc.scalar.copy(hhi[:, :], hfp[:, :]))
            wt(nc.gpsimd, e_hi, e_te_lo)
            e_lo = GE.tag(nc.gpsimd.tensor_sub(hlo[:, :], hfp[:, :],
                                               hhi[:, :]))
            e_te_hi, _ = self.emit_fm_store(hhi, self.hfm_hi, sc, e_hi)
            e_te_lo, _ = self.emit_fm_store(hlo, self.hfm_lo, sc, e_lo)
            e_hi_prev, e_lo_prev = e_hi, e_lo
        self.e_h1 = VE.evt()
        if self.want("h1fm"):
            t1 = self.dbg_dram("h1fm_hi", [128, 32768])
            t2 = self.dbg_dram("h1fm_lo", [128, 32768])
            wt(nc.gpsimd, VE.evt())
            self.dma(nc.gpsimd, "dbg", t1[:, :], self.hfm_hi[:, :])
            self.dma(nc.gpsimd, "dbg", t2[:, :], self.hfm_lo[:, :])

    def phase2(self):
        nc = self.nc
        TE, VE, AE, GE = self.TE, self.VE, self.AE, self.GE
        self.barrier()
        stt = self.stats

        self.p1_count = 0
        p1_mats = [self.wupT, self.wgateT, self.wdownT]
        self.e_p1_act = None
        pbuf = [self.wview(F32, 2048, 36864), self.wview(F32, 2048, 45056)]

        def pass1_chunk():
            c = self.p1_count
            if c >= 192:
                return
            self.p1_count += 1
            mat = p1_mats[c // 64]
            sub = c % 64
            par = c % 2
            wt(nc.sync, self.e_p1_act if c >= 2 else None)
            e_ld = self.dma(nc.sync, f"p1{par}", pbuf[par][:, :],
                            bass.AP(mat.tensor, sub * 128 * 2048,
                                    [[2048, 128], [1, 2048]]))
            wt(nc.scalar, e_ld)
            self.e_p1_act = AE.tag(nc.scalar.activation(
                pbuf[par][:, 0:2048], pbuf[par][:, 0:2048], AF.Abs,
                accum_out=stt[:, 64 + c:65 + c]))

        wsf = self.wview(F32, 2048, 0)
        whi = [self.wview(BF16, 2048, 8192), self.wview(BF16, 2048, 12288)]
        wlo = [self.wview(BF16, 2048, 16384), self.wview(BF16, 2048, 20480)]
        sthi = self.wview(BF16, 2048, 24576)
        stlo = self.wview(BF16, 2048, 28672)
        qfp = self.wview(F32, 512, 32768)
        e_split = [None, None]
        e_bank = [None, None]
        e_te_hl = [None, None]
        e_spill = None
        e_stage_lo = None
        for mat_i, (wmat, dst_hi, dst_lo, qs) in enumerate(
                [(self.wq, self.sq_hi, self.sq_lo, SCALE),
                 (self.wk, self.sk_hi, self.sk_lo, 1.0)]):
            for m in range(NCH):
                par = m % 2
                wt(nc.sync, e_split[0], e_split[1])
                e_ld = self.dma(nc.sync, "ws",
                                self.wap3(F32, 0, [[128, 16], [1, 128]]),
                                bass.AP(wmat.tensor, m * 128,
                                        [[E, 128], [E * 128, 16], [1, 128]]))
                wt(nc.scalar, e_ld, e_te_hl[par])
                e_hi = AE.tag(nc.scalar.copy(whi[par][:, :], wsf[:, :]))
                wt(nc.gpsimd, e_hi)
                e_lo = GE.tag(nc.gpsimd.tensor_sub(wlo[par][:, :], wsf[:, :],
                                                   whi[par][:, :]))
                e_split[par] = e_lo
                for st in range(4):
                    bk = st % 2
                    wt(nc.tensor, e_lo, e_bank[bk])
                    ins = None
                    for f in range(16):
                        lt_ = whi[par][:, f * 128:(f + 1) * 128]
                        nc.tensor.matmul(
                            self.psb[bk][:, :], lt_,
                            self.hfm_hi[:, f * 2048 + st * 512:
                                        f * 2048 + st * 512 + 512],
                            start=(f == 0), stop=False)
                        nc.tensor.matmul(
                            self.psb[bk][:, :], lt_,
                            self.hfm_lo[:, f * 2048 + st * 512:
                                        f * 2048 + st * 512 + 512],
                            start=False, stop=False)
                    for f in range(16):
                        ins = nc.tensor.matmul(
                            self.psb[bk][:, :],
                            wlo[par][:, f * 128:(f + 1) * 128],
                            self.hfm_hi[:, f * 2048 + st * 512:
                                        f * 2048 + st * 512 + 512],
                            start=False, stop=(f == 15))
                    e_mm = TE.tag(ins)
                    wt(nc.vector, e_mm, e_stage_lo)
                    e_q = VE.tag(nc.vector.tensor_scalar(
                        qfp[:, :], self.psb[bk][:, :], qs, None, ALU.mult))
                    e_bank[bk] = e_q
                    wt(nc.scalar, e_q, e_spill)
                    e_sh = AE.tag(nc.scalar.copy(
                        sthi[:, st * 512:(st + 1) * 512], qfp[:, :]))
                    wt(nc.gpsimd, e_sh)
                    e_stage_lo = GE.tag(nc.gpsimd.tensor_sub(
                        stlo[:, st * 512:(st + 1) * 512], qfp[:, :],
                        sthi[:, st * 512:(st + 1) * 512]))
                e_te_hl[par] = TE.evt()
                wt(nc.scalar, e_stage_lo)
                self.dma(nc.scalar, "spq", dst_hi[m, :, :], sthi[:, :])
                e_spill = self.dma(nc.scalar, "spq", dst_lo[m, :, :],
                                   stlo[:, :])
                for _ in range(6):
                    pass1_chunk()
        while self.p1_count < 192:
            pass1_chunk()

        self.barrier()
        e_bank = [None, None]
        e_spill = None
        e_stage_lo = None
        wvf = self.wview(F32, 4096, 0)
        vhi = self.wview(BF16, 4096, 16384)
        vlo = self.wview(BF16, 4096, 24576)
        vsh = self.wview(BF16, 256, 32768)
        vsl = self.wview(BF16, 256, 33280)
        vfp = self.wview(F32, 256, 33792)
        e_vsplit = None
        for h in range(NH):
            wt(nc.sync, e_vsplit, e_te_hl[0], e_te_hl[1])
            e_ld = self.dma(nc.sync, "ws",
                            self.wap3(F32, 0, [[256, 16], [1, 256]]),
                            bass.AP(self.wv.tensor, h * 256,
                                    [[E, 128], [E * 128, 16], [1, 256]]))
            wt(nc.scalar, e_ld)
            e_hi = AE.tag(nc.scalar.copy(vhi[:, :], wvf[:, :]))
            wt(nc.gpsimd, e_hi)
            e_lo = GE.tag(nc.gpsimd.tensor_sub(vlo[:, :], wvf[:, :],
                                               vhi[:, :]))
            e_vsplit = e_lo
            for sc in range(NCH):
                bk = sc % 2
                wt(nc.tensor, e_lo, e_bank[bk])
                ins = None
                for f in range(16):
                    lh = self.hfm_hi[:, f * 2048 + sc * 128:
                                     f * 2048 + sc * 128 + 128]
                    rh = vhi[:, f * 256:(f + 1) * 256]
                    rl = vlo[:, f * 256:(f + 1) * 256]
                    nc.tensor.matmul(self.psb[bk][:, 0:256], lh, rh,
                                     start=(f == 0), stop=False)
                    nc.tensor.matmul(self.psb[bk][:, 0:256], lh, rl,
                                     start=False, stop=False)
                for f in range(16):
                    ll = self.hfm_lo[:, f * 2048 + sc * 128:
                                     f * 2048 + sc * 128 + 128]
                    rh = vhi[:, f * 256:(f + 1) * 256]
                    ins = nc.tensor.matmul(self.psb[bk][:, 0:256], ll, rh,
                                           start=False, stop=(f == 15))
                e_mm = TE.tag(ins)
                wt(nc.vector, e_mm, e_stage_lo)
                e_q = VE.tag(nc.vector.tensor_copy(vfp[:, 0:256],
                                                   self.psb[bk][:, 0:256]))
                e_bank[bk] = e_q
                wt(nc.scalar, e_q, e_spill)
                e_sh = AE.tag(nc.scalar.copy(vsh[:, 0:256], vfp[:, 0:256]))
                wt(nc.gpsimd, e_sh)
                e_stage_lo = GE.tag(nc.gpsimd.tensor_sub(
                    vsl[:, 0:256], vfp[:, 0:256], vsh[:, 0:256]))
                wt(nc.scalar, e_stage_lo)
                self.dma(nc.scalar, "spv", self.sv_hi[h, sc, :, :],
                         vsh[:, 0:256])
                e_spill = self.dma(nc.scalar, "spv", self.sv_lo[h, sc, :, :],
                                   vsl[:, 0:256])
            e_te_hl = [TE.evt(), TE.evt()]

        wt(nc.vector, self.e_p1_act)
        for mi in range(3):
            c0 = 64 + mi * 64
            VE.tag(nc.vector.reduce_sum(stt[:, 50 + mi:51 + mi],
                                        stt[:, c0:c0 + 64], axis=AX.X))
        e_ps = VE.evt()
        wt(nc.tensor, e_ps, e_bank[0], e_bank[1])
        ins = None
        for mi in range(3):
            ins = nc.tensor.matmul(self.psb[0][0:1, mi:mi + 1],
                                   self.cols[:, 60:61],
                                   stt[:, 50 + mi:51 + mi],
                                   start=(mi == 0), stop=(mi == 2))
        e_sc3 = TE.tag(ins)
        wt(nc.vector, e_sc3)
        VE.tag(nc.vector.tensor_scalar(stt[0:1, 53:56], self.psb[0][0:1, 0:3],
                                       1.0 / (E * DEX), EPS_Q, ALU.mult,
                                       ALU.add))
        VE.tag(nc.vector.tensor_scalar(stt[0:1, 56:59], stt[0:1, 53:56], 0.5,
                                       None, ALU.mult))
        e_fin = VE.tag(nc.vector.tensor_tensor(
            stt[0:1, 59:60], stt[0:1, 54:55], stt[0:1, 55:56], ALU.mult))
        self.e_bank2 = VE.evt()
        wt(nc.gpsimd, e_fin)
        for slot_i, col in [(SLOT_SUP, 53), (SLOT_SGD, 59), (SLOT_THR_UP, 56),
                            (SLOT_THR_GATE, 57), (SLOT_THR_DOWN, 58)]:
            self.e_srow_sc = self.dma(
                nc.gpsimd, "srow_sc",
                bass.AP(self.srow.tensor, SROW_SC + slot_i, [[1, 1], [1, 1]]),
                stt[0:1, col:col + 1])

    def phase3(self):
        nc = self.nc
        TE, VE, AE, GE = self.TE, self.VE, self.AE, self.GE
        self.barrier()
        stt = self.stats
        e_nc = self.dma(nc.sync, "ncol", self.cols[:, 0:1],
                        bass.AP(self.srow.tensor, SROW_SC + SLOT_NEGLAM,
                                [[0, 128], [1, 1]]))
        AR = [self.hfm_hi, self.hfm_lo]

        def hv(par, c0, w):
            return AR[par][:, c0:c0 + w]

        pfp = self.wview(F32, 2048, 0)
        phi = self.wview(BF16, 2048, 8192)
        plo = self.wview(BF16, 2048, 12288)
        ptb_hi = self.wview(BF16, 2048, 16384)
        ptb_lo = self.wview(BF16, 2048, 20480)
        afp = self.wview(F32, 4096, 24576)
        abuf_hi = self.wview(BF16, 4096, 40960)
        abuf_lo = self.wview(BF16, 4096, 49152)

        def load_head(h):
            par = h % 2
            evs = []
            for i, src in enumerate([self.sq_hi, self.sq_lo,
                                     self.sk_hi, self.sk_lo]):
                for br in range(2):
                    m = 2 * h + br
                    evs.append(self.dma(
                        nc.sync, f"qk{par}",
                        hv(par, (2 * i + br) * 2048, 2048),
                        src[m, :, :]))
            for i, src in enumerate([self.sv_hi, self.sv_lo]):
                evs.append(self.dma(
                    nc.sync, f"qk{par}",
                    hv(par, 16384 + i * 4096, 4096),
                    bass.AP(src.tensor, h * NCH * 128 * 256,
                            [[256, 128], [128 * 256, 16], [1, 256]])))
            return evs[-1]

        e_ld = [None, None]
        e_ld[0] = load_head(0)
        e_arena_free = [None, None]
        e_bank_exp = [None, None, None, None]
        e_bank_mask = [None] * 4
        e_pst_ev = [None, None]
        e_o_read = None
        e_spill_a = None
        e_pfp_free = None
        e_plo = None
        for h in range(NH):
            par = h % 2
            if h + 1 < NH:
                wt(nc.sync, e_arena_free[1 - par])
                e_ld[1 - par] = load_head(h + 1)
            q_of = [0, 2048]
            k_of = [8192, 10240]
            for qc in range(NCH):
                nfull = qc // 4
                rem = qc % 4
                remw = (rem + 1) * 128
                for bi in range(2):
                    qhi = hv(par, q_of[bi], 2048)[:, qc * 128:(qc + 1) * 128]
                    qlo = hv(par, q_of[bi] + 4096, 2048)[:, qc * 128:(qc + 1) * 128]
                    khi = hv(par, k_of[bi], 2048)
                    klo = hv(par, k_of[bi] + 4096, 2048)
                    wt(nc.tensor, e_ld[par])
                    for kt in range(nfull + 1):
                        w = 512 if kt < nfull else remw
                        bank = self.psb[kt]
                        wt(nc.tensor, e_bank_exp[kt], e_bank_mask[kt])
                        c0 = kt * 512
                        nc.tensor.matmul(bank[:, 0:w], qhi,
                                         khi[:, c0:c0 + w],
                                         start=True, stop=False)
                        nc.tensor.matmul(bank[:, 0:w], qhi,
                                         klo[:, c0:c0 + w],
                                         start=False, stop=False)
                        ins = nc.tensor.matmul(bank[:, 0:w], qlo,
                                               khi[:, c0:c0 + w],
                                               start=False, stop=True)
                        e_mm = TE.tag(ins)
                        e_bank_exp[kt] = None
                        e_bank_mask[kt] = None
                    wt(nc.vector, e_mm, self.e_consts)
                    e_msk = VE.tag(nc.vector.tensor_tensor(
                        self.psb[nfull][:, remw - 128:remw],
                        self.psb[nfull][:, remw - 128:remw],
                        self.mask[:, :], ALU.add))
                    e_bank_mask[nfull] = e_msk
                    for kt in range(nfull + 1):
                        w = 512 if kt < nfull else remw
                        VE.tag(nc.vector.tensor_reduce(
                            stt[:, kt:kt + 1], self.psb[kt][:, 0:w],
                            axis=AX.X, op=ALU.max))
                    e_mx = VE.tag(nc.vector.tensor_reduce(
                        stt[:, 5:6], stt[:, 0:nfull + 1], axis=AX.X,
                        op=ALU.max, negate=True))
                    wt(nc.scalar, e_mx, e_pfp_free, e_plo)
                    for kt in range(nfull + 1):
                        w = 512 if kt < nfull else remw
                        e_exp = AE.tag(nc.scalar.activation(
                            pfp[:, kt * 512:kt * 512 + w],
                            self.psb[kt][:, 0:w], AF.Exp,
                            bias=stt[:, 5:6],
                            accum_out=stt[:, 8 + kt:9 + kt]))
                        e_bank_exp[kt] = e_exp
                    kact = qc * 128 + 128
                    wt(nc.vector, e_exp)
                    VE.tag(nc.vector.reduce_sum(stt[:, 13:14],
                                                stt[:, 8:9 + nfull], axis=AX.X))
                    e_zr = VE.tag(nc.vector.reciprocal(
                        stt[:, 14 + bi:15 + bi], stt[:, 13:14]))
                    if self.want("attn0") and h == 0 and qc == 1:
                        tpp = self.dbg_dram(f"pfp{bi}", [128, 256])
                        tzz = self.dbg_dram(f"zst{bi}", [128, 24])
                        wt(nc.gpsimd, e_zr)
                        self.dma(nc.gpsimd, "dbg", tpp[:, :], pfp[:, 0:256])
                        e_dbg = self.dma(nc.gpsimd, "dbg", tzz[:, :],
                                         stt[:, 0:24])
                        wt(nc.vector, e_dbg)
                        wt(nc.scalar, e_dbg)
                    e_phi = VE.tag(nc.vector.tensor_copy(phi[:, 0:kact],
                                                         pfp[:, 0:kact]))
                    wt(nc.gpsimd, e_phi)
                    e_plo = GE.tag(nc.gpsimd.tensor_sub(
                        plo[:, 0:kact], pfp[:, 0:kact], phi[:, 0:kact]))
                    e_pfp_free = e_plo
                    nkc = qc + 1
                    for src, dstv, e_src in ((phi, ptb_hi, e_phi),
                                             (plo, ptb_lo, e_plo)):
                        for g in range(0, nkc, 8):
                            gb = min(8, nkc - g)
                            tb = (g // 8) % 2
                            wt(nc.tensor, e_pst_ev[tb], e_src, e_spill_a)
                            for jj in range(gb):
                                kc = g + jj
                                ins = nc.tensor.transpose(
                                    self.pst[tb][:, jj * 128:(jj + 1) * 128],
                                    src[:, kc * 128:(kc + 1) * 128],
                                    self.ident[:, :])
                            e_t = TE.tag(ins)
                            wt(nc.vector, e_t)
                            outap = bass.AP(
                                dstv.tensor, dstv.offset + g * 128,
                                [[28672, 128], [1, gb * 128]])
                            inap = bass.AP(self.pst[tb], 0,
                                           [[1024, 128], [1, gb * 128]])
                            e_pst_ev[tb] = VE.tag(
                                nc.vector.tensor_copy(outap, inap))
                    e_ptb = e_pst_ev
                    ops = self.ps6 if bi == 0 else self.ps7
                    wt(nc.tensor, e_pst_ev[0], e_pst_ev[1], e_o_read)
                    ins = None
                    for kc in range(nkc):
                        vh = hv(par, 16384, 4096)[:, kc * 256:(kc + 1) * 256]
                        vl = hv(par, 20480, 4096)[:, kc * 256:(kc + 1) * 256]
                        ph_ = ptb_hi[:, kc * 128:(kc + 1) * 128]
                        pl_ = ptb_lo[:, kc * 128:(kc + 1) * 128]
                        nc.tensor.matmul(ops[:, 0:256], ph_, vh,
                                         start=(kc == 0), stop=False)
                        nc.tensor.matmul(ops[:, 0:256], ph_, vl,
                                         start=False, stop=False)
                        ins = nc.tensor.matmul(ops[:, 0:256], pl_, vh,
                                               start=False,
                                               stop=(kc == nkc - 1))
                    e_pv = TE.tag(ins)
                    if bi == 1:
                        e_pv2 = e_pv
                wt(nc.vector, e_pv2, e_nc)
                e_nlz = VE.tag(nc.vector.tensor_tensor(
                    stt[:, 16:17], self.cols[:, 0:1], stt[:, 15:16],
                    ALU.mult))
                VE.tag(nc.vector.tensor_scalar(
                    pfp[:, 0:256], self.ps6[:, 0:256], stt[:, 14:15], None,
                    ALU.mult))
                e_comb = VE.tag(nc.vector.scalar_tensor_tensor(
                    afp[:, qc * 256:(qc + 1) * 256], self.ps7[:, 0:256],
                    stt[:, 16:17], pfp[:, 0:256], ALU.mult, ALU.add))
                e_o_read = e_comb
                e_pfp_free = e_comb
                if self.want("attn0") and h == 0 and qc == 1:
                    to_ = self.dbg_dram("acomb", [128, 256])
                    to6 = self.dbg_dram("o6", [128, 256])
                    to7 = self.dbg_dram("o7", [128, 256])
                    wt(nc.vector, e_comb)
                    e_c6 = VE.tag(nc.vector.tensor_copy(pfp[:, 512:768],
                                                        self.ps6[:, 0:256]))
                    e_c7 = VE.tag(nc.vector.tensor_copy(pfp[:, 768:1024],
                                                        self.ps7[:, 0:256]))
                    wt(nc.gpsimd, e_c7)
                    self.dma(nc.gpsimd, "dbg", to_[:, :],
                             afp[:, qc * 256:(qc + 1) * 256])
                    self.dma(nc.gpsimd, "dbg", to6[:, :], pfp[:, 512:768])
                    e_dbg2 = self.dma(nc.gpsimd, "dbg", to7[:, :],
                                      pfp[:, 768:1024])
                    wt(nc.vector, e_dbg2)
                    wt(nc.scalar, e_dbg2)
                wt(nc.scalar, e_comb)
                AE.tag(nc.scalar.activation(
                    pfp[:, 256:512], afp[:, qc * 256:(qc + 1) * 256],
                    AF.Square, accum_out=stt[:, 160 + qc:161 + qc]))
            e_ss = self.AE.evt()
            wt(nc.vector, e_ss)
            VE.tag(nc.vector.tensor_scalar(stt[:, 176:192],
                                           stt[:, 160:176], 1.0 / 256,
                                           EPS_LN, ALU.mult, ALU.add))
            e_u = VE.tag(nc.vector.tensor_copy(stt[:, 144:160],
                                               stt[:, 176:192]))
            wt(nc.scalar, e_u)
            e_sq = AE.tag(nc.scalar.activation(stt[:, 176:192],
                                               stt[:, 144:160], AF.Sqrt))
            wt(nc.vector, e_sq)
            VE.tag(nc.vector.reciprocal(stt[:, 176:192], stt[:, 176:192]))
            VE.tag(nc.vector.tensor_tensor(stt[:, 128:144], stt[:, 176:192],
                                           stt[:, 176:192], ALU.mult))
            VE.tag(nc.vector.tensor_tensor(stt[:, 128:144], stt[:, 128:144],
                                           stt[:, 144:160], ALU.mult))
            VE.tag(nc.vector.tensor_scalar(stt[:, 128:144], stt[:, 128:144],
                                           -0.5, 1.5, ALU.mult, ALU.add))
            e_rst = VE.tag(nc.vector.tensor_tensor(
                stt[:, 176:192], stt[:, 176:192], stt[:, 128:144], ALU.mult))
            e_ab = [None, None]
            for qc in range(NCH):
                VE.tag(nc.vector.tensor_scalar(
                    pfp[:, 0:256], afp[:, qc * 256:(qc + 1) * 256],
                    stt[:, 176 + qc:177 + qc], None, ALU.mult))
                wt(nc.vector, self.e_sw2, e_ab[0], e_ab[1])
                e_af = VE.tag(nc.vector.tensor_tensor(
                    pfp[:, 1024:1280], pfp[:, 0:256], self.sw2[:, :],
                    ALU.mult))
                wt(nc.scalar, e_af)
                e_h_ = AE.tag(nc.scalar.copy(
                    abuf_hi[:, qc * 256:(qc + 1) * 256], pfp[:, 1024:1280]))
                wt(nc.gpsimd, e_h_)
                e_l_ = GE.tag(nc.gpsimd.tensor_sub(
                    abuf_lo[:, qc * 256:(qc + 1) * 256], pfp[:, 1024:1280],
                    abuf_hi[:, qc * 256:(qc + 1) * 256]))
                e_ab = [e_h_, e_l_]
                e_pfp_free = e_l_
            for srcv, dstv, sh in ((abuf_hi, ptb_hi, self.sa_hi),
                                   (abuf_lo, ptb_lo, self.sa_lo)):
                e_src = e_ab[0] if srcv is abuf_hi else e_ab[1]
                for eb in range(2):
                    for g in range(0, NCH, 8):
                        tb = (g // 8) % 2
                        wt(nc.tensor, e_pst_ev[tb], e_src, e_spill_a)
                        for jj in range(8):
                            qc = g + jj
                            ins = nc.tensor.transpose(
                                self.pst[tb][:, jj * 128:(jj + 1) * 128],
                                srcv[:, qc * 256 + eb * 128:
                                     qc * 256 + eb * 128 + 128],
                                self.ident[:, :])
                        e_t = TE.tag(ins)
                        wt(nc.vector, e_t)
                        outap = bass.AP(dstv.tensor,
                                        dstv.offset + g * 128,
                                        [[28672, 128], [1, 1024]])
                        inap = bass.AP(self.pst[tb], 0,
                                       [[1024, 128], [1, 1024]])
                        e_pst_ev[tb] = VE.tag(
                            nc.vector.tensor_copy(outap, inap))
                    wt(nc.scalar, e_pst_ev[0], e_pst_ev[1])
                    e_spill_a = self.dma(
                        nc.scalar, "spa",
                        bass.AP(sh.tensor, (2 * h + eb) * 128 * 128,
                                [[128, 128], [NCH * 128 * 128, 16],
                                 [1, 128]]),
                        bass.AP(dstv.tensor, dstv.offset,
                                [[28672, 128], [128, 16], [1, 128]]))
            e_arena_free[par] = TE.evt()
        self.e_attn_done = e_spill_a
    def phase4(self):
        nc = self.nc
        TE, VE, AE, GE = self.TE, self.VE, self.AE, self.GE
        self.barrier()
        stt = self.stats

        e_g = self.dma(nc.sync, "bc0", self.bc0[:, :],
                       bass.AP(self.srow.tensor, SROW_MOD + 2 * E,
                               [[0, 128], [1, E]]))
        t1 = self.wview(F32, 2048, 24576)
        t2 = self.wview(F32, 2048, 32768)
        wt(nc.vector, e_g)
        VE.tag(nc.vector.tensor_scalar(t1[:, :], self.bc0[:, :], 0.25, None,
                                       ALU.max))
        VE.tag(nc.vector.tensor_scalar(t2[:, :], self.bc0[:, :], -0.25, None,
                                       ALU.min))
        VE.tag(nc.vector.tensor_scalar(self.bc0[:, :], self.bc0[:, :], 0.0,
                                       None, ALU.is_ge))
        VE.tag(nc.vector.tensor_tensor(t1[:, :], t1[:, :], t2[:, :],
                                       ALU.subtract))
        VE.tag(nc.vector.tensor_tensor(t1[:, :], self.bc0[:, :], t1[:, :],
                                       ALU.mult))
        e_ga = VE.tag(nc.vector.tensor_tensor(self.bc0[:, :], t2[:, :],
                                              t1[:, :], ALU.add))

        woc = self.wview(F32, 2048, 16384)
        e_wsp = [None, None]
        for wc in range(NCH):
            wt(nc.sync, e_wsp[0], e_wsp[1])
            e_ld = self.dma(nc.sync, "woc", woc[:, :],
                            self.wo[wc * 128:(wc + 1) * 128, :])
            wt(nc.scalar, e_ld)
            e_hi = AE.tag(nc.scalar.copy(
                self.hfm_hi[:, wc * 2048:(wc + 1) * 2048], woc[:, :]))
            wt(nc.gpsimd, e_hi)
            e_lo = GE.tag(nc.gpsimd.tensor_sub(
                self.hfm_lo[:, wc * 2048:(wc + 1) * 2048], woc[:, :],
                self.hfm_hi[:, wc * 2048:(wc + 1) * 2048]))
            e_wsp = [e_hi, e_lo]

        ab_hi = [self.wview(BF16, 2048, 0), self.wview(BF16, 2048, 8192)]
        ab_lo = [self.wview(BF16, 2048, 4096), self.wview(BF16, 2048, 12288)]
        xb = [self.wview(F32, 2048, 24576), self.wview(F32, 2048, 32768)]
        x1b = [self.wview(F32, 2048, 40960), self.wview(F32, 2048, 49152)]
        ot = self.wview(F32, 512, 16384)
        e_a_used = [None, None]
        e_x_used = [None, None]
        e_bank = [None, None]
        e_x1_done = [None, None]
        e_sx = [None, None]
        e_sq_prev = None
        for sc in range(NCH):
            par = sc % 2
            wt(nc.sync, e_a_used[par])
            self.dma(nc.sync, f"pa{par}",
                     bass.AP(ab_hi[par].tensor, ab_hi[par].offset,
                             [[28672, 128], [128, 16], [1, 128]]),
                     bass.AP(self.sa_hi.tensor, sc * 16 * 128 * 128,
                             [[128, 128], [128 * 128, 16], [1, 128]]))
            e_a = self.dma(nc.sync, f"pa{par}",
                           bass.AP(ab_lo[par].tensor, ab_lo[par].offset,
                                   [[28672, 128], [128, 16], [1, 128]]),
                           bass.AP(self.sa_lo.tensor, sc * 16 * 128 * 128,
                                   [[128, 128], [128 * 128, 16], [1, 128]]))
            wt(nc.sync, e_x_used[par], e_ga)
            e_x = self.dma(nc.sync, f"px{par}", xb[par][:, :],
                           self.xin[sc * 128:(sc + 1) * 128, :])
            wt(nc.vector, e_x1_done[par])
            for ft in range(4):
                bk = ft % 2
                wt(nc.tensor, e_a, e_wsp[0], e_wsp[1], e_bank[bk])
                ins = None
                for af in range(16):
                    lh = ab_hi[par][:, af * 128:(af + 1) * 128]
                    rh = self.hfm_hi[:, af * 2048 + ft * 512:
                                     af * 2048 + ft * 512 + 512]
                    rl = self.hfm_lo[:, af * 2048 + ft * 512:
                                     af * 2048 + ft * 512 + 512]
                    nc.tensor.matmul(self.psb[bk][:, :], lh, rh,
                                     start=(af == 0), stop=False)
                    nc.tensor.matmul(self.psb[bk][:, :], lh, rl,
                                     start=False, stop=False)
                for af in range(16):
                    ll = ab_lo[par][:, af * 128:(af + 1) * 128]
                    rh = self.hfm_hi[:, af * 2048 + ft * 512:
                                     af * 2048 + ft * 512 + 512]
                    ins = nc.tensor.matmul(self.psb[bk][:, :], ll, rh,
                                           start=False, stop=(af == 15))
                e_mm = TE.tag(ins)
                wt(nc.vector, e_mm, e_ga, e_x)
                if ft == 0:
                    wt(nc.vector, e_sx[par], e_sq_prev)
                VE.tag(nc.vector.tensor_tensor(
                    ot[:, :], self.psb[bk][:, :],
                    self.bc0[:, ft * 512:(ft + 1) * 512], ALU.mult))
                e_ev = VE.tag(nc.vector.tensor_tensor(
                    x1b[par][:, ft * 512:(ft + 1) * 512], ot[:, :],
                    xb[par][:, ft * 512:(ft + 1) * 512], ALU.add))
                e_bank[bk] = e_ev
            e_a_used[par] = TE.evt()
            e_x_used[par] = e_ev
            e_r = self.emit_stats(x1b[par][:, :], stt[:, 18 + sc:19 + sc],
                                  stt[:, 34 + sc:35 + sc], [e_ev], par=par)
            e_x1_done[par] = e_r
            e_sq_prev = self.AE.evt()
            wt(nc.scalar, e_ev)
            e_sx[par] = self.dma(nc.scalar, f"sx{par}", self.sx1[sc, :, :],
                                 x1b[par][:, :])
        self.e_p4 = VE.evt()

    def phase5(self):
        nc = self.nc
        VE, AE, GE = self.VE, self.AE, self.GE
        self.barrier()
        self.pst_guard = [None, None]
        stt = self.stats
        e_b0 = self.dma(nc.sync, "bc0", self.bc0[:, :],
                        bass.AP(self.srow.tensor, SROW_MOD + 4 * E,
                                [[0, 128], [1, E]]))
        e_b1 = self.dma(nc.sync, "bc1", self.bc1[:, :],
                        bass.AP(self.srow.tensor, SROW_MOD + 3 * E,
                                [[0, 128], [1, E]]))
        wt(nc.vector, e_b0)
        e_bc = VE.tag(nc.vector.tensor_scalar(
            self.bc0[:, :], self.bc0[:, :], 1.0, None, ALU.add))
        xb = [self.wview(F32, 2048, 0), self.wview(F32, 2048, 8192)]
        zb = self.wview(F32, 2048, 24576)
        hfp = self.wview(F32, 2048, 32768)
        hhi = self.wview(BF16, 2048, 40960)
        hlo = self.wview(BF16, 2048, 45056)
        e_used = [None, None]
        e_hi_prev = e_lo_prev = None
        e_te_hi = e_te_lo = None
        for sc in range(NCH):
            par = sc % 2
            wt(nc.sync, e_used[par])
            e_x = self.dma(nc.sync, f"x{par}", xb[par][:, :],
                           self.sx1[sc, :, :])
            wt(nc.vector, e_x)
            e_z = VE.tag(nc.vector.tensor_scalar(
                zb[:, :], xb[par][:, :], stt[:, 18 + sc:19 + sc],
                stt[:, 34 + sc:35 + sc], ALU.subtract, ALU.mult))
            e_used[par] = e_z
            wt(nc.vector, e_bc, e_b1, e_hi_prev, e_lo_prev)
            VE.tag(nc.vector.tensor_tensor(hfp[:, :], zb[:, :],
                                           self.bc0[:, :], ALU.mult))
            e_h = VE.tag(nc.vector.tensor_tensor(hfp[:, :], hfp[:, :],
                                                 self.bc1[:, :], ALU.add))
            wt(nc.scalar, e_h, e_te_hi)
            e_hi = AE.tag(nc.scalar.copy(hhi[:, :], hfp[:, :]))
            wt(nc.gpsimd, e_hi, e_te_lo)
            e_lo = GE.tag(nc.gpsimd.tensor_sub(hlo[:, :], hfp[:, :],
                                               hhi[:, :]))
            e_te_hi, _ = self.emit_fm_store(hhi, self.hfm_hi, sc, e_hi)
            e_te_lo, _ = self.emit_fm_store(hlo, self.hfm_lo, sc, e_lo)
            e_hi_prev, e_lo_prev = e_hi, e_lo
        if self.want("h2fm"):
            t1 = self.dbg_dram("h2fm_hi", [128, 32768])
            t2 = self.dbg_dram("h2fm_lo", [128, 32768])
            wt(nc.gpsimd, VE.evt())
            self.dma(nc.gpsimd, "dbg", t1[:, :], self.hfm_hi[:, :])
            self.dma(nc.gpsimd, "dbg", t2[:, :], self.hfm_lo[:, :])

    def phase6(self):
        nc = self.nc
        TE, VE, AE, GE = self.TE, self.VE, self.AE, self.GE
        self.barrier()
        e_c1 = self.dma(nc.sync, "ncol", self.cols[:, 1:2],
                        bass.AP(self.srow.tensor, SROW_SC + SLOT_THR_UP,
                                [[0, 128], [1, 1]]))
        e_c2 = self.dma(nc.sync, "ncol", self.cols[:, 2:3],
                        bass.AP(self.srow.tensor, SROW_SC + SLOT_THR_GATE,
                                [[0, 128], [1, 1]]))
        e_c3 = self.dma(nc.sync, "ncol", self.cols[:, 3:4],
                        bass.AP(self.srow.tensor, SROW_SC + SLOT_SUP,
                                [[0, 128], [1, 1]]))
        wt(nc.vector, e_c3)
        e_neg = VE.tag(nc.vector.tensor_scalar(
            self.cols[:, 4:6], self.cols[:, 1:3], -1.0, None, ALU.mult))

        wus = self.wview(F32, 2048, 0)
        wgs = self.wview(F32, 2048, 8192)
        tu = [self.wview(BF16, 2048, 16384), self.wview(BF16, 2048, 20480)]
        tg = [self.wview(BF16, 2048, 24576), self.wview(BF16, 2048, 28672)]
        m2 = self.wview(BF16, 2048, 32768)
        silu = self.wview(F32, 512, 36864)
        hfp = self.wview(F32, 2048, 40960)
        hh = self.wview(BF16, 2048, 49152)
        hl = self.wview(BF16, 2048, 53248)
        e_q = [None, None]
        e_ld_used = None
        e_hid = None
        e_spill = None
        e_hf_prev = None
        e_bank = [None, None, None, None]
        for ec in range(ECH):
            par = ec % 2
            wt(nc.sync, e_ld_used)
            e_lu = self.dma(nc.sync, "wus",
                            self.wap3(F32, 0, [[128, 16], [1, 128]]),
                            bass.AP(self.wupT.tensor, ec * 128,
                                    [[DEX, 128], [DEX * 128, 16], [1, 128]]))
            e_lg = self.dma(nc.sync, "wgs",
                            self.wap3(F32, 8192, [[128, 16], [1, 128]]),
                            bass.AP(self.wgateT.tensor, ec * 128,
                                    [[DEX, 128], [DEX * 128, 16], [1, 128]]))
            wt(nc.gpsimd, e_lu, e_neg, e_q[par])
            g1 = GE.tag(nc.gpsimd.tensor_scalar(
                m2[:, :], wus[:, :], self.cols[:, 4:5], None, ALU.is_le))
            wt(nc.vector, g1, e_c3)
            v1 = VE.tag(nc.vector.scalar_tensor_tensor(
                tu[par][:, :], wus[:, :], self.cols[:, 1:2], m2[:, :],
                ALU.is_ge, ALU.subtract))
            wt(nc.gpsimd, e_lg, v1)
            g2 = GE.tag(nc.gpsimd.tensor_scalar(
                m2[:, :], wgs[:, :], self.cols[:, 5:6], None, ALU.is_le))
            wt(nc.vector, g2, e_c3)
            e_ld_used = VE.tag(nc.vector.scalar_tensor_tensor(
                tg[par][:, :], wgs[:, :], self.cols[:, 2:3], m2[:, :],
                ALU.is_ge, ALU.subtract))
            e_tq = e_ld_used
            for st in range(4):
                bu = st % 2
                bg = 2 + st % 2
                wt(nc.tensor, e_tq, e_bank[bu], e_bank[bg])
                ins = None
                for d in range(16):
                    lt_ = tu[par][:, d * 128:(d + 1) * 128]
                    nc.tensor.matmul(
                        self.psb[bu][:, :], lt_,
                        self.hfm_hi[:, d * 2048 + st * 512:
                                    d * 2048 + st * 512 + 512],
                        start=(d == 0), stop=False)
                    ins = nc.tensor.matmul(
                        self.psb[bu][:, :], lt_,
                        self.hfm_lo[:, d * 2048 + st * 512:
                                    d * 2048 + st * 512 + 512],
                        start=False, stop=(d == 15))
                e_mu = TE.tag(ins)
                for d in range(16):
                    lt_ = tg[par][:, d * 128:(d + 1) * 128]
                    nc.tensor.matmul(
                        self.psb[bg][:, :], lt_,
                        self.hfm_hi[:, d * 2048 + st * 512:
                                    d * 2048 + st * 512 + 512],
                        start=(d == 0), stop=False)
                    ins = nc.tensor.matmul(
                        self.psb[bg][:, :], lt_,
                        self.hfm_lo[:, d * 2048 + st * 512:
                                    d * 2048 + st * 512 + 512],
                        start=False, stop=(d == 15))
                e_mg = TE.tag(ins)
                wt(nc.scalar, e_mu, e_hid, e_c3, e_hf_prev)
                e_si = AE.tag(nc.scalar.activation(
                    silu[:, :], self.psb[bu][:, :], AF.Silu,
                    scale=self.cols[:, 3:4]))
                e_bank[bu] = e_si
                wt(nc.vector, e_si, e_mg)
                e_hf = VE.tag(nc.vector.tensor_tensor(
                    hfp[:, st * 512:(st + 1) * 512], silu[:, :],
                    self.psb[bg][:, :], ALU.mult))
                e_bank[bg] = e_hf
                e_hf_prev = e_hf
            wt(nc.scalar, e_hf, e_spill)
            e_hh = AE.tag(nc.scalar.copy(hh[:, :], hfp[:, :]))
            wt(nc.gpsimd, e_hh)
            e_hid = GE.tag(nc.gpsimd.tensor_sub(hl[:, :], hfp[:, :],
                                                hh[:, :]))
            wt(nc.scalar, e_hid)
            self.dma(nc.scalar, "sph", self.shid_hi[ec, :, :], hh[:, :])
            e_spill = self.dma(nc.scalar, "sph", self.shid_lo[ec, :, :],
                               hl[:, :])
            e_q[par] = TE.evt()
        self.e_p6 = e_spill

    def phase7(self):
        nc = self.nc
        TE, VE, AE, GE = self.TE, self.VE, self.AE, self.GE
        self.barrier()
        stt = self.stats
        yA = self.arenaA.bitcast(F32)
        hb = self.arenaB.bitcast(BF16)
        e_g = self.dma(nc.sync, "bc0", self.bc0[:, :],
                       bass.AP(self.srow.tensor, SROW_MOD + 5 * E,
                               [[0, 128], [1, E]]))
        e_sg = self.dma(nc.sync, "ncol", self.cols[:, 7:8],
                        bass.AP(self.srow.tensor, SROW_SC + SLOT_SGD,
                                [[0, 128], [1, 1]]))
        e_c4 = self.dma(nc.sync, "ncol", self.cols[:, 8:9],
                        bass.AP(self.srow.tensor, SROW_SC + SLOT_THR_DOWN,
                                [[0, 128], [1, 1]]))
        t1 = self.wview(F32, 2048, 0)
        t2 = self.wview(F32, 2048, 8192)
        wt(nc.vector, e_g, e_c4)
        VE.tag(nc.vector.tensor_scalar(t1[:, :], self.bc0[:, :], 0.25, None,
                                       ALU.max))
        VE.tag(nc.vector.tensor_scalar(t2[:, :], self.bc0[:, :], -0.25, None,
                                       ALU.min))
        VE.tag(nc.vector.tensor_scalar(self.bc0[:, :], self.bc0[:, :], 0.0,
                                       None, ALU.is_ge))
        VE.tag(nc.vector.tensor_tensor(t1[:, :], t1[:, :], t2[:, :],
                                       ALU.subtract))
        VE.tag(nc.vector.tensor_tensor(t1[:, :], self.bc0[:, :], t1[:, :],
                                       ALU.mult))
        VE.tag(nc.vector.tensor_tensor(self.bc0[:, :], t2[:, :], t1[:, :],
                                       ALU.add))
        VE.tag(nc.vector.tensor_scalar(self.cols[:, 9:10], self.cols[:, 8:9],
                                       -1.0, None, ALU.mult))
        e_ga = VE.tag(nc.vector.tensor_scalar(
            self.bc0[:, :], self.bc0[:, :], self.cols[:, 7:8], None,
            ALU.mult))

        tdc = self.wview(F32, 2048, 0)
        td = [self.wview(BF16, 2048, 8192 + 4096 * i) for i in range(8)]
        m2 = self.wview(BF16, 2048, 40960)
        ocb = self.wview(F32, 1024, 45056)
        x1h = [self.wview(F32, 1024, 49152), self.wview(F32, 1024, 53248)]
        e_bank = [None, None, None, None]
        e_hb_used = None
        e_td_used = [None] * 8
        e_out = None
        e_y_rd = None
        for half in range(2):
            e_ms = VE.tag(nc.vector.memset(yA[:, :], 0.0)) if half == 0 else \
                VE.tag(nc.vector.memset(yA[:, :], 0.0))
            for eb in range(8):
                wt(nc.sync, e_hb_used)
                for el in range(8):
                    self.dma(nc.sync, "hid",
                             hb[:, el * 2048:(el + 1) * 2048],
                             self.shid_hi[eb * 8 + el, :, :])
                    e_hb = self.dma(nc.sync, "hid",
                                    hb[:, 16384 + el * 2048:
                                       16384 + (el + 1) * 2048],
                                    self.shid_lo[eb * 8 + el, :, :])
                e_tq = None
                for el in range(8):
                    wt(nc.sync, e_td_used[el], e_tq, e_ga)
                    e_lt = self.dma(nc.sync, "tdc", tdc[:, :],
                                    self.wdownT[(eb * 8 + el) * 128:
                                                (eb * 8 + el + 1) * 128, :])
                    wt(nc.gpsimd, e_lt, e_ga)
                    g1 = GE.tag(nc.gpsimd.tensor_scalar(
                        m2[:, :], tdc[:, :], self.cols[:, 9:10], None,
                        ALU.is_le))
                    wt(nc.vector, g1, e_ms)
                    e_tq = VE.tag(nc.vector.scalar_tensor_tensor(
                        td[el][:, :], tdc[:, :], self.cols[:, 8:9], m2[:, :],
                        ALU.is_ge, ALU.subtract))
                    e_td_used[el] = e_tq
                last_ev = None
                for sc in range(NCH):
                    for dt_ in range(2):
                        bk = (sc * 2 + dt_) % 4
                        wt(nc.tensor, e_hb, e_tq, e_bank[bk])
                        ins = None
                        for el in range(8):
                            lh = hb[:, el * 2048 + sc * 128:
                                    el * 2048 + sc * 128 + 128]
                            ll = hb[:, 16384 + el * 2048 + sc * 128:
                                    16384 + el * 2048 + sc * 128 + 128]
                            rt = td[el][:, half * 1024 + dt_ * 512:
                                        half * 1024 + dt_ * 512 + 512]
                            nc.tensor.matmul(self.psb[bk][:, :], lh, rt,
                                             start=(el == 0), stop=False)
                            ins = nc.tensor.matmul(self.psb[bk][:, :], ll, rt,
                                                   start=False,
                                                   stop=(el == 7))
                        e_mm = TE.tag(ins)
                        wt(nc.vector, e_mm, e_y_rd)
                        e_acc = VE.tag(nc.vector.tensor_tensor(
                            yA[:, sc * 1024 + dt_ * 512:
                               sc * 1024 + dt_ * 512 + 512],
                            yA[:, sc * 1024 + dt_ * 512:
                               sc * 1024 + dt_ * 512 + 512],
                            self.psb[bk][:, :], ALU.add))
                        e_bank[bk] = e_acc
                        last_ev = e_acc
                e_hb_used = TE.evt()
                for el in range(8):
                    e_td_used[el] = e_hb_used
            e_x1u = [None, None]
            e_oc = None
            for sc in range(NCH):
                par = sc % 2
                wt(nc.sync, e_x1u[par])
                e_x1 = self.dma(nc.sync, f"x1h{par}", x1h[par][:, :],
                                bass.AP(self.sx1.tensor,
                                        sc * 128 * E + half * 1024,
                                        [[E, 128], [1, 1024]]))
                wt(nc.vector, e_x1, last_ev, e_out)
                VE.tag(nc.vector.tensor_tensor(
                    ocb[:, :],
                    yA[:, sc * 1024:sc * 1024 + 1024],
                    self.bc0[:, half * 1024:(half + 1) * 1024], ALU.mult))
                e_oc = VE.tag(nc.vector.tensor_tensor(
                    ocb[:, :], ocb[:, :], x1h[par][:, :], ALU.add))
                e_x1u[par] = e_oc
                wt(nc.scalar, e_oc)
                e_out = self.dma(
                    nc.scalar, "out",
                    bass.AP(self.out.tensor, sc * 128 * E + half * 1024,
                            [[E, 128], [1, 1024]]),
                    ocb[:, :])
            e_y_rd = e_oc

    def finish(self):
        nc = self.nc
        evs = [sl.evt() for sl in self._slots.values()]
        evs += [self.TE.evt(), self.VE.evt(), self.AE.evt()]
        wt(nc.gpsimd, *evs)


def build(dbg=(), upto=0):
    b = Builder(dbg=dbg, upto=upto)
    ncp = b.build()
    return b, ncp


from concourse.bass_utils import run_bass_kernel_spmd


_NC_CACHE = {}


def kernel(L0_attn_x, L0_attn_Wq, L0_attn_Wk, L0_attn_Wv, L0_attn_Wo,
           L0_attn_lambda_q1, L0_attn_lambda_k1, L0_attn_lambda_q2,
           L0_attn_lambda_k2, L0_attn_subln_weight, L0_ffn_W_up,
           L0_ffn_W_gate, L0_ffn_W_down, L0_c, L0_wmod):
    n = 8
    if "nc" not in _NC_CACHE:
        b = Builder(dbg=(), upto=7)
        _NC_CACHE["nc"] = b.build()
    ncp = _NC_CACHE["nc"]
    f32 = np.float32
    shared = {
        "wq": np.ascontiguousarray(L0_attn_Wq, dtype=f32),
        "wk": np.ascontiguousarray(L0_attn_Wk, dtype=f32),
        "wv": np.ascontiguousarray(L0_attn_Wv, dtype=f32),
        "wo": np.ascontiguousarray(L0_attn_Wo, dtype=f32),
        "wmod": np.ascontiguousarray(L0_wmod, dtype=f32),
        "wupT": np.ascontiguousarray(np.asarray(L0_ffn_W_up, dtype=f32).T),
        "wgateT": np.ascontiguousarray(np.asarray(L0_ffn_W_gate, dtype=f32).T),
        "wdownT": np.ascontiguousarray(np.asarray(L0_ffn_W_down, dtype=f32).T),
        "lq1": np.ascontiguousarray(L0_attn_lambda_q1, dtype=f32),
        "lk1": np.ascontiguousarray(L0_attn_lambda_k1, dtype=f32),
        "lq2": np.ascontiguousarray(L0_attn_lambda_q2, dtype=f32),
        "lk2": np.ascontiguousarray(L0_attn_lambda_k2, dtype=f32),
        "subw": np.ascontiguousarray(L0_attn_subln_weight, dtype=f32),
    }
    x = np.asarray(L0_attn_x, dtype=f32)
    c = np.asarray(L0_c, dtype=f32)
    in_maps = []
    for bb in range(n):
        m = dict(shared)
        m["xin"] = np.ascontiguousarray(x[bb])
        m["cvec"] = np.ascontiguousarray(c[bb])
        in_maps.append(m)
    res = run_bass_kernel_spmd(ncp, in_maps, core_ids=list(range(n)))
    return np.stack([np.asarray(r["out"]) for r in res.results], axis=0)


# version 02
# speedup vs baseline: 1.1637x; 1.1637x over previous
import contextlib
import numpy as np
import concourse.bass as bass
import concourse.mybir as mybir

F32 = mybir.dt.float32
BF16 = mybir.dt.bfloat16
U8 = mybir.dt.uint8
AF = mybir.ActivationFunctionType
ALU = mybir.AluOpType
AX = mybir.AxisListType

S = 2048
E = 2048
NCH = 16
NH = 8
HD = 128
DEX = 8192
ECH = 64
LAM_INIT = 0.8 - 0.6 * float(np.exp(-0.3 * 12))
SCALE = HD ** -0.5
EPS_LN = 1e-5
EPS_Q = 1e-8
MASKVAL = -1e30

SROW_MOD = 0
SROW_SC = 6 * E
SLOT_NEGLAM = 0
SLOT_SUP = 1
SLOT_SGD = 2
SLOT_THR_UP = 3
SLOT_THR_GATE = 4
SLOT_THR_DOWN = 5
NSLOT = 8


class Evt:
    __slots__ = ("sem", "v")

    def __init__(self, sem, v):
        self.sem = sem
        self.v = v


class ESem:

    def __init__(self, nc, es, name, eng, flush):
        self.sem = es.enter_context(nc.semaphore(name))
        self.n = 0
        self.eng = eng
        self.flush = flush

    def tag(self, ins):
        ins.then_inc(self.sem, 1)
        self.n += 1
        if self.flush:
            self.eng.wait_ge(self.sem, self.n)
        return Evt(self.sem, self.n)

    def bump(self):
        if self.n > 0:
            self.eng.wait_ge(self.sem, self.n)
        self.eng.nop().then_inc(self.sem, 1)
        self.n += 1
        return Evt(self.sem, self.n)

    def evt(self):
        return Evt(self.sem, self.n)


class Slot:

    def __init__(self, nc, es, name):
        self.sem = es.enter_context(nc.semaphore(name))
        self.n = 0

    def evt(self):
        return Evt(self.sem, self.n)


def wt(eng, *evts):
    for e in evts:
        if e is not None and e.v > 0:
            eng.wait_ge(e.sem, e.v)


class Builder:
    def __init__(self, dbg=(), upto=99):
        self.dbg_names = set(dbg)
        self.upto = upto
        nc = self.nc = bass.Bass("TRN2", target_bir_lowering=False)
        self.es = contextlib.ExitStack()

        def din(name, shape):
            return nc.dram_tensor(name, shape, F32, kind="ExternalInput").ap()

        self.xin = din("xin", [S, E])
        self.cvec = din("cvec", [E])
        self.wq = din("wq", [E, E])
        self.wk = din("wk", [E, E])
        self.wv = din("wv", [E, E])
        self.wo = din("wo", [E, E])
        self.wmod = din("wmod", [E, 6 * E])
        self.wupT = din("wupT", [E, DEX])
        self.wgateT = din("wgateT", [E, DEX])
        self.wdownT = din("wdownT", [DEX, E])
        self.lq1 = din("lq1", [HD])
        self.lk1 = din("lk1", [HD])
        self.lq2 = din("lq2", [HD])
        self.lk2 = din("lk2", [HD])
        self.subw = din("subw", [2 * HD])
        self.out = nc.dram_tensor("out", [S, E], F32, kind="ExternalOutput").ap()

        def scr(name, shape, dt=BF16):
            kind = "ExternalOutput" if ("spill" in self.dbg_names) else "Internal"
            return nc.dram_tensor(name, shape, dt, kind=kind).ap()

        self.srow = scr("srow", [SROW_SC + NSLOT], F32)
        self.sq_hi = scr("sq_hi", [NCH, 128, S])
        self.sq_lo = scr("sq_lo", [NCH, 128, S])
        self.sk_hi = scr("sk_hi", [NCH, 128, S])
        self.sk_lo = scr("sk_lo", [NCH, 128, S])
        self.sv_hi = scr("sv_hi", [NH, NCH, 128, 256])
        self.sv_lo = scr("sv_lo", [NH, NCH, 128, 256])
        self.sa_hi = scr("sa_hi", [NCH, NCH, 128, 128])
        self.sa_lo = scr("sa_lo", [NCH, NCH, 128, 128])
        self.shid_hi = scr("shid_hi", [ECH, 128, S])
        self.shid_lo = scr("shid_lo", [ECH, 128, S])
        self.sx1 = scr("sx1", [NCH, 128, E], F32)

        self.arenaA = nc.alloc_sbuf_tensor("arenaA", [128, 65536], U8)
        self.arenaB = nc.alloc_sbuf_tensor("arenaB", [128, 65536], U8)
        self.arenaW = nc.alloc_sbuf_tensor("arenaW", [128, 57344], U8)
        self.arenaD = nc.alloc_sbuf_tensor("arenaD", [128, 22528], U8)

        self.hfm_hi = self.arenaA.bitcast(BF16)
        self.hfm_lo = self.arenaB.bitcast(BF16)
        D = self.arenaD.bitcast(F32)
        Db = self.arenaD.bitcast(BF16)
        self.bc0 = D[:, 0:2048]
        self.bc1 = D[:, 2048:4096]
        self.sw2 = D[:, 4096:4352]
        self.mask = D[:, 4352:4480]
        self.ident = Db[:, 8960:9088]
        self.cols = D[:, 4544:4608]
        self.stats = D[:, 4608:4864]

        self.psb = [nc.alloc_psum_tensor(f"psb{i}", [128, 512], F32)
                    for i in range(4)]
        self.pst = [nc.alloc_psum_tensor(f"pst{i}", [128, 1024], BF16)
                    for i in range(2)]
        self.ps6 = nc.alloc_psum_tensor("ps6", [128, 512], F32)
        self.ps7 = nc.alloc_psum_tensor("ps7", [128, 512], F32)

        es = self.es
        self.TE = ESem(nc, es, "sTE", nc.tensor, flush=False)
        self.VE = ESem(nc, es, "sVE", nc.vector, flush=True)
        self.AE = ESem(nc, es, "sAE", nc.scalar, flush=True)
        self.GE = ESem(nc, es, "sGE", nc.gpsimd, flush=True)
        self.SE = ESem(nc, es, "sSE", nc.sync, flush=False)
        self.engines = [(nc.tensor, self.TE), (nc.vector, self.VE),
                        (nc.scalar, self.AE), (nc.gpsimd, self.GE),
                        (nc.sync, self.SE)]
        self._slots = {}
        self.dbg_tensors = {}

    def slot(self, name):
        if name not in self._slots:
            self._slots[name] = Slot(self.nc, self.es, "sl_" + name)
        return self._slots[name]

    def dma(self, eng, slotname, out, in_, **kw):
        sl = self.slot(slotname)
        eng.dma_start(out=out, in_=in_, **kw).then_inc(sl.sem, 16)
        sl.n += 16
        return Evt(sl.sem, sl.n)

    def barrier(self):
        for eng, es_ in self.engines:
            es_.bump()
        evs = [es_.evt() for _, es_ in self.engines]
        evs += [sl.evt() for sl in self._slots.values()]
        for eng, _ in self.engines:
            wt(eng, *evs)

    def want(self, name):
        return name in self.dbg_names

    def dbg_dram(self, name, shape, dt=F32):
        t = self.nc.dram_tensor("dbg_" + name, list(shape), dt,
                                kind="ExternalOutput").ap()
        self.dbg_tensors[name] = t
        return t

    def wview(self, dt, cols, off_bytes):
        a = self.arenaW.bitcast(dt)
        isz = {F32: 4, BF16: 2, U8: 1}[dt]
        assert off_bytes % isz == 0
        c0 = off_bytes // isz
        assert (c0 + cols) * isz <= 57344, (off_bytes, cols)
        return a[:, c0:c0 + cols]

    def wap3(self, dt, off_bytes, dims):
        h = self.arenaW.bitcast(dt)
        isz = {F32: 4, BF16: 2, U8: 1}[dt]
        width = 57344 // isz
        assert off_bytes % isz == 0
        return bass.AP(h, off_bytes // isz, [[width, 128]] + list(dims))

    def build(self):
        nc = self.nc
        with self.es, nc.Block():
            with nc.named_scope("p0mod"):
                self.phase0()
            if self.upto >= 1:
                with nc.named_scope("p1ln1"):
                    self.phase1()
            if self.upto >= 2:
                with nc.named_scope("p2qkv"):
                    self.phase2()
            if self.upto >= 3:
                with nc.named_scope("p3attn"):
                    self.phase3()
            if self.upto >= 4:
                with nc.named_scope("p4wo"):
                    self.phase4()
            if self.upto >= 5:
                with nc.named_scope("p5ln2"):
                    self.phase5()
            if self.upto >= 6:
                with nc.named_scope("p6upgate"):
                    self.phase6()
            if self.upto >= 7:
                with nc.named_scope("p7down"):
                    self.phase7()
            self.finish()
        return nc

    def phase0(self):
        nc = self.nc
        TE, VE, AE, GE = self.TE, self.VE, self.AE, self.GE

        cfm = self.wview(F32, 16, 0)
        lv = self.wview(F32, 512, 256)
        lt = self.wview(F32, 16, 2304)
        mrow = [self.wview(F32, 512, 3584), self.wview(F32, 512, 5632)]
        wmb = [self.wview(F32, 6144, 8192), self.wview(F32, 6144, 32768)]

        GE.tag(nc.gpsimd.memset(self.stats[:, :], 0.0))
        GE.tag(nc.gpsimd.memset(self.cols[:, :], 0.0))
        GE.tag(nc.gpsimd.memset(self.cols[:, 60:64], 1.0))
        GE.tag(nc.gpsimd.memset(self.ident[:, :], 0.0))
        GE.tag(nc.gpsimd.affine_select(
            out=self.ident[:, :], in_=self.ident[:, :],
            compare_op=ALU.not_equal, fill=1.0, base=0,
            pattern=[[-1, 128]], channel_multiplier=1))
        GE.tag(nc.gpsimd.memset(self.mask[:, :], 0.0))
        self.e_consts = GE.tag(nc.gpsimd.affine_select(
            out=self.mask[:, :], in_=self.mask[:, :],
            compare_op=ALU.is_ge, fill=MASKVAL, base=0,
            pattern=[[-1, 128]], channel_multiplier=1))

        self.dma(nc.sync, "lam", lv[0:1, 0:128], self.lq1[None, :])
        self.dma(nc.sync, "lam", lv[0:1, 128:256], self.lk1[None, :])
        self.dma(nc.sync, "lam", lv[0:1, 256:384], self.lq2[None, :])
        e_l = self.dma(nc.sync, "lam", lv[0:1, 384:512], self.lk2[None, :])
        e_c = self.dma(nc.sync, "cfm", cfm[:, 0:16],
                       bass.AP(self.cvec.tensor, 0, [[1, 128], [128, 16]]),
                       allow_slow_non_contiguous=True)
        e_swb = self.dma(nc.sync, "swb", self.sw2[:, :],
                         bass.AP(self.subw.tensor, 0, [[0, 128], [1, 256]]))
        wt(nc.vector, e_swb)
        self.e_sw2 = VE.tag(nc.vector.tensor_scalar(
            self.sw2[:, :], self.sw2[:, :], 1.0 - LAM_INIT, None, ALU.mult))

        wt(nc.scalar, e_c)
        e_sc = AE.tag(nc.scalar.activation(cfm[:, 0:16], cfm[:, 0:16], AF.Silu))

        wt(nc.vector, e_l)
        VE.tag(nc.vector.tensor_tensor(lv[0:1, 0:128], lv[0:1, 0:128],
                                       lv[0:1, 128:256], ALU.mult))
        VE.tag(nc.vector.tensor_tensor(lv[0:1, 256:384], lv[0:1, 256:384],
                                       lv[0:1, 384:512], ALU.mult))
        nc.vector.reduce_sum(lt[0:1, 0:1], lv[0:1, 0:128], axis=AX.X)
        e_vr = VE.tag(nc.vector.reduce_sum(lt[0:1, 1:2], lv[0:1, 256:384],
                                           axis=AX.X))
        wt(nc.scalar, e_vr)
        e_ae = AE.tag(nc.scalar.activation(lt[0:1, 2:4], lt[0:1, 0:2], AF.Exp))
        wt(nc.vector, e_ae)
        VE.tag(nc.vector.tensor_tensor(lt[0:1, 4:5], lt[0:1, 2:3],
                                       lt[0:1, 3:4], ALU.subtract))
        e_nl = VE.tag(nc.vector.tensor_scalar(
            lt[0:1, 5:6], lt[0:1, 4:5], LAM_INIT, -1.0, ALU.add, ALU.mult))
        wt(nc.gpsimd, e_nl)
        self.dma(nc.gpsimd, "srow_sc",
                 bass.AP(self.srow.tensor, SROW_SC + SLOT_NEGLAM,
                         [[1, 1], [1, 1]]),
                 lt[0:1, 5:6])

        wmt = self.wmod.tensor
        psm = [self.ps6, self.ps7]
        ev_te = [None, None]
        ev_cp = [None, None]
        ev_wr = [None, None]
        WMW = 384
        for t in range(32):
            par = t % 2
            wt(nc.sync, ev_te[par])
            w3 = self.wap3(F32, 8192 + par * 24576, [[WMW, 16], [1, WMW]])
            e_ld = self.dma(nc.sync, f"wm{par}", w3,
                            bass.AP(wmt, t * WMW,
                                    [[6 * E, 128], [6 * E * 128, 16],
                                     [1, WMW]]))
            wt(nc.tensor, e_ld, e_sc, ev_cp[par])
            ins = None
            for j in range(16):
                rj = wmb[par][:, j * WMW:(j + 1) * WMW]
                ins = nc.tensor.matmul(
                    psm[par][0:1, 0:WMW], cfm[:, j:j + 1], rj,
                    start=(j == 0), stop=(j == 15))
            ev_te[par] = TE.tag(ins)
            wt(nc.vector, ev_te[par], ev_wr[par])
            e_cp = VE.tag(nc.vector.tensor_copy(mrow[par][0:1, 0:WMW],
                                                psm[par][0:1, 0:WMW]))
            ev_cp[par] = e_cp
            wt(nc.gpsimd, e_cp)
            ev_wr[par] = self.dma(
                nc.gpsimd, f"mrow{par}",
                bass.AP(self.srow.tensor, SROW_MOD + t * WMW,
                        [[1, 1], [1, WMW]]),
                mrow[par][0:1, 0:WMW])
        self.e_modrow = [ev_wr[0], ev_wr[1]]

        if self.want("mod"):
            t = self.dbg_dram("mod", [6 * E])
            wt(nc.gpsimd, *self.e_modrow)
            self.dma(nc.gpsimd, "dbg", t[None, :],
                     bass.AP(self.srow.tensor, 0, [[1, 1], [1, 6 * E]]))

        wt(nc.sync, *self.e_modrow)
        e_b0 = self.dma(nc.sync, "bc0", self.bc0[:, :],
                        bass.AP(self.srow.tensor, SROW_MOD + E,
                                [[0, 128], [1, E]]))
        e_b1 = self.dma(nc.sync, "bc1", self.bc1[:, :],
                        bass.AP(self.srow.tensor, SROW_MOD + 0,
                                [[0, 128], [1, E]]))
        wt(nc.vector, e_b0)
        self.e_bc = VE.tag(nc.vector.tensor_scalar(
            self.bc0[:, :], self.bc0[:, :], 1.0, None, ALU.add))
        self.e_bc_ld = e_b1

    def emit_stats(self, xb, mcol, rcol, pre_evts, eps=EPS_LN, n=E, par=0):
        nc = self.nc
        VE, AE = self.VE, self.AE
        stt = self.stats
        dump = self.wview(F32, 2048, 16384)
        wt(nc.scalar, *pre_evts)
        e_sq = AE.tag(nc.scalar.activation(dump[:, 0:n], xb, AF.Square,
                                           accum_out=stt[:, 12 + par:13 + par]))
        wt(nc.vector, *pre_evts)
        VE.tag(nc.vector.reduce_sum(stt[:, 1:2], xb, axis=AX.X))
        VE.tag(nc.vector.tensor_scalar(mcol, stt[:, 1:2], 1.0 / n, None,
                                       ALU.mult))
        nc.vector.tensor_tensor(stt[:, 3:4], mcol, mcol, ALU.mult)
        wt(nc.vector, e_sq)
        VE.tag(nc.vector.tensor_scalar(stt[:, 4:5], stt[:, 12 + par:13 + par],
                                       1.0 / n, None, ALU.mult))
        VE.tag(nc.vector.tensor_tensor(stt[:, 5:6], stt[:, 4:5], stt[:, 3:4],
                                       ALU.subtract))
        e_u = VE.tag(nc.vector.tensor_scalar(stt[:, 5:6], stt[:, 5:6], eps,
                                             None, ALU.add))
        wt(nc.scalar, e_u)
        e_sq2 = AE.tag(nc.scalar.activation(stt[:, 6:7], stt[:, 5:6], AF.Sqrt))
        wt(nc.vector, e_sq2)
        VE.tag(nc.vector.reciprocal(stt[:, 7:8], stt[:, 6:7]))
        VE.tag(nc.vector.tensor_tensor(stt[:, 8:9], stt[:, 7:8], stt[:, 7:8],
                                       ALU.mult))
        VE.tag(nc.vector.tensor_tensor(stt[:, 9:10], stt[:, 8:9], stt[:, 5:6],
                                       ALU.mult))
        VE.tag(nc.vector.tensor_scalar(stt[:, 9:10], stt[:, 9:10], -0.5, 1.5,
                                       ALU.mult, ALU.add))
        e_r = VE.tag(nc.vector.tensor_tensor(rcol, stt[:, 7:8], stt[:, 9:10],
                                             ALU.mult))
        return e_r

    def emit_fm_store(self, src_bf16, dest_handle, sc, e_src, w=2048,
                      dest_width=32768, dest_col0=0, sc_stride=2048):
        nc = self.nc
        TE, VE = self.TE, self.VE
        nj = w // 128
        e_ev = None
        e_t = None
        for g in range(0, nj, 8):
            gb = min(8, nj - g)
            bi = (g // 8) % 2
            pbank = self.pst[bi]
            wt(nc.tensor, self.pst_guard[bi], e_src)
            ins = None
            for jj in range(gb):
                j = g + jj
                ins = nc.tensor.transpose(
                    pbank[:, jj * 128:(jj + 1) * 128],
                    src_bf16[:, j * 128:(j + 1) * 128], self.ident[:, :])
            e_t = TE.tag(ins)
            wt(nc.vector, e_t)
            outap = bass.AP(dest_handle,
                            dest_col0 + g * sc_stride + sc * 128,
                            [[dest_width, 128], [sc_stride, gb], [1, 128]])
            inap = bass.AP(pbank, 0, [[1024, 128], [128, gb], [1, 128]])
            e_ev = VE.tag(nc.vector.tensor_copy(outap, inap))
            self.pst_guard[bi] = e_ev
        return e_t, e_ev

    def phase1(self):
        nc = self.nc
        VE, AE, GE = self.VE, self.AE, self.GE
        self.barrier()
        self.pst_guard = [None, None]
        xb = [self.wview(F32, 2048, 0), self.wview(F32, 2048, 8192)]
        zb = self.wview(F32, 2048, 24576)
        hfp = self.wview(F32, 2048, 32768)
        hhi = self.wview(BF16, 2048, 40960)
        hlo = self.wview(BF16, 2048, 45056)
        stt = self.stats
        e_used = [None, None]
        e_hi_prev = e_lo_prev = None
        e_te_hi = e_te_lo = None
        for sc in range(NCH):
            par = sc % 2
            wt(nc.sync, e_used[par])
            e_x = self.dma(nc.sync, f"x{par}", xb[par][:, :],
                           self.xin[sc * 128:(sc + 1) * 128, :])
            e_r = self.emit_stats(xb[par], stt[:, 16:17], stt[:, 17:18],
                                  [e_x], par=par)
            e_z = VE.tag(nc.vector.tensor_scalar(
                zb[:, :], xb[par][:, :], stt[:, 16:17],
                stt[:, 17:18], ALU.subtract, ALU.mult))
            e_used[par] = e_z
            wt(nc.vector, self.e_bc, self.e_bc_ld, e_hi_prev, e_lo_prev)
            VE.tag(nc.vector.tensor_tensor(hfp[:, :], zb[:, :],
                                           self.bc0[:, :], ALU.mult))
            e_h = VE.tag(nc.vector.tensor_tensor(hfp[:, :], hfp[:, :],
                                                 self.bc1[:, :], ALU.add))
            wt(nc.scalar, e_h, e_te_hi)
            e_hi = AE.tag(nc.scalar.copy(hhi[:, :], hfp[:, :]))
            wt(nc.gpsimd, e_hi, e_te_lo)
            e_lo = GE.tag(nc.gpsimd.tensor_sub(hlo[:, :], hfp[:, :],
                                               hhi[:, :]))
            e_te_hi, _ = self.emit_fm_store(hhi, self.hfm_hi, sc, e_hi)
            e_te_lo, _ = self.emit_fm_store(hlo, self.hfm_lo, sc, e_lo)
            e_hi_prev, e_lo_prev = e_hi, e_lo
        self.e_h1 = VE.evt()
        if self.want("h1fm"):
            t1 = self.dbg_dram("h1fm_hi", [128, 32768])
            t2 = self.dbg_dram("h1fm_lo", [128, 32768])
            wt(nc.gpsimd, VE.evt())
            self.dma(nc.gpsimd, "dbg", t1[:, :], self.hfm_hi[:, :])
            self.dma(nc.gpsimd, "dbg", t2[:, :], self.hfm_lo[:, :])

    def phase2(self):
        nc = self.nc
        TE, VE, AE, GE = self.TE, self.VE, self.AE, self.GE
        self.barrier()
        stt = self.stats

        self.p1_count = 0
        p1_mats = [self.wupT, self.wgateT, self.wdownT]
        self.e_p1_act = None
        pbuf = [self.wview(F32, 2048, 36864), self.wview(F32, 2048, 45056)]

        def pass1_chunk():
            c = self.p1_count
            if c >= 192:
                return
            self.p1_count += 1
            mat = p1_mats[c // 64]
            sub = c % 64
            par = c % 2
            wt(nc.sync, self.e_p1_act if c >= 2 else None)
            e_ld = self.dma(nc.sync, f"p1{par}", pbuf[par][:, :],
                            bass.AP(mat.tensor, sub * 128 * 2048,
                                    [[2048, 128], [1, 2048]]))
            wt(nc.scalar, e_ld)
            self.e_p1_act = AE.tag(nc.scalar.activation(
                pbuf[par][:, 0:2048], pbuf[par][:, 0:2048], AF.Abs,
                accum_out=stt[:, 64 + c:65 + c]))

        wsf = self.wview(F32, 2048, 0)
        whi = [self.wview(BF16, 2048, 8192), self.wview(BF16, 2048, 12288)]
        wlo = [self.wview(BF16, 2048, 16384), self.wview(BF16, 2048, 20480)]
        sthi = self.wview(BF16, 2048, 24576)
        stlo = self.wview(BF16, 2048, 28672)
        qfp = self.wview(F32, 512, 32768)
        e_split = [None, None]
        e_bank = [None, None]
        e_te_hl = [None, None]
        e_spill = None
        e_stage_lo = None
        for mat_i, (wmat, dst_hi, dst_lo, qs) in enumerate(
                [(self.wq, self.sq_hi, self.sq_lo, SCALE),
                 (self.wk, self.sk_hi, self.sk_lo, 1.0)]):
            for m in range(NCH):
                par = m % 2
                wt(nc.sync, e_split[0], e_split[1])
                e_ld = self.dma(nc.sync, "ws",
                                self.wap3(F32, 0, [[128, 16], [1, 128]]),
                                bass.AP(wmat.tensor, m * 128,
                                        [[E, 128], [E * 128, 16], [1, 128]]))
                wt(nc.scalar, e_ld, e_te_hl[par])
                e_hi = AE.tag(nc.scalar.copy(whi[par][:, :], wsf[:, :]))
                wt(nc.gpsimd, e_hi)
                e_lo = GE.tag(nc.gpsimd.tensor_sub(wlo[par][:, :], wsf[:, :],
                                                   whi[par][:, :]))
                e_split[par] = e_lo
                for st in range(4):
                    bk = st % 2
                    wt(nc.tensor, e_lo, e_bank[bk])
                    ins = None
                    for f in range(16):
                        lt_ = whi[par][:, f * 128:(f + 1) * 128]
                        nc.tensor.matmul(
                            self.psb[bk][:, :], lt_,
                            self.hfm_hi[:, f * 2048 + st * 512:
                                        f * 2048 + st * 512 + 512],
                            start=(f == 0), stop=False)
                        nc.tensor.matmul(
                            self.psb[bk][:, :], lt_,
                            self.hfm_lo[:, f * 2048 + st * 512:
                                        f * 2048 + st * 512 + 512],
                            start=False, stop=False)
                    for f in range(16):
                        ins = nc.tensor.matmul(
                            self.psb[bk][:, :],
                            wlo[par][:, f * 128:(f + 1) * 128],
                            self.hfm_hi[:, f * 2048 + st * 512:
                                        f * 2048 + st * 512 + 512],
                            start=False, stop=(f == 15))
                    e_mm = TE.tag(ins)
                    wt(nc.vector, e_mm, e_stage_lo)
                    e_q = VE.tag(nc.vector.tensor_scalar(
                        qfp[:, :], self.psb[bk][:, :], qs, None, ALU.mult))
                    e_bank[bk] = e_q
                    wt(nc.scalar, e_q, e_spill)
                    e_sh = AE.tag(nc.scalar.copy(
                        sthi[:, st * 512:(st + 1) * 512], qfp[:, :]))
                    wt(nc.gpsimd, e_sh)
                    e_stage_lo = GE.tag(nc.gpsimd.tensor_sub(
                        stlo[:, st * 512:(st + 1) * 512], qfp[:, :],
                        sthi[:, st * 512:(st + 1) * 512]))
                e_te_hl[par] = TE.evt()
                wt(nc.scalar, e_stage_lo)
                self.dma(nc.scalar, "spq", dst_hi[m, :, :], sthi[:, :])
                e_spill = self.dma(nc.scalar, "spq", dst_lo[m, :, :],
                                   stlo[:, :])
                for _ in range(6):
                    pass1_chunk()
        while self.p1_count < 192:
            pass1_chunk()

        self.barrier()
        e_bank = [None, None]
        e_spill = None
        e_stage_lo = None
        wvf = self.wview(F32, 4096, 0)
        vhi = self.wview(BF16, 4096, 16384)
        vlo = self.wview(BF16, 4096, 24576)
        vsh = self.wview(BF16, 256, 32768)
        vsl = self.wview(BF16, 256, 33280)
        vfp = self.wview(F32, 256, 33792)
        e_vsplit = None
        for h in range(NH):
            wt(nc.sync, e_vsplit, e_te_hl[0], e_te_hl[1])
            e_ld = self.dma(nc.sync, "ws",
                            self.wap3(F32, 0, [[256, 16], [1, 256]]),
                            bass.AP(self.wv.tensor, h * 256,
                                    [[E, 128], [E * 128, 16], [1, 256]]))
            wt(nc.scalar, e_ld)
            e_hi = AE.tag(nc.scalar.copy(vhi[:, :], wvf[:, :]))
            wt(nc.gpsimd, e_hi)
            e_lo = GE.tag(nc.gpsimd.tensor_sub(vlo[:, :], wvf[:, :],
                                               vhi[:, :]))
            e_vsplit = e_lo
            for sc in range(NCH):
                bk = sc % 2
                wt(nc.tensor, e_lo, e_bank[bk])
                ins = None
                for f in range(16):
                    lh = self.hfm_hi[:, f * 2048 + sc * 128:
                                     f * 2048 + sc * 128 + 128]
                    rh = vhi[:, f * 256:(f + 1) * 256]
                    rl = vlo[:, f * 256:(f + 1) * 256]
                    nc.tensor.matmul(self.psb[bk][:, 0:256], lh, rh,
                                     start=(f == 0), stop=False)
                    nc.tensor.matmul(self.psb[bk][:, 0:256], lh, rl,
                                     start=False, stop=False)
                for f in range(16):
                    ll = self.hfm_lo[:, f * 2048 + sc * 128:
                                     f * 2048 + sc * 128 + 128]
                    rh = vhi[:, f * 256:(f + 1) * 256]
                    ins = nc.tensor.matmul(self.psb[bk][:, 0:256], ll, rh,
                                           start=False, stop=(f == 15))
                e_mm = TE.tag(ins)
                wt(nc.vector, e_mm, e_stage_lo)
                e_q = VE.tag(nc.vector.tensor_copy(vfp[:, 0:256],
                                                   self.psb[bk][:, 0:256]))
                e_bank[bk] = e_q
                wt(nc.scalar, e_q, e_spill)
                e_sh = AE.tag(nc.scalar.copy(vsh[:, 0:256], vfp[:, 0:256]))
                wt(nc.gpsimd, e_sh)
                e_stage_lo = GE.tag(nc.gpsimd.tensor_sub(
                    vsl[:, 0:256], vfp[:, 0:256], vsh[:, 0:256]))
                wt(nc.scalar, e_stage_lo)
                self.dma(nc.scalar, "spv", self.sv_hi[h, sc, :, :],
                         vsh[:, 0:256])
                e_spill = self.dma(nc.scalar, "spv", self.sv_lo[h, sc, :, :],
                                   vsl[:, 0:256])
            e_te_hl = [TE.evt(), TE.evt()]

        wt(nc.vector, self.e_p1_act)
        for mi in range(3):
            c0 = 64 + mi * 64
            VE.tag(nc.vector.reduce_sum(stt[:, 50 + mi:51 + mi],
                                        stt[:, c0:c0 + 64], axis=AX.X))
        e_ps = VE.evt()
        wt(nc.tensor, e_ps, e_bank[0], e_bank[1])
        ins = None
        for mi in range(3):
            ins = nc.tensor.matmul(self.psb[0][0:1, mi:mi + 1],
                                   self.cols[:, 60:61],
                                   stt[:, 50 + mi:51 + mi],
                                   start=(mi == 0), stop=(mi == 2))
        e_sc3 = TE.tag(ins)
        wt(nc.vector, e_sc3)
        VE.tag(nc.vector.tensor_scalar(stt[0:1, 53:56], self.psb[0][0:1, 0:3],
                                       1.0 / (E * DEX), EPS_Q, ALU.mult,
                                       ALU.add))
        VE.tag(nc.vector.tensor_scalar(stt[0:1, 56:59], stt[0:1, 53:56], 0.5,
                                       None, ALU.mult))
        e_fin = VE.tag(nc.vector.tensor_tensor(
            stt[0:1, 59:60], stt[0:1, 54:55], stt[0:1, 55:56], ALU.mult))
        self.e_bank2 = VE.evt()
        wt(nc.gpsimd, e_fin)
        for slot_i, col in [(SLOT_SUP, 53), (SLOT_SGD, 59), (SLOT_THR_UP, 56),
                            (SLOT_THR_GATE, 57), (SLOT_THR_DOWN, 58)]:
            self.e_srow_sc = self.dma(
                nc.gpsimd, "srow_sc",
                bass.AP(self.srow.tensor, SROW_SC + slot_i, [[1, 1], [1, 1]]),
                stt[0:1, col:col + 1])

    def phase3(self):
        nc = self.nc
        TE, VE, AE, GE = self.TE, self.VE, self.AE, self.GE
        self.barrier()
        stt = self.stats
        e_nc = self.dma(nc.sync, "ncol", self.cols[:, 0:1],
                        bass.AP(self.srow.tensor, SROW_SC + SLOT_NEGLAM,
                                [[0, 128], [1, 1]]))
        AR = [self.hfm_hi, self.hfm_lo]

        def hv(par, c0, w):
            return AR[par][:, c0:c0 + w]

        pfp = self.wview(F32, 2048, 0)
        phi = self.wview(BF16, 2048, 8192)
        plo = self.wview(BF16, 2048, 12288)
        ptb_hi = self.wview(BF16, 2048, 16384)
        ptb_lo = self.wview(BF16, 2048, 20480)
        afp = self.wview(F32, 4096, 24576)
        abuf_hi = self.wview(BF16, 4096, 40960)
        abuf_lo = self.wview(BF16, 4096, 49152)

        def load_head(h):
            par = h % 2
            evs = []
            for i, src in enumerate([self.sq_hi, self.sq_lo,
                                     self.sk_hi, self.sk_lo]):
                for br in range(2):
                    m = 2 * h + br
                    evs.append(self.dma(
                        nc.sync, f"qk{par}",
                        hv(par, (2 * i + br) * 2048, 2048),
                        src[m, :, :]))
            for i, src in enumerate([self.sv_hi, self.sv_lo]):
                evs.append(self.dma(
                    nc.sync, f"qk{par}",
                    hv(par, 16384 + i * 4096, 4096),
                    bass.AP(src.tensor, h * NCH * 128 * 256,
                            [[256, 128], [128 * 256, 16], [1, 256]])))
            return evs[-1]

        e_ld = [None, None]
        e_ld[0] = load_head(0)
        e_arena_free = [None, None]
        e_bank_exp = [None, None, None, None]
        e_bank_mask = [None] * 4
        e_pst_ev = [None, None]
        e_o_read = None
        e_spill_a = None
        e_pfp_free = None
        e_plo = None
        for h in range(NH):
            par = h % 2
            if h + 1 < NH:
                wt(nc.sync, e_arena_free[1 - par])
                e_ld[1 - par] = load_head(h + 1)
            q_of = [0, 2048]
            k_of = [8192, 10240]
            for qc in range(NCH):
                nfull = qc // 4
                rem = qc % 4
                remw = (rem + 1) * 128
                for bi in range(2):
                    qhi = hv(par, q_of[bi], 2048)[:, qc * 128:(qc + 1) * 128]
                    qlo = hv(par, q_of[bi] + 4096, 2048)[:, qc * 128:(qc + 1) * 128]
                    khi = hv(par, k_of[bi], 2048)
                    klo = hv(par, k_of[bi] + 4096, 2048)
                    wt(nc.tensor, e_ld[par])
                    for kt in range(nfull + 1):
                        w = 512 if kt < nfull else remw
                        bank = self.psb[kt]
                        wt(nc.tensor, e_bank_exp[kt], e_bank_mask[kt])
                        c0 = kt * 512
                        nc.tensor.matmul(bank[:, 0:w], qhi,
                                         khi[:, c0:c0 + w],
                                         start=True, stop=False)
                        nc.tensor.matmul(bank[:, 0:w], qhi,
                                         klo[:, c0:c0 + w],
                                         start=False, stop=False)
                        ins = nc.tensor.matmul(bank[:, 0:w], qlo,
                                               khi[:, c0:c0 + w],
                                               start=False, stop=True)
                        e_mm = TE.tag(ins)
                        e_bank_exp[kt] = None
                        e_bank_mask[kt] = None
                    wt(nc.vector, e_mm, self.e_consts)
                    e_msk = VE.tag(nc.vector.tensor_tensor(
                        self.psb[nfull][:, remw - 128:remw],
                        self.psb[nfull][:, remw - 128:remw],
                        self.mask[:, :], ALU.add))
                    e_bank_mask[nfull] = e_msk
                    for kt in range(nfull + 1):
                        w = 512 if kt < nfull else remw
                        VE.tag(nc.vector.tensor_reduce(
                            stt[:, kt:kt + 1], self.psb[kt][:, 0:w],
                            axis=AX.X, op=ALU.max))
                    e_mx = VE.tag(nc.vector.tensor_reduce(
                        stt[:, 5:6], stt[:, 0:nfull + 1], axis=AX.X,
                        op=ALU.max, negate=True))
                    wt(nc.scalar, e_mx, e_pfp_free, e_plo)
                    for kt in range(nfull + 1):
                        w = 512 if kt < nfull else remw
                        e_exp = AE.tag(nc.scalar.activation(
                            pfp[:, kt * 512:kt * 512 + w],
                            self.psb[kt][:, 0:w], AF.Exp,
                            bias=stt[:, 5:6],
                            accum_out=stt[:, 8 + kt:9 + kt]))
                        e_bank_exp[kt] = e_exp
                    kact = qc * 128 + 128
                    wt(nc.vector, e_exp)
                    VE.tag(nc.vector.reduce_sum(stt[:, 13:14],
                                                stt[:, 8:9 + nfull], axis=AX.X))
                    e_zr = VE.tag(nc.vector.reciprocal(
                        stt[:, 14 + bi:15 + bi], stt[:, 13:14]))
                    if self.want("attn0") and h == 0 and qc == 1:
                        tpp = self.dbg_dram(f"pfp{bi}", [128, 256])
                        tzz = self.dbg_dram(f"zst{bi}", [128, 24])
                        wt(nc.gpsimd, e_zr)
                        self.dma(nc.gpsimd, "dbg", tpp[:, :], pfp[:, 0:256])
                        e_dbg = self.dma(nc.gpsimd, "dbg", tzz[:, :],
                                         stt[:, 0:24])
                        wt(nc.vector, e_dbg)
                        wt(nc.scalar, e_dbg)
                    e_phi = VE.tag(nc.vector.tensor_copy(phi[:, 0:kact],
                                                         pfp[:, 0:kact]))
                    wt(nc.gpsimd, e_phi)
                    e_plo = GE.tag(nc.gpsimd.tensor_sub(
                        plo[:, 0:kact], pfp[:, 0:kact], phi[:, 0:kact]))
                    e_pfp_free = e_plo
                    nkc = qc + 1
                    for src, dstv, e_src in ((phi, ptb_hi, e_phi),
                                             (plo, ptb_lo, e_plo)):
                        for g in range(0, nkc, 8):
                            gb = min(8, nkc - g)
                            tb = (g // 8) % 2
                            wt(nc.tensor, e_pst_ev[tb], e_src, e_spill_a)
                            for jj in range(gb):
                                kc = g + jj
                                ins = nc.tensor.transpose(
                                    self.pst[tb][:, jj * 128:(jj + 1) * 128],
                                    src[:, kc * 128:(kc + 1) * 128],
                                    self.ident[:, :])
                            e_t = TE.tag(ins)
                            wt(nc.vector, e_t)
                            outap = bass.AP(
                                dstv.tensor, dstv.offset + g * 128,
                                [[28672, 128], [1, gb * 128]])
                            inap = bass.AP(self.pst[tb], 0,
                                           [[1024, 128], [1, gb * 128]])
                            e_pst_ev[tb] = VE.tag(
                                nc.vector.tensor_copy(outap, inap))
                    e_ptb = e_pst_ev
                    ops = self.ps6 if bi == 0 else self.ps7
                    wt(nc.tensor, e_pst_ev[0], e_pst_ev[1], e_o_read)
                    ins = None
                    for kc in range(nkc):
                        vh = hv(par, 16384, 4096)[:, kc * 256:(kc + 1) * 256]
                        vl = hv(par, 20480, 4096)[:, kc * 256:(kc + 1) * 256]
                        ph_ = ptb_hi[:, kc * 128:(kc + 1) * 128]
                        pl_ = ptb_lo[:, kc * 128:(kc + 1) * 128]
                        nc.tensor.matmul(ops[:, 0:256], ph_, vh,
                                         start=(kc == 0), stop=False)
                        nc.tensor.matmul(ops[:, 0:256], ph_, vl,
                                         start=False, stop=False)
                        ins = nc.tensor.matmul(ops[:, 0:256], pl_, vh,
                                               start=False,
                                               stop=(kc == nkc - 1))
                    e_pv = TE.tag(ins)
                    if bi == 1:
                        e_pv2 = e_pv
                wt(nc.vector, e_pv2, e_nc)
                e_nlz = VE.tag(nc.vector.tensor_tensor(
                    stt[:, 16:17], self.cols[:, 0:1], stt[:, 15:16],
                    ALU.mult))
                VE.tag(nc.vector.tensor_scalar(
                    pfp[:, 0:256], self.ps6[:, 0:256], stt[:, 14:15], None,
                    ALU.mult))
                e_comb = VE.tag(nc.vector.scalar_tensor_tensor(
                    afp[:, qc * 256:(qc + 1) * 256], self.ps7[:, 0:256],
                    stt[:, 16:17], pfp[:, 0:256], ALU.mult, ALU.add))
                e_o_read = e_comb
                e_pfp_free = e_comb
                if self.want("attn0") and h == 0 and qc == 1:
                    to_ = self.dbg_dram("acomb", [128, 256])
                    to6 = self.dbg_dram("o6", [128, 256])
                    to7 = self.dbg_dram("o7", [128, 256])
                    wt(nc.vector, e_comb)
                    e_c6 = VE.tag(nc.vector.tensor_copy(pfp[:, 512:768],
                                                        self.ps6[:, 0:256]))
                    e_c7 = VE.tag(nc.vector.tensor_copy(pfp[:, 768:1024],
                                                        self.ps7[:, 0:256]))
                    wt(nc.gpsimd, e_c7)
                    self.dma(nc.gpsimd, "dbg", to_[:, :],
                             afp[:, qc * 256:(qc + 1) * 256])
                    self.dma(nc.gpsimd, "dbg", to6[:, :], pfp[:, 512:768])
                    e_dbg2 = self.dma(nc.gpsimd, "dbg", to7[:, :],
                                      pfp[:, 768:1024])
                    wt(nc.vector, e_dbg2)
                    wt(nc.scalar, e_dbg2)
                wt(nc.scalar, e_comb)
                AE.tag(nc.scalar.activation(
                    pfp[:, 256:512], afp[:, qc * 256:(qc + 1) * 256],
                    AF.Square, accum_out=stt[:, 160 + qc:161 + qc]))
            e_ss = self.AE.evt()
            wt(nc.vector, e_ss)
            VE.tag(nc.vector.tensor_scalar(stt[:, 176:192],
                                           stt[:, 160:176], 1.0 / 256,
                                           EPS_LN, ALU.mult, ALU.add))
            e_u = VE.tag(nc.vector.tensor_copy(stt[:, 144:160],
                                               stt[:, 176:192]))
            wt(nc.scalar, e_u)
            e_sq = AE.tag(nc.scalar.activation(stt[:, 176:192],
                                               stt[:, 144:160], AF.Sqrt))
            wt(nc.vector, e_sq)
            VE.tag(nc.vector.reciprocal(stt[:, 176:192], stt[:, 176:192]))
            VE.tag(nc.vector.tensor_tensor(stt[:, 128:144], stt[:, 176:192],
                                           stt[:, 176:192], ALU.mult))
            VE.tag(nc.vector.tensor_tensor(stt[:, 128:144], stt[:, 128:144],
                                           stt[:, 144:160], ALU.mult))
            VE.tag(nc.vector.tensor_scalar(stt[:, 128:144], stt[:, 128:144],
                                           -0.5, 1.5, ALU.mult, ALU.add))
            e_rst = VE.tag(nc.vector.tensor_tensor(
                stt[:, 176:192], stt[:, 176:192], stt[:, 128:144], ALU.mult))
            e_ab = [None, None]
            for qc in range(NCH):
                VE.tag(nc.vector.tensor_scalar(
                    pfp[:, 0:256], afp[:, qc * 256:(qc + 1) * 256],
                    stt[:, 176 + qc:177 + qc], None, ALU.mult))
                wt(nc.vector, self.e_sw2, e_ab[0], e_ab[1])
                e_af = VE.tag(nc.vector.tensor_tensor(
                    pfp[:, 1024:1280], pfp[:, 0:256], self.sw2[:, :],
                    ALU.mult))
                wt(nc.scalar, e_af)
                e_h_ = AE.tag(nc.scalar.copy(
                    abuf_hi[:, qc * 256:(qc + 1) * 256], pfp[:, 1024:1280]))
                wt(nc.gpsimd, e_h_)
                e_l_ = GE.tag(nc.gpsimd.tensor_sub(
                    abuf_lo[:, qc * 256:(qc + 1) * 256], pfp[:, 1024:1280],
                    abuf_hi[:, qc * 256:(qc + 1) * 256]))
                e_ab = [e_h_, e_l_]
                e_pfp_free = e_l_
            for srcv, dstv, sh in ((abuf_hi, ptb_hi, self.sa_hi),
                                   (abuf_lo, ptb_lo, self.sa_lo)):
                e_src = e_ab[0] if srcv is abuf_hi else e_ab[1]
                for eb in range(2):
                    for g in range(0, NCH, 8):
                        tb = (g // 8) % 2
                        wt(nc.tensor, e_pst_ev[tb], e_src, e_spill_a)
                        for jj in range(8):
                            qc = g + jj
                            ins = nc.tensor.transpose(
                                self.pst[tb][:, jj * 128:(jj + 1) * 128],
                                srcv[:, qc * 256 + eb * 128:
                                     qc * 256 + eb * 128 + 128],
                                self.ident[:, :])
                        e_t = TE.tag(ins)
                        wt(nc.vector, e_t)
                        outap = bass.AP(dstv.tensor,
                                        dstv.offset + g * 128,
                                        [[28672, 128], [1, 1024]])
                        inap = bass.AP(self.pst[tb], 0,
                                       [[1024, 128], [1, 1024]])
                        e_pst_ev[tb] = VE.tag(
                            nc.vector.tensor_copy(outap, inap))
                    wt(nc.scalar, e_pst_ev[0], e_pst_ev[1])
                    e_spill_a = self.dma(
                        nc.scalar, "spa",
                        bass.AP(sh.tensor, (2 * h + eb) * 128 * 128,
                                [[128, 128], [NCH * 128 * 128, 16],
                                 [1, 128]]),
                        bass.AP(dstv.tensor, dstv.offset,
                                [[28672, 128], [128, 16], [1, 128]]))
            e_arena_free[par] = TE.evt()
        self.e_attn_done = e_spill_a
    def phase4(self):
        nc = self.nc
        TE, VE, AE, GE = self.TE, self.VE, self.AE, self.GE
        self.barrier()
        stt = self.stats

        e_g = self.dma(nc.sync, "bc0", self.bc0[:, :],
                       bass.AP(self.srow.tensor, SROW_MOD + 2 * E,
                               [[0, 128], [1, E]]))
        t1 = self.wview(F32, 2048, 24576)
        t2 = self.wview(F32, 2048, 32768)
        wt(nc.vector, e_g)
        VE.tag(nc.vector.tensor_scalar(t1[:, :], self.bc0[:, :], 0.25, None,
                                       ALU.max))
        VE.tag(nc.vector.tensor_scalar(t2[:, :], self.bc0[:, :], -0.25, None,
                                       ALU.min))
        VE.tag(nc.vector.tensor_scalar(self.bc0[:, :], self.bc0[:, :], 0.0,
                                       None, ALU.is_ge))
        VE.tag(nc.vector.tensor_tensor(t1[:, :], t1[:, :], t2[:, :],
                                       ALU.subtract))
        VE.tag(nc.vector.tensor_tensor(t1[:, :], self.bc0[:, :], t1[:, :],
                                       ALU.mult))
        e_ga = VE.tag(nc.vector.tensor_tensor(self.bc0[:, :], t2[:, :],
                                              t1[:, :], ALU.add))

        woc = self.wview(F32, 2048, 16384)
        e_wsp = [None, None]
        for wc in range(NCH):
            wt(nc.sync, e_wsp[0], e_wsp[1])
            e_ld = self.dma(nc.sync, "woc", woc[:, :],
                            self.wo[wc * 128:(wc + 1) * 128, :])
            wt(nc.scalar, e_ld)
            e_hi = AE.tag(nc.scalar.copy(
                self.hfm_hi[:, wc * 2048:(wc + 1) * 2048], woc[:, :]))
            wt(nc.gpsimd, e_hi)
            e_lo = GE.tag(nc.gpsimd.tensor_sub(
                self.hfm_lo[:, wc * 2048:(wc + 1) * 2048], woc[:, :],
                self.hfm_hi[:, wc * 2048:(wc + 1) * 2048]))
            e_wsp = [e_hi, e_lo]

        ab_hi = [self.wview(BF16, 2048, 0), self.wview(BF16, 2048, 8192)]
        ab_lo = [self.wview(BF16, 2048, 4096), self.wview(BF16, 2048, 12288)]
        xb = [self.wview(F32, 2048, 24576), self.wview(F32, 2048, 32768)]
        x1b = [self.wview(F32, 2048, 40960), self.wview(F32, 2048, 49152)]
        ot = self.wview(F32, 512, 16384)
        e_a_used = [None, None]
        e_x_used = [None, None]
        e_bank = [None, None]
        e_x1_done = [None, None]
        e_sx = [None, None]
        e_sq_prev = None
        for sc in range(NCH):
            par = sc % 2
            wt(nc.sync, e_a_used[par])
            self.dma(nc.sync, f"pa{par}",
                     bass.AP(ab_hi[par].tensor, ab_hi[par].offset,
                             [[28672, 128], [128, 16], [1, 128]]),
                     bass.AP(self.sa_hi.tensor, sc * 16 * 128 * 128,
                             [[128, 128], [128 * 128, 16], [1, 128]]))
            e_a = self.dma(nc.sync, f"pa{par}",
                           bass.AP(ab_lo[par].tensor, ab_lo[par].offset,
                                   [[28672, 128], [128, 16], [1, 128]]),
                           bass.AP(self.sa_lo.tensor, sc * 16 * 128 * 128,
                                   [[128, 128], [128 * 128, 16], [1, 128]]))
            wt(nc.sync, e_x_used[par], e_ga)
            e_x = self.dma(nc.sync, f"px{par}", xb[par][:, :],
                           self.xin[sc * 128:(sc + 1) * 128, :])
            wt(nc.vector, e_x1_done[par])
            for ft in range(4):
                bk = ft % 2
                wt(nc.tensor, e_a, e_wsp[0], e_wsp[1], e_bank[bk])
                ins = None
                for af in range(16):
                    lh = ab_hi[par][:, af * 128:(af + 1) * 128]
                    rh = self.hfm_hi[:, af * 2048 + ft * 512:
                                     af * 2048 + ft * 512 + 512]
                    rl = self.hfm_lo[:, af * 2048 + ft * 512:
                                     af * 2048 + ft * 512 + 512]
                    nc.tensor.matmul(self.psb[bk][:, :], lh, rh,
                                     start=(af == 0), stop=False)
                    nc.tensor.matmul(self.psb[bk][:, :], lh, rl,
                                     start=False, stop=False)
                for af in range(16):
                    ll = ab_lo[par][:, af * 128:(af + 1) * 128]
                    rh = self.hfm_hi[:, af * 2048 + ft * 512:
                                     af * 2048 + ft * 512 + 512]
                    ins = nc.tensor.matmul(self.psb[bk][:, :], ll, rh,
                                           start=False, stop=(af == 15))
                e_mm = TE.tag(ins)
                wt(nc.vector, e_mm, e_ga, e_x)
                if ft == 0:
                    wt(nc.vector, e_sx[par], e_sq_prev)
                VE.tag(nc.vector.tensor_tensor(
                    ot[:, :], self.psb[bk][:, :],
                    self.bc0[:, ft * 512:(ft + 1) * 512], ALU.mult))
                e_ev = VE.tag(nc.vector.tensor_tensor(
                    x1b[par][:, ft * 512:(ft + 1) * 512], ot[:, :],
                    xb[par][:, ft * 512:(ft + 1) * 512], ALU.add))
                e_bank[bk] = e_ev
            e_a_used[par] = TE.evt()
            e_x_used[par] = e_ev
            e_r = self.emit_stats(x1b[par][:, :], stt[:, 18 + sc:19 + sc],
                                  stt[:, 34 + sc:35 + sc], [e_ev], par=par)
            e_x1_done[par] = e_r
            e_sq_prev = self.AE.evt()
            wt(nc.scalar, e_ev)
            e_sx[par] = self.dma(nc.scalar, f"sx{par}", self.sx1[sc, :, :],
                                 x1b[par][:, :])
        self.e_p4 = VE.evt()

    def phase5(self):
        nc = self.nc
        VE, AE, GE = self.VE, self.AE, self.GE
        self.barrier()
        self.pst_guard = [None, None]
        stt = self.stats
        e_b0 = self.dma(nc.sync, "bc0", self.bc0[:, :],
                        bass.AP(self.srow.tensor, SROW_MOD + 4 * E,
                                [[0, 128], [1, E]]))
        e_b1 = self.dma(nc.sync, "bc1", self.bc1[:, :],
                        bass.AP(self.srow.tensor, SROW_MOD + 3 * E,
                                [[0, 128], [1, E]]))
        wt(nc.vector, e_b0)
        e_bc = VE.tag(nc.vector.tensor_scalar(
            self.bc0[:, :], self.bc0[:, :], 1.0, None, ALU.add))
        xb = [self.wview(F32, 2048, 0), self.wview(F32, 2048, 8192)]
        zb = self.wview(F32, 2048, 24576)
        hfp = self.wview(F32, 2048, 32768)
        hhi = self.wview(BF16, 2048, 40960)
        hlo = self.wview(BF16, 2048, 45056)
        e_used = [None, None]
        e_hi_prev = e_lo_prev = None
        e_te_hi = e_te_lo = None
        for sc in range(NCH):
            par = sc % 2
            wt(nc.sync, e_used[par])
            e_x = self.dma(nc.sync, f"x{par}", xb[par][:, :],
                           self.sx1[sc, :, :])
            wt(nc.vector, e_x)
            e_z = VE.tag(nc.vector.tensor_scalar(
                zb[:, :], xb[par][:, :], stt[:, 18 + sc:19 + sc],
                stt[:, 34 + sc:35 + sc], ALU.subtract, ALU.mult))
            e_used[par] = e_z
            wt(nc.vector, e_bc, e_b1, e_hi_prev, e_lo_prev)
            VE.tag(nc.vector.tensor_tensor(hfp[:, :], zb[:, :],
                                           self.bc0[:, :], ALU.mult))
            e_h = VE.tag(nc.vector.tensor_tensor(hfp[:, :], hfp[:, :],
                                                 self.bc1[:, :], ALU.add))
            wt(nc.scalar, e_h, e_te_hi)
            e_hi = AE.tag(nc.scalar.copy(hhi[:, :], hfp[:, :]))
            wt(nc.gpsimd, e_hi, e_te_lo)
            e_lo = GE.tag(nc.gpsimd.tensor_sub(hlo[:, :], hfp[:, :],
                                               hhi[:, :]))
            e_te_hi, _ = self.emit_fm_store(hhi, self.hfm_hi, sc, e_hi)
            e_te_lo, _ = self.emit_fm_store(hlo, self.hfm_lo, sc, e_lo)
            e_hi_prev, e_lo_prev = e_hi, e_lo
        if self.want("h2fm"):
            t1 = self.dbg_dram("h2fm_hi", [128, 32768])
            t2 = self.dbg_dram("h2fm_lo", [128, 32768])
            wt(nc.gpsimd, VE.evt())
            self.dma(nc.gpsimd, "dbg", t1[:, :], self.hfm_hi[:, :])
            self.dma(nc.gpsimd, "dbg", t2[:, :], self.hfm_lo[:, :])

    def phase6(self):
        nc = self.nc
        TE, VE, AE, GE = self.TE, self.VE, self.AE, self.GE
        self.barrier()
        e_c1 = self.dma(nc.sync, "ncol", self.cols[:, 1:2],
                        bass.AP(self.srow.tensor, SROW_SC + SLOT_THR_UP,
                                [[0, 128], [1, 1]]))
        e_c2 = self.dma(nc.sync, "ncol", self.cols[:, 2:3],
                        bass.AP(self.srow.tensor, SROW_SC + SLOT_THR_GATE,
                                [[0, 128], [1, 1]]))
        e_c3 = self.dma(nc.sync, "ncol", self.cols[:, 3:4],
                        bass.AP(self.srow.tensor, SROW_SC + SLOT_SUP,
                                [[0, 128], [1, 1]]))
        wt(nc.vector, e_c3)
        e_neg = VE.tag(nc.vector.tensor_scalar(
            self.cols[:, 4:6], self.cols[:, 1:3], -1.0, None, ALU.mult))

        wus = self.wview(F32, 2048, 0)
        wgs = self.wview(F32, 2048, 8192)
        tu = [self.wview(BF16, 2048, 16384), self.wview(BF16, 2048, 20480)]
        tg = [self.wview(BF16, 2048, 24576), self.wview(BF16, 2048, 28672)]
        m2 = self.wview(BF16, 2048, 32768)
        silu = self.wview(F32, 512, 36864)
        hfp = self.wview(F32, 2048, 40960)
        hh = self.wview(BF16, 2048, 49152)
        hl = self.wview(BF16, 2048, 53248)
        e_q = [None, None]
        e_ld_used = None
        e_hid = None
        e_spill = None
        e_hf_prev = None
        e_bank = [None, None, None, None]
        for ec in range(ECH):
            par = ec % 2
            wt(nc.sync, e_ld_used)
            e_lu = self.dma(nc.sync, "wus",
                            self.wap3(F32, 0, [[128, 16], [1, 128]]),
                            bass.AP(self.wupT.tensor, ec * 128,
                                    [[DEX, 128], [DEX * 128, 16], [1, 128]]))
            e_lg = self.dma(nc.sync, "wgs",
                            self.wap3(F32, 8192, [[128, 16], [1, 128]]),
                            bass.AP(self.wgateT.tensor, ec * 128,
                                    [[DEX, 128], [DEX * 128, 16], [1, 128]]))
            wt(nc.gpsimd, e_lu, e_neg, e_q[par])
            g1 = GE.tag(nc.gpsimd.tensor_scalar(
                m2[:, :], wus[:, :], self.cols[:, 4:5], None, ALU.is_le))
            wt(nc.vector, g1, e_c3)
            v1 = VE.tag(nc.vector.scalar_tensor_tensor(
                tu[par][:, :], wus[:, :], self.cols[:, 1:2], m2[:, :],
                ALU.is_ge, ALU.subtract))
            wt(nc.gpsimd, e_lg, v1)
            g2 = GE.tag(nc.gpsimd.tensor_scalar(
                m2[:, :], wgs[:, :], self.cols[:, 5:6], None, ALU.is_le))
            wt(nc.vector, g2, e_c3)
            e_ld_used = VE.tag(nc.vector.scalar_tensor_tensor(
                tg[par][:, :], wgs[:, :], self.cols[:, 2:3], m2[:, :],
                ALU.is_ge, ALU.subtract))
            e_tq = e_ld_used
            for st in range(4):
                bu = st % 2
                bg = 2 + st % 2
                wt(nc.tensor, e_tq, e_bank[bu], e_bank[bg])
                ins = None
                for d in range(16):
                    lt_ = tu[par][:, d * 128:(d + 1) * 128]
                    nc.tensor.matmul(
                        self.psb[bu][:, :], lt_,
                        self.hfm_hi[:, d * 2048 + st * 512:
                                    d * 2048 + st * 512 + 512],
                        start=(d == 0), stop=False)
                    ins = nc.tensor.matmul(
                        self.psb[bu][:, :], lt_,
                        self.hfm_lo[:, d * 2048 + st * 512:
                                    d * 2048 + st * 512 + 512],
                        start=False, stop=(d == 15))
                e_mu = TE.tag(ins)
                for d in range(16):
                    lt_ = tg[par][:, d * 128:(d + 1) * 128]
                    nc.tensor.matmul(
                        self.psb[bg][:, :], lt_,
                        self.hfm_hi[:, d * 2048 + st * 512:
                                    d * 2048 + st * 512 + 512],
                        start=(d == 0), stop=False)
                    ins = nc.tensor.matmul(
                        self.psb[bg][:, :], lt_,
                        self.hfm_lo[:, d * 2048 + st * 512:
                                    d * 2048 + st * 512 + 512],
                        start=False, stop=(d == 15))
                e_mg = TE.tag(ins)
                wt(nc.scalar, e_mu, e_hid, e_c3, e_hf_prev)
                e_si = AE.tag(nc.scalar.activation(
                    silu[:, :], self.psb[bu][:, :], AF.Silu,
                    scale=self.cols[:, 3:4]))
                e_bank[bu] = e_si
                wt(nc.vector, e_si, e_mg)
                e_hf = VE.tag(nc.vector.tensor_tensor(
                    hfp[:, st * 512:(st + 1) * 512], silu[:, :],
                    self.psb[bg][:, :], ALU.mult))
                e_bank[bg] = e_hf
                e_hf_prev = e_hf
            wt(nc.scalar, e_hf, e_spill)
            e_hh = AE.tag(nc.scalar.copy(hh[:, :], hfp[:, :]))
            wt(nc.gpsimd, e_hh)
            e_hid = GE.tag(nc.gpsimd.tensor_sub(hl[:, :], hfp[:, :],
                                                hh[:, :]))
            wt(nc.scalar, e_hid)
            self.dma(nc.scalar, "sph", self.shid_hi[ec, :, :], hh[:, :])
            e_spill = self.dma(nc.scalar, "sph", self.shid_lo[ec, :, :],
                               hl[:, :])
            e_q[par] = TE.evt()
        self.e_p6 = e_spill

    def phase7(self):
        nc = self.nc
        TE, VE, AE, GE = self.TE, self.VE, self.AE, self.GE
        self.barrier()
        stt = self.stats
        yA = self.arenaA.bitcast(F32)
        hb = self.arenaB.bitcast(BF16)
        e_g = self.dma(nc.sync, "bc0", self.bc0[:, :],
                       bass.AP(self.srow.tensor, SROW_MOD + 5 * E,
                               [[0, 128], [1, E]]))
        e_sg = self.dma(nc.sync, "ncol", self.cols[:, 7:8],
                        bass.AP(self.srow.tensor, SROW_SC + SLOT_SGD,
                                [[0, 128], [1, 1]]))
        e_c4 = self.dma(nc.sync, "ncol", self.cols[:, 8:9],
                        bass.AP(self.srow.tensor, SROW_SC + SLOT_THR_DOWN,
                                [[0, 128], [1, 1]]))
        t1 = self.wview(F32, 2048, 0)
        t2 = self.wview(F32, 2048, 8192)
        wt(nc.vector, e_g, e_c4)
        VE.tag(nc.vector.tensor_scalar(t1[:, :], self.bc0[:, :], 0.25, None,
                                       ALU.max))
        VE.tag(nc.vector.tensor_scalar(t2[:, :], self.bc0[:, :], -0.25, None,
                                       ALU.min))
        VE.tag(nc.vector.tensor_scalar(self.bc0[:, :], self.bc0[:, :], 0.0,
                                       None, ALU.is_ge))
        VE.tag(nc.vector.tensor_tensor(t1[:, :], t1[:, :], t2[:, :],
                                       ALU.subtract))
        VE.tag(nc.vector.tensor_tensor(t1[:, :], self.bc0[:, :], t1[:, :],
                                       ALU.mult))
        VE.tag(nc.vector.tensor_tensor(self.bc0[:, :], t2[:, :], t1[:, :],
                                       ALU.add))
        VE.tag(nc.vector.tensor_scalar(self.cols[:, 9:10], self.cols[:, 8:9],
                                       -1.0, None, ALU.mult))
        e_ga = VE.tag(nc.vector.tensor_scalar(
            self.bc0[:, :], self.bc0[:, :], self.cols[:, 7:8], None,
            ALU.mult))

        tdc = [self.wview(F32, 2048, 0), self.wview(F32, 2048, 0)]
        td = [self.wview(BF16, 2048, 8192 + 4096 * i) for i in range(8)]
        m2 = self.wview(BF16, 2048, 40960)
        ocb = self.wview(F32, 1024, 45056)
        x1h = [self.wview(F32, 1024, 49152), self.wview(F32, 1024, 53248)]
        e_bank = [None, None, None, None]
        e_hb_used = None
        e_td_used = [None] * 8
        e_out = None
        e_y_rd = None
        for half in range(2):
            e_ms = VE.tag(nc.vector.memset(yA[:, :], 0.0)) if half == 0 else \
                VE.tag(nc.vector.memset(yA[:, :], 0.0))
            for eb in range(8):
                wt(nc.sync, e_hb_used)
                for el in range(8):
                    self.dma(nc.sync, "hid",
                             hb[:, el * 2048:(el + 1) * 2048],
                             self.shid_hi[eb * 8 + el, :, :])
                    e_hb = self.dma(nc.sync, "hid",
                                    hb[:, 16384 + el * 2048:
                                       16384 + (el + 1) * 2048],
                                    self.shid_lo[eb * 8 + el, :, :])
                e_tq = None
                e_tq_par = [None, None]
                for el in range(8):
                    tp_ = 0
                    wt(nc.sync, e_td_used[el], e_tq, e_ga)
                    e_lt = self.dma(nc.sync, f"tdc{tp_}", tdc[tp_][:, :],
                                    self.wdownT[(eb * 8 + el) * 128:
                                                (eb * 8 + el + 1) * 128, :])
                    wt(nc.vector, e_lt, e_ga, e_ms)
                    VE.tag(nc.vector.tensor_scalar(
                        m2[:, :], tdc[tp_][:, :], self.cols[:, 9:10], None,
                        ALU.is_le))
                    e_tq = VE.tag(nc.vector.scalar_tensor_tensor(
                        td[el][:, :], tdc[tp_][:, :], self.cols[:, 8:9],
                        m2[:, :], ALU.is_ge, ALU.subtract))
                    e_td_used[el] = e_tq
                    e_tq_par[tp_] = e_tq
                last_ev = None
                for sc in range(NCH):
                    for dt_ in range(2):
                        bk = (sc * 2 + dt_) % 4
                        wt(nc.tensor, e_hb, e_tq, e_bank[bk])
                        ins = None
                        for el in range(8):
                            lh = hb[:, el * 2048 + sc * 128:
                                    el * 2048 + sc * 128 + 128]
                            ll = hb[:, 16384 + el * 2048 + sc * 128:
                                    16384 + el * 2048 + sc * 128 + 128]
                            rt = td[el][:, half * 1024 + dt_ * 512:
                                        half * 1024 + dt_ * 512 + 512]
                            nc.tensor.matmul(self.psb[bk][:, :], lh, rt,
                                             start=(el == 0), stop=False)
                            ins = nc.tensor.matmul(self.psb[bk][:, :], ll, rt,
                                                   start=False,
                                                   stop=(el == 7))
                        e_mm = TE.tag(ins)
                        wt(nc.vector, e_mm, e_y_rd)
                        e_acc = VE.tag(nc.vector.tensor_tensor(
                            yA[:, sc * 1024 + dt_ * 512:
                               sc * 1024 + dt_ * 512 + 512],
                            yA[:, sc * 1024 + dt_ * 512:
                               sc * 1024 + dt_ * 512 + 512],
                            self.psb[bk][:, :], ALU.add))
                        e_bank[bk] = e_acc
                        last_ev = e_acc
                e_hb_used = TE.evt()
                for el in range(8):
                    e_td_used[el] = e_hb_used
            e_x1u = [None, None]
            e_oc = None
            for sc in range(NCH):
                par = sc % 2
                wt(nc.sync, e_x1u[par])
                e_x1 = self.dma(nc.sync, f"x1h{par}", x1h[par][:, :],
                                bass.AP(self.sx1.tensor,
                                        sc * 128 * E + half * 1024,
                                        [[E, 128], [1, 1024]]))
                wt(nc.vector, e_x1, last_ev, e_out)
                VE.tag(nc.vector.tensor_tensor(
                    ocb[:, :],
                    yA[:, sc * 1024:sc * 1024 + 1024],
                    self.bc0[:, half * 1024:(half + 1) * 1024], ALU.mult))
                e_oc = VE.tag(nc.vector.tensor_tensor(
                    ocb[:, :], ocb[:, :], x1h[par][:, :], ALU.add))
                e_x1u[par] = e_oc
                wt(nc.scalar, e_oc)
                e_out = self.dma(
                    nc.scalar, "out",
                    bass.AP(self.out.tensor, sc * 128 * E + half * 1024,
                            [[E, 128], [1, 1024]]),
                    ocb[:, :])
            e_y_rd = e_oc

    def finish(self):
        nc = self.nc
        evs = [sl.evt() for sl in self._slots.values()]
        evs += [self.TE.evt(), self.VE.evt(), self.AE.evt()]
        wt(nc.gpsimd, *evs)


def build(dbg=(), upto=0):
    b = Builder(dbg=dbg, upto=upto)
    ncp = b.build()
    return b, ncp


from concourse.bass_utils import run_bass_kernel_spmd


_NC_CACHE = {}


def kernel(L0_attn_x, L0_attn_Wq, L0_attn_Wk, L0_attn_Wv, L0_attn_Wo,
           L0_attn_lambda_q1, L0_attn_lambda_k1, L0_attn_lambda_q2,
           L0_attn_lambda_k2, L0_attn_subln_weight, L0_ffn_W_up,
           L0_ffn_W_gate, L0_ffn_W_down, L0_c, L0_wmod):
    n = 8
    if "nc" not in _NC_CACHE:
        b = Builder(dbg=(), upto=7)
        _NC_CACHE["nc"] = b.build()
    ncp = _NC_CACHE["nc"]
    f32 = np.float32
    shared = {
        "wq": np.ascontiguousarray(L0_attn_Wq, dtype=f32),
        "wk": np.ascontiguousarray(L0_attn_Wk, dtype=f32),
        "wv": np.ascontiguousarray(L0_attn_Wv, dtype=f32),
        "wo": np.ascontiguousarray(L0_attn_Wo, dtype=f32),
        "wmod": np.ascontiguousarray(L0_wmod, dtype=f32),
        "wupT": np.ascontiguousarray(np.asarray(L0_ffn_W_up, dtype=f32).T),
        "wgateT": np.ascontiguousarray(np.asarray(L0_ffn_W_gate, dtype=f32).T),
        "wdownT": np.ascontiguousarray(np.asarray(L0_ffn_W_down, dtype=f32).T),
        "lq1": np.ascontiguousarray(L0_attn_lambda_q1, dtype=f32),
        "lk1": np.ascontiguousarray(L0_attn_lambda_k1, dtype=f32),
        "lq2": np.ascontiguousarray(L0_attn_lambda_q2, dtype=f32),
        "lk2": np.ascontiguousarray(L0_attn_lambda_k2, dtype=f32),
        "subw": np.ascontiguousarray(L0_attn_subln_weight, dtype=f32),
    }
    x = np.asarray(L0_attn_x, dtype=f32)
    c = np.asarray(L0_c, dtype=f32)
    in_maps = []
    for bb in range(n):
        m = dict(shared)
        m["xin"] = np.ascontiguousarray(x[bb])
        m["cvec"] = np.ascontiguousarray(c[bb])
        in_maps.append(m)
    res = run_bass_kernel_spmd(ncp, in_maps, core_ids=list(range(n)))
    return np.stack([np.asarray(r["out"]) for r in res.results], axis=0)


# version 03
# speedup vs baseline: 1.4583x; 1.2532x over previous
import contextlib
import numpy as np
import concourse.bass as bass
import concourse.mybir as mybir

F32 = mybir.dt.float32
BF16 = mybir.dt.bfloat16
U8 = mybir.dt.uint8
AF = mybir.ActivationFunctionType
ALU = mybir.AluOpType
AX = mybir.AxisListType

S = 2048
E = 2048
NCH = 16
NH = 8
HD = 128
DEX = 8192
ECH = 64
LAM_INIT = 0.8 - 0.6 * float(np.exp(-0.3 * 12))
SCALE = HD ** -0.5
EPS_LN = 1e-5
EPS_Q = 1e-8
MASKVAL = -1e30

SROW_MOD = 0
SROW_SC = 6 * E
SLOT_NEGLAM = 0
SLOT_SUP = 1
SLOT_SGD = 2
SLOT_THR_UP = 3
SLOT_THR_GATE = 4
SLOT_THR_DOWN = 5
NSLOT = 8


class Evt:
    __slots__ = ("sem", "v")

    def __init__(self, sem, v):
        self.sem = sem
        self.v = v


class ESem:

    def __init__(self, nc, es, name, eng, flush):
        self.sem = es.enter_context(nc.semaphore(name))
        self.n = 0
        self.eng = eng
        self.flush = flush

    def tag(self, ins):
        ins.then_inc(self.sem, 1)
        self.n += 1
        if self.flush:
            self.eng.wait_ge(self.sem, self.n)
        return Evt(self.sem, self.n)

    def bump(self):
        if self.n > 0:
            self.eng.wait_ge(self.sem, self.n)
        self.eng.nop().then_inc(self.sem, 1)
        self.n += 1
        return Evt(self.sem, self.n)

    def evt(self):
        return Evt(self.sem, self.n)


class Slot:

    def __init__(self, nc, es, name):
        self.sem = es.enter_context(nc.semaphore(name))
        self.n = 0

    def evt(self):
        return Evt(self.sem, self.n)


def wt(eng, *evts):
    for e in evts:
        if e is not None and e.v > 0:
            eng.wait_ge(e.sem, e.v)


class Builder:
    def __init__(self, dbg=(), upto=99):
        self.dbg_names = set(dbg)
        self.upto = upto
        nc = self.nc = bass.Bass("TRN2", target_bir_lowering=False)
        self.es = contextlib.ExitStack()

        def din(name, shape):
            return nc.dram_tensor(name, shape, F32, kind="ExternalInput").ap()

        self.xin = din("xin", [S, E])
        self.cvec = din("cvec", [E])
        self.wq = din("wq", [E, E])
        self.wk = din("wk", [E, E])
        self.wv = din("wv", [E, E])
        self.wo = din("wo", [E, E])
        self.wmod = din("wmod", [E, 6 * E])
        self.wupT = din("wupT", [E, DEX])
        self.wgateT = din("wgateT", [E, DEX])
        self.wdownT = din("wdownT", [DEX, E])
        self.lq1 = din("lq1", [HD])
        self.lk1 = din("lk1", [HD])
        self.lq2 = din("lq2", [HD])
        self.lk2 = din("lk2", [HD])
        self.subw = din("subw", [2 * HD])
        self.out = nc.dram_tensor("out", [S, E], F32, kind="ExternalOutput").ap()

        def scr(name, shape, dt=BF16):
            kind = "ExternalOutput" if ("spill" in self.dbg_names) else "Internal"
            return nc.dram_tensor(name, shape, dt, kind=kind).ap()

        self.srow = scr("srow", [SROW_SC + NSLOT], F32)
        self.sq_hi = scr("sq_hi", [NCH, 128, S])
        self.sq_lo = scr("sq_lo", [NCH, 128, S])
        self.sk_hi = scr("sk_hi", [NCH, 128, S])
        self.sk_lo = scr("sk_lo", [NCH, 128, S])
        self.sv_hi = scr("sv_hi", [NH, NCH, 128, 256])
        self.sv_lo = scr("sv_lo", [NH, NCH, 128, 256])
        self.sa_hi = scr("sa_hi", [NCH, NCH, 128, 128])
        self.sa_lo = scr("sa_lo", [NCH, NCH, 128, 128])
        self.shid_hi = scr("shid_hi", [ECH, 128, S])
        self.shid_lo = scr("shid_lo", [ECH, 128, S])
        self.sx1 = scr("sx1", [NCH, 128, E], F32)

        self.arenaA = nc.alloc_sbuf_tensor("arenaA", [128, 65536], U8)
        self.arenaB = nc.alloc_sbuf_tensor("arenaB", [128, 65536], U8)
        self.arenaW = nc.alloc_sbuf_tensor("arenaW", [128, 57344], U8)
        self.arenaD = nc.alloc_sbuf_tensor("arenaD", [128, 22528], U8)

        self.hfm_hi = self.arenaA.bitcast(BF16)
        self.hfm_lo = self.arenaB.bitcast(BF16)
        D = self.arenaD.bitcast(F32)
        Db = self.arenaD.bitcast(BF16)
        self.bc0 = D[:, 0:2048]
        self.bc1 = D[:, 2048:4096]
        self.sw2 = D[:, 4096:4352]
        self.mask = D[:, 4352:4480]
        self.ident = Db[:, 8960:9088]
        self.cols = D[:, 4544:4608]
        self.stats = D[:, 4608:4864]

        self.psb = [nc.alloc_psum_tensor(f"psb{i}", [128, 512], F32)
                    for i in range(4)]
        self.pst = [nc.alloc_psum_tensor(f"pst{i}", [128, 1024], BF16)
                    for i in range(2)]
        self.ps6 = nc.alloc_psum_tensor("ps6", [128, 512], F32)
        self.ps7 = nc.alloc_psum_tensor("ps7", [128, 512], F32)

        es = self.es
        self.TE = ESem(nc, es, "sTE", nc.tensor, flush=False)
        self.VE = ESem(nc, es, "sVE", nc.vector, flush=True)
        self.AE = ESem(nc, es, "sAE", nc.scalar, flush=True)
        self.GE = ESem(nc, es, "sGE", nc.gpsimd, flush=True)
        self.SE = ESem(nc, es, "sSE", nc.sync, flush=False)
        self.engines = [(nc.tensor, self.TE), (nc.vector, self.VE),
                        (nc.scalar, self.AE), (nc.gpsimd, self.GE),
                        (nc.sync, self.SE)]
        self._slots = {}
        self.dbg_tensors = {}

    def slot(self, name):
        if name not in self._slots:
            self._slots[name] = Slot(self.nc, self.es, "sl_" + name)
        return self._slots[name]

    def dma(self, eng, slotname, out, in_, **kw):
        sl = self.slot(slotname)
        eng.dma_start(out=out, in_=in_, **kw).then_inc(sl.sem, 16)
        sl.n += 16
        return Evt(sl.sem, sl.n)

    def barrier(self):
        for eng, es_ in self.engines:
            es_.bump()
        evs = [es_.evt() for _, es_ in self.engines]
        evs += [sl.evt() for sl in self._slots.values()]
        for eng, _ in self.engines:
            wt(eng, *evs)

    def want(self, name):
        return name in self.dbg_names

    def dbg_dram(self, name, shape, dt=F32):
        t = self.nc.dram_tensor("dbg_" + name, list(shape), dt,
                                kind="ExternalOutput").ap()
        self.dbg_tensors[name] = t
        return t

    def wview(self, dt, cols, off_bytes):
        a = self.arenaW.bitcast(dt)
        isz = {F32: 4, BF16: 2, U8: 1}[dt]
        assert off_bytes % isz == 0
        c0 = off_bytes // isz
        assert (c0 + cols) * isz <= 57344, (off_bytes, cols)
        return a[:, c0:c0 + cols]

    def wap3(self, dt, off_bytes, dims):
        h = self.arenaW.bitcast(dt)
        isz = {F32: 4, BF16: 2, U8: 1}[dt]
        width = 57344 // isz
        assert off_bytes % isz == 0
        return bass.AP(h, off_bytes // isz, [[width, 128]] + list(dims))

    def build(self):
        nc = self.nc
        with self.es, nc.Block():
            with nc.named_scope("p0mod"):
                self.phase0()
            if self.upto >= 1:
                with nc.named_scope("p1ln1"):
                    self.phase1()
            if self.upto >= 2:
                with nc.named_scope("p2qkv"):
                    self.phase2()
            if self.upto >= 3:
                with nc.named_scope("p3attn"):
                    self.phase3()
            if self.upto >= 4:
                with nc.named_scope("p4wo"):
                    self.phase4()
            if self.upto >= 5:
                with nc.named_scope("p5ln2"):
                    self.phase5()
            if self.upto >= 6:
                with nc.named_scope("p6upgate"):
                    self.phase6()
            if self.upto >= 7:
                with nc.named_scope("p7down"):
                    self.phase7()
            self.finish()
        return nc

    def phase0(self):
        nc = self.nc
        TE, VE, AE, GE = self.TE, self.VE, self.AE, self.GE

        cfm = self.wview(F32, 16, 0)
        lv = self.wview(F32, 512, 256)
        lt = self.wview(F32, 16, 2304)
        mrow = [self.wview(F32, 512, 3584), self.wview(F32, 512, 5632)]
        wmb = [self.wview(F32, 6144, 8192), self.wview(F32, 6144, 32768)]

        GE.tag(nc.gpsimd.memset(self.stats[:, :], 0.0))
        GE.tag(nc.gpsimd.memset(self.cols[:, :], 0.0))
        GE.tag(nc.gpsimd.memset(self.cols[:, 60:64], 1.0))
        GE.tag(nc.gpsimd.memset(self.ident[:, :], 0.0))
        GE.tag(nc.gpsimd.affine_select(
            out=self.ident[:, :], in_=self.ident[:, :],
            compare_op=ALU.not_equal, fill=1.0, base=0,
            pattern=[[-1, 128]], channel_multiplier=1))
        GE.tag(nc.gpsimd.memset(self.mask[:, :], 0.0))
        self.e_consts = GE.tag(nc.gpsimd.affine_select(
            out=self.mask[:, :], in_=self.mask[:, :],
            compare_op=ALU.is_ge, fill=MASKVAL, base=0,
            pattern=[[-1, 128]], channel_multiplier=1))

        self.dma(nc.sync, "lam", lv[0:1, 0:128], self.lq1[None, :])
        self.dma(nc.sync, "lam", lv[0:1, 128:256], self.lk1[None, :])
        self.dma(nc.sync, "lam", lv[0:1, 256:384], self.lq2[None, :])
        e_l = self.dma(nc.sync, "lam", lv[0:1, 384:512], self.lk2[None, :])
        e_c = self.dma(nc.sync, "cfm", cfm[:, 0:16],
                       bass.AP(self.cvec.tensor, 0, [[1, 128], [128, 16]]),
                       allow_slow_non_contiguous=True)
        e_swb = self.dma(nc.sync, "swb", self.sw2[:, :],
                         bass.AP(self.subw.tensor, 0, [[0, 128], [1, 256]]))
        wt(nc.vector, e_swb)
        self.e_sw2 = VE.tag(nc.vector.tensor_scalar(
            self.sw2[:, :], self.sw2[:, :], 1.0 - LAM_INIT, None, ALU.mult))

        wt(nc.scalar, e_c)
        e_sc = AE.tag(nc.scalar.activation(cfm[:, 0:16], cfm[:, 0:16], AF.Silu))

        wt(nc.vector, e_l)
        VE.tag(nc.vector.tensor_tensor(lv[0:1, 0:128], lv[0:1, 0:128],
                                       lv[0:1, 128:256], ALU.mult))
        VE.tag(nc.vector.tensor_tensor(lv[0:1, 256:384], lv[0:1, 256:384],
                                       lv[0:1, 384:512], ALU.mult))
        nc.vector.reduce_sum(lt[0:1, 0:1], lv[0:1, 0:128], axis=AX.X)
        e_vr = VE.tag(nc.vector.reduce_sum(lt[0:1, 1:2], lv[0:1, 256:384],
                                           axis=AX.X))
        wt(nc.scalar, e_vr)
        e_ae = AE.tag(nc.scalar.activation(lt[0:1, 2:4], lt[0:1, 0:2], AF.Exp))
        wt(nc.vector, e_ae)
        VE.tag(nc.vector.tensor_tensor(lt[0:1, 4:5], lt[0:1, 2:3],
                                       lt[0:1, 3:4], ALU.subtract))
        e_nl = VE.tag(nc.vector.tensor_scalar(
            lt[0:1, 5:6], lt[0:1, 4:5], LAM_INIT, -1.0, ALU.add, ALU.mult))
        wt(nc.gpsimd, e_nl)
        self.dma(nc.gpsimd, "srow_sc",
                 bass.AP(self.srow.tensor, SROW_SC + SLOT_NEGLAM,
                         [[1, 1], [1, 1]]),
                 lt[0:1, 5:6])

        wmt = self.wmod.tensor
        psm = [self.ps6, self.ps7]
        ev_te = [None, None]
        ev_cp = [None, None]
        ev_wr = [None, None]
        WMW = 384
        for t in range(32):
            par = t % 2
            wt(nc.sync, ev_te[par])
            w3 = self.wap3(F32, 8192 + par * 24576, [[WMW, 16], [1, WMW]])
            e_ld = self.dma(nc.sync, f"wm{par}", w3,
                            bass.AP(wmt, t * WMW,
                                    [[6 * E, 128], [6 * E * 128, 16],
                                     [1, WMW]]))
            wt(nc.tensor, e_ld, e_sc, ev_cp[par])
            ins = None
            for j in range(16):
                rj = wmb[par][:, j * WMW:(j + 1) * WMW]
                ins = nc.tensor.matmul(
                    psm[par][0:1, 0:WMW], cfm[:, j:j + 1], rj,
                    start=(j == 0), stop=(j == 15))
            ev_te[par] = TE.tag(ins)
            wt(nc.vector, ev_te[par], ev_wr[par])
            e_cp = VE.tag(nc.vector.tensor_copy(mrow[par][0:1, 0:WMW],
                                                psm[par][0:1, 0:WMW]))
            ev_cp[par] = e_cp
            wt(nc.gpsimd, e_cp)
            ev_wr[par] = self.dma(
                nc.gpsimd, f"mrow{par}",
                bass.AP(self.srow.tensor, SROW_MOD + t * WMW,
                        [[1, 1], [1, WMW]]),
                mrow[par][0:1, 0:WMW])
        self.e_modrow = [ev_wr[0], ev_wr[1]]

        if self.want("mod"):
            t = self.dbg_dram("mod", [6 * E])
            wt(nc.gpsimd, *self.e_modrow)
            self.dma(nc.gpsimd, "dbg", t[None, :],
                     bass.AP(self.srow.tensor, 0, [[1, 1], [1, 6 * E]]))

        wt(nc.sync, *self.e_modrow)
        e_b0 = self.dma(nc.sync, "bc0", self.bc0[:, :],
                        bass.AP(self.srow.tensor, SROW_MOD + E,
                                [[0, 128], [1, E]]))
        e_b1 = self.dma(nc.sync, "bc1", self.bc1[:, :],
                        bass.AP(self.srow.tensor, SROW_MOD + 0,
                                [[0, 128], [1, E]]))
        wt(nc.vector, e_b0)
        self.e_bc = VE.tag(nc.vector.tensor_scalar(
            self.bc0[:, :], self.bc0[:, :], 1.0, None, ALU.add))
        self.e_bc_ld = e_b1

    def emit_stats(self, xb, mcol, rcol, pre_evts, eps=EPS_LN, n=E, par=0):
        nc = self.nc
        VE, AE = self.VE, self.AE
        stt = self.stats
        dump = self.wview(F32, 2048, 16384)
        wt(nc.scalar, *pre_evts)
        e_sq = AE.tag(nc.scalar.activation(dump[:, 0:n], xb, AF.Square,
                                           accum_out=stt[:, 12 + par:13 + par]))
        wt(nc.vector, *pre_evts)
        VE.tag(nc.vector.reduce_sum(stt[:, 1:2], xb, axis=AX.X))
        VE.tag(nc.vector.tensor_scalar(mcol, stt[:, 1:2], 1.0 / n, None,
                                       ALU.mult))
        nc.vector.tensor_tensor(stt[:, 3:4], mcol, mcol, ALU.mult)
        wt(nc.vector, e_sq)
        VE.tag(nc.vector.tensor_scalar(stt[:, 4:5], stt[:, 12 + par:13 + par],
                                       1.0 / n, None, ALU.mult))
        VE.tag(nc.vector.tensor_tensor(stt[:, 5:6], stt[:, 4:5], stt[:, 3:4],
                                       ALU.subtract))
        e_u = VE.tag(nc.vector.tensor_scalar(stt[:, 5:6], stt[:, 5:6], eps,
                                             None, ALU.add))
        wt(nc.scalar, e_u)
        e_sq2 = AE.tag(nc.scalar.activation(stt[:, 6:7], stt[:, 5:6], AF.Sqrt))
        wt(nc.vector, e_sq2)
        VE.tag(nc.vector.reciprocal(stt[:, 7:8], stt[:, 6:7]))
        VE.tag(nc.vector.tensor_tensor(stt[:, 8:9], stt[:, 7:8], stt[:, 7:8],
                                       ALU.mult))
        VE.tag(nc.vector.tensor_tensor(stt[:, 9:10], stt[:, 8:9], stt[:, 5:6],
                                       ALU.mult))
        VE.tag(nc.vector.tensor_scalar(stt[:, 9:10], stt[:, 9:10], -0.5, 1.5,
                                       ALU.mult, ALU.add))
        e_r = VE.tag(nc.vector.tensor_tensor(rcol, stt[:, 7:8], stt[:, 9:10],
                                             ALU.mult))
        return e_r

    def emit_fm_store(self, src_bf16, dest_handle, sc, e_src, w=2048,
                      dest_width=32768, dest_col0=0, sc_stride=2048):
        nc = self.nc
        TE, VE = self.TE, self.VE
        nj = w // 128
        e_ev = None
        e_t = None
        for g in range(0, nj, 8):
            gb = min(8, nj - g)
            bi = (g // 8) % 2
            pbank = self.pst[bi]
            wt(nc.tensor, self.pst_guard[bi], e_src)
            ins = None
            for jj in range(gb):
                j = g + jj
                ins = nc.tensor.transpose(
                    pbank[:, jj * 128:(jj + 1) * 128],
                    src_bf16[:, j * 128:(j + 1) * 128], self.ident[:, :])
            e_t = TE.tag(ins)
            wt(nc.vector, e_t)
            outap = bass.AP(dest_handle,
                            dest_col0 + g * sc_stride + sc * 128,
                            [[dest_width, 128], [sc_stride, gb], [1, 128]])
            inap = bass.AP(pbank, 0, [[1024, 128], [128, gb], [1, 128]])
            e_ev = VE.tag(nc.vector.tensor_copy(outap, inap))
            self.pst_guard[bi] = e_ev
        return e_t, e_ev

    def phase1(self):
        nc = self.nc
        VE, AE, GE = self.VE, self.AE, self.GE
        self.barrier()
        self.pst_guard = [None, None]
        xb = [self.wview(F32, 2048, 0), self.wview(F32, 2048, 8192)]
        zb = self.wview(F32, 2048, 24576)
        hfp = self.wview(F32, 2048, 32768)
        hhi = self.wview(BF16, 2048, 40960)
        hlo = self.wview(BF16, 2048, 45056)
        stt = self.stats
        e_used = [None, None]
        e_hi_prev = e_lo_prev = None
        e_te_hi = e_te_lo = None
        for sc in range(NCH):
            par = sc % 2
            wt(nc.sync, e_used[par])
            e_x = self.dma(nc.sync, f"x{par}", xb[par][:, :],
                           self.xin[sc * 128:(sc + 1) * 128, :])
            e_r = self.emit_stats(xb[par], stt[:, 16:17], stt[:, 17:18],
                                  [e_x], par=par)
            e_z = VE.tag(nc.vector.tensor_scalar(
                zb[:, :], xb[par][:, :], stt[:, 16:17],
                stt[:, 17:18], ALU.subtract, ALU.mult))
            e_used[par] = e_z
            wt(nc.vector, self.e_bc, self.e_bc_ld, e_hi_prev, e_lo_prev)
            VE.tag(nc.vector.tensor_tensor(hfp[:, :], zb[:, :],
                                           self.bc0[:, :], ALU.mult))
            e_h = VE.tag(nc.vector.tensor_tensor(hfp[:, :], hfp[:, :],
                                                 self.bc1[:, :], ALU.add))
            wt(nc.scalar, e_h, e_te_hi)
            e_hi = AE.tag(nc.scalar.copy(hhi[:, :], hfp[:, :]))
            wt(nc.gpsimd, e_hi, e_te_lo)
            e_lo = GE.tag(nc.gpsimd.tensor_sub(hlo[:, :], hfp[:, :],
                                               hhi[:, :]))
            e_te_hi, _ = self.emit_fm_store(hhi, self.hfm_hi, sc, e_hi)
            e_te_lo, _ = self.emit_fm_store(hlo, self.hfm_lo, sc, e_lo)
            e_hi_prev, e_lo_prev = e_hi, e_lo
        self.e_h1 = VE.evt()
        if self.want("h1fm"):
            t1 = self.dbg_dram("h1fm_hi", [128, 32768])
            t2 = self.dbg_dram("h1fm_lo", [128, 32768])
            wt(nc.gpsimd, VE.evt())
            self.dma(nc.gpsimd, "dbg", t1[:, :], self.hfm_hi[:, :])
            self.dma(nc.gpsimd, "dbg", t2[:, :], self.hfm_lo[:, :])

    def phase2(self):
        nc = self.nc
        TE, VE, AE, GE = self.TE, self.VE, self.AE, self.GE
        self.barrier()
        stt = self.stats

        self.p1_count = 0
        p1_mats = [self.wupT, self.wgateT, self.wdownT]
        self.e_p1_act = None
        pbuf = [self.wview(F32, 2048, 36864), self.wview(F32, 2048, 36864)]

        def pass1_chunk():
            c = self.p1_count
            if c >= 192:
                return
            self.p1_count += 1
            mat = p1_mats[c // 64]
            sub = c % 64
            par = 0
            wt(nc.gpsimd, self.e_p1_act if c >= 1 else None)
            e_ld = self.dma(nc.gpsimd, "p1a", pbuf[par][:, :],
                            bass.AP(mat.tensor, sub * 128 * 2048,
                                    [[2048, 128], [1, 2048]]))
            wt(nc.scalar, e_ld)
            self.e_p1_act = AE.tag(nc.scalar.activation(
                pbuf[par][:, 0:2048], pbuf[par][:, 0:2048], AF.Abs,
                accum_out=stt[:, 64 + c:65 + c]))

        wsf = [self.wview(F32, 2048, 0), self.wview(F32, 2048, 4096 + 4096)]
        whi = [self.wview(BF16, 2048, 16384), self.wview(BF16, 2048, 20480)]
        wlo = [self.wview(BF16, 2048, 24576 - 16384 + 12288) if False else
               self.wview(BF16, 2048, 24576), self.wview(BF16, 2048, 28672)]
        sthi = self.wview(BF16, 2048, 45056)
        stlo = self.wview(BF16, 2048, 49152)
        qfp = self.wview(F32, 512, 32768)
        mats = [(self.wq, self.sq_hi, self.sq_lo, SCALE),
                (self.wk, self.sk_hi, self.sk_lo, 1.0)]

        def qk_load(i):
            wmat = mats[i // NCH][0]
            m = i % NCH
            return self.dma(nc.sync, f"ws{i % 2}",
                            self.wap3(F32, [0, 8192][i % 2],
                                      [[128, 16], [1, 128]]),
                            bass.AP(wmat.tensor, m * 128,
                                    [[E, 128], [E * 128, 16], [1, 128]]))

        e_split = [None, None]
        e_te_hl = [None, None]
        e_bank = [None, None]
        e_spill = None
        e_stage_lo = None
        e_ld = [None, None]
        e_ld[0] = qk_load(0)
        for i in range(2 * NCH):
            par = i % 2
            mat_i, m = i // NCH, i % NCH
            dst_hi, dst_lo, qs = mats[mat_i][1], mats[mat_i][2], mats[mat_i][3]
            if i + 1 < 2 * NCH:
                wt(nc.sync, e_split[1 - par])
                e_ld[1 - par] = qk_load(i + 1)
                wt(nc.scalar, e_ld[1 - par], e_te_hl[1 - par])
                e_hi_n = AE.tag(nc.scalar.copy(whi[1 - par][:, :],
                                               wsf[1 - par][:, :]))
                wt(nc.vector, e_hi_n)
                e_split[1 - par] = VE.tag(nc.vector.tensor_sub(
                    wlo[1 - par][:, :], wsf[1 - par][:, :],
                    whi[1 - par][:, :]))
            if i == 0:
                wt(nc.scalar, e_ld[0])
                e_hi0 = AE.tag(nc.scalar.copy(whi[0][:, :], wsf[0][:, :]))
                wt(nc.vector, e_hi0)
                e_split[0] = VE.tag(nc.vector.tensor_sub(
                    wlo[0][:, :], wsf[0][:, :], whi[0][:, :]))
            for st in range(4):
                bk = st % 2
                wt(nc.tensor, e_split[par], e_bank[bk])
                ins = None
                for f in range(16):
                    lt_ = whi[par][:, f * 128:(f + 1) * 128]
                    nc.tensor.matmul(
                        self.psb[bk][:, :], lt_,
                        self.hfm_hi[:, f * 2048 + st * 512:
                                    f * 2048 + st * 512 + 512],
                        start=(f == 0), stop=False)
                    nc.tensor.matmul(
                        self.psb[bk][:, :], lt_,
                        self.hfm_lo[:, f * 2048 + st * 512:
                                    f * 2048 + st * 512 + 512],
                        start=False, stop=False)
                for f in range(16):
                    ins = nc.tensor.matmul(
                        self.psb[bk][:, :],
                        wlo[par][:, f * 128:(f + 1) * 128],
                        self.hfm_hi[:, f * 2048 + st * 512:
                                    f * 2048 + st * 512 + 512],
                        start=False, stop=(f == 15))
                e_mm = TE.tag(ins)
                wt(nc.vector, e_mm, e_stage_lo)
                e_q = VE.tag(nc.vector.tensor_scalar(
                    qfp[:, :], self.psb[bk][:, :], qs, None, ALU.mult))
                e_bank[bk] = e_q
                wt(nc.scalar, e_q, e_spill)
                e_sh = AE.tag(nc.scalar.copy(
                    sthi[:, st * 512:(st + 1) * 512], qfp[:, :]))
                wt(nc.gpsimd, e_sh)
                e_stage_lo = GE.tag(nc.gpsimd.tensor_sub(
                    stlo[:, st * 512:(st + 1) * 512], qfp[:, :],
                    sthi[:, st * 512:(st + 1) * 512]))
            e_te_hl[par] = TE.evt()
            wt(nc.scalar, e_stage_lo)
            self.dma(nc.scalar, "spq", dst_hi[m, :, :], sthi[:, :])
            e_spill = self.dma(nc.scalar, "spq", dst_lo[m, :, :],
                               stlo[:, :])
            for _ in range(6):
                pass1_chunk()
        while self.p1_count < 192:
            pass1_chunk()

        self.barrier()
        e_bank = [None, None]
        e_spill = None
        e_stage_lo = None
        wvf = self.wview(F32, 4096, 0)
        vhi = self.wview(BF16, 4096, 16384)
        vlo = self.wview(BF16, 4096, 24576)
        vsh = self.wview(BF16, 256, 32768)
        vsl = self.wview(BF16, 256, 33280)
        vfp = self.wview(F32, 256, 33792)
        e_vsplit = None
        for h in range(NH):
            wt(nc.sync, e_vsplit, e_te_hl[0], e_te_hl[1])
            e_ld = self.dma(nc.sync, "ws",
                            self.wap3(F32, 0, [[256, 16], [1, 256]]),
                            bass.AP(self.wv.tensor, h * 256,
                                    [[E, 128], [E * 128, 16], [1, 256]]))
            wt(nc.scalar, e_ld)
            e_hi = AE.tag(nc.scalar.copy(vhi[:, :], wvf[:, :]))
            wt(nc.gpsimd, e_hi)
            e_lo = GE.tag(nc.gpsimd.tensor_sub(vlo[:, :], wvf[:, :],
                                               vhi[:, :]))
            e_vsplit = e_lo
            for sc in range(NCH):
                bk = sc % 2
                wt(nc.tensor, e_lo, e_bank[bk])
                ins = None
                for f in range(16):
                    lh = self.hfm_hi[:, f * 2048 + sc * 128:
                                     f * 2048 + sc * 128 + 128]
                    rh = vhi[:, f * 256:(f + 1) * 256]
                    rl = vlo[:, f * 256:(f + 1) * 256]
                    nc.tensor.matmul(self.psb[bk][:, 0:256], lh, rh,
                                     start=(f == 0), stop=False)
                    nc.tensor.matmul(self.psb[bk][:, 0:256], lh, rl,
                                     start=False, stop=False)
                for f in range(16):
                    ll = self.hfm_lo[:, f * 2048 + sc * 128:
                                     f * 2048 + sc * 128 + 128]
                    rh = vhi[:, f * 256:(f + 1) * 256]
                    ins = nc.tensor.matmul(self.psb[bk][:, 0:256], ll, rh,
                                           start=False, stop=(f == 15))
                e_mm = TE.tag(ins)
                wt(nc.vector, e_mm, e_stage_lo)
                e_q = VE.tag(nc.vector.tensor_copy(vfp[:, 0:256],
                                                   self.psb[bk][:, 0:256]))
                e_bank[bk] = e_q
                wt(nc.scalar, e_q, e_spill)
                e_sh = AE.tag(nc.scalar.copy(vsh[:, 0:256], vfp[:, 0:256]))
                wt(nc.gpsimd, e_sh)
                e_stage_lo = GE.tag(nc.gpsimd.tensor_sub(
                    vsl[:, 0:256], vfp[:, 0:256], vsh[:, 0:256]))
                wt(nc.scalar, e_stage_lo)
                self.dma(nc.scalar, "spv", self.sv_hi[h, sc, :, :],
                         vsh[:, 0:256])
                e_spill = self.dma(nc.scalar, "spv", self.sv_lo[h, sc, :, :],
                                   vsl[:, 0:256])
            e_te_hl = [TE.evt(), TE.evt()]

        wt(nc.vector, self.e_p1_act)
        for mi in range(3):
            c0 = 64 + mi * 64
            VE.tag(nc.vector.reduce_sum(stt[:, 50 + mi:51 + mi],
                                        stt[:, c0:c0 + 64], axis=AX.X))
        e_ps = VE.evt()
        wt(nc.tensor, e_ps, e_bank[0], e_bank[1])
        ins = None
        for mi in range(3):
            ins = nc.tensor.matmul(self.psb[0][0:1, mi:mi + 1],
                                   self.cols[:, 60:61],
                                   stt[:, 50 + mi:51 + mi],
                                   start=(mi == 0), stop=(mi == 2))
        e_sc3 = TE.tag(ins)
        wt(nc.vector, e_sc3)
        VE.tag(nc.vector.tensor_scalar(stt[0:1, 53:56], self.psb[0][0:1, 0:3],
                                       1.0 / (E * DEX), EPS_Q, ALU.mult,
                                       ALU.add))
        VE.tag(nc.vector.tensor_scalar(stt[0:1, 56:59], stt[0:1, 53:56], 0.5,
                                       None, ALU.mult))
        e_fin = VE.tag(nc.vector.tensor_tensor(
            stt[0:1, 59:60], stt[0:1, 54:55], stt[0:1, 55:56], ALU.mult))
        self.e_bank2 = VE.evt()
        wt(nc.gpsimd, e_fin)
        for slot_i, col in [(SLOT_SUP, 53), (SLOT_SGD, 59), (SLOT_THR_UP, 56),
                            (SLOT_THR_GATE, 57), (SLOT_THR_DOWN, 58)]:
            self.e_srow_sc = self.dma(
                nc.gpsimd, "srow_sc",
                bass.AP(self.srow.tensor, SROW_SC + slot_i, [[1, 1], [1, 1]]),
                stt[0:1, col:col + 1])

    def phase3(self):
        nc = self.nc
        TE, VE, AE, GE = self.TE, self.VE, self.AE, self.GE
        self.barrier()
        stt = self.stats
        e_nc = self.dma(nc.sync, "ncol", self.cols[:, 0:1],
                        bass.AP(self.srow.tensor, SROW_SC + SLOT_NEGLAM,
                                [[0, 128], [1, 1]]))
        AR = [self.hfm_hi, self.hfm_lo]

        def hv(par, c0, w):
            return AR[par][:, c0:c0 + w]

        pfp = self.wview(F32, 2048, 0)
        phi = self.wview(BF16, 2048, 8192)
        plo = self.wview(BF16, 2048, 12288)
        ptb_hi = self.wview(BF16, 2048, 16384)
        ptb_lo = self.wview(BF16, 2048, 20480)
        afp = self.wview(F32, 4096, 24576)
        abuf_hi = self.wview(BF16, 4096, 40960)
        abuf_lo = self.wview(BF16, 4096, 49152)

        def load_head(h):
            par = h % 2
            evs = []
            for i, src in enumerate([self.sq_hi, self.sq_lo,
                                     self.sk_hi, self.sk_lo]):
                for br in range(2):
                    m = 2 * h + br
                    evs.append(self.dma(
                        nc.sync, f"qk{par}",
                        hv(par, (2 * i + br) * 2048, 2048),
                        src[m, :, :]))
            for i, src in enumerate([self.sv_hi, self.sv_lo]):
                evs.append(self.dma(
                    nc.sync, f"qk{par}",
                    hv(par, 16384 + i * 4096, 4096),
                    bass.AP(src.tensor, h * NCH * 128 * 256,
                            [[256, 128], [128 * 256, 16], [1, 256]])))
            return evs[-1]

        e_ld = [None, None]
        e_ld[0] = load_head(0)
        e_arena_free = [None, None]
        e_bank_exp = [None, None, None, None]
        e_bank_mask = [None] * 4
        e_pst_ev = [None, None]
        e_o_read = None
        e_spill_a = None
        e_pfp_free = None
        e_plo = None
        for h in range(NH):
            par = h % 2
            if h + 1 < NH:
                wt(nc.sync, e_arena_free[1 - par])
                e_ld[1 - par] = load_head(h + 1)
            q_of = [0, 2048]
            k_of = [8192, 10240]
            for qc in range(NCH):
                nfull = qc // 4
                rem = qc % 4
                remw = (rem + 1) * 128
                for bi in range(2):
                    qhi = hv(par, q_of[bi], 2048)[:, qc * 128:(qc + 1) * 128]
                    qlo = hv(par, q_of[bi] + 4096, 2048)[:, qc * 128:(qc + 1) * 128]
                    khi = hv(par, k_of[bi], 2048)
                    klo = hv(par, k_of[bi] + 4096, 2048)
                    wt(nc.tensor, e_ld[par])
                    for kt in range(nfull + 1):
                        w = 512 if kt < nfull else remw
                        bank = self.psb[kt]
                        wt(nc.tensor, e_bank_exp[kt], e_bank_mask[kt])
                        c0 = kt * 512
                        nc.tensor.matmul(bank[:, 0:w], qhi,
                                         khi[:, c0:c0 + w],
                                         start=True, stop=False)
                        nc.tensor.matmul(bank[:, 0:w], qhi,
                                         klo[:, c0:c0 + w],
                                         start=False, stop=False)
                        ins = nc.tensor.matmul(bank[:, 0:w], qlo,
                                               khi[:, c0:c0 + w],
                                               start=False, stop=True)
                        e_mm = TE.tag(ins)
                        e_bank_exp[kt] = None
                        e_bank_mask[kt] = None
                    wt(nc.vector, e_mm, self.e_consts)
                    e_msk = VE.tag(nc.vector.tensor_tensor(
                        self.psb[nfull][:, remw - 128:remw],
                        self.psb[nfull][:, remw - 128:remw],
                        self.mask[:, :], ALU.add))
                    e_bank_mask[nfull] = e_msk
                    for kt in range(nfull + 1):
                        w = 512 if kt < nfull else remw
                        VE.tag(nc.vector.tensor_reduce(
                            stt[:, kt:kt + 1], self.psb[kt][:, 0:w],
                            axis=AX.X, op=ALU.max))
                    e_mx = VE.tag(nc.vector.tensor_reduce(
                        stt[:, 5:6], stt[:, 0:nfull + 1], axis=AX.X,
                        op=ALU.max, negate=True))
                    wt(nc.scalar, e_mx, e_pfp_free, e_plo)
                    for kt in range(nfull + 1):
                        w = 512 if kt < nfull else remw
                        e_exp = AE.tag(nc.scalar.activation(
                            pfp[:, kt * 512:kt * 512 + w],
                            self.psb[kt][:, 0:w], AF.Exp,
                            bias=stt[:, 5:6],
                            accum_out=stt[:, 8 + kt:9 + kt]))
                        e_bank_exp[kt] = e_exp
                    kact = qc * 128 + 128
                    wt(nc.vector, e_exp)
                    VE.tag(nc.vector.reduce_sum(stt[:, 13:14],
                                                stt[:, 8:9 + nfull], axis=AX.X))
                    e_zr = VE.tag(nc.vector.reciprocal(
                        stt[:, 14 + bi:15 + bi], stt[:, 13:14]))
                    if self.want("attn0") and h == 0 and qc == 1:
                        tpp = self.dbg_dram(f"pfp{bi}", [128, 256])
                        tzz = self.dbg_dram(f"zst{bi}", [128, 24])
                        wt(nc.gpsimd, e_zr)
                        self.dma(nc.gpsimd, "dbg", tpp[:, :], pfp[:, 0:256])
                        e_dbg = self.dma(nc.gpsimd, "dbg", tzz[:, :],
                                         stt[:, 0:24])
                        wt(nc.vector, e_dbg)
                        wt(nc.scalar, e_dbg)
                    e_phi = VE.tag(nc.vector.tensor_copy(phi[:, 0:kact],
                                                         pfp[:, 0:kact]))
                    wt(nc.gpsimd, e_phi)
                    e_plo = GE.tag(nc.gpsimd.tensor_sub(
                        plo[:, 0:kact], pfp[:, 0:kact], phi[:, 0:kact]))
                    e_pfp_free = e_plo
                    nkc = qc + 1
                    for src, dstv, e_src in ((phi, ptb_hi, e_phi),
                                             (plo, ptb_lo, e_plo)):
                        for g in range(0, nkc, 8):
                            gb = min(8, nkc - g)
                            tb = (g // 8) % 2
                            wt(nc.tensor, e_pst_ev[tb], e_src, e_spill_a)
                            for jj in range(gb):
                                kc = g + jj
                                ins = nc.tensor.transpose(
                                    self.pst[tb][:, jj * 128:(jj + 1) * 128],
                                    src[:, kc * 128:(kc + 1) * 128],
                                    self.ident[:, :])
                            e_t = TE.tag(ins)
                            wt(nc.vector, e_t)
                            outap = bass.AP(
                                dstv.tensor, dstv.offset + g * 128,
                                [[28672, 128], [1, gb * 128]])
                            inap = bass.AP(self.pst[tb], 0,
                                           [[1024, 128], [1, gb * 128]])
                            e_pst_ev[tb] = VE.tag(
                                nc.vector.tensor_copy(outap, inap))
                    e_ptb = e_pst_ev
                    ops = self.ps6 if bi == 0 else self.ps7
                    wt(nc.tensor, e_pst_ev[0], e_pst_ev[1], e_o_read)
                    ins = None
                    for kc in range(nkc):
                        vh = hv(par, 16384, 4096)[:, kc * 256:(kc + 1) * 256]
                        vl = hv(par, 20480, 4096)[:, kc * 256:(kc + 1) * 256]
                        ph_ = ptb_hi[:, kc * 128:(kc + 1) * 128]
                        pl_ = ptb_lo[:, kc * 128:(kc + 1) * 128]
                        nc.tensor.matmul(ops[:, 0:256], ph_, vh,
                                         start=(kc == 0), stop=False)
                        nc.tensor.matmul(ops[:, 0:256], ph_, vl,
                                         start=False, stop=False)
                        ins = nc.tensor.matmul(ops[:, 0:256], pl_, vh,
                                               start=False,
                                               stop=(kc == nkc - 1))
                    e_pv = TE.tag(ins)
                    if bi == 1:
                        e_pv2 = e_pv
                wt(nc.vector, e_pv2, e_nc)
                e_nlz = VE.tag(nc.vector.tensor_tensor(
                    stt[:, 16:17], self.cols[:, 0:1], stt[:, 15:16],
                    ALU.mult))
                VE.tag(nc.vector.tensor_scalar(
                    pfp[:, 0:256], self.ps6[:, 0:256], stt[:, 14:15], None,
                    ALU.mult))
                e_comb = VE.tag(nc.vector.scalar_tensor_tensor(
                    afp[:, qc * 256:(qc + 1) * 256], self.ps7[:, 0:256],
                    stt[:, 16:17], pfp[:, 0:256], ALU.mult, ALU.add))
                e_o_read = e_comb
                e_pfp_free = e_comb
                if self.want("attn0") and h == 0 and qc == 1:
                    to_ = self.dbg_dram("acomb", [128, 256])
                    to6 = self.dbg_dram("o6", [128, 256])
                    to7 = self.dbg_dram("o7", [128, 256])
                    wt(nc.vector, e_comb)
                    e_c6 = VE.tag(nc.vector.tensor_copy(pfp[:, 512:768],
                                                        self.ps6[:, 0:256]))
                    e_c7 = VE.tag(nc.vector.tensor_copy(pfp[:, 768:1024],
                                                        self.ps7[:, 0:256]))
                    wt(nc.gpsimd, e_c7)
                    self.dma(nc.gpsimd, "dbg", to_[:, :],
                             afp[:, qc * 256:(qc + 1) * 256])
                    self.dma(nc.gpsimd, "dbg", to6[:, :], pfp[:, 512:768])
                    e_dbg2 = self.dma(nc.gpsimd, "dbg", to7[:, :],
                                      pfp[:, 768:1024])
                    wt(nc.vector, e_dbg2)
                    wt(nc.scalar, e_dbg2)
                wt(nc.scalar, e_comb)
                AE.tag(nc.scalar.activation(
                    pfp[:, 256:512], afp[:, qc * 256:(qc + 1) * 256],
                    AF.Square, accum_out=stt[:, 160 + qc:161 + qc]))
            e_ss = self.AE.evt()
            wt(nc.vector, e_ss)
            VE.tag(nc.vector.tensor_scalar(stt[:, 176:192],
                                           stt[:, 160:176], 1.0 / 256,
                                           EPS_LN, ALU.mult, ALU.add))
            e_u = VE.tag(nc.vector.tensor_copy(stt[:, 144:160],
                                               stt[:, 176:192]))
            wt(nc.scalar, e_u)
            e_sq = AE.tag(nc.scalar.activation(stt[:, 176:192],
                                               stt[:, 144:160], AF.Sqrt))
            wt(nc.vector, e_sq)
            VE.tag(nc.vector.reciprocal(stt[:, 176:192], stt[:, 176:192]))
            VE.tag(nc.vector.tensor_tensor(stt[:, 128:144], stt[:, 176:192],
                                           stt[:, 176:192], ALU.mult))
            VE.tag(nc.vector.tensor_tensor(stt[:, 128:144], stt[:, 128:144],
                                           stt[:, 144:160], ALU.mult))
            VE.tag(nc.vector.tensor_scalar(stt[:, 128:144], stt[:, 128:144],
                                           -0.5, 1.5, ALU.mult, ALU.add))
            e_rst = VE.tag(nc.vector.tensor_tensor(
                stt[:, 176:192], stt[:, 176:192], stt[:, 128:144], ALU.mult))
            e_ab = [None, None]
            for qc in range(NCH):
                VE.tag(nc.vector.tensor_scalar(
                    pfp[:, 0:256], afp[:, qc * 256:(qc + 1) * 256],
                    stt[:, 176 + qc:177 + qc], None, ALU.mult))
                wt(nc.vector, self.e_sw2, e_ab[0], e_ab[1])
                e_af = VE.tag(nc.vector.tensor_tensor(
                    pfp[:, 1024:1280], pfp[:, 0:256], self.sw2[:, :],
                    ALU.mult))
                wt(nc.scalar, e_af)
                e_h_ = AE.tag(nc.scalar.copy(
                    abuf_hi[:, qc * 256:(qc + 1) * 256], pfp[:, 1024:1280]))
                wt(nc.gpsimd, e_h_)
                e_l_ = GE.tag(nc.gpsimd.tensor_sub(
                    abuf_lo[:, qc * 256:(qc + 1) * 256], pfp[:, 1024:1280],
                    abuf_hi[:, qc * 256:(qc + 1) * 256]))
                e_ab = [e_h_, e_l_]
                e_pfp_free = e_l_
            for srcv, dstv, sh in ((abuf_hi, ptb_hi, self.sa_hi),
                                   (abuf_lo, ptb_lo, self.sa_lo)):
                e_src = e_ab[0] if srcv is abuf_hi else e_ab[1]
                for eb in range(2):
                    for g in range(0, NCH, 8):
                        tb = (g // 8) % 2
                        wt(nc.tensor, e_pst_ev[tb], e_src, e_spill_a)
                        for jj in range(8):
                            qc = g + jj
                            ins = nc.tensor.transpose(
                                self.pst[tb][:, jj * 128:(jj + 1) * 128],
                                srcv[:, qc * 256 + eb * 128:
                                     qc * 256 + eb * 128 + 128],
                                self.ident[:, :])
                        e_t = TE.tag(ins)
                        wt(nc.vector, e_t)
                        outap = bass.AP(dstv.tensor,
                                        dstv.offset + g * 128,
                                        [[28672, 128], [1, 1024]])
                        inap = bass.AP(self.pst[tb], 0,
                                       [[1024, 128], [1, 1024]])
                        e_pst_ev[tb] = VE.tag(
                            nc.vector.tensor_copy(outap, inap))
                    wt(nc.scalar, e_pst_ev[0], e_pst_ev[1])
                    e_spill_a = self.dma(
                        nc.scalar, "spa",
                        bass.AP(sh.tensor, (2 * h + eb) * 128 * 128,
                                [[128, 128], [NCH * 128 * 128, 16],
                                 [1, 128]]),
                        bass.AP(dstv.tensor, dstv.offset,
                                [[28672, 128], [128, 16], [1, 128]]))
            e_arena_free[par] = TE.evt()
        self.e_attn_done = e_spill_a
    def phase4(self):
        nc = self.nc
        TE, VE, AE, GE = self.TE, self.VE, self.AE, self.GE
        self.barrier()
        stt = self.stats

        e_g = self.dma(nc.sync, "bc0", self.bc0[:, :],
                       bass.AP(self.srow.tensor, SROW_MOD + 2 * E,
                               [[0, 128], [1, E]]))
        t1 = self.wview(F32, 2048, 24576)
        t2 = self.wview(F32, 2048, 32768)
        wt(nc.vector, e_g)
        VE.tag(nc.vector.tensor_scalar(t1[:, :], self.bc0[:, :], 0.25, None,
                                       ALU.max))
        VE.tag(nc.vector.tensor_scalar(t2[:, :], self.bc0[:, :], -0.25, None,
                                       ALU.min))
        VE.tag(nc.vector.tensor_scalar(self.bc0[:, :], self.bc0[:, :], 0.0,
                                       None, ALU.is_ge))
        VE.tag(nc.vector.tensor_tensor(t1[:, :], t1[:, :], t2[:, :],
                                       ALU.subtract))
        VE.tag(nc.vector.tensor_tensor(t1[:, :], self.bc0[:, :], t1[:, :],
                                       ALU.mult))
        e_ga = VE.tag(nc.vector.tensor_tensor(self.bc0[:, :], t2[:, :],
                                              t1[:, :], ALU.add))

        woc = self.wview(F32, 2048, 16384)
        e_wsp = [None, None]
        for wc in range(NCH):
            wt(nc.sync, e_wsp[0], e_wsp[1])
            e_ld = self.dma(nc.sync, "woc", woc[:, :],
                            self.wo[wc * 128:(wc + 1) * 128, :])
            wt(nc.scalar, e_ld)
            e_hi = AE.tag(nc.scalar.copy(
                self.hfm_hi[:, wc * 2048:(wc + 1) * 2048], woc[:, :]))
            wt(nc.gpsimd, e_hi)
            e_lo = GE.tag(nc.gpsimd.tensor_sub(
                self.hfm_lo[:, wc * 2048:(wc + 1) * 2048], woc[:, :],
                self.hfm_hi[:, wc * 2048:(wc + 1) * 2048]))
            e_wsp = [e_hi, e_lo]

        ab_hi = [self.wview(BF16, 2048, 0), self.wview(BF16, 2048, 8192)]
        ab_lo = [self.wview(BF16, 2048, 4096), self.wview(BF16, 2048, 12288)]
        xb = [self.wview(F32, 2048, 24576), self.wview(F32, 2048, 32768)]
        x1b = [self.wview(F32, 2048, 40960), self.wview(F32, 2048, 49152)]
        ot = self.wview(F32, 512, 16384)
        e_a_used = [None, None]
        e_x_used = [None, None]
        e_bank = [None, None]
        e_x1_done = [None, None]
        e_sx = [None, None]
        e_sq_prev = None
        for sc in range(NCH):
            par = sc % 2
            wt(nc.sync, e_a_used[par])
            self.dma(nc.sync, f"pa{par}",
                     bass.AP(ab_hi[par].tensor, ab_hi[par].offset,
                             [[28672, 128], [128, 16], [1, 128]]),
                     bass.AP(self.sa_hi.tensor, sc * 16 * 128 * 128,
                             [[128, 128], [128 * 128, 16], [1, 128]]))
            e_a = self.dma(nc.sync, f"pa{par}",
                           bass.AP(ab_lo[par].tensor, ab_lo[par].offset,
                                   [[28672, 128], [128, 16], [1, 128]]),
                           bass.AP(self.sa_lo.tensor, sc * 16 * 128 * 128,
                                   [[128, 128], [128 * 128, 16], [1, 128]]))
            wt(nc.sync, e_x_used[par], e_ga)
            e_x = self.dma(nc.sync, f"px{par}", xb[par][:, :],
                           self.xin[sc * 128:(sc + 1) * 128, :])
            wt(nc.vector, e_x1_done[par])
            for ft in range(4):
                bk = ft % 2
                wt(nc.tensor, e_a, e_wsp[0], e_wsp[1], e_bank[bk])
                ins = None
                for af in range(16):
                    lh = ab_hi[par][:, af * 128:(af + 1) * 128]
                    rh = self.hfm_hi[:, af * 2048 + ft * 512:
                                     af * 2048 + ft * 512 + 512]
                    rl = self.hfm_lo[:, af * 2048 + ft * 512:
                                     af * 2048 + ft * 512 + 512]
                    nc.tensor.matmul(self.psb[bk][:, :], lh, rh,
                                     start=(af == 0), stop=False)
                    nc.tensor.matmul(self.psb[bk][:, :], lh, rl,
                                     start=False, stop=False)
                for af in range(16):
                    ll = ab_lo[par][:, af * 128:(af + 1) * 128]
                    rh = self.hfm_hi[:, af * 2048 + ft * 512:
                                     af * 2048 + ft * 512 + 512]
                    ins = nc.tensor.matmul(self.psb[bk][:, :], ll, rh,
                                           start=False, stop=(af == 15))
                e_mm = TE.tag(ins)
                wt(nc.vector, e_mm, e_ga, e_x)
                if ft == 0:
                    wt(nc.vector, e_sx[par], e_sq_prev)
                VE.tag(nc.vector.tensor_tensor(
                    ot[:, :], self.psb[bk][:, :],
                    self.bc0[:, ft * 512:(ft + 1) * 512], ALU.mult))
                e_ev = VE.tag(nc.vector.tensor_tensor(
                    x1b[par][:, ft * 512:(ft + 1) * 512], ot[:, :],
                    xb[par][:, ft * 512:(ft + 1) * 512], ALU.add))
                e_bank[bk] = e_ev
            e_a_used[par] = TE.evt()
            e_x_used[par] = e_ev
            e_r = self.emit_stats(x1b[par][:, :], stt[:, 18 + sc:19 + sc],
                                  stt[:, 34 + sc:35 + sc], [e_ev], par=par)
            e_x1_done[par] = e_r
            e_sq_prev = self.AE.evt()
            wt(nc.scalar, e_ev)
            e_sx[par] = self.dma(nc.scalar, f"sx{par}", self.sx1[sc, :, :],
                                 x1b[par][:, :])
        self.e_p4 = VE.evt()

    def phase5(self):
        nc = self.nc
        VE, AE, GE = self.VE, self.AE, self.GE
        self.barrier()
        self.pst_guard = [None, None]
        stt = self.stats
        e_b0 = self.dma(nc.sync, "bc0", self.bc0[:, :],
                        bass.AP(self.srow.tensor, SROW_MOD + 4 * E,
                                [[0, 128], [1, E]]))
        e_b1 = self.dma(nc.sync, "bc1", self.bc1[:, :],
                        bass.AP(self.srow.tensor, SROW_MOD + 3 * E,
                                [[0, 128], [1, E]]))
        wt(nc.vector, e_b0)
        e_bc = VE.tag(nc.vector.tensor_scalar(
            self.bc0[:, :], self.bc0[:, :], 1.0, None, ALU.add))
        xb = [self.wview(F32, 2048, 0), self.wview(F32, 2048, 8192)]
        zb = self.wview(F32, 2048, 24576)
        hfp = self.wview(F32, 2048, 32768)
        hhi = self.wview(BF16, 2048, 40960)
        hlo = self.wview(BF16, 2048, 45056)
        e_used = [None, None]
        e_hi_prev = e_lo_prev = None
        e_te_hi = e_te_lo = None
        for sc in range(NCH):
            par = sc % 2
            wt(nc.sync, e_used[par])
            e_x = self.dma(nc.sync, f"x{par}", xb[par][:, :],
                           self.sx1[sc, :, :])
            wt(nc.vector, e_x)
            e_z = VE.tag(nc.vector.tensor_scalar(
                zb[:, :], xb[par][:, :], stt[:, 18 + sc:19 + sc],
                stt[:, 34 + sc:35 + sc], ALU.subtract, ALU.mult))
            e_used[par] = e_z
            wt(nc.vector, e_bc, e_b1, e_hi_prev, e_lo_prev)
            VE.tag(nc.vector.tensor_tensor(hfp[:, :], zb[:, :],
                                           self.bc0[:, :], ALU.mult))
            e_h = VE.tag(nc.vector.tensor_tensor(hfp[:, :], hfp[:, :],
                                                 self.bc1[:, :], ALU.add))
            wt(nc.scalar, e_h, e_te_hi)
            e_hi = AE.tag(nc.scalar.copy(hhi[:, :], hfp[:, :]))
            wt(nc.gpsimd, e_hi, e_te_lo)
            e_lo = GE.tag(nc.gpsimd.tensor_sub(hlo[:, :], hfp[:, :],
                                               hhi[:, :]))
            e_te_hi, _ = self.emit_fm_store(hhi, self.hfm_hi, sc, e_hi)
            e_te_lo, _ = self.emit_fm_store(hlo, self.hfm_lo, sc, e_lo)
            e_hi_prev, e_lo_prev = e_hi, e_lo
        if self.want("h2fm"):
            t1 = self.dbg_dram("h2fm_hi", [128, 32768])
            t2 = self.dbg_dram("h2fm_lo", [128, 32768])
            wt(nc.gpsimd, VE.evt())
            self.dma(nc.gpsimd, "dbg", t1[:, :], self.hfm_hi[:, :])
            self.dma(nc.gpsimd, "dbg", t2[:, :], self.hfm_lo[:, :])

    def phase6(self):
        nc = self.nc
        TE, VE, AE, GE = self.TE, self.VE, self.AE, self.GE
        self.barrier()
        e_c1 = self.dma(nc.sync, "ncol", self.cols[:, 1:2],
                        bass.AP(self.srow.tensor, SROW_SC + SLOT_THR_UP,
                                [[0, 128], [1, 1]]))
        e_c2 = self.dma(nc.sync, "ncol", self.cols[:, 2:3],
                        bass.AP(self.srow.tensor, SROW_SC + SLOT_THR_GATE,
                                [[0, 128], [1, 1]]))
        e_c3 = self.dma(nc.sync, "ncol", self.cols[:, 3:4],
                        bass.AP(self.srow.tensor, SROW_SC + SLOT_SUP,
                                [[0, 128], [1, 1]]))
        wt(nc.vector, e_c3)
        e_neg = VE.tag(nc.vector.tensor_scalar(
            self.cols[:, 4:6], self.cols[:, 1:3], -1.0, None, ALU.mult))

        wus = [self.wview(F32, 2048, 0), self.wview(F32, 2048, 8192)]
        wgs = [self.wview(F32, 2048, 16384), self.wview(F32, 2048, 24576)]
        tu = [self.wview(BF16, 2048, 32768), self.wview(BF16, 2048, 36864)]
        tg = [self.wview(BF16, 2048, 40960), self.wview(BF16, 2048, 45056)]
        m2 = self.wview(BF16, 2048, 49152)
        silu = self.wview(F32, 512, 53248)
        hfp = self.wview(F32, 512, 55296)
        Db = self.arenaD.bitcast(BF16)
        hh = Db[:, 9728:10240]
        hl = Db[:, 10240:10752]

        def p6_load(ec, par):
            e1 = self.dma(nc.sync, f"wus{par}",
                          self.wap3(F32, par * 8192, [[128, 16], [1, 128]]),
                          bass.AP(self.wupT.tensor, ec * 128,
                                  [[DEX, 128], [DEX * 128, 16], [1, 128]]))
            e2 = self.dma(nc.sync, f"wgs{par}",
                          self.wap3(F32, 16384 + par * 8192,
                                    [[128, 16], [1, 128]]),
                          bass.AP(self.wgateT.tensor, ec * 128,
                                  [[DEX, 128], [DEX * 128, 16], [1, 128]]))
            return (e1, e2)

        def p6_quant(par, e_ld, e_teq):
            wt(nc.vector, e_ld[0], e_ld[1], e_neg, e_teq)
            VE.tag(nc.vector.tensor_scalar(
                m2[:, :], wus[par][:, :], self.cols[:, 4:5], None, ALU.is_le))
            VE.tag(nc.vector.scalar_tensor_tensor(
                tu[par][:, :], wus[par][:, :], self.cols[:, 1:2], m2[:, :],
                ALU.is_ge, ALU.subtract))
            VE.tag(nc.vector.tensor_scalar(
                m2[:, :], wgs[par][:, :], self.cols[:, 5:6], None, ALU.is_le))
            return VE.tag(nc.vector.scalar_tensor_tensor(
                tg[par][:, :], wgs[par][:, :], self.cols[:, 2:3], m2[:, :],
                ALU.is_ge, ALU.subtract))

        e_q = [None, None]
        e_tq = [None, None]
        e_hid = None
        e_spill = None
        e_hf_prev = None
        e_bank = [None, None, None, None]
        e_tq[0] = p6_quant(0, p6_load(0, 0), None)
        for ec in range(ECH):
            par = ec % 2
            if ec + 1 < ECH:
                np_ = 1 - par
                wt(nc.sync, e_tq[np_])
                e_ldn = p6_load(ec + 1, np_)
                e_tq[np_] = p6_quant(np_, e_ldn, e_q[np_])
            for st in range(4):
                bu = st % 2
                bg = 2 + st % 2
                wt(nc.tensor, e_tq[par], e_bank[bu], e_bank[bg])
                ins = None
                for d in range(16):
                    lt_ = tu[par][:, d * 128:(d + 1) * 128]
                    nc.tensor.matmul(
                        self.psb[bu][:, :], lt_,
                        self.hfm_hi[:, d * 2048 + st * 512:
                                    d * 2048 + st * 512 + 512],
                        start=(d == 0), stop=False)
                    ins = nc.tensor.matmul(
                        self.psb[bu][:, :], lt_,
                        self.hfm_lo[:, d * 2048 + st * 512:
                                    d * 2048 + st * 512 + 512],
                        start=False, stop=(d == 15))
                e_mu = TE.tag(ins)
                for d in range(16):
                    lt_ = tg[par][:, d * 128:(d + 1) * 128]
                    nc.tensor.matmul(
                        self.psb[bg][:, :], lt_,
                        self.hfm_hi[:, d * 2048 + st * 512:
                                    d * 2048 + st * 512 + 512],
                        start=(d == 0), stop=False)
                    ins = nc.tensor.matmul(
                        self.psb[bg][:, :], lt_,
                        self.hfm_lo[:, d * 2048 + st * 512:
                                    d * 2048 + st * 512 + 512],
                        start=False, stop=(d == 15))
                e_mg = TE.tag(ins)
                wt(nc.scalar, e_mu, e_hid, e_c3, e_hf_prev)
                e_si = AE.tag(nc.scalar.activation(
                    silu[:, :], self.psb[bu][:, :], AF.Silu,
                    scale=self.cols[:, 3:4]))
                e_bank[bu] = e_si
                wt(nc.vector, e_si, e_mg, e_hid)
                e_hf = VE.tag(nc.vector.tensor_tensor(
                    hfp[:, :], silu[:, :], self.psb[bg][:, :], ALU.mult))
                e_bank[bg] = e_hf
                e_hf_prev = e_hf
                wt(nc.scalar, e_hf, e_spill)
                e_hh = AE.tag(nc.scalar.copy(hh[:, :], hfp[:, :]))
                wt(nc.vector, e_hh)
                e_hid = VE.tag(nc.vector.tensor_sub(hl[:, :], hfp[:, :],
                                                    hh[:, :]))
                wt(nc.scalar, e_hid)
                self.dma(nc.scalar, "sph",
                         self.shid_hi[ec, :, st * 512:(st + 1) * 512],
                         hh[:, :])
                e_spill = self.dma(nc.scalar, "sph",
                                   self.shid_lo[ec, :,
                                                st * 512:(st + 1) * 512],
                                   hl[:, :])
            e_q[par] = TE.evt()
        self.e_p6 = e_spill

    def phase7(self):
        nc = self.nc
        TE, VE, AE, GE = self.TE, self.VE, self.AE, self.GE
        self.barrier()
        yA = self.arenaA.bitcast(F32)
        hb = self.arenaB.bitcast(BF16)
        e_g = self.dma(nc.sync, "bc0", self.bc0[:, :],
                       bass.AP(self.srow.tensor, SROW_MOD + 5 * E,
                               [[0, 128], [1, E]]))
        self.dma(nc.sync, "ncol", self.cols[:, 7:8],
                 bass.AP(self.srow.tensor, SROW_SC + SLOT_SGD,
                         [[0, 128], [1, 1]]))
        e_c4 = self.dma(nc.sync, "ncol", self.cols[:, 8:9],
                        bass.AP(self.srow.tensor, SROW_SC + SLOT_THR_DOWN,
                                [[0, 128], [1, 1]]))
        t1 = self.wview(F32, 2048, 0)
        t2 = self.wview(F32, 2048, 8192)
        wt(nc.vector, e_g, e_c4)
        VE.tag(nc.vector.tensor_scalar(t1[:, :], self.bc0[:, :], 0.25, None,
                                       ALU.max))
        VE.tag(nc.vector.tensor_scalar(t2[:, :], self.bc0[:, :], -0.25, None,
                                       ALU.min))
        VE.tag(nc.vector.tensor_scalar(self.bc0[:, :], self.bc0[:, :], 0.0,
                                       None, ALU.is_ge))
        VE.tag(nc.vector.tensor_tensor(t1[:, :], t1[:, :], t2[:, :],
                                       ALU.subtract))
        VE.tag(nc.vector.tensor_tensor(t1[:, :], self.bc0[:, :], t1[:, :],
                                       ALU.mult))
        VE.tag(nc.vector.tensor_tensor(self.bc0[:, :], t2[:, :], t1[:, :],
                                       ALU.add))
        VE.tag(nc.vector.tensor_scalar(self.cols[:, 9:10], self.cols[:, 8:9],
                                       -1.0, None, ALU.mult))
        e_ga = VE.tag(nc.vector.tensor_scalar(
            self.bc0[:, :], self.bc0[:, :], self.cols[:, 7:8], None,
            ALU.mult))

        tdc = self.wview(F32, 2048, 0)
        td = [[self.wview(BF16, 2048, 8192 + (p * 4 + i) * 4096)
               for i in range(4)] for p in range(2)]
        m2 = self.wview(BF16, 2048, 40960)
        ocb = self.wview(F32, 1024, 45056)
        x1h = [self.wview(F32, 1024, 49152), self.wview(F32, 1024, 53248)]
        NBLK = 16

        def p7_hid_load(half, blk, par):
            e = None
            for el in range(4):
                self.dma(nc.sync, f"hid{par}",
                         hb[:, par * 16384 + el * 2048:
                            par * 16384 + (el + 1) * 2048],
                         self.shid_hi[blk * 4 + el, :, :])
                e = self.dma(nc.sync, f"hid{par}",
                             hb[:, par * 16384 + 8192 + el * 2048:
                                par * 16384 + 8192 + (el + 1) * 2048],
                             self.shid_lo[blk * 4 + el, :, :])
            return e

        self._p7_tdq = None

        def p7_td_quant(blk, par, e_teq):
            e_tq = self._p7_tdq
            for el in range(4):
                wt(nc.sync, e_tq, e_teq if el == 0 else None, e_ga)
                e_lt = self.dma(nc.sync, "tdc", tdc[:, :],
                                self.wdownT[(blk * 4 + el) * 128:
                                            (blk * 4 + el + 1) * 128, :])
                wt(nc.vector, e_lt, e_ga)
                VE.tag(nc.vector.tensor_scalar(
                    m2[:, :], tdc[:, :], self.cols[:, 9:10], None,
                    ALU.is_le))
                e_tq = VE.tag(nc.vector.scalar_tensor_tensor(
                    td[par][el][:, :], tdc[:, :], self.cols[:, 8:9],
                    m2[:, :], ALU.is_ge, ALU.subtract))
            self._p7_tdq = e_tq
            return e_tq

        e_bank = [None, None, None, None]
        e_teB = [None, None]
        e_out = None
        e_y_rd = None
        e_oc = None
        for half in range(2):
            e_ms = VE.tag(nc.vector.memset(yA[:, :], 0.0))
            e_hb = [None, None]
            e_tq = [None, None]
            e_hb[0] = p7_hid_load(half, 0, 0)
            e_tq[0] = p7_td_quant(0, 0, e_teB[0])
            for blk in range(NBLK):
                par = blk % 2
                if blk + 1 < NBLK:
                    np_ = 1 - par
                    wt(nc.sync, e_teB[np_])
                    e_hb[np_] = p7_hid_load(half, blk + 1, np_)
                    e_tq[np_] = p7_td_quant(blk + 1, np_, e_teB[np_])
                for sc in range(NCH):
                    for dt_ in range(2):
                        bk = (sc * 2 + dt_) % 4
                        wt(nc.tensor, e_hb[par], e_tq[par], e_bank[bk])
                        ins = None
                        for el in range(4):
                            lh = hb[:, par * 16384 + el * 2048 + sc * 128:
                                    par * 16384 + el * 2048 + sc * 128 + 128]
                            ll = hb[:, par * 16384 + 8192 + el * 2048 +
                                    sc * 128:
                                    par * 16384 + 8192 + el * 2048 +
                                    sc * 128 + 128]
                            rt = td[par][el][:, half * 1024 + dt_ * 512:
                                             half * 1024 + dt_ * 512 + 512]
                            nc.tensor.matmul(self.psb[bk][:, :], lh, rt,
                                             start=(el == 0), stop=False)
                            ins = nc.tensor.matmul(self.psb[bk][:, :], ll,
                                                   rt, start=False,
                                                   stop=(el == 3))
                        e_mm = TE.tag(ins)
                        wt(nc.vector, e_mm, e_y_rd, e_ms)
                        e_acc = VE.tag(nc.vector.tensor_tensor(
                            yA[:, sc * 1024 + dt_ * 512:
                               sc * 1024 + dt_ * 512 + 512],
                            yA[:, sc * 1024 + dt_ * 512:
                               sc * 1024 + dt_ * 512 + 512],
                            self.psb[bk][:, :], ALU.add))
                        e_bank[bk] = e_acc
                        last_ev = e_acc
                e_teB[par] = TE.evt()
            e_x1u = [None, None]
            for sc in range(NCH):
                par = sc % 2
                wt(nc.sync, e_x1u[par])
                e_x1 = self.dma(nc.sync, f"x1h{par}", x1h[par][:, :],
                                bass.AP(self.sx1.tensor,
                                        sc * 128 * E + half * 1024,
                                        [[E, 128], [1, 1024]]))
                wt(nc.vector, e_x1, last_ev, e_out)
                VE.tag(nc.vector.tensor_tensor(
                    ocb[:, :],
                    yA[:, sc * 1024:sc * 1024 + 1024],
                    self.bc0[:, half * 1024:(half + 1) * 1024], ALU.mult))
                e_oc = VE.tag(nc.vector.tensor_tensor(
                    ocb[:, :], ocb[:, :], x1h[par][:, :], ALU.add))
                e_x1u[par] = e_oc
                wt(nc.scalar, e_oc)
                e_out = self.dma(
                    nc.scalar, "out",
                    bass.AP(self.out.tensor, sc * 128 * E + half * 1024,
                            [[E, 128], [1, 1024]]),
                    ocb[:, :])
            e_y_rd = e_oc

    def finish(self):
        nc = self.nc
        evs = [sl.evt() for sl in self._slots.values()]
        evs += [self.TE.evt(), self.VE.evt(), self.AE.evt()]
        wt(nc.gpsimd, *evs)


def build(dbg=(), upto=0):
    b = Builder(dbg=dbg, upto=upto)
    ncp = b.build()
    return b, ncp


from concourse.bass_utils import run_bass_kernel_spmd


_NC_CACHE = {}


def kernel(L0_attn_x, L0_attn_Wq, L0_attn_Wk, L0_attn_Wv, L0_attn_Wo,
           L0_attn_lambda_q1, L0_attn_lambda_k1, L0_attn_lambda_q2,
           L0_attn_lambda_k2, L0_attn_subln_weight, L0_ffn_W_up,
           L0_ffn_W_gate, L0_ffn_W_down, L0_c, L0_wmod):
    n = 8
    if "nc" not in _NC_CACHE:
        b = Builder(dbg=(), upto=7)
        _NC_CACHE["nc"] = b.build()
    ncp = _NC_CACHE["nc"]
    f32 = np.float32
    shared = {
        "wq": np.ascontiguousarray(L0_attn_Wq, dtype=f32),
        "wk": np.ascontiguousarray(L0_attn_Wk, dtype=f32),
        "wv": np.ascontiguousarray(L0_attn_Wv, dtype=f32),
        "wo": np.ascontiguousarray(L0_attn_Wo, dtype=f32),
        "wmod": np.ascontiguousarray(L0_wmod, dtype=f32),
        "wupT": np.ascontiguousarray(np.asarray(L0_ffn_W_up, dtype=f32).T),
        "wgateT": np.ascontiguousarray(np.asarray(L0_ffn_W_gate, dtype=f32).T),
        "wdownT": np.ascontiguousarray(np.asarray(L0_ffn_W_down, dtype=f32).T),
        "lq1": np.ascontiguousarray(L0_attn_lambda_q1, dtype=f32),
        "lk1": np.ascontiguousarray(L0_attn_lambda_k1, dtype=f32),
        "lq2": np.ascontiguousarray(L0_attn_lambda_q2, dtype=f32),
        "lk2": np.ascontiguousarray(L0_attn_lambda_k2, dtype=f32),
        "subw": np.ascontiguousarray(L0_attn_subln_weight, dtype=f32),
    }
    x = np.asarray(L0_attn_x, dtype=f32)
    c = np.asarray(L0_c, dtype=f32)
    in_maps = []
    for bb in range(n):
        m = dict(shared)
        m["xin"] = np.ascontiguousarray(x[bb])
        m["cvec"] = np.ascontiguousarray(c[bb])
        in_maps.append(m)
    res = run_bass_kernel_spmd(ncp, in_maps, core_ids=list(range(n)))
    return np.stack([np.asarray(r["out"]) for r in res.results], axis=0)


# version 04
# speedup vs baseline: 1.5444x; 1.0590x over previous
import contextlib
import numpy as np
import concourse.bass as bass
import concourse.mybir as mybir

F32 = mybir.dt.float32
BF16 = mybir.dt.bfloat16
U8 = mybir.dt.uint8
AF = mybir.ActivationFunctionType
ALU = mybir.AluOpType
AX = mybir.AxisListType

S = 2048
E = 2048
NCH = 16
NH = 8
HD = 128
DEX = 8192
ECH = 64
LAM_INIT = 0.8 - 0.6 * float(np.exp(-0.3 * 12))
SCALE = HD ** -0.5
EPS_LN = 1e-5
EPS_Q = 1e-8
MASKVAL = -1e30

SROW_MOD = 0
SROW_SC = 6 * E
SLOT_NEGLAM = 0
SLOT_SUP = 1
SLOT_SGD = 2
SLOT_THR_UP = 3
SLOT_THR_GATE = 4
SLOT_THR_DOWN = 5
NSLOT = 8


class Evt:
    __slots__ = ("sem", "v")

    def __init__(self, sem, v):
        self.sem = sem
        self.v = v


class ESem:

    def __init__(self, nc, es, name, eng, flush):
        self.sem = es.enter_context(nc.semaphore(name))
        self.n = 0
        self.eng = eng
        self.flush = flush

    def tag(self, ins):
        ins.then_inc(self.sem, 1)
        self.n += 1
        if self.flush:
            self.eng.wait_ge(self.sem, self.n)
        return Evt(self.sem, self.n)

    def bump(self):
        if self.n > 0:
            self.eng.wait_ge(self.sem, self.n)
        self.eng.nop().then_inc(self.sem, 1)
        self.n += 1
        return Evt(self.sem, self.n)

    def evt(self):
        return Evt(self.sem, self.n)


class Slot:

    def __init__(self, nc, es, name):
        self.sem = es.enter_context(nc.semaphore(name))
        self.n = 0

    def evt(self):
        return Evt(self.sem, self.n)


def wt(eng, *evts):
    for e in evts:
        if e is not None and e.v > 0:
            eng.wait_ge(e.sem, e.v)


class Builder:
    def __init__(self, dbg=(), upto=99):
        self.dbg_names = set(dbg)
        self.upto = upto
        nc = self.nc = bass.Bass("TRN2", target_bir_lowering=False)
        self.es = contextlib.ExitStack()

        def din(name, shape):
            return nc.dram_tensor(name, shape, F32, kind="ExternalInput").ap()

        self.xin = din("xin", [S, E])
        self.cvec = din("cvec", [E])
        self.wq = din("wq", [E, E])
        self.wk = din("wk", [E, E])
        self.wv = din("wv", [E, E])
        self.wo = din("wo", [E, E])
        self.wmod = din("wmod", [E, 6 * E])
        self.wupT = din("wupT", [E, DEX])
        self.wgateT = din("wgateT", [E, DEX])
        self.wdownT = din("wdownT", [DEX, E])
        self.lq1 = din("lq1", [HD])
        self.lk1 = din("lk1", [HD])
        self.lq2 = din("lq2", [HD])
        self.lk2 = din("lk2", [HD])
        self.subw = din("subw", [2 * HD])
        self.out = nc.dram_tensor("out", [S, E], F32, kind="ExternalOutput").ap()

        def scr(name, shape, dt=BF16):
            kind = "ExternalOutput" if ("spill" in self.dbg_names) else "Internal"
            return nc.dram_tensor(name, shape, dt, kind=kind).ap()

        self.srow = scr("srow", [SROW_SC + NSLOT], F32)
        self.sq_hi = scr("sq_hi", [NCH, 128, S])
        self.sq_lo = scr("sq_lo", [NCH, 128, S])
        self.sk_hi = scr("sk_hi", [NCH, 128, S])
        self.sk_lo = scr("sk_lo", [NCH, 128, S])
        self.sv_hi = scr("sv_hi", [NH, NCH, 128, 256])
        self.sv_lo = scr("sv_lo", [NH, NCH, 128, 256])
        self.sa_hi = scr("sa_hi", [NCH, NCH, 128, 128])
        self.sa_lo = scr("sa_lo", [NCH, NCH, 128, 128])
        self.shid_hi = scr("shid_hi", [ECH, 128, S])
        self.shid_lo = scr("shid_lo", [ECH, 128, S])
        self.sx1 = scr("sx1", [NCH, 128, E], F32)

        self.arenaA = nc.alloc_sbuf_tensor("arenaA", [128, 65536], U8)
        self.arenaB = nc.alloc_sbuf_tensor("arenaB", [128, 65536], U8)
        self.arenaW = nc.alloc_sbuf_tensor("arenaW", [128, 57344], U8)
        self.arenaD = nc.alloc_sbuf_tensor("arenaD", [128, 22528], U8)

        self.hfm_hi = self.arenaA.bitcast(BF16)
        self.hfm_lo = self.arenaB.bitcast(BF16)
        D = self.arenaD.bitcast(F32)
        Db = self.arenaD.bitcast(BF16)
        self.bc0 = D[:, 0:2048]
        self.bc1 = D[:, 2048:4096]
        self.sw2 = D[:, 4096:4352]
        self.mask = D[:, 4352:4480]
        self.ident = Db[:, 8960:9088]
        self.cols = D[:, 4544:4608]
        self.stats = D[:, 4608:4864]

        self.psb = [nc.alloc_psum_tensor(f"psb{i}", [128, 512], F32)
                    for i in range(4)]
        self.pst = [nc.alloc_psum_tensor(f"pst{i}", [128, 1024], BF16)
                    for i in range(2)]
        self.ps6 = nc.alloc_psum_tensor("ps6", [128, 512], F32)
        self.ps7 = nc.alloc_psum_tensor("ps7", [128, 512], F32)

        es = self.es
        self.TE = ESem(nc, es, "sTE", nc.tensor, flush=False)
        self.VE = ESem(nc, es, "sVE", nc.vector, flush=True)
        self.AE = ESem(nc, es, "sAE", nc.scalar, flush=True)
        self.GE = ESem(nc, es, "sGE", nc.gpsimd, flush=True)
        self.SE = ESem(nc, es, "sSE", nc.sync, flush=False)
        self.engines = [(nc.tensor, self.TE), (nc.vector, self.VE),
                        (nc.scalar, self.AE), (nc.gpsimd, self.GE),
                        (nc.sync, self.SE)]
        self._slots = {}
        self.dbg_tensors = {}

    def slot(self, name):
        if name not in self._slots:
            self._slots[name] = Slot(self.nc, self.es, "sl_" + name)
        return self._slots[name]

    def dma(self, eng, slotname, out, in_, **kw):
        sl = self.slot(slotname)
        eng.dma_start(out=out, in_=in_, **kw).then_inc(sl.sem, 16)
        sl.n += 16
        return Evt(sl.sem, sl.n)

    def barrier(self):
        for eng, es_ in self.engines:
            es_.bump()
        evs = [es_.evt() for _, es_ in self.engines]
        evs += [sl.evt() for sl in self._slots.values()]
        for eng, _ in self.engines:
            wt(eng, *evs)

    def want(self, name):
        return name in self.dbg_names

    def dbg_dram(self, name, shape, dt=F32):
        t = self.nc.dram_tensor("dbg_" + name, list(shape), dt,
                                kind="ExternalOutput").ap()
        self.dbg_tensors[name] = t
        return t

    def wview(self, dt, cols, off_bytes):
        a = self.arenaW.bitcast(dt)
        isz = {F32: 4, BF16: 2, U8: 1}[dt]
        assert off_bytes % isz == 0
        c0 = off_bytes // isz
        assert (c0 + cols) * isz <= 57344, (off_bytes, cols)
        return a[:, c0:c0 + cols]

    def wap3(self, dt, off_bytes, dims):
        h = self.arenaW.bitcast(dt)
        isz = {F32: 4, BF16: 2, U8: 1}[dt]
        width = 57344 // isz
        assert off_bytes % isz == 0
        return bass.AP(h, off_bytes // isz, [[width, 128]] + list(dims))

    def build(self):
        nc = self.nc
        with self.es, nc.Block():
            with nc.named_scope("p0mod"):
                self.phase0()
            if self.upto >= 1:
                with nc.named_scope("p1ln1"):
                    self.phase1()
            if self.upto >= 2:
                with nc.named_scope("p2qkv"):
                    self.phase2()
            if self.upto >= 3:
                with nc.named_scope("p3attn"):
                    self.phase3()
            if self.upto >= 4:
                with nc.named_scope("p4wo"):
                    self.phase4()
            if self.upto >= 5:
                with nc.named_scope("p5ln2"):
                    self.phase5()
            if self.upto >= 6:
                with nc.named_scope("p6upgate"):
                    self.phase6()
            if self.upto >= 7:
                with nc.named_scope("p7down"):
                    self.phase7()
            self.finish()
        return nc

    def phase0(self):
        nc = self.nc
        TE, VE, AE, GE = self.TE, self.VE, self.AE, self.GE

        cfm = self.wview(F32, 16, 0)
        lv = self.wview(F32, 512, 256)
        lt = self.wview(F32, 16, 2304)
        mrow = [self.wview(F32, 512, 3584), self.wview(F32, 512, 5632)]
        wmb = [self.wview(F32, 6144, 8192), self.wview(F32, 6144, 32768)]

        GE.tag(nc.gpsimd.memset(self.stats[:, :], 0.0))
        GE.tag(nc.gpsimd.memset(self.cols[:, :], 0.0))
        GE.tag(nc.gpsimd.memset(self.cols[:, 60:64], 1.0))
        GE.tag(nc.gpsimd.memset(self.ident[:, :], 0.0))
        GE.tag(nc.gpsimd.affine_select(
            out=self.ident[:, :], in_=self.ident[:, :],
            compare_op=ALU.not_equal, fill=1.0, base=0,
            pattern=[[-1, 128]], channel_multiplier=1))
        GE.tag(nc.gpsimd.memset(self.mask[:, :], 0.0))
        self.e_consts = GE.tag(nc.gpsimd.affine_select(
            out=self.mask[:, :], in_=self.mask[:, :],
            compare_op=ALU.is_ge, fill=MASKVAL, base=0,
            pattern=[[-1, 128]], channel_multiplier=1))

        self.dma(nc.sync, "lam", lv[0:1, 0:128], self.lq1[None, :])
        self.dma(nc.sync, "lam", lv[0:1, 128:256], self.lk1[None, :])
        self.dma(nc.sync, "lam", lv[0:1, 256:384], self.lq2[None, :])
        e_l = self.dma(nc.sync, "lam", lv[0:1, 384:512], self.lk2[None, :])
        e_c = self.dma(nc.sync, "cfm", cfm[:, 0:16],
                       bass.AP(self.cvec.tensor, 0, [[1, 128], [128, 16]]),
                       allow_slow_non_contiguous=True)
        e_swb = self.dma(nc.sync, "swb", self.sw2[:, :],
                         bass.AP(self.subw.tensor, 0, [[0, 128], [1, 256]]))
        wt(nc.vector, e_swb)
        self.e_sw2 = VE.tag(nc.vector.tensor_scalar(
            self.sw2[:, :], self.sw2[:, :], 1.0 - LAM_INIT, None, ALU.mult))

        wt(nc.scalar, e_c)
        e_sc = AE.tag(nc.scalar.activation(cfm[:, 0:16], cfm[:, 0:16], AF.Silu))

        wt(nc.vector, e_l)
        VE.tag(nc.vector.tensor_tensor(lv[0:1, 0:128], lv[0:1, 0:128],
                                       lv[0:1, 128:256], ALU.mult))
        VE.tag(nc.vector.tensor_tensor(lv[0:1, 256:384], lv[0:1, 256:384],
                                       lv[0:1, 384:512], ALU.mult))
        nc.vector.reduce_sum(lt[0:1, 0:1], lv[0:1, 0:128], axis=AX.X)
        e_vr = VE.tag(nc.vector.reduce_sum(lt[0:1, 1:2], lv[0:1, 256:384],
                                           axis=AX.X))
        wt(nc.scalar, e_vr)
        e_ae = AE.tag(nc.scalar.activation(lt[0:1, 2:4], lt[0:1, 0:2], AF.Exp))
        wt(nc.vector, e_ae)
        VE.tag(nc.vector.tensor_tensor(lt[0:1, 4:5], lt[0:1, 2:3],
                                       lt[0:1, 3:4], ALU.subtract))
        e_nl = VE.tag(nc.vector.tensor_scalar(
            lt[0:1, 5:6], lt[0:1, 4:5], LAM_INIT, -1.0, ALU.add, ALU.mult))
        wt(nc.gpsimd, e_nl)
        self.dma(nc.gpsimd, "srow_sc",
                 bass.AP(self.srow.tensor, SROW_SC + SLOT_NEGLAM,
                         [[1, 1], [1, 1]]),
                 lt[0:1, 5:6])

        wmt = self.wmod.tensor
        psm = [self.ps6, self.ps7]
        ev_te = [None, None]
        ev_cp = [None, None]
        ev_wr = [None, None]
        WMW = 384
        for t in range(32):
            par = t % 2
            wt(nc.sync, ev_te[par])
            w3 = self.wap3(F32, 8192 + par * 24576, [[WMW, 16], [1, WMW]])
            e_ld = self.dma(nc.sync, f"wm{par}", w3,
                            bass.AP(wmt, t * WMW,
                                    [[6 * E, 128], [6 * E * 128, 16],
                                     [1, WMW]]))
            wt(nc.tensor, e_ld, e_sc, ev_cp[par])
            ins = None
            for j in range(16):
                rj = wmb[par][:, j * WMW:(j + 1) * WMW]
                ins = nc.tensor.matmul(
                    psm[par][0:1, 0:WMW], cfm[:, j:j + 1], rj,
                    start=(j == 0), stop=(j == 15))
            ev_te[par] = TE.tag(ins)
            wt(nc.vector, ev_te[par], ev_wr[par])
            e_cp = VE.tag(nc.vector.tensor_copy(mrow[par][0:1, 0:WMW],
                                                psm[par][0:1, 0:WMW]))
            ev_cp[par] = e_cp
            wt(nc.gpsimd, e_cp)
            ev_wr[par] = self.dma(
                nc.gpsimd, f"mrow{par}",
                bass.AP(self.srow.tensor, SROW_MOD + t * WMW,
                        [[1, 1], [1, WMW]]),
                mrow[par][0:1, 0:WMW])
        self.e_modrow = [ev_wr[0], ev_wr[1]]

        if self.want("mod"):
            t = self.dbg_dram("mod", [6 * E])
            wt(nc.gpsimd, *self.e_modrow)
            self.dma(nc.gpsimd, "dbg", t[None, :],
                     bass.AP(self.srow.tensor, 0, [[1, 1], [1, 6 * E]]))

        wt(nc.sync, *self.e_modrow)
        e_b0 = self.dma(nc.sync, "bc0", self.bc0[:, :],
                        bass.AP(self.srow.tensor, SROW_MOD + E,
                                [[0, 128], [1, E]]))
        e_b1 = self.dma(nc.sync, "bc1", self.bc1[:, :],
                        bass.AP(self.srow.tensor, SROW_MOD + 0,
                                [[0, 128], [1, E]]))
        wt(nc.vector, e_b0)
        self.e_bc = VE.tag(nc.vector.tensor_scalar(
            self.bc0[:, :], self.bc0[:, :], 1.0, None, ALU.add))
        self.e_bc_ld = e_b1

    def emit_stats(self, xb, mcol, rcol, pre_evts, eps=EPS_LN, n=E, par=0):
        nc = self.nc
        VE, AE = self.VE, self.AE
        stt = self.stats
        dump = self.wview(F32, 2048, 16384)
        wt(nc.scalar, *pre_evts)
        e_sq = AE.tag(nc.scalar.activation(dump[:, 0:n], xb, AF.Square,
                                           accum_out=stt[:, 12 + par:13 + par]))
        wt(nc.vector, *pre_evts)
        VE.tag(nc.vector.reduce_sum(stt[:, 1:2], xb, axis=AX.X))
        VE.tag(nc.vector.tensor_scalar(mcol, stt[:, 1:2], 1.0 / n, None,
                                       ALU.mult))
        nc.vector.tensor_tensor(stt[:, 3:4], mcol, mcol, ALU.mult)
        wt(nc.vector, e_sq)
        VE.tag(nc.vector.tensor_scalar(stt[:, 4:5], stt[:, 12 + par:13 + par],
                                       1.0 / n, None, ALU.mult))
        VE.tag(nc.vector.tensor_tensor(stt[:, 5:6], stt[:, 4:5], stt[:, 3:4],
                                       ALU.subtract))
        e_u = VE.tag(nc.vector.tensor_scalar(stt[:, 5:6], stt[:, 5:6], eps,
                                             None, ALU.add))
        wt(nc.scalar, e_u)
        e_sq2 = AE.tag(nc.scalar.activation(stt[:, 6:7], stt[:, 5:6], AF.Sqrt))
        wt(nc.vector, e_sq2)
        VE.tag(nc.vector.reciprocal(stt[:, 7:8], stt[:, 6:7]))
        VE.tag(nc.vector.tensor_tensor(stt[:, 8:9], stt[:, 7:8], stt[:, 7:8],
                                       ALU.mult))
        VE.tag(nc.vector.tensor_tensor(stt[:, 9:10], stt[:, 8:9], stt[:, 5:6],
                                       ALU.mult))
        VE.tag(nc.vector.tensor_scalar(stt[:, 9:10], stt[:, 9:10], -0.5, 1.5,
                                       ALU.mult, ALU.add))
        e_r = VE.tag(nc.vector.tensor_tensor(rcol, stt[:, 7:8], stt[:, 9:10],
                                             ALU.mult))
        return e_r

    def emit_fm_store(self, src_bf16, dest_handle, sc, e_src, w=2048,
                      dest_width=32768, dest_col0=0, sc_stride=2048):
        nc = self.nc
        TE, VE = self.TE, self.VE
        nj = w // 128
        e_ev = None
        e_t = None
        for g in range(0, nj, 8):
            gb = min(8, nj - g)
            bi = (g // 8) % 2
            pbank = self.pst[bi]
            wt(nc.tensor, self.pst_guard[bi], e_src)
            ins = None
            for jj in range(gb):
                j = g + jj
                ins = nc.tensor.transpose(
                    pbank[:, jj * 128:(jj + 1) * 128],
                    src_bf16[:, j * 128:(j + 1) * 128], self.ident[:, :])
            e_t = TE.tag(ins)
            wt(nc.vector, e_t)
            outap = bass.AP(dest_handle,
                            dest_col0 + g * sc_stride + sc * 128,
                            [[dest_width, 128], [sc_stride, gb], [1, 128]])
            inap = bass.AP(pbank, 0, [[1024, 128], [128, gb], [1, 128]])
            e_ev = VE.tag(nc.vector.tensor_copy(outap, inap))
            self.pst_guard[bi] = e_ev
        return e_t, e_ev

    def phase1(self):
        nc = self.nc
        VE, AE, GE = self.VE, self.AE, self.GE
        self.barrier()
        self.pst_guard = [None, None]
        xb = [self.wview(F32, 2048, 0), self.wview(F32, 2048, 8192)]
        zb = self.wview(F32, 2048, 24576)
        hfp = self.wview(F32, 2048, 32768)
        hhi = self.wview(BF16, 2048, 40960)
        hlo = self.wview(BF16, 2048, 45056)
        stt = self.stats
        e_used = [None, None]
        e_hi_prev = e_lo_prev = None
        e_te_hi = e_te_lo = None
        for sc in range(NCH):
            par = sc % 2
            wt(nc.sync, e_used[par])
            e_x = self.dma(nc.sync, f"x{par}", xb[par][:, :],
                           self.xin[sc * 128:(sc + 1) * 128, :])
            e_r = self.emit_stats(xb[par], stt[:, 16:17], stt[:, 17:18],
                                  [e_x], par=par)
            e_z = VE.tag(nc.vector.tensor_scalar(
                zb[:, :], xb[par][:, :], stt[:, 16:17],
                stt[:, 17:18], ALU.subtract, ALU.mult))
            e_used[par] = e_z
            wt(nc.vector, self.e_bc, self.e_bc_ld, e_hi_prev, e_lo_prev)
            VE.tag(nc.vector.tensor_tensor(hfp[:, :], zb[:, :],
                                           self.bc0[:, :], ALU.mult))
            e_h = VE.tag(nc.vector.tensor_tensor(hfp[:, :], hfp[:, :],
                                                 self.bc1[:, :], ALU.add))
            wt(nc.scalar, e_h, e_te_hi)
            e_hi = AE.tag(nc.scalar.copy(hhi[:, :], hfp[:, :]))
            wt(nc.gpsimd, e_hi, e_te_lo)
            e_lo = GE.tag(nc.gpsimd.tensor_sub(hlo[:, :], hfp[:, :],
                                               hhi[:, :]))
            e_te_hi, _ = self.emit_fm_store(hhi, self.hfm_hi, sc, e_hi)
            e_te_lo, _ = self.emit_fm_store(hlo, self.hfm_lo, sc, e_lo)
            e_hi_prev, e_lo_prev = e_hi, e_lo
        self.e_h1 = VE.evt()
        if self.want("h1fm"):
            t1 = self.dbg_dram("h1fm_hi", [128, 32768])
            t2 = self.dbg_dram("h1fm_lo", [128, 32768])
            wt(nc.gpsimd, VE.evt())
            self.dma(nc.gpsimd, "dbg", t1[:, :], self.hfm_hi[:, :])
            self.dma(nc.gpsimd, "dbg", t2[:, :], self.hfm_lo[:, :])

    def phase2(self):
        nc = self.nc
        TE, VE, AE, GE = self.TE, self.VE, self.AE, self.GE
        self.barrier()
        stt = self.stats

        self.p1_count = 0
        p1_mats = [self.wupT, self.wgateT, self.wdownT]
        self.e_p1_act = None
        pbuf = [self.wview(F32, 2048, 36864), self.wview(F32, 2048, 36864)]

        def pass1_chunk():
            c = self.p1_count
            if c >= 192:
                return
            self.p1_count += 1
            mat = p1_mats[c // 64]
            sub = c % 64
            par = 0
            wt(nc.gpsimd, self.e_p1_act if c >= 1 else None)
            e_ld = self.dma(nc.gpsimd, "p1a", pbuf[par][:, :],
                            bass.AP(mat.tensor, sub * 128 * 2048,
                                    [[2048, 128], [1, 2048]]))
            wt(nc.scalar, e_ld)
            self.e_p1_act = AE.tag(nc.scalar.activation(
                pbuf[par][:, 0:2048], pbuf[par][:, 0:2048], AF.Abs,
                accum_out=stt[:, 64 + c:65 + c]))

        wsf = [self.wview(F32, 2048, 0), self.wview(F32, 2048, 4096 + 4096)]
        whi = [self.wview(BF16, 2048, 16384), self.wview(BF16, 2048, 20480)]
        wlo = [self.wview(BF16, 2048, 24576 - 16384 + 12288) if False else
               self.wview(BF16, 2048, 24576), self.wview(BF16, 2048, 28672)]
        sthi = self.wview(BF16, 2048, 45056)
        stlo = self.wview(BF16, 2048, 49152)
        qfp = self.wview(F32, 512, 32768)
        mats = [(self.wq, self.sq_hi, self.sq_lo, SCALE),
                (self.wk, self.sk_hi, self.sk_lo, 1.0)]

        def qk_load(i):
            wmat = mats[i // NCH][0]
            m = i % NCH
            return self.dma(nc.sync, f"ws{i % 2}",
                            self.wap3(F32, [0, 8192][i % 2],
                                      [[128, 16], [1, 128]]),
                            bass.AP(wmat.tensor, m * 128,
                                    [[E, 128], [E * 128, 16], [1, 128]]))

        e_split = [None, None]
        e_te_hl = [None, None]
        e_bank = [None, None]
        e_spill = None
        e_stage_lo = None
        e_ld = [None, None]
        e_ld[0] = qk_load(0)
        for i in range(2 * NCH):
            par = i % 2
            mat_i, m = i // NCH, i % NCH
            dst_hi, dst_lo, qs = mats[mat_i][1], mats[mat_i][2], mats[mat_i][3]
            if i + 1 < 2 * NCH:
                wt(nc.sync, e_split[1 - par])
                e_ld[1 - par] = qk_load(i + 1)
                wt(nc.scalar, e_ld[1 - par], e_te_hl[1 - par])
                e_hi_n = AE.tag(nc.scalar.copy(whi[1 - par][:, :],
                                               wsf[1 - par][:, :]))
                wt(nc.vector, e_hi_n)
                e_split[1 - par] = VE.tag(nc.vector.tensor_sub(
                    wlo[1 - par][:, :], wsf[1 - par][:, :],
                    whi[1 - par][:, :]))
            if i == 0:
                wt(nc.scalar, e_ld[0])
                e_hi0 = AE.tag(nc.scalar.copy(whi[0][:, :], wsf[0][:, :]))
                wt(nc.vector, e_hi0)
                e_split[0] = VE.tag(nc.vector.tensor_sub(
                    wlo[0][:, :], wsf[0][:, :], whi[0][:, :]))
            for st in range(4):
                bk = st % 2
                wt(nc.tensor, e_split[par], e_bank[bk])
                ins = None
                for f in range(16):
                    lt_ = whi[par][:, f * 128:(f + 1) * 128]
                    nc.tensor.matmul(
                        self.psb[bk][:, :], lt_,
                        self.hfm_hi[:, f * 2048 + st * 512:
                                    f * 2048 + st * 512 + 512],
                        start=(f == 0), stop=False)
                    nc.tensor.matmul(
                        self.psb[bk][:, :], lt_,
                        self.hfm_lo[:, f * 2048 + st * 512:
                                    f * 2048 + st * 512 + 512],
                        start=False, stop=False)
                for f in range(16):
                    ins = nc.tensor.matmul(
                        self.psb[bk][:, :],
                        wlo[par][:, f * 128:(f + 1) * 128],
                        self.hfm_hi[:, f * 2048 + st * 512:
                                    f * 2048 + st * 512 + 512],
                        start=False, stop=(f == 15))
                e_mm = TE.tag(ins)
                wt(nc.vector, e_mm, e_stage_lo)
                e_q = VE.tag(nc.vector.tensor_scalar(
                    qfp[:, :], self.psb[bk][:, :], qs, None, ALU.mult))
                e_bank[bk] = e_q
                wt(nc.scalar, e_q, e_spill)
                e_sh = AE.tag(nc.scalar.copy(
                    sthi[:, st * 512:(st + 1) * 512], qfp[:, :]))
                wt(nc.gpsimd, e_sh)
                e_stage_lo = GE.tag(nc.gpsimd.tensor_sub(
                    stlo[:, st * 512:(st + 1) * 512], qfp[:, :],
                    sthi[:, st * 512:(st + 1) * 512]))
            e_te_hl[par] = TE.evt()
            wt(nc.scalar, e_stage_lo)
            self.dma(nc.scalar, "spq", dst_hi[m, :, :], sthi[:, :])
            e_spill = self.dma(nc.scalar, "spq", dst_lo[m, :, :],
                               stlo[:, :])
            for _ in range(6):
                pass1_chunk()
        while self.p1_count < 192:
            pass1_chunk()

        self.barrier()
        e_bank = [None, None]
        e_spill = None
        e_stage_lo = None
        wvf = self.wview(F32, 4096, 0)
        vhi = self.wview(BF16, 4096, 16384)
        vlo = self.wview(BF16, 4096, 24576)
        vsh = self.wview(BF16, 256, 32768)
        vsl = self.wview(BF16, 256, 33280)
        vfp = self.wview(F32, 256, 33792)
        e_vsplit = None
        for h in range(NH):
            wt(nc.sync, e_vsplit, e_te_hl[0], e_te_hl[1])
            e_ld = self.dma(nc.sync, "ws",
                            self.wap3(F32, 0, [[256, 16], [1, 256]]),
                            bass.AP(self.wv.tensor, h * 256,
                                    [[E, 128], [E * 128, 16], [1, 256]]))
            wt(nc.scalar, e_ld)
            e_hi = AE.tag(nc.scalar.copy(vhi[:, :], wvf[:, :]))
            wt(nc.gpsimd, e_hi)
            e_lo = GE.tag(nc.gpsimd.tensor_sub(vlo[:, :], wvf[:, :],
                                               vhi[:, :]))
            e_vsplit = e_lo
            for sc in range(NCH):
                bk = sc % 2
                wt(nc.tensor, e_lo, e_bank[bk])
                ins = None
                for f in range(16):
                    lh = self.hfm_hi[:, f * 2048 + sc * 128:
                                     f * 2048 + sc * 128 + 128]
                    rh = vhi[:, f * 256:(f + 1) * 256]
                    rl = vlo[:, f * 256:(f + 1) * 256]
                    nc.tensor.matmul(self.psb[bk][:, 0:256], lh, rh,
                                     start=(f == 0), stop=False)
                    nc.tensor.matmul(self.psb[bk][:, 0:256], lh, rl,
                                     start=False, stop=False)
                for f in range(16):
                    ll = self.hfm_lo[:, f * 2048 + sc * 128:
                                     f * 2048 + sc * 128 + 128]
                    rh = vhi[:, f * 256:(f + 1) * 256]
                    ins = nc.tensor.matmul(self.psb[bk][:, 0:256], ll, rh,
                                           start=False, stop=(f == 15))
                e_mm = TE.tag(ins)
                wt(nc.vector, e_mm, e_stage_lo)
                e_q = VE.tag(nc.vector.tensor_copy(vfp[:, 0:256],
                                                   self.psb[bk][:, 0:256]))
                e_bank[bk] = e_q
                wt(nc.scalar, e_q, e_spill)
                e_sh = AE.tag(nc.scalar.copy(vsh[:, 0:256], vfp[:, 0:256]))
                wt(nc.gpsimd, e_sh)
                e_stage_lo = GE.tag(nc.gpsimd.tensor_sub(
                    vsl[:, 0:256], vfp[:, 0:256], vsh[:, 0:256]))
                wt(nc.scalar, e_stage_lo)
                self.dma(nc.scalar, "spv", self.sv_hi[h, sc, :, :],
                         vsh[:, 0:256])
                e_spill = self.dma(nc.scalar, "spv", self.sv_lo[h, sc, :, :],
                                   vsl[:, 0:256])
            e_te_hl = [TE.evt(), TE.evt()]

        wt(nc.vector, self.e_p1_act)
        for mi in range(3):
            c0 = 64 + mi * 64
            VE.tag(nc.vector.reduce_sum(stt[:, 50 + mi:51 + mi],
                                        stt[:, c0:c0 + 64], axis=AX.X))
        e_ps = VE.evt()
        wt(nc.tensor, e_ps, e_bank[0], e_bank[1])
        ins = None
        for mi in range(3):
            ins = nc.tensor.matmul(self.psb[0][0:1, mi:mi + 1],
                                   self.cols[:, 60:61],
                                   stt[:, 50 + mi:51 + mi],
                                   start=(mi == 0), stop=(mi == 2))
        e_sc3 = TE.tag(ins)
        wt(nc.vector, e_sc3)
        VE.tag(nc.vector.tensor_scalar(stt[0:1, 53:56], self.psb[0][0:1, 0:3],
                                       1.0 / (E * DEX), EPS_Q, ALU.mult,
                                       ALU.add))
        VE.tag(nc.vector.tensor_scalar(stt[0:1, 56:59], stt[0:1, 53:56], 0.5,
                                       None, ALU.mult))
        e_fin = VE.tag(nc.vector.tensor_tensor(
            stt[0:1, 59:60], stt[0:1, 54:55], stt[0:1, 55:56], ALU.mult))
        self.e_bank2 = VE.evt()
        wt(nc.gpsimd, e_fin)
        for slot_i, col in [(SLOT_SUP, 53), (SLOT_SGD, 59), (SLOT_THR_UP, 56),
                            (SLOT_THR_GATE, 57), (SLOT_THR_DOWN, 58)]:
            self.e_srow_sc = self.dma(
                nc.gpsimd, "srow_sc",
                bass.AP(self.srow.tensor, SROW_SC + slot_i, [[1, 1], [1, 1]]),
                stt[0:1, col:col + 1])

    def phase3(self):
        nc = self.nc
        TE, VE, AE, GE = self.TE, self.VE, self.AE, self.GE
        self.barrier()
        stt = self.stats
        e_nc = self.dma(nc.sync, "ncol", self.cols[:, 0:1],
                        bass.AP(self.srow.tensor, SROW_SC + SLOT_NEGLAM,
                                [[0, 128], [1, 1]]))
        AR = [self.hfm_hi, self.hfm_lo]

        def hv(par, c0, w):
            return AR[par][:, c0:c0 + w]

        pfp = [self.wview(F32, 2048, 0), self.wview(F32, 2048, 16384)]
        phi = self.wview(BF16, 2048, 8192)
        plo = self.wview(BF16, 2048, 12288)
        afp = self.wview(F32, 4096, 24576)
        abuf_hi = self.wview(BF16, 4096, 40960)
        abuf_lo = self.wview(BF16, 4096, 49152)
        Dv = self.arenaD.bitcast(F32)
        ct1 = Dv[:, 4864:5120]

        def load_head(h):
            par = h % 2
            evs = []
            for i, src in enumerate([self.sq_hi, self.sq_lo,
                                     self.sk_hi, self.sk_lo]):
                for br in range(2):
                    m = 2 * h + br
                    evs.append(self.dma(
                        nc.sync, f"qk{par}",
                        hv(par, (2 * i + br) * 2048, 2048),
                        src[m, :, :]))
            for i, src in enumerate([self.sv_hi, self.sv_lo]):
                evs.append(self.dma(
                    nc.sync, f"qk{par}",
                    hv(par, 16384 + i * 4096, 4096),
                    bass.AP(src.tensor, h * NCH * 128 * 256,
                            [[256, 128], [128 * 256, 16], [1, 256]])))
            return evs[-1]

        e_ld = [None, None]
        e_ld[0] = load_head(0)
        e_arena_free = [None, None]
        e_bsc = [None, None, None, None]
        e_pst_ev = [None, None]
        e_o_read = None
        e_spill_a = None
        e_pfp_free = [None, None]
        e_plo = None
        e_phi = None
        for h in range(NH):
            par = h % 2
            if h + 1 < NH:
                wt(nc.sync, e_arena_free[1 - par])
                e_ld[1 - par] = load_head(h + 1)
            q_of = [0, 2048]
            k_of = [8192, 10240]
            ptb_hi = hv(par, 24576, 2048)
            ptb_lo = hv(par, 26624, 2048)
            rr = 0
            for qc in range(NCH):
                nfull = qc // 4
                rem = qc % 4
                remw = (rem + 1) * 128
                kact = (qc + 1) * 128
                nkc = qc + 1
                e_exp = [None, None]
                for bi in range(2):
                    qhi = hv(par, q_of[bi], 2048)[:, qc * 128:(qc + 1) * 128]
                    qlo = hv(par, q_of[bi] + 4096, 2048)[:, qc * 128:
                                                         (qc + 1) * 128]
                    khi = hv(par, k_of[bi], 2048)
                    klo = hv(par, k_of[bi] + 4096, 2048)
                    wt(nc.tensor, e_ld[par])
                    for kt in range(nfull + 1):
                        w = 512 if kt < nfull else remw
                        bidx = rr % 4
                        rr += 1
                        bank = self.psb[bidx]
                        wt(nc.tensor, e_bsc[bidx])
                        c0 = kt * 512
                        nc.tensor.matmul(bank[:, 0:w], qhi,
                                         khi[:, c0:c0 + w],
                                         start=True, stop=False)
                        nc.tensor.matmul(bank[:, 0:w], qhi,
                                         klo[:, c0:c0 + w],
                                         start=False, stop=False)
                        ins = nc.tensor.matmul(bank[:, 0:w], qlo,
                                               khi[:, c0:c0 + w],
                                               start=False, stop=True)
                        e_mm = TE.tag(ins)
                        wt(nc.vector, e_mm, e_pfp_free[bi])
                        e_bsc[bidx] = VE.tag(nc.vector.tensor_copy(
                            pfp[bi][:, c0:c0 + w], bank[:, 0:w]))
                e_pv2 = None
                for bi in range(2):
                    sf = pfp[bi]
                    wt(nc.vector, self.e_consts)
                    VE.tag(nc.vector.tensor_tensor(
                        sf[:, kact - 128:kact], sf[:, kact - 128:kact],
                        self.mask[:, :], ALU.add))
                    e_mx = VE.tag(nc.vector.tensor_reduce(
                        stt[:, 5:6], sf[:, 0:kact], axis=AX.X,
                        op=ALU.max, negate=True))
                    wt(nc.scalar, e_mx)
                    e_ex = AE.tag(nc.scalar.activation(
                        sf[:, 0:kact], sf[:, 0:kact], AF.Exp,
                        bias=stt[:, 5:6],
                        accum_out=stt[:, 8 + bi:9 + bi]))
                    wt(nc.vector, e_ex)
                    e_zr = VE.tag(nc.vector.reciprocal(
                        stt[:, 14 + bi:15 + bi], stt[:, 8 + bi:9 + bi]))
                    wt(nc.vector, e_pst_ev[0], e_pst_ev[1])
                    e_phi = VE.tag(nc.vector.tensor_copy(phi[:, 0:kact],
                                                         sf[:, 0:kact]))
                    wt(nc.gpsimd, e_phi)
                    e_plo = GE.tag(nc.gpsimd.tensor_sub(
                        plo[:, 0:kact], sf[:, 0:kact], phi[:, 0:kact]))
                    e_pfp_free[bi] = e_plo
                    for src, dstv, e_src in ((phi, ptb_hi, e_phi),
                                             (plo, ptb_lo, e_plo)):
                        for g in range(0, nkc, 8):
                            gb = min(8, nkc - g)
                            tb = (g // 8) % 2
                            wt(nc.tensor, e_pst_ev[tb], e_src, e_spill_a)
                            for jj in range(gb):
                                kc = g + jj
                                ins = nc.tensor.transpose(
                                    self.pst[tb][:, jj * 128:(jj + 1) * 128],
                                    src[:, kc * 128:(kc + 1) * 128],
                                    self.ident[:, :])
                            e_t = TE.tag(ins)
                            wt(nc.vector, e_t)
                            outap = bass.AP(
                                dstv.tensor, dstv.offset + g * 128,
                                [[32768, 128], [1, gb * 128]])
                            inap = bass.AP(self.pst[tb], 0,
                                           [[1024, 128], [1, gb * 128]])
                            e_pst_ev[tb] = VE.tag(
                                nc.vector.tensor_copy(outap, inap))
                    ops = self.ps6 if bi == 0 else self.ps7
                    wt(nc.tensor, e_pst_ev[0], e_pst_ev[1], e_o_read)
                    ins = None
                    for kc in range(nkc):
                        vh = hv(par, 16384, 4096)[:, kc * 256:(kc + 1) * 256]
                        vl = hv(par, 20480, 4096)[:, kc * 256:(kc + 1) * 256]
                        ph_ = ptb_hi[:, kc * 128:(kc + 1) * 128]
                        pl_ = ptb_lo[:, kc * 128:(kc + 1) * 128]
                        nc.tensor.matmul(ops[:, 0:256], ph_, vh,
                                         start=(kc == 0), stop=False)
                        nc.tensor.matmul(ops[:, 0:256], ph_, vl,
                                         start=False, stop=False)
                        ins = nc.tensor.matmul(ops[:, 0:256], pl_, vh,
                                               start=False,
                                               stop=(kc == nkc - 1))
                    e_pv = TE.tag(ins)
                    if bi == 1:
                        e_pv2 = e_pv
                wt(nc.vector, e_pv2, e_nc)
                e_nlz = VE.tag(nc.vector.tensor_tensor(
                    stt[:, 16:17], self.cols[:, 0:1], stt[:, 15:16],
                    ALU.mult))
                VE.tag(nc.vector.tensor_scalar(
                    ct1[:, :], self.ps6[:, 0:256], stt[:, 14:15], None,
                    ALU.mult))
                e_comb = VE.tag(nc.vector.scalar_tensor_tensor(
                    afp[:, qc * 256:(qc + 1) * 256], self.ps7[:, 0:256],
                    stt[:, 16:17], ct1[:, :], ALU.mult, ALU.add))
                e_o_read = e_comb
                wt(nc.scalar, e_comb)
                AE.tag(nc.scalar.activation(
                    ct1[:, :], afp[:, qc * 256:(qc + 1) * 256],
                    AF.Square, accum_out=stt[:, 160 + qc:161 + qc]))
            e_ss = self.AE.evt()
            wt(nc.vector, e_ss)
            VE.tag(nc.vector.tensor_scalar(stt[:, 176:192],
                                           stt[:, 160:176], 1.0 / 256,
                                           EPS_LN, ALU.mult, ALU.add))
            e_u = VE.tag(nc.vector.tensor_copy(stt[:, 144:160],
                                               stt[:, 176:192]))
            wt(nc.scalar, e_u)
            e_sq = AE.tag(nc.scalar.activation(stt[:, 176:192],
                                               stt[:, 144:160], AF.Sqrt))
            wt(nc.vector, e_sq)
            VE.tag(nc.vector.reciprocal(stt[:, 176:192], stt[:, 176:192]))
            VE.tag(nc.vector.tensor_tensor(stt[:, 128:144], stt[:, 176:192],
                                           stt[:, 176:192], ALU.mult))
            VE.tag(nc.vector.tensor_tensor(stt[:, 128:144], stt[:, 128:144],
                                           stt[:, 144:160], ALU.mult))
            VE.tag(nc.vector.tensor_scalar(stt[:, 128:144], stt[:, 128:144],
                                           -0.5, 1.5, ALU.mult, ALU.add))
            e_rst = VE.tag(nc.vector.tensor_tensor(
                stt[:, 176:192], stt[:, 176:192], stt[:, 128:144], ALU.mult))
            e_ab = [None, None]
            for qc in range(NCH):
                VE.tag(nc.vector.tensor_scalar(
                    pfp[0][:, 0:256], afp[:, qc * 256:(qc + 1) * 256],
                    stt[:, 176 + qc:177 + qc], None, ALU.mult))
                wt(nc.vector, self.e_sw2, e_ab[0], e_ab[1])
                e_af = VE.tag(nc.vector.tensor_tensor(
                    pfp[0][:, 1024:1280], pfp[0][:, 0:256], self.sw2[:, :],
                    ALU.mult))
                wt(nc.scalar, e_af)
                e_h_ = AE.tag(nc.scalar.copy(
                    abuf_hi[:, qc * 256:(qc + 1) * 256],
                    pfp[0][:, 1024:1280]))
                wt(nc.gpsimd, e_h_)
                e_l_ = GE.tag(nc.gpsimd.tensor_sub(
                    abuf_lo[:, qc * 256:(qc + 1) * 256],
                    pfp[0][:, 1024:1280],
                    abuf_hi[:, qc * 256:(qc + 1) * 256]))
                e_ab = [e_h_, e_l_]
                e_pfp_free[0] = e_l_
            for srcv, dstv, sh in ((abuf_hi, ptb_hi, self.sa_hi),
                                   (abuf_lo, ptb_lo, self.sa_lo)):
                e_src = e_ab[0] if srcv is abuf_hi else e_ab[1]
                for eb in range(2):
                    for g in range(0, NCH, 8):
                        tb = (g // 8) % 2
                        wt(nc.tensor, e_pst_ev[tb], e_src, e_spill_a)
                        for jj in range(8):
                            qc = g + jj
                            ins = nc.tensor.transpose(
                                self.pst[tb][:, jj * 128:(jj + 1) * 128],
                                srcv[:, qc * 256 + eb * 128:
                                     qc * 256 + eb * 128 + 128],
                                self.ident[:, :])
                        e_t = TE.tag(ins)
                        wt(nc.vector, e_t)
                        outap = bass.AP(dstv.tensor,
                                        dstv.offset + g * 128,
                                        [[32768, 128], [1, 1024]])
                        inap = bass.AP(self.pst[tb], 0,
                                       [[1024, 128], [1, 1024]])
                        e_pst_ev[tb] = VE.tag(
                            nc.vector.tensor_copy(outap, inap))
                    wt(nc.scalar, e_pst_ev[0], e_pst_ev[1])
                    e_spill_a = self.dma(
                        nc.scalar, "spa",
                        bass.AP(sh.tensor, (2 * h + eb) * 128 * 128,
                                [[128, 128], [NCH * 128 * 128, 16],
                                 [1, 128]]),
                        bass.AP(dstv.tensor, dstv.offset,
                                [[32768, 128], [128, 16], [1, 128]]))
            e_arena_free[par] = TE.evt()
        self.e_attn_done = e_spill_a
    def phase4(self):
        nc = self.nc
        TE, VE, AE, GE = self.TE, self.VE, self.AE, self.GE
        self.barrier()
        stt = self.stats

        e_g = self.dma(nc.sync, "bc0", self.bc0[:, :],
                       bass.AP(self.srow.tensor, SROW_MOD + 2 * E,
                               [[0, 128], [1, E]]))
        t1 = self.wview(F32, 2048, 24576)
        t2 = self.wview(F32, 2048, 32768)
        wt(nc.vector, e_g)
        VE.tag(nc.vector.tensor_scalar(t1[:, :], self.bc0[:, :], 0.25, None,
                                       ALU.max))
        VE.tag(nc.vector.tensor_scalar(t2[:, :], self.bc0[:, :], -0.25, None,
                                       ALU.min))
        VE.tag(nc.vector.tensor_scalar(self.bc0[:, :], self.bc0[:, :], 0.0,
                                       None, ALU.is_ge))
        VE.tag(nc.vector.tensor_tensor(t1[:, :], t1[:, :], t2[:, :],
                                       ALU.subtract))
        VE.tag(nc.vector.tensor_tensor(t1[:, :], self.bc0[:, :], t1[:, :],
                                       ALU.mult))
        e_ga = VE.tag(nc.vector.tensor_tensor(self.bc0[:, :], t2[:, :],
                                              t1[:, :], ALU.add))

        woc = self.wview(F32, 2048, 16384)
        e_wsp = [None, None]
        for wc in range(NCH):
            wt(nc.sync, e_wsp[0], e_wsp[1])
            e_ld = self.dma(nc.sync, "woc", woc[:, :],
                            self.wo[wc * 128:(wc + 1) * 128, :])
            wt(nc.scalar, e_ld)
            e_hi = AE.tag(nc.scalar.copy(
                self.hfm_hi[:, wc * 2048:(wc + 1) * 2048], woc[:, :]))
            wt(nc.gpsimd, e_hi)
            e_lo = GE.tag(nc.gpsimd.tensor_sub(
                self.hfm_lo[:, wc * 2048:(wc + 1) * 2048], woc[:, :],
                self.hfm_hi[:, wc * 2048:(wc + 1) * 2048]))
            e_wsp = [e_hi, e_lo]

        ab_hi = [self.wview(BF16, 2048, 0), self.wview(BF16, 2048, 8192)]
        ab_lo = [self.wview(BF16, 2048, 4096), self.wview(BF16, 2048, 12288)]
        xb = [self.wview(F32, 2048, 24576), self.wview(F32, 2048, 32768)]
        x1b = [self.wview(F32, 2048, 40960), self.wview(F32, 2048, 49152)]
        ot = self.wview(F32, 512, 16384)
        e_a_used = [None, None]
        e_x_used = [None, None]
        e_bank = [None, None]
        e_x1_done = [None, None]
        e_sx = [None, None]
        e_sq_prev = None
        for sc in range(NCH):
            par = sc % 2
            wt(nc.sync, e_a_used[par])
            self.dma(nc.sync, f"pa{par}",
                     bass.AP(ab_hi[par].tensor, ab_hi[par].offset,
                             [[28672, 128], [128, 16], [1, 128]]),
                     bass.AP(self.sa_hi.tensor, sc * 16 * 128 * 128,
                             [[128, 128], [128 * 128, 16], [1, 128]]))
            e_a = self.dma(nc.sync, f"pa{par}",
                           bass.AP(ab_lo[par].tensor, ab_lo[par].offset,
                                   [[28672, 128], [128, 16], [1, 128]]),
                           bass.AP(self.sa_lo.tensor, sc * 16 * 128 * 128,
                                   [[128, 128], [128 * 128, 16], [1, 128]]))
            wt(nc.sync, e_x_used[par], e_ga)
            e_x = self.dma(nc.sync, f"px{par}", xb[par][:, :],
                           self.xin[sc * 128:(sc + 1) * 128, :])
            wt(nc.vector, e_x1_done[par])
            for ft in range(4):
                bk = ft % 2
                wt(nc.tensor, e_a, e_wsp[0], e_wsp[1], e_bank[bk])
                ins = None
                for af in range(16):
                    lh = ab_hi[par][:, af * 128:(af + 1) * 128]
                    rh = self.hfm_hi[:, af * 2048 + ft * 512:
                                     af * 2048 + ft * 512 + 512]
                    rl = self.hfm_lo[:, af * 2048 + ft * 512:
                                     af * 2048 + ft * 512 + 512]
                    nc.tensor.matmul(self.psb[bk][:, :], lh, rh,
                                     start=(af == 0), stop=False)
                    nc.tensor.matmul(self.psb[bk][:, :], lh, rl,
                                     start=False, stop=False)
                for af in range(16):
                    ll = ab_lo[par][:, af * 128:(af + 1) * 128]
                    rh = self.hfm_hi[:, af * 2048 + ft * 512:
                                     af * 2048 + ft * 512 + 512]
                    ins = nc.tensor.matmul(self.psb[bk][:, :], ll, rh,
                                           start=False, stop=(af == 15))
                e_mm = TE.tag(ins)
                wt(nc.vector, e_mm, e_ga, e_x)
                if ft == 0:
                    wt(nc.vector, e_sx[par], e_sq_prev)
                VE.tag(nc.vector.tensor_tensor(
                    ot[:, :], self.psb[bk][:, :],
                    self.bc0[:, ft * 512:(ft + 1) * 512], ALU.mult))
                e_ev = VE.tag(nc.vector.tensor_tensor(
                    x1b[par][:, ft * 512:(ft + 1) * 512], ot[:, :],
                    xb[par][:, ft * 512:(ft + 1) * 512], ALU.add))
                e_bank[bk] = e_ev
            e_a_used[par] = TE.evt()
            e_x_used[par] = e_ev
            e_r = self.emit_stats(x1b[par][:, :], stt[:, 18 + sc:19 + sc],
                                  stt[:, 34 + sc:35 + sc], [e_ev], par=par)
            e_x1_done[par] = e_r
            e_sq_prev = self.AE.evt()
            wt(nc.scalar, e_ev)
            e_sx[par] = self.dma(nc.scalar, f"sx{par}", self.sx1[sc, :, :],
                                 x1b[par][:, :])
        self.e_p4 = VE.evt()

    def phase5(self):
        nc = self.nc
        VE, AE, GE = self.VE, self.AE, self.GE
        self.barrier()
        self.pst_guard = [None, None]
        stt = self.stats
        e_b0 = self.dma(nc.sync, "bc0", self.bc0[:, :],
                        bass.AP(self.srow.tensor, SROW_MOD + 4 * E,
                                [[0, 128], [1, E]]))
        e_b1 = self.dma(nc.sync, "bc1", self.bc1[:, :],
                        bass.AP(self.srow.tensor, SROW_MOD + 3 * E,
                                [[0, 128], [1, E]]))
        wt(nc.vector, e_b0)
        e_bc = VE.tag(nc.vector.tensor_scalar(
            self.bc0[:, :], self.bc0[:, :], 1.0, None, ALU.add))
        xb = [self.wview(F32, 2048, 0), self.wview(F32, 2048, 8192)]
        zb = self.wview(F32, 2048, 24576)
        hfp = self.wview(F32, 2048, 32768)
        hhi = self.wview(BF16, 2048, 40960)
        hlo = self.wview(BF16, 2048, 45056)
        e_used = [None, None]
        e_hi_prev = e_lo_prev = None
        e_te_hi = e_te_lo = None
        for sc in range(NCH):
            par = sc % 2
            wt(nc.sync, e_used[par])
            e_x = self.dma(nc.sync, f"x{par}", xb[par][:, :],
                           self.sx1[sc, :, :])
            wt(nc.vector, e_x)
            e_z = VE.tag(nc.vector.tensor_scalar(
                zb[:, :], xb[par][:, :], stt[:, 18 + sc:19 + sc],
                stt[:, 34 + sc:35 + sc], ALU.subtract, ALU.mult))
            e_used[par] = e_z
            wt(nc.vector, e_bc, e_b1, e_hi_prev, e_lo_prev)
            VE.tag(nc.vector.tensor_tensor(hfp[:, :], zb[:, :],
                                           self.bc0[:, :], ALU.mult))
            e_h = VE.tag(nc.vector.tensor_tensor(hfp[:, :], hfp[:, :],
                                                 self.bc1[:, :], ALU.add))
            wt(nc.scalar, e_h, e_te_hi)
            e_hi = AE.tag(nc.scalar.copy(hhi[:, :], hfp[:, :]))
            wt(nc.gpsimd, e_hi, e_te_lo)
            e_lo = GE.tag(nc.gpsimd.tensor_sub(hlo[:, :], hfp[:, :],
                                               hhi[:, :]))
            e_te_hi, _ = self.emit_fm_store(hhi, self.hfm_hi, sc, e_hi)
            e_te_lo, _ = self.emit_fm_store(hlo, self.hfm_lo, sc, e_lo)
            e_hi_prev, e_lo_prev = e_hi, e_lo
        if self.want("h2fm"):
            t1 = self.dbg_dram("h2fm_hi", [128, 32768])
            t2 = self.dbg_dram("h2fm_lo", [128, 32768])
            wt(nc.gpsimd, VE.evt())
            self.dma(nc.gpsimd, "dbg", t1[:, :], self.hfm_hi[:, :])
            self.dma(nc.gpsimd, "dbg", t2[:, :], self.hfm_lo[:, :])

    def phase6(self):
        nc = self.nc
        TE, VE, AE, GE = self.TE, self.VE, self.AE, self.GE
        self.barrier()
        e_c1 = self.dma(nc.sync, "ncol", self.cols[:, 1:2],
                        bass.AP(self.srow.tensor, SROW_SC + SLOT_THR_UP,
                                [[0, 128], [1, 1]]))
        e_c2 = self.dma(nc.sync, "ncol", self.cols[:, 2:3],
                        bass.AP(self.srow.tensor, SROW_SC + SLOT_THR_GATE,
                                [[0, 128], [1, 1]]))
        e_c3 = self.dma(nc.sync, "ncol", self.cols[:, 3:4],
                        bass.AP(self.srow.tensor, SROW_SC + SLOT_SUP,
                                [[0, 128], [1, 1]]))
        wt(nc.vector, e_c3)
        e_neg = VE.tag(nc.vector.tensor_scalar(
            self.cols[:, 4:6], self.cols[:, 1:3], -1.0, None, ALU.mult))

        wus = [self.wview(F32, 2048, 0), self.wview(F32, 2048, 8192)]
        wgs = [self.wview(F32, 2048, 16384), self.wview(F32, 2048, 24576)]
        tu = [self.wview(BF16, 2048, 32768), self.wview(BF16, 2048, 36864)]
        tg = [self.wview(BF16, 2048, 40960), self.wview(BF16, 2048, 45056)]
        m2 = self.wview(BF16, 2048, 49152)
        silu = self.wview(F32, 512, 53248)
        hfp = self.wview(F32, 512, 55296)
        Db = self.arenaD.bitcast(BF16)
        hh = Db[:, 9728:10240]
        hl = Db[:, 10240:10752]

        def p6_load(ec, par):
            e1 = self.dma(nc.sync, f"wus{par}",
                          self.wap3(F32, par * 8192, [[128, 16], [1, 128]]),
                          bass.AP(self.wupT.tensor, ec * 128,
                                  [[DEX, 128], [DEX * 128, 16], [1, 128]]))
            e2 = self.dma(nc.sync, f"wgs{par}",
                          self.wap3(F32, 16384 + par * 8192,
                                    [[128, 16], [1, 128]]),
                          bass.AP(self.wgateT.tensor, ec * 128,
                                  [[DEX, 128], [DEX * 128, 16], [1, 128]]))
            return (e1, e2)

        def p6_quant(par, e_ld, e_teq):
            wt(nc.vector, e_ld[0], e_ld[1], e_neg, e_teq)
            VE.tag(nc.vector.tensor_scalar(
                m2[:, :], wus[par][:, :], self.cols[:, 4:5], None, ALU.is_le))
            VE.tag(nc.vector.scalar_tensor_tensor(
                tu[par][:, :], wus[par][:, :], self.cols[:, 1:2], m2[:, :],
                ALU.is_ge, ALU.subtract))
            VE.tag(nc.vector.tensor_scalar(
                m2[:, :], wgs[par][:, :], self.cols[:, 5:6], None, ALU.is_le))
            return VE.tag(nc.vector.scalar_tensor_tensor(
                tg[par][:, :], wgs[par][:, :], self.cols[:, 2:3], m2[:, :],
                ALU.is_ge, ALU.subtract))

        e_q = [None, None]
        e_tq = [None, None]
        e_hid = None
        e_spill = None
        e_hf_prev = None
        e_bank = [None, None, None, None]
        e_tq[0] = p6_quant(0, p6_load(0, 0), None)
        for ec in range(ECH):
            par = ec % 2
            if ec + 1 < ECH:
                np_ = 1 - par
                wt(nc.sync, e_tq[np_])
                e_ldn = p6_load(ec + 1, np_)
                e_tq[np_] = p6_quant(np_, e_ldn, e_q[np_])
            for st in range(4):
                bu = st % 2
                bg = 2 + st % 2
                wt(nc.tensor, e_tq[par], e_bank[bu], e_bank[bg])
                ins = None
                for d in range(16):
                    lt_ = tu[par][:, d * 128:(d + 1) * 128]
                    nc.tensor.matmul(
                        self.psb[bu][:, :], lt_,
                        self.hfm_hi[:, d * 2048 + st * 512:
                                    d * 2048 + st * 512 + 512],
                        start=(d == 0), stop=False)
                    ins = nc.tensor.matmul(
                        self.psb[bu][:, :], lt_,
                        self.hfm_lo[:, d * 2048 + st * 512:
                                    d * 2048 + st * 512 + 512],
                        start=False, stop=(d == 15))
                e_mu = TE.tag(ins)
                for d in range(16):
                    lt_ = tg[par][:, d * 128:(d + 1) * 128]
                    nc.tensor.matmul(
                        self.psb[bg][:, :], lt_,
                        self.hfm_hi[:, d * 2048 + st * 512:
                                    d * 2048 + st * 512 + 512],
                        start=(d == 0), stop=False)
                    ins = nc.tensor.matmul(
                        self.psb[bg][:, :], lt_,
                        self.hfm_lo[:, d * 2048 + st * 512:
                                    d * 2048 + st * 512 + 512],
                        start=False, stop=(d == 15))
                e_mg = TE.tag(ins)
                wt(nc.scalar, e_mu, e_hid, e_c3, e_hf_prev)
                e_si = AE.tag(nc.scalar.activation(
                    silu[:, :], self.psb[bu][:, :], AF.Silu,
                    scale=self.cols[:, 3:4]))
                e_bank[bu] = e_si
                wt(nc.vector, e_si, e_mg, e_hid)
                e_hf = VE.tag(nc.vector.tensor_tensor(
                    hfp[:, :], silu[:, :], self.psb[bg][:, :], ALU.mult))
                e_bank[bg] = e_hf
                e_hf_prev = e_hf
                wt(nc.scalar, e_hf, e_spill)
                e_hh = AE.tag(nc.scalar.copy(hh[:, :], hfp[:, :]))
                wt(nc.vector, e_hh)
                e_hid = VE.tag(nc.vector.tensor_sub(hl[:, :], hfp[:, :],
                                                    hh[:, :]))
                wt(nc.scalar, e_hid)
                self.dma(nc.scalar, "sph",
                         self.shid_hi[ec, :, st * 512:(st + 1) * 512],
                         hh[:, :])
                e_spill = self.dma(nc.scalar, "sph",
                                   self.shid_lo[ec, :,
                                                st * 512:(st + 1) * 512],
                                   hl[:, :])
            e_q[par] = TE.evt()
        self.e_p6 = e_spill

    def phase7(self):
        nc = self.nc
        TE, VE, AE, GE = self.TE, self.VE, self.AE, self.GE
        self.barrier()
        yA = self.arenaA.bitcast(F32)
        hb = self.arenaB.bitcast(BF16)
        e_g = self.dma(nc.sync, "bc0", self.bc0[:, :],
                       bass.AP(self.srow.tensor, SROW_MOD + 5 * E,
                               [[0, 128], [1, E]]))
        self.dma(nc.sync, "ncol", self.cols[:, 7:8],
                 bass.AP(self.srow.tensor, SROW_SC + SLOT_SGD,
                         [[0, 128], [1, 1]]))
        e_c4 = self.dma(nc.sync, "ncol", self.cols[:, 8:9],
                        bass.AP(self.srow.tensor, SROW_SC + SLOT_THR_DOWN,
                                [[0, 128], [1, 1]]))
        t1 = self.wview(F32, 2048, 0)
        t2 = self.wview(F32, 2048, 8192)
        wt(nc.vector, e_g, e_c4)
        VE.tag(nc.vector.tensor_scalar(t1[:, :], self.bc0[:, :], 0.25, None,
                                       ALU.max))
        VE.tag(nc.vector.tensor_scalar(t2[:, :], self.bc0[:, :], -0.25, None,
                                       ALU.min))
        VE.tag(nc.vector.tensor_scalar(self.bc0[:, :], self.bc0[:, :], 0.0,
                                       None, ALU.is_ge))
        VE.tag(nc.vector.tensor_tensor(t1[:, :], t1[:, :], t2[:, :],
                                       ALU.subtract))
        VE.tag(nc.vector.tensor_tensor(t1[:, :], self.bc0[:, :], t1[:, :],
                                       ALU.mult))
        VE.tag(nc.vector.tensor_tensor(self.bc0[:, :], t2[:, :], t1[:, :],
                                       ALU.add))
        VE.tag(nc.vector.tensor_scalar(self.cols[:, 9:10], self.cols[:, 8:9],
                                       -1.0, None, ALU.mult))
        e_ga = VE.tag(nc.vector.tensor_scalar(
            self.bc0[:, :], self.bc0[:, :], self.cols[:, 7:8], None,
            ALU.mult))

        tdc = self.wview(F32, 2048, 0)
        td = [[self.wview(BF16, 2048, 8192 + (p * 4 + i) * 4096)
               for i in range(4)] for p in range(2)]
        m2 = self.wview(BF16, 2048, 40960)
        ocb = self.wview(F32, 1024, 45056)
        x1h = [self.wview(F32, 1024, 49152), self.wview(F32, 1024, 53248)]
        NBLK = 16

        def p7_hid_load(half, blk, par):
            e = None
            for el in range(4):
                self.dma(nc.sync, f"hid{par}",
                         hb[:, par * 16384 + el * 2048:
                            par * 16384 + (el + 1) * 2048],
                         self.shid_hi[blk * 4 + el, :, :])
                e = self.dma(nc.sync, f"hid{par}",
                             hb[:, par * 16384 + 8192 + el * 2048:
                                par * 16384 + 8192 + (el + 1) * 2048],
                             self.shid_lo[blk * 4 + el, :, :])
            return e

        self._p7_tdq = None

        def p7_td_quant(blk, par, e_teq):
            e_tq = self._p7_tdq
            for el in range(4):
                wt(nc.sync, e_tq, e_teq if el == 0 else None, e_ga)
                e_lt = self.dma(nc.sync, "tdc", tdc[:, :],
                                self.wdownT[(blk * 4 + el) * 128:
                                            (blk * 4 + el + 1) * 128, :])
                wt(nc.vector, e_lt, e_ga)
                VE.tag(nc.vector.tensor_scalar(
                    m2[:, :], tdc[:, :], self.cols[:, 9:10], None,
                    ALU.is_le))
                e_tq = VE.tag(nc.vector.scalar_tensor_tensor(
                    td[par][el][:, :], tdc[:, :], self.cols[:, 8:9],
                    m2[:, :], ALU.is_ge, ALU.subtract))
            self._p7_tdq = e_tq
            return e_tq

        e_bank = [None, None, None, None]
        e_teB = [None, None]
        e_out = None
        e_y_rd = None
        e_oc = None
        for half in range(2):
            e_ms = VE.tag(nc.vector.memset(yA[:, :], 0.0))
            e_hb = [None, None]
            e_tq = [None, None]
            e_hb[0] = p7_hid_load(half, 0, 0)
            e_tq[0] = p7_td_quant(0, 0, e_teB[0])
            for blk in range(NBLK):
                par = blk % 2
                if blk + 1 < NBLK:
                    np_ = 1 - par
                    wt(nc.sync, e_teB[np_])
                    e_hb[np_] = p7_hid_load(half, blk + 1, np_)
                    e_tq[np_] = p7_td_quant(blk + 1, np_, e_teB[np_])
                for sc in range(NCH):
                    for dt_ in range(2):
                        bk = (sc * 2 + dt_) % 4
                        wt(nc.tensor, e_hb[par], e_tq[par], e_bank[bk])
                        ins = None
                        for el in range(4):
                            lh = hb[:, par * 16384 + el * 2048 + sc * 128:
                                    par * 16384 + el * 2048 + sc * 128 + 128]
                            ll = hb[:, par * 16384 + 8192 + el * 2048 +
                                    sc * 128:
                                    par * 16384 + 8192 + el * 2048 +
                                    sc * 128 + 128]
                            rt = td[par][el][:, half * 1024 + dt_ * 512:
                                             half * 1024 + dt_ * 512 + 512]
                            nc.tensor.matmul(self.psb[bk][:, :], lh, rt,
                                             start=(el == 0), stop=False)
                            ins = nc.tensor.matmul(self.psb[bk][:, :], ll,
                                                   rt, start=False,
                                                   stop=(el == 3))
                        e_mm = TE.tag(ins)
                        wt(nc.vector, e_mm, e_y_rd, e_ms)
                        e_acc = VE.tag(nc.vector.tensor_tensor(
                            yA[:, sc * 1024 + dt_ * 512:
                               sc * 1024 + dt_ * 512 + 512],
                            yA[:, sc * 1024 + dt_ * 512:
                               sc * 1024 + dt_ * 512 + 512],
                            self.psb[bk][:, :], ALU.add))
                        e_bank[bk] = e_acc
                        last_ev = e_acc
                e_teB[par] = TE.evt()
            e_x1u = [None, None]
            for sc in range(NCH):
                par = sc % 2
                wt(nc.sync, e_x1u[par])
                e_x1 = self.dma(nc.sync, f"x1h{par}", x1h[par][:, :],
                                bass.AP(self.sx1.tensor,
                                        sc * 128 * E + half * 1024,
                                        [[E, 128], [1, 1024]]))
                wt(nc.vector, e_x1, last_ev, e_out)
                VE.tag(nc.vector.tensor_tensor(
                    ocb[:, :],
                    yA[:, sc * 1024:sc * 1024 + 1024],
                    self.bc0[:, half * 1024:(half + 1) * 1024], ALU.mult))
                e_oc = VE.tag(nc.vector.tensor_tensor(
                    ocb[:, :], ocb[:, :], x1h[par][:, :], ALU.add))
                e_x1u[par] = e_oc
                wt(nc.scalar, e_oc)
                e_out = self.dma(
                    nc.scalar, "out",
                    bass.AP(self.out.tensor, sc * 128 * E + half * 1024,
                            [[E, 128], [1, 1024]]),
                    ocb[:, :])
            e_y_rd = e_oc

    def finish(self):
        nc = self.nc
        evs = [sl.evt() for sl in self._slots.values()]
        evs += [self.TE.evt(), self.VE.evt(), self.AE.evt()]
        wt(nc.gpsimd, *evs)


def build(dbg=(), upto=0):
    b = Builder(dbg=dbg, upto=upto)
    ncp = b.build()
    return b, ncp


from concourse.bass_utils import run_bass_kernel_spmd


_NC_CACHE = {}


def kernel(L0_attn_x, L0_attn_Wq, L0_attn_Wk, L0_attn_Wv, L0_attn_Wo,
           L0_attn_lambda_q1, L0_attn_lambda_k1, L0_attn_lambda_q2,
           L0_attn_lambda_k2, L0_attn_subln_weight, L0_ffn_W_up,
           L0_ffn_W_gate, L0_ffn_W_down, L0_c, L0_wmod):
    n = 8
    if "nc" not in _NC_CACHE:
        b = Builder(dbg=(), upto=7)
        _NC_CACHE["nc"] = b.build()
    ncp = _NC_CACHE["nc"]
    f32 = np.float32
    shared = {
        "wq": np.ascontiguousarray(L0_attn_Wq, dtype=f32),
        "wk": np.ascontiguousarray(L0_attn_Wk, dtype=f32),
        "wv": np.ascontiguousarray(L0_attn_Wv, dtype=f32),
        "wo": np.ascontiguousarray(L0_attn_Wo, dtype=f32),
        "wmod": np.ascontiguousarray(L0_wmod, dtype=f32),
        "wupT": np.ascontiguousarray(np.asarray(L0_ffn_W_up, dtype=f32).T),
        "wgateT": np.ascontiguousarray(np.asarray(L0_ffn_W_gate, dtype=f32).T),
        "wdownT": np.ascontiguousarray(np.asarray(L0_ffn_W_down, dtype=f32).T),
        "lq1": np.ascontiguousarray(L0_attn_lambda_q1, dtype=f32),
        "lk1": np.ascontiguousarray(L0_attn_lambda_k1, dtype=f32),
        "lq2": np.ascontiguousarray(L0_attn_lambda_q2, dtype=f32),
        "lk2": np.ascontiguousarray(L0_attn_lambda_k2, dtype=f32),
        "subw": np.ascontiguousarray(L0_attn_subln_weight, dtype=f32),
    }
    x = np.asarray(L0_attn_x, dtype=f32)
    c = np.asarray(L0_c, dtype=f32)
    in_maps = []
    for bb in range(n):
        m = dict(shared)
        m["xin"] = np.ascontiguousarray(x[bb])
        m["cvec"] = np.ascontiguousarray(c[bb])
        in_maps.append(m)
    res = run_bass_kernel_spmd(ncp, in_maps, core_ids=list(range(n)))
    return np.stack([np.asarray(r["out"]) for r in res.results], axis=0)


# version 05
# speedup vs baseline: 1.6106x; 1.0429x over previous
import contextlib
import numpy as np
import concourse.bass as bass
import concourse.mybir as mybir

F32 = mybir.dt.float32
BF16 = mybir.dt.bfloat16
U8 = mybir.dt.uint8
AF = mybir.ActivationFunctionType
ALU = mybir.AluOpType
AX = mybir.AxisListType

S = 2048
E = 2048
NCH = 16
NH = 8
HD = 128
DEX = 8192
ECH = 64
LAM_INIT = 0.8 - 0.6 * float(np.exp(-0.3 * 12))
SCALE = HD ** -0.5
EPS_LN = 1e-5
EPS_Q = 1e-8
MASKVAL = -1e30

SROW_MOD = 0
SROW_SC = 6 * E
SLOT_NEGLAM = 0
SLOT_SUP = 1
SLOT_SGD = 2
SLOT_THR_UP = 3
SLOT_THR_GATE = 4
SLOT_THR_DOWN = 5
NSLOT = 8


class Evt:
    __slots__ = ("sem", "v")

    def __init__(self, sem, v):
        self.sem = sem
        self.v = v


class ESem:

    def __init__(self, nc, es, name, eng, flush):
        self.sem = es.enter_context(nc.semaphore(name))
        self.n = 0
        self.eng = eng
        self.flush = flush

    def tag(self, ins):
        ins.then_inc(self.sem, 1)
        self.n += 1
        if self.flush:
            self.eng.wait_ge(self.sem, self.n)
        return Evt(self.sem, self.n)

    def bump(self):
        if self.n > 0:
            self.eng.wait_ge(self.sem, self.n)
        self.eng.nop().then_inc(self.sem, 1)
        self.n += 1
        return Evt(self.sem, self.n)

    def evt(self):
        return Evt(self.sem, self.n)


class Slot:

    def __init__(self, nc, es, name):
        self.sem = es.enter_context(nc.semaphore(name))
        self.n = 0

    def evt(self):
        return Evt(self.sem, self.n)


def wt(eng, *evts):
    for e in evts:
        if e is not None and e.v > 0:
            eng.wait_ge(e.sem, e.v)


class Builder:
    def __init__(self, dbg=(), upto=99):
        self.dbg_names = set(dbg)
        self.upto = upto
        nc = self.nc = bass.Bass("TRN2", target_bir_lowering=False)
        self.es = contextlib.ExitStack()

        def din(name, shape):
            return nc.dram_tensor(name, shape, F32, kind="ExternalInput").ap()

        self.xin = din("xin", [S, E])
        self.cvec = din("cvec", [E])
        self.wq = din("wq", [E, E])
        self.wk = din("wk", [E, E])
        self.wv = din("wv", [E, E])
        self.wo = din("wo", [E, E])
        self.wmod = din("wmod", [E, 6 * E])
        self.wupT = din("wupT", [E, DEX])
        self.wgateT = din("wgateT", [E, DEX])
        self.wdownT = din("wdownT", [DEX, E])
        self.lq1 = din("lq1", [HD])
        self.lk1 = din("lk1", [HD])
        self.lq2 = din("lq2", [HD])
        self.lk2 = din("lk2", [HD])
        self.subw = din("subw", [2 * HD])
        self.out = nc.dram_tensor("out", [S, E], F32, kind="ExternalOutput").ap()

        def scr(name, shape, dt=BF16):
            kind = "ExternalOutput" if ("spill" in self.dbg_names) else "Internal"
            return nc.dram_tensor(name, shape, dt, kind=kind).ap()

        self.srow = scr("srow", [SROW_SC + NSLOT], F32)
        self.sq_hi = scr("sq_hi", [NCH, 128, S])
        self.sq_lo = scr("sq_lo", [NCH, 128, S])
        self.sk_hi = scr("sk_hi", [NCH, 128, S])
        self.sk_lo = scr("sk_lo", [NCH, 128, S])
        self.sv_hi = scr("sv_hi", [NH, NCH, 128, 256])
        self.sv_lo = scr("sv_lo", [NH, NCH, 128, 256])
        self.sa_hi = scr("sa_hi", [NCH, NCH, 128, 128])
        self.sa_lo = scr("sa_lo", [NCH, NCH, 128, 128])
        self.shid_hi = scr("shid_hi", [ECH, 128, S])
        self.shid_lo = scr("shid_lo", [ECH, 128, S])
        self.sx1 = scr("sx1", [NCH, 128, E], F32)

        self.arenaA = nc.alloc_sbuf_tensor("arenaA", [128, 65536], U8)
        self.arenaB = nc.alloc_sbuf_tensor("arenaB", [128, 65536], U8)
        self.arenaW = nc.alloc_sbuf_tensor("arenaW", [128, 57344], U8)
        self.arenaD = nc.alloc_sbuf_tensor("arenaD", [128, 22528], U8)

        self.hfm_hi = self.arenaA.bitcast(BF16)
        self.hfm_lo = self.arenaB.bitcast(BF16)
        D = self.arenaD.bitcast(F32)
        Db = self.arenaD.bitcast(BF16)
        self.bc0 = D[:, 0:2048]
        self.bc1 = D[:, 2048:4096]
        self.sw2 = D[:, 4096:4352]
        self.mask = D[:, 4352:4480]
        self.ident = Db[:, 8960:9088]
        self.cols = D[:, 4544:4608]
        self.stats = D[:, 4608:4864]

        self.psb = [nc.alloc_psum_tensor(f"psb{i}", [128, 512], F32)
                    for i in range(4)]
        self.pst = [nc.alloc_psum_tensor(f"pst{i}", [128, 1024], BF16)
                    for i in range(2)]
        self.ps6 = nc.alloc_psum_tensor("ps6", [128, 512], F32)
        self.ps7 = nc.alloc_psum_tensor("ps7", [128, 512], F32)

        es = self.es
        self.TE = ESem(nc, es, "sTE", nc.tensor, flush=False)
        self.VE = ESem(nc, es, "sVE", nc.vector, flush=True)
        self.AE = ESem(nc, es, "sAE", nc.scalar, flush=True)
        self.GE = ESem(nc, es, "sGE", nc.gpsimd, flush=True)
        self.SE = ESem(nc, es, "sSE", nc.sync, flush=False)
        self.engines = [(nc.tensor, self.TE), (nc.vector, self.VE),
                        (nc.scalar, self.AE), (nc.gpsimd, self.GE),
                        (nc.sync, self.SE)]
        self._slots = {}
        self.dbg_tensors = {}

    def slot(self, name):
        if name not in self._slots:
            self._slots[name] = Slot(self.nc, self.es, "sl_" + name)
        return self._slots[name]

    def dma(self, eng, slotname, out, in_, **kw):
        sl = self.slot(slotname)
        eng.dma_start(out=out, in_=in_, **kw).then_inc(sl.sem, 16)
        sl.n += 16
        return Evt(sl.sem, sl.n)

    def barrier(self):
        for eng, es_ in self.engines:
            es_.bump()
        evs = [es_.evt() for _, es_ in self.engines]
        evs += [sl.evt() for sl in self._slots.values()]
        for eng, _ in self.engines:
            wt(eng, *evs)

    def want(self, name):
        return name in self.dbg_names

    def dbg_dram(self, name, shape, dt=F32):
        t = self.nc.dram_tensor("dbg_" + name, list(shape), dt,
                                kind="ExternalOutput").ap()
        self.dbg_tensors[name] = t
        return t

    def wview(self, dt, cols, off_bytes):
        a = self.arenaW.bitcast(dt)
        isz = {F32: 4, BF16: 2, U8: 1}[dt]
        assert off_bytes % isz == 0
        c0 = off_bytes // isz
        assert (c0 + cols) * isz <= 57344, (off_bytes, cols)
        return a[:, c0:c0 + cols]

    def wap3(self, dt, off_bytes, dims):
        h = self.arenaW.bitcast(dt)
        isz = {F32: 4, BF16: 2, U8: 1}[dt]
        width = 57344 // isz
        assert off_bytes % isz == 0
        return bass.AP(h, off_bytes // isz, [[width, 128]] + list(dims))

    def build(self):
        nc = self.nc
        with self.es, nc.Block():
            with nc.named_scope("p0mod"):
                self.phase0()
            if self.upto >= 1:
                with nc.named_scope("p1ln1"):
                    self.phase1()
            if self.upto >= 2:
                with nc.named_scope("p2qkv"):
                    self.phase2()
            if self.upto >= 3:
                with nc.named_scope("p3attn"):
                    self.phase3()
            if self.upto >= 4:
                with nc.named_scope("p4wo"):
                    self.phase4()
            if self.upto >= 5:
                with nc.named_scope("p5ln2"):
                    self.phase5()
            if self.upto >= 6:
                with nc.named_scope("p6upgate"):
                    self.phase6()
            if self.upto >= 7:
                with nc.named_scope("p7down"):
                    self.phase7()
            self.finish()
        return nc

    def phase0(self):
        nc = self.nc
        TE, VE, AE, GE = self.TE, self.VE, self.AE, self.GE

        cfm = self.wview(F32, 16, 0)
        lv = self.wview(F32, 512, 256)
        lt = self.wview(F32, 16, 2304)
        mrow = [self.wview(F32, 512, 3584), self.wview(F32, 512, 5632)]
        wmb = [self.wview(F32, 6144, 8192), self.wview(F32, 6144, 32768)]

        GE.tag(nc.gpsimd.memset(self.stats[:, :], 0.0))
        GE.tag(nc.gpsimd.memset(self.cols[:, :], 0.0))
        GE.tag(nc.gpsimd.memset(self.cols[:, 60:64], 1.0))
        GE.tag(nc.gpsimd.memset(self.ident[:, :], 0.0))
        GE.tag(nc.gpsimd.affine_select(
            out=self.ident[:, :], in_=self.ident[:, :],
            compare_op=ALU.not_equal, fill=1.0, base=0,
            pattern=[[-1, 128]], channel_multiplier=1))
        GE.tag(nc.gpsimd.memset(self.mask[:, :], 0.0))
        self.e_consts = GE.tag(nc.gpsimd.affine_select(
            out=self.mask[:, :], in_=self.mask[:, :],
            compare_op=ALU.is_ge, fill=MASKVAL, base=0,
            pattern=[[-1, 128]], channel_multiplier=1))

        self.dma(nc.sync, "lam", lv[0:1, 0:128], self.lq1[None, :])
        self.dma(nc.sync, "lam", lv[0:1, 128:256], self.lk1[None, :])
        self.dma(nc.sync, "lam", lv[0:1, 256:384], self.lq2[None, :])
        e_l = self.dma(nc.sync, "lam", lv[0:1, 384:512], self.lk2[None, :])
        e_c = self.dma(nc.sync, "cfm", cfm[:, 0:16],
                       bass.AP(self.cvec.tensor, 0, [[1, 128], [128, 16]]),
                       allow_slow_non_contiguous=True)
        e_swb = self.dma(nc.sync, "swb", self.sw2[:, :],
                         bass.AP(self.subw.tensor, 0, [[0, 128], [1, 256]]))
        wt(nc.vector, e_swb)
        self.e_sw2 = VE.tag(nc.vector.tensor_scalar(
            self.sw2[:, :], self.sw2[:, :], 1.0 - LAM_INIT, None, ALU.mult))

        wt(nc.scalar, e_c)
        e_sc = AE.tag(nc.scalar.activation(cfm[:, 0:16], cfm[:, 0:16], AF.Silu))

        wt(nc.vector, e_l)
        VE.tag(nc.vector.tensor_tensor(lv[0:1, 0:128], lv[0:1, 0:128],
                                       lv[0:1, 128:256], ALU.mult))
        VE.tag(nc.vector.tensor_tensor(lv[0:1, 256:384], lv[0:1, 256:384],
                                       lv[0:1, 384:512], ALU.mult))
        nc.vector.reduce_sum(lt[0:1, 0:1], lv[0:1, 0:128], axis=AX.X)
        e_vr = VE.tag(nc.vector.reduce_sum(lt[0:1, 1:2], lv[0:1, 256:384],
                                           axis=AX.X))
        wt(nc.scalar, e_vr)
        e_ae = AE.tag(nc.scalar.activation(lt[0:1, 2:4], lt[0:1, 0:2], AF.Exp))
        wt(nc.vector, e_ae)
        VE.tag(nc.vector.tensor_tensor(lt[0:1, 4:5], lt[0:1, 2:3],
                                       lt[0:1, 3:4], ALU.subtract))
        e_nl = VE.tag(nc.vector.tensor_scalar(
            lt[0:1, 5:6], lt[0:1, 4:5], LAM_INIT, -1.0, ALU.add, ALU.mult))
        wt(nc.gpsimd, e_nl)
        self.dma(nc.gpsimd, "srow_sc",
                 bass.AP(self.srow.tensor, SROW_SC + SLOT_NEGLAM,
                         [[1, 1], [1, 1]]),
                 lt[0:1, 5:6])

        Af = self.arenaA.bitcast(F32)
        xbs = [Af[:, 0:2048], Af[:, 2048:4096]]
        sdump = Af[:, 4096:6144]
        e_xs = [None, None]
        stt = self.stats

        def ln1_stats_chunk(sc):
            spar = sc % 2
            wt(nc.sync, e_xs[spar])
            e_x = self.dma(nc.sync, f"x{spar}", xbs[spar][:, :],
                           self.xin[sc * 128:(sc + 1) * 128, :])
            e_r = self.emit_stats(xbs[spar], stt[:, 18 + sc:19 + sc],
                                  stt[:, 34 + sc:35 + sc],
                                  [e_x, self.e_consts], par=spar,
                                  dump=sdump)
            e_xs[spar] = e_r

        wmt = self.wmod.tensor
        psm = [self.ps6, self.ps7]
        ev_te = [None, None]
        ev_cp = [None, None]
        ev_wr = [None, None]
        WMW = 384
        for t in range(32):
            if t % 2 == 0:
                ln1_stats_chunk(t // 2)
            par = t % 2
            wt(nc.sync, ev_te[par])
            w3 = self.wap3(F32, 8192 + par * 24576, [[WMW, 16], [1, WMW]])
            e_ld = self.dma(nc.sync, f"wm{par}", w3,
                            bass.AP(wmt, t * WMW,
                                    [[6 * E, 128], [6 * E * 128, 16],
                                     [1, WMW]]))
            wt(nc.tensor, e_ld, e_sc, ev_cp[par])
            ins = None
            for j in range(16):
                rj = wmb[par][:, j * WMW:(j + 1) * WMW]
                ins = nc.tensor.matmul(
                    psm[par][0:1, 0:WMW], cfm[:, j:j + 1], rj,
                    start=(j == 0), stop=(j == 15))
            ev_te[par] = TE.tag(ins)
            wt(nc.vector, ev_te[par], ev_wr[par])
            e_cp = VE.tag(nc.vector.tensor_copy(mrow[par][0:1, 0:WMW],
                                                psm[par][0:1, 0:WMW]))
            ev_cp[par] = e_cp
            wt(nc.gpsimd, e_cp)
            ev_wr[par] = self.dma(
                nc.gpsimd, f"mrow{par}",
                bass.AP(self.srow.tensor, SROW_MOD + t * WMW,
                        [[1, 1], [1, WMW]]),
                mrow[par][0:1, 0:WMW])
        self.e_modrow = [ev_wr[0], ev_wr[1]]

        if self.want("mod"):
            t = self.dbg_dram("mod", [6 * E])
            wt(nc.gpsimd, *self.e_modrow)
            self.dma(nc.gpsimd, "dbg", t[None, :],
                     bass.AP(self.srow.tensor, 0, [[1, 1], [1, 6 * E]]))

        wt(nc.sync, *self.e_modrow)
        e_b0 = self.dma(nc.sync, "bc0", self.bc0[:, :],
                        bass.AP(self.srow.tensor, SROW_MOD + E,
                                [[0, 128], [1, E]]))
        e_b1 = self.dma(nc.sync, "bc1", self.bc1[:, :],
                        bass.AP(self.srow.tensor, SROW_MOD + 0,
                                [[0, 128], [1, E]]))
        wt(nc.vector, e_b0)
        self.e_bc = VE.tag(nc.vector.tensor_scalar(
            self.bc0[:, :], self.bc0[:, :], 1.0, None, ALU.add))
        self.e_bc_ld = e_b1

    def emit_stats(self, xb, mcol, rcol, pre_evts, eps=EPS_LN, n=E, par=0,
                   dump=None):
        nc = self.nc
        VE, AE = self.VE, self.AE
        stt = self.stats
        if dump is None:
            dump = self.wview(F32, 2048, 16384)
        wt(nc.scalar, *pre_evts)
        e_sq = AE.tag(nc.scalar.activation(dump[:, 0:n], xb, AF.Square,
                                           accum_out=stt[:, 12 + par:13 + par]))
        wt(nc.vector, *pre_evts)
        VE.tag(nc.vector.reduce_sum(stt[:, 1:2], xb, axis=AX.X))
        VE.tag(nc.vector.tensor_scalar(mcol, stt[:, 1:2], 1.0 / n, None,
                                       ALU.mult))
        nc.vector.tensor_tensor(stt[:, 3:4], mcol, mcol, ALU.mult)
        wt(nc.vector, e_sq)
        VE.tag(nc.vector.tensor_scalar(stt[:, 4:5], stt[:, 12 + par:13 + par],
                                       1.0 / n, None, ALU.mult))
        VE.tag(nc.vector.tensor_tensor(stt[:, 5:6], stt[:, 4:5], stt[:, 3:4],
                                       ALU.subtract))
        e_u = VE.tag(nc.vector.tensor_scalar(stt[:, 5:6], stt[:, 5:6], eps,
                                             None, ALU.add))
        wt(nc.scalar, e_u)
        e_sq2 = AE.tag(nc.scalar.activation(stt[:, 6:7], stt[:, 5:6], AF.Sqrt))
        wt(nc.vector, e_sq2)
        VE.tag(nc.vector.reciprocal(stt[:, 7:8], stt[:, 6:7]))
        VE.tag(nc.vector.tensor_tensor(stt[:, 8:9], stt[:, 7:8], stt[:, 7:8],
                                       ALU.mult))
        VE.tag(nc.vector.tensor_tensor(stt[:, 9:10], stt[:, 8:9], stt[:, 5:6],
                                       ALU.mult))
        VE.tag(nc.vector.tensor_scalar(stt[:, 9:10], stt[:, 9:10], -0.5, 1.5,
                                       ALU.mult, ALU.add))
        e_r = VE.tag(nc.vector.tensor_tensor(rcol, stt[:, 7:8], stt[:, 9:10],
                                             ALU.mult))
        return e_r

    def emit_fm_store(self, src_bf16, dest_handle, sc, e_src, w=2048,
                      dest_width=32768, dest_col0=0, sc_stride=2048):
        nc = self.nc
        TE, VE = self.TE, self.VE
        nj = w // 128
        e_ev = None
        e_t = None
        for g in range(0, nj, 8):
            gb = min(8, nj - g)
            bi = (g // 8) % 2
            pbank = self.pst[bi]
            wt(nc.tensor, self.pst_guard[bi], e_src)
            ins = None
            for jj in range(gb):
                j = g + jj
                ins = nc.tensor.transpose(
                    pbank[:, jj * 128:(jj + 1) * 128],
                    src_bf16[:, j * 128:(j + 1) * 128], self.ident[:, :])
            e_t = TE.tag(ins)
            wt(nc.vector, e_t)
            outap = bass.AP(dest_handle,
                            dest_col0 + g * sc_stride + sc * 128,
                            [[dest_width, 128], [sc_stride, gb], [1, 128]])
            inap = bass.AP(pbank, 0, [[1024, 128], [128, gb], [1, 128]])
            e_ev = VE.tag(nc.vector.tensor_copy(outap, inap))
            self.pst_guard[bi] = e_ev
        return e_t, e_ev

    def phase1(self):
        nc = self.nc
        VE, AE, GE = self.VE, self.AE, self.GE
        self.barrier()
        self.pst_guard = [None, None]
        xb = [self.wview(F32, 2048, 0), self.wview(F32, 2048, 8192)]
        zb = self.wview(F32, 2048, 24576)
        hfp = self.wview(F32, 2048, 32768)
        hhi = self.wview(BF16, 2048, 40960)
        hlo = self.wview(BF16, 2048, 45056)
        stt = self.stats
        e_used = [None, None]
        e_hi_prev = e_lo_prev = None
        e_te_hi = e_te_lo = None
        for sc in range(NCH):
            par = sc % 2
            wt(nc.sync, e_used[par])
            e_x = self.dma(nc.sync, f"x{par}", xb[par][:, :],
                           self.xin[sc * 128:(sc + 1) * 128, :])
            wt(nc.vector, e_x)
            e_z = VE.tag(nc.vector.tensor_scalar(
                zb[:, :], xb[par][:, :], stt[:, 18 + sc:19 + sc],
                stt[:, 34 + sc:35 + sc], ALU.subtract, ALU.mult))
            e_used[par] = e_z
            wt(nc.vector, self.e_bc, self.e_bc_ld, e_hi_prev, e_lo_prev)
            VE.tag(nc.vector.tensor_tensor(hfp[:, :], zb[:, :],
                                           self.bc0[:, :], ALU.mult))
            e_h = VE.tag(nc.vector.tensor_tensor(hfp[:, :], hfp[:, :],
                                                 self.bc1[:, :], ALU.add))
            wt(nc.scalar, e_h, e_te_hi)
            e_hi = AE.tag(nc.scalar.copy(hhi[:, :], hfp[:, :]))
            wt(nc.gpsimd, e_hi, e_te_lo)
            e_lo = GE.tag(nc.gpsimd.tensor_sub(hlo[:, :], hfp[:, :],
                                               hhi[:, :]))
            e_te_hi, _ = self.emit_fm_store(hhi, self.hfm_hi, sc, e_hi)
            e_te_lo, _ = self.emit_fm_store(hlo, self.hfm_lo, sc, e_lo)
            e_hi_prev, e_lo_prev = e_hi, e_lo
        self.e_h1 = VE.evt()
        if self.want("h1fm"):
            t1 = self.dbg_dram("h1fm_hi", [128, 32768])
            t2 = self.dbg_dram("h1fm_lo", [128, 32768])
            wt(nc.gpsimd, VE.evt())
            self.dma(nc.gpsimd, "dbg", t1[:, :], self.hfm_hi[:, :])
            self.dma(nc.gpsimd, "dbg", t2[:, :], self.hfm_lo[:, :])

    def phase2(self):
        nc = self.nc
        TE, VE, AE, GE = self.TE, self.VE, self.AE, self.GE
        self.barrier()
        stt = self.stats

        self.p1_count = 0
        p1_mats = [self.wupT, self.wgateT, self.wdownT]
        self.e_p1_act = None
        pbuf = [self.wview(F32, 2048, 36864), self.wview(F32, 2048, 36864)]

        def pass1_chunk():
            c = self.p1_count
            if c >= 192:
                return
            self.p1_count += 1
            mat = p1_mats[c // 64]
            sub = c % 64
            par = 0
            wt(nc.gpsimd, self.e_p1_act if c >= 1 else None)
            e_ld = self.dma(nc.gpsimd, "p1a", pbuf[par][:, :],
                            bass.AP(mat.tensor, sub * 128 * 2048,
                                    [[2048, 128], [1, 2048]]))
            wt(nc.scalar, e_ld)
            self.e_p1_act = AE.tag(nc.scalar.activation(
                pbuf[par][:, 0:2048], pbuf[par][:, 0:2048], AF.Abs,
                accum_out=stt[:, 64 + c:65 + c]))

        wsf = [self.wview(F32, 2048, 0), self.wview(F32, 2048, 4096 + 4096)]
        whi = [self.wview(BF16, 2048, 16384), self.wview(BF16, 2048, 20480)]
        wlo = [self.wview(BF16, 2048, 24576 - 16384 + 12288) if False else
               self.wview(BF16, 2048, 24576), self.wview(BF16, 2048, 28672)]
        sthi = self.wview(BF16, 2048, 45056)
        stlo = self.wview(BF16, 2048, 49152)
        qfp = self.wview(F32, 512, 32768)
        mats = [(self.wq, self.sq_hi, self.sq_lo, SCALE),
                (self.wk, self.sk_hi, self.sk_lo, 1.0)]

        def qk_load(i):
            wmat = mats[i // NCH][0]
            m = i % NCH
            return self.dma(nc.sync, f"ws{i % 2}",
                            self.wap3(F32, [0, 8192][i % 2],
                                      [[128, 16], [1, 128]]),
                            bass.AP(wmat.tensor, m * 128,
                                    [[E, 128], [E * 128, 16], [1, 128]]))

        e_split = [None, None]
        e_te_hl = [None, None]
        e_bank = [None, None]
        e_spill = None
        e_stage_lo = None
        e_ld = [None, None]
        e_ld[0] = qk_load(0)
        for i in range(2 * NCH):
            par = i % 2
            mat_i, m = i // NCH, i % NCH
            dst_hi, dst_lo, qs = mats[mat_i][1], mats[mat_i][2], mats[mat_i][3]
            if i + 1 < 2 * NCH:
                wt(nc.sync, e_split[1 - par])
                e_ld[1 - par] = qk_load(i + 1)
                wt(nc.scalar, e_ld[1 - par], e_te_hl[1 - par])
                e_hi_n = AE.tag(nc.scalar.copy(whi[1 - par][:, :],
                                               wsf[1 - par][:, :]))
                wt(nc.vector, e_hi_n)
                e_split[1 - par] = VE.tag(nc.vector.tensor_sub(
                    wlo[1 - par][:, :], wsf[1 - par][:, :],
                    whi[1 - par][:, :]))
            if i == 0:
                wt(nc.scalar, e_ld[0])
                e_hi0 = AE.tag(nc.scalar.copy(whi[0][:, :], wsf[0][:, :]))
                wt(nc.vector, e_hi0)
                e_split[0] = VE.tag(nc.vector.tensor_sub(
                    wlo[0][:, :], wsf[0][:, :], whi[0][:, :]))
            for st in range(4):
                bk = st % 2
                wt(nc.tensor, e_split[par], e_bank[bk])
                ins = None
                for f in range(16):
                    lt_ = whi[par][:, f * 128:(f + 1) * 128]
                    nc.tensor.matmul(
                        self.psb[bk][:, :], lt_,
                        self.hfm_hi[:, f * 2048 + st * 512:
                                    f * 2048 + st * 512 + 512],
                        start=(f == 0), stop=False)
                    nc.tensor.matmul(
                        self.psb[bk][:, :], lt_,
                        self.hfm_lo[:, f * 2048 + st * 512:
                                    f * 2048 + st * 512 + 512],
                        start=False, stop=False)
                for f in range(16):
                    ins = nc.tensor.matmul(
                        self.psb[bk][:, :],
                        wlo[par][:, f * 128:(f + 1) * 128],
                        self.hfm_hi[:, f * 2048 + st * 512:
                                    f * 2048 + st * 512 + 512],
                        start=False, stop=(f == 15))
                e_mm = TE.tag(ins)
                wt(nc.vector, e_mm, e_stage_lo)
                e_q = VE.tag(nc.vector.tensor_scalar(
                    qfp[:, :], self.psb[bk][:, :], qs, None, ALU.mult))
                e_bank[bk] = e_q
                wt(nc.scalar, e_q, e_spill)
                e_sh = AE.tag(nc.scalar.copy(
                    sthi[:, st * 512:(st + 1) * 512], qfp[:, :]))
                wt(nc.gpsimd, e_sh)
                e_stage_lo = GE.tag(nc.gpsimd.tensor_sub(
                    stlo[:, st * 512:(st + 1) * 512], qfp[:, :],
                    sthi[:, st * 512:(st + 1) * 512]))
            e_te_hl[par] = TE.evt()
            wt(nc.scalar, e_stage_lo)
            self.dma(nc.scalar, "spq", dst_hi[m, :, :], sthi[:, :])
            e_spill = self.dma(nc.scalar, "spq", dst_lo[m, :, :],
                               stlo[:, :])
            for _ in range(6):
                pass1_chunk()
        while self.p1_count < 192:
            pass1_chunk()

        self.barrier()
        e_bank = [None, None]
        e_spill = None
        e_stage_lo = None
        wvf = self.wview(F32, 4096, 0)
        vhi = self.wview(BF16, 4096, 16384)
        vlo = self.wview(BF16, 4096, 24576)
        vsh = self.wview(BF16, 256, 32768)
        vsl = self.wview(BF16, 256, 33280)
        vfp = self.wview(F32, 256, 33792)
        e_vsplit = None
        for h in range(NH):
            wt(nc.sync, e_vsplit, e_te_hl[0], e_te_hl[1])
            e_ld = self.dma(nc.sync, "ws",
                            self.wap3(F32, 0, [[256, 16], [1, 256]]),
                            bass.AP(self.wv.tensor, h * 256,
                                    [[E, 128], [E * 128, 16], [1, 256]]))
            wt(nc.scalar, e_ld)
            e_hi = AE.tag(nc.scalar.copy(vhi[:, :], wvf[:, :]))
            wt(nc.gpsimd, e_hi)
            e_lo = GE.tag(nc.gpsimd.tensor_sub(vlo[:, :], wvf[:, :],
                                               vhi[:, :]))
            e_vsplit = e_lo
            for sc in range(NCH):
                bk = sc % 2
                wt(nc.tensor, e_lo, e_bank[bk])
                ins = None
                for f in range(16):
                    lh = self.hfm_hi[:, f * 2048 + sc * 128:
                                     f * 2048 + sc * 128 + 128]
                    rh = vhi[:, f * 256:(f + 1) * 256]
                    rl = vlo[:, f * 256:(f + 1) * 256]
                    nc.tensor.matmul(self.psb[bk][:, 0:256], lh, rh,
                                     start=(f == 0), stop=False)
                    nc.tensor.matmul(self.psb[bk][:, 0:256], lh, rl,
                                     start=False, stop=False)
                for f in range(16):
                    ll = self.hfm_lo[:, f * 2048 + sc * 128:
                                     f * 2048 + sc * 128 + 128]
                    rh = vhi[:, f * 256:(f + 1) * 256]
                    ins = nc.tensor.matmul(self.psb[bk][:, 0:256], ll, rh,
                                           start=False, stop=(f == 15))
                e_mm = TE.tag(ins)
                wt(nc.vector, e_mm, e_stage_lo)
                e_q = VE.tag(nc.vector.tensor_copy(vfp[:, 0:256],
                                                   self.psb[bk][:, 0:256]))
                e_bank[bk] = e_q
                wt(nc.scalar, e_q, e_spill)
                e_sh = AE.tag(nc.scalar.copy(vsh[:, 0:256], vfp[:, 0:256]))
                wt(nc.gpsimd, e_sh)
                e_stage_lo = GE.tag(nc.gpsimd.tensor_sub(
                    vsl[:, 0:256], vfp[:, 0:256], vsh[:, 0:256]))
                wt(nc.scalar, e_stage_lo)
                self.dma(nc.scalar, "spv", self.sv_hi[h, sc, :, :],
                         vsh[:, 0:256])
                e_spill = self.dma(nc.scalar, "spv", self.sv_lo[h, sc, :, :],
                                   vsl[:, 0:256])
            e_te_hl = [TE.evt(), TE.evt()]

        wt(nc.vector, self.e_p1_act)
        for mi in range(3):
            c0 = 64 + mi * 64
            VE.tag(nc.vector.reduce_sum(stt[:, 50 + mi:51 + mi],
                                        stt[:, c0:c0 + 64], axis=AX.X))
        e_ps = VE.evt()
        wt(nc.tensor, e_ps, e_bank[0], e_bank[1])
        ins = None
        for mi in range(3):
            ins = nc.tensor.matmul(self.psb[0][0:1, mi:mi + 1],
                                   self.cols[:, 60:61],
                                   stt[:, 50 + mi:51 + mi],
                                   start=(mi == 0), stop=(mi == 2))
        e_sc3 = TE.tag(ins)
        wt(nc.vector, e_sc3)
        VE.tag(nc.vector.tensor_scalar(stt[0:1, 53:56], self.psb[0][0:1, 0:3],
                                       1.0 / (E * DEX), EPS_Q, ALU.mult,
                                       ALU.add))
        VE.tag(nc.vector.tensor_scalar(stt[0:1, 56:59], stt[0:1, 53:56], 0.5,
                                       None, ALU.mult))
        e_fin = VE.tag(nc.vector.tensor_tensor(
            stt[0:1, 59:60], stt[0:1, 54:55], stt[0:1, 55:56], ALU.mult))
        self.e_bank2 = VE.evt()
        wt(nc.gpsimd, e_fin)
        for slot_i, col in [(SLOT_SUP, 53), (SLOT_SGD, 59), (SLOT_THR_UP, 56),
                            (SLOT_THR_GATE, 57), (SLOT_THR_DOWN, 58)]:
            self.e_srow_sc = self.dma(
                nc.gpsimd, "srow_sc",
                bass.AP(self.srow.tensor, SROW_SC + slot_i, [[1, 1], [1, 1]]),
                stt[0:1, col:col + 1])

    def phase3(self):
        nc = self.nc
        TE, VE, AE, GE = self.TE, self.VE, self.AE, self.GE
        self.barrier()
        stt = self.stats
        e_nc = self.dma(nc.sync, "ncol", self.cols[:, 0:1],
                        bass.AP(self.srow.tensor, SROW_SC + SLOT_NEGLAM,
                                [[0, 128], [1, 1]]))
        AR = [self.hfm_hi, self.hfm_lo]

        def hv(par, c0, w):
            return AR[par][:, c0:c0 + w]

        pfp = [self.wview(F32, 2048, 0), self.wview(F32, 2048, 16384)]
        phi = [self.wview(BF16, 2048, 8192), self.wview(BF16, 2048, 40960)]
        plo = [self.wview(BF16, 2048, 12288), self.wview(BF16, 2048, 45056)]
        afp = self.wview(F32, 4096, 24576)
        abuf_lo = self.wview(BF16, 4096, 49152)
        Dv = self.arenaD.bitcast(F32)
        ct1 = Dv[:, 4864:5120]

        def load_head(h):
            par = h % 2
            evs = []
            for i, src in enumerate([self.sq_hi, self.sq_lo,
                                     self.sk_hi, self.sk_lo]):
                for br in range(2):
                    m = 2 * h + br
                    evs.append(self.dma(
                        nc.sync, f"qk{par}",
                        hv(par, (2 * i + br) * 2048, 2048),
                        src[m, :, :]))
            for i, src in enumerate([self.sv_hi, self.sv_lo]):
                evs.append(self.dma(
                    nc.sync, f"qk{par}",
                    hv(par, 16384 + i * 4096, 4096),
                    bass.AP(src.tensor, h * NCH * 128 * 256,
                            [[256, 128], [128 * 256, 16], [1, 256]])))
            return evs[-1]

        e_ld = [None, None]
        e_ld[0] = load_head(0)
        e_arena_free = [None, None]
        e_bsc = [None, None, None, None]
        e_pst_ev = [None, None]
        e_o_read = None
        e_spill_a = None
        e_pfp_free = [None, None]
        self.e_phi_free = [None, None]
        e_plo = None
        e_phi = None
        for h in range(NH):
            par = h % 2
            if h + 1 < NH:
                wt(nc.sync, e_arena_free[1 - par])
                e_ld[1 - par] = load_head(h + 1)
            q_of = [0, 2048]
            k_of = [8192, 10240]
            ptb_hi = hv(par, 24576, 2048)
            ptb_lo = hv(par, 26624, 2048)
            abuf_hi = hv(par, 28672, 4096)
            rr = 0
            for qc in range(NCH):
                nfull = qc // 4
                rem = qc % 4
                remw = (rem + 1) * 128
                kact = (qc + 1) * 128
                nkc = qc + 1
                e_exp = [None, None]
                for bi in range(2):
                    qhi = hv(par, q_of[bi], 2048)[:, qc * 128:(qc + 1) * 128]
                    qlo = hv(par, q_of[bi] + 4096, 2048)[:, qc * 128:
                                                         (qc + 1) * 128]
                    khi = hv(par, k_of[bi], 2048)
                    klo = hv(par, k_of[bi] + 4096, 2048)
                    wt(nc.tensor, e_ld[par])
                    for kt in range(nfull + 1):
                        w = 512 if kt < nfull else remw
                        bidx = rr % 4
                        rr += 1
                        bank = self.psb[bidx]
                        wt(nc.tensor, e_bsc[bidx])
                        c0 = kt * 512
                        nc.tensor.matmul(bank[:, 0:w], qhi,
                                         khi[:, c0:c0 + w],
                                         start=True, stop=False)
                        nc.tensor.matmul(bank[:, 0:w], qhi,
                                         klo[:, c0:c0 + w],
                                         start=False, stop=False)
                        ins = nc.tensor.matmul(bank[:, 0:w], qlo,
                                               khi[:, c0:c0 + w],
                                               start=False, stop=True)
                        e_mm = TE.tag(ins)
                        wt(nc.vector, e_mm, e_pfp_free[bi])
                        e_bsc[bidx] = VE.tag(nc.vector.tensor_copy(
                            pfp[bi][:, c0:c0 + w], bank[:, 0:w]))
                e_pv2 = None
                for bi in range(2):
                    sf = pfp[bi]
                    wt(nc.vector, self.e_consts)
                    VE.tag(nc.vector.tensor_tensor(
                        sf[:, kact - 128:kact], sf[:, kact - 128:kact],
                        self.mask[:, :], ALU.add))
                    e_mx = VE.tag(nc.vector.tensor_reduce(
                        stt[:, 5:6], sf[:, 0:kact], axis=AX.X,
                        op=ALU.max, negate=True))
                    wt(nc.scalar, e_mx)
                    e_ex = AE.tag(nc.scalar.activation(
                        sf[:, 0:kact], sf[:, 0:kact], AF.Exp,
                        bias=stt[:, 5:6],
                        accum_out=stt[:, 8 + bi:9 + bi]))
                    wt(nc.vector, e_ex)
                    e_zr = VE.tag(nc.vector.reciprocal(
                        stt[:, 14 + bi:15 + bi], stt[:, 8 + bi:9 + bi]))
                    wt(nc.vector, self.e_phi_free[bi])
                    e_phi = VE.tag(nc.vector.tensor_copy(phi[bi][:, 0:kact],
                                                         sf[:, 0:kact]))
                    wt(nc.gpsimd, e_phi)
                    e_plo = GE.tag(nc.gpsimd.tensor_sub(
                        plo[bi][:, 0:kact], sf[:, 0:kact], phi[bi][:, 0:kact]))
                    e_pfp_free[bi] = e_plo
                    for src, dstv, e_src in ((phi[bi], ptb_hi, e_phi),
                                             (plo[bi], ptb_lo, e_plo)):
                        for g in range(0, nkc, 8):
                            gb = min(8, nkc - g)
                            tb = (g // 8) % 2
                            wt(nc.tensor, e_pst_ev[tb], e_src, e_spill_a)
                            for jj in range(gb):
                                kc = g + jj
                                ins = nc.tensor.transpose(
                                    self.pst[tb][:, jj * 128:(jj + 1) * 128],
                                    src[:, kc * 128:(kc + 1) * 128],
                                    self.ident[:, :])
                            e_t = TE.tag(ins)
                            wt(nc.vector, e_t)
                            outap = bass.AP(
                                dstv.tensor, dstv.offset + g * 128,
                                [[32768, 128], [1, gb * 128]])
                            inap = bass.AP(self.pst[tb], 0,
                                           [[1024, 128], [1, gb * 128]])
                            e_pst_ev[tb] = VE.tag(
                                nc.vector.tensor_copy(outap, inap))
                    self.e_phi_free[bi] = e_pst_ev[(nkc - 1) // 8 % 2]
                    ops = self.ps6 if bi == 0 else self.ps7
                    wt(nc.tensor, e_pst_ev[0], e_pst_ev[1], e_o_read)
                    ins = None
                    for kc in range(nkc):
                        vh = hv(par, 16384, 4096)[:, kc * 256:(kc + 1) * 256]
                        vl = hv(par, 20480, 4096)[:, kc * 256:(kc + 1) * 256]
                        ph_ = ptb_hi[:, kc * 128:(kc + 1) * 128]
                        pl_ = ptb_lo[:, kc * 128:(kc + 1) * 128]
                        nc.tensor.matmul(ops[:, 0:256], ph_, vh,
                                         start=(kc == 0), stop=False)
                        nc.tensor.matmul(ops[:, 0:256], ph_, vl,
                                         start=False, stop=False)
                        ins = nc.tensor.matmul(ops[:, 0:256], pl_, vh,
                                               start=False,
                                               stop=(kc == nkc - 1))
                    e_pv = TE.tag(ins)
                    if bi == 1:
                        e_pv2 = e_pv
                wt(nc.vector, e_pv2, e_nc)
                e_nlz = VE.tag(nc.vector.tensor_tensor(
                    stt[:, 16:17], self.cols[:, 0:1], stt[:, 15:16],
                    ALU.mult))
                VE.tag(nc.vector.tensor_scalar(
                    ct1[:, :], self.ps6[:, 0:256], stt[:, 14:15], None,
                    ALU.mult))
                e_comb = VE.tag(nc.vector.scalar_tensor_tensor(
                    afp[:, qc * 256:(qc + 1) * 256], self.ps7[:, 0:256],
                    stt[:, 16:17], ct1[:, :], ALU.mult, ALU.add))
                e_o_read = e_comb
                wt(nc.scalar, e_comb)
                AE.tag(nc.scalar.activation(
                    ct1[:, :], afp[:, qc * 256:(qc + 1) * 256],
                    AF.Square, accum_out=stt[:, 160 + qc:161 + qc]))
            e_ss = self.AE.evt()
            wt(nc.vector, e_ss)
            VE.tag(nc.vector.tensor_scalar(stt[:, 176:192],
                                           stt[:, 160:176], 1.0 / 256,
                                           EPS_LN, ALU.mult, ALU.add))
            e_u = VE.tag(nc.vector.tensor_copy(stt[:, 144:160],
                                               stt[:, 176:192]))
            wt(nc.scalar, e_u)
            e_sq = AE.tag(nc.scalar.activation(stt[:, 176:192],
                                               stt[:, 144:160], AF.Sqrt))
            wt(nc.vector, e_sq)
            VE.tag(nc.vector.reciprocal(stt[:, 176:192], stt[:, 176:192]))
            VE.tag(nc.vector.tensor_tensor(stt[:, 128:144], stt[:, 176:192],
                                           stt[:, 176:192], ALU.mult))
            VE.tag(nc.vector.tensor_tensor(stt[:, 128:144], stt[:, 128:144],
                                           stt[:, 144:160], ALU.mult))
            VE.tag(nc.vector.tensor_scalar(stt[:, 128:144], stt[:, 128:144],
                                           -0.5, 1.5, ALU.mult, ALU.add))
            e_rst = VE.tag(nc.vector.tensor_tensor(
                stt[:, 176:192], stt[:, 176:192], stt[:, 128:144], ALU.mult))
            e_ab = [None, None]
            for qc in range(NCH):
                VE.tag(nc.vector.tensor_scalar(
                    pfp[0][:, 0:256], afp[:, qc * 256:(qc + 1) * 256],
                    stt[:, 176 + qc:177 + qc], None, ALU.mult))
                wt(nc.vector, self.e_sw2, e_ab[0], e_ab[1])
                e_af = VE.tag(nc.vector.tensor_tensor(
                    pfp[0][:, 1024:1280], pfp[0][:, 0:256], self.sw2[:, :],
                    ALU.mult))
                wt(nc.scalar, e_af)
                e_h_ = AE.tag(nc.scalar.copy(
                    abuf_hi[:, qc * 256:(qc + 1) * 256],
                    pfp[0][:, 1024:1280]))
                wt(nc.gpsimd, e_h_)
                e_l_ = GE.tag(nc.gpsimd.tensor_sub(
                    abuf_lo[:, qc * 256:(qc + 1) * 256],
                    pfp[0][:, 1024:1280],
                    abuf_hi[:, qc * 256:(qc + 1) * 256]))
                e_ab = [e_h_, e_l_]
                e_pfp_free[0] = e_l_
            for srcv, dstv, sh in ((abuf_hi, ptb_hi, self.sa_hi),
                                   (abuf_lo, ptb_lo, self.sa_lo)):
                e_src = e_ab[0] if srcv is abuf_hi else e_ab[1]
                for eb in range(2):
                    for g in range(0, NCH, 8):
                        tb = (g // 8) % 2
                        wt(nc.tensor, e_pst_ev[tb], e_src, e_spill_a)
                        for jj in range(8):
                            qc = g + jj
                            ins = nc.tensor.transpose(
                                self.pst[tb][:, jj * 128:(jj + 1) * 128],
                                srcv[:, qc * 256 + eb * 128:
                                     qc * 256 + eb * 128 + 128],
                                self.ident[:, :])
                        e_t = TE.tag(ins)
                        wt(nc.vector, e_t)
                        outap = bass.AP(dstv.tensor,
                                        dstv.offset + g * 128,
                                        [[32768, 128], [1, 1024]])
                        inap = bass.AP(self.pst[tb], 0,
                                       [[1024, 128], [1, 1024]])
                        e_pst_ev[tb] = VE.tag(
                            nc.vector.tensor_copy(outap, inap))
                    wt(nc.scalar, e_pst_ev[0], e_pst_ev[1])
                    e_spill_a = self.dma(
                        nc.scalar, "spa",
                        bass.AP(sh.tensor, (2 * h + eb) * 128 * 128,
                                [[128, 128], [NCH * 128 * 128, 16],
                                 [1, 128]]),
                        bass.AP(dstv.tensor, dstv.offset,
                                [[32768, 128], [128, 16], [1, 128]]))
            e_arena_free[par] = TE.evt()
        self.e_attn_done = e_spill_a
    def phase4(self):
        nc = self.nc
        TE, VE, AE, GE = self.TE, self.VE, self.AE, self.GE
        self.barrier()
        stt = self.stats

        e_g = self.dma(nc.sync, "bc0", self.bc0[:, :],
                       bass.AP(self.srow.tensor, SROW_MOD + 2 * E,
                               [[0, 128], [1, E]]))
        t1 = self.wview(F32, 2048, 24576)
        t2 = self.wview(F32, 2048, 32768)
        wt(nc.vector, e_g)
        VE.tag(nc.vector.tensor_scalar(t1[:, :], self.bc0[:, :], 0.25, None,
                                       ALU.max))
        VE.tag(nc.vector.tensor_scalar(t2[:, :], self.bc0[:, :], -0.25, None,
                                       ALU.min))
        VE.tag(nc.vector.tensor_scalar(self.bc0[:, :], self.bc0[:, :], 0.0,
                                       None, ALU.is_ge))
        VE.tag(nc.vector.tensor_tensor(t1[:, :], t1[:, :], t2[:, :],
                                       ALU.subtract))
        VE.tag(nc.vector.tensor_tensor(t1[:, :], self.bc0[:, :], t1[:, :],
                                       ALU.mult))
        e_ga = VE.tag(nc.vector.tensor_tensor(self.bc0[:, :], t2[:, :],
                                              t1[:, :], ALU.add))

        woc = self.wview(F32, 2048, 16384)
        e_wsp = [None, None]
        for wc in range(NCH):
            wt(nc.sync, e_wsp[0], e_wsp[1])
            e_ld = self.dma(nc.sync, "woc", woc[:, :],
                            self.wo[wc * 128:(wc + 1) * 128, :])
            wt(nc.scalar, e_ld)
            e_hi = AE.tag(nc.scalar.copy(
                self.hfm_hi[:, wc * 2048:(wc + 1) * 2048], woc[:, :]))
            wt(nc.gpsimd, e_hi)
            e_lo = GE.tag(nc.gpsimd.tensor_sub(
                self.hfm_lo[:, wc * 2048:(wc + 1) * 2048], woc[:, :],
                self.hfm_hi[:, wc * 2048:(wc + 1) * 2048]))
            e_wsp = [e_hi, e_lo]

        ab_hi = [self.wview(BF16, 2048, 0), self.wview(BF16, 2048, 8192)]
        ab_lo = [self.wview(BF16, 2048, 4096), self.wview(BF16, 2048, 12288)]
        xb = [self.wview(F32, 2048, 24576), self.wview(F32, 2048, 32768)]
        x1b = [self.wview(F32, 2048, 40960), self.wview(F32, 2048, 49152)]
        ot = self.wview(F32, 512, 16384)
        e_a_used = [None, None]
        e_x_used = [None, None]
        e_bank = [None, None]
        e_x1_done = [None, None]
        e_sx = [None, None]
        e_sq_prev = None
        for sc in range(NCH):
            par = sc % 2
            wt(nc.sync, e_a_used[par])
            self.dma(nc.sync, f"pa{par}",
                     bass.AP(ab_hi[par].tensor, ab_hi[par].offset,
                             [[28672, 128], [128, 16], [1, 128]]),
                     bass.AP(self.sa_hi.tensor, sc * 16 * 128 * 128,
                             [[128, 128], [128 * 128, 16], [1, 128]]))
            e_a = self.dma(nc.sync, f"pa{par}",
                           bass.AP(ab_lo[par].tensor, ab_lo[par].offset,
                                   [[28672, 128], [128, 16], [1, 128]]),
                           bass.AP(self.sa_lo.tensor, sc * 16 * 128 * 128,
                                   [[128, 128], [128 * 128, 16], [1, 128]]))
            wt(nc.sync, e_x_used[par], e_ga)
            e_x = self.dma(nc.sync, f"px{par}", xb[par][:, :],
                           self.xin[sc * 128:(sc + 1) * 128, :])
            wt(nc.vector, e_x1_done[par])
            for ft in range(4):
                bk = ft % 2
                wt(nc.tensor, e_a, e_wsp[0], e_wsp[1], e_bank[bk])
                ins = None
                for af in range(16):
                    lh = ab_hi[par][:, af * 128:(af + 1) * 128]
                    rh = self.hfm_hi[:, af * 2048 + ft * 512:
                                     af * 2048 + ft * 512 + 512]
                    rl = self.hfm_lo[:, af * 2048 + ft * 512:
                                     af * 2048 + ft * 512 + 512]
                    nc.tensor.matmul(self.psb[bk][:, :], lh, rh,
                                     start=(af == 0), stop=False)
                    nc.tensor.matmul(self.psb[bk][:, :], lh, rl,
                                     start=False, stop=False)
                for af in range(16):
                    ll = ab_lo[par][:, af * 128:(af + 1) * 128]
                    rh = self.hfm_hi[:, af * 2048 + ft * 512:
                                     af * 2048 + ft * 512 + 512]
                    ins = nc.tensor.matmul(self.psb[bk][:, :], ll, rh,
                                           start=False, stop=(af == 15))
                e_mm = TE.tag(ins)
                wt(nc.vector, e_mm, e_ga, e_x)
                if ft == 0:
                    wt(nc.vector, e_sx[par], e_sq_prev)
                VE.tag(nc.vector.tensor_tensor(
                    ot[:, :], self.psb[bk][:, :],
                    self.bc0[:, ft * 512:(ft + 1) * 512], ALU.mult))
                e_ev = VE.tag(nc.vector.tensor_tensor(
                    x1b[par][:, ft * 512:(ft + 1) * 512], ot[:, :],
                    xb[par][:, ft * 512:(ft + 1) * 512], ALU.add))
                e_bank[bk] = e_ev
            e_a_used[par] = TE.evt()
            e_x_used[par] = e_ev
            e_r = self.emit_stats(x1b[par][:, :], stt[:, 18 + sc:19 + sc],
                                  stt[:, 34 + sc:35 + sc], [e_ev], par=par)
            e_x1_done[par] = e_r
            e_sq_prev = self.AE.evt()
            wt(nc.scalar, e_ev)
            e_sx[par] = self.dma(nc.scalar, f"sx{par}", self.sx1[sc, :, :],
                                 x1b[par][:, :])
        self.e_p4 = VE.evt()

    def phase5(self):
        nc = self.nc
        VE, AE, GE = self.VE, self.AE, self.GE
        self.barrier()
        self.pst_guard = [None, None]
        stt = self.stats
        e_b0 = self.dma(nc.sync, "bc0", self.bc0[:, :],
                        bass.AP(self.srow.tensor, SROW_MOD + 4 * E,
                                [[0, 128], [1, E]]))
        e_b1 = self.dma(nc.sync, "bc1", self.bc1[:, :],
                        bass.AP(self.srow.tensor, SROW_MOD + 3 * E,
                                [[0, 128], [1, E]]))
        wt(nc.vector, e_b0)
        e_bc = VE.tag(nc.vector.tensor_scalar(
            self.bc0[:, :], self.bc0[:, :], 1.0, None, ALU.add))
        xb = [self.wview(F32, 2048, 0), self.wview(F32, 2048, 8192)]
        zb = self.wview(F32, 2048, 24576)
        hfp = self.wview(F32, 2048, 32768)
        hhi = self.wview(BF16, 2048, 40960)
        hlo = self.wview(BF16, 2048, 45056)
        e_used = [None, None]
        e_hi_prev = e_lo_prev = None
        e_te_hi = e_te_lo = None
        for sc in range(NCH):
            par = sc % 2
            wt(nc.sync, e_used[par])
            e_x = self.dma(nc.sync, f"x{par}", xb[par][:, :],
                           self.sx1[sc, :, :])
            wt(nc.vector, e_x)
            e_z = VE.tag(nc.vector.tensor_scalar(
                zb[:, :], xb[par][:, :], stt[:, 18 + sc:19 + sc],
                stt[:, 34 + sc:35 + sc], ALU.subtract, ALU.mult))
            e_used[par] = e_z
            wt(nc.vector, e_bc, e_b1, e_hi_prev, e_lo_prev)
            VE.tag(nc.vector.tensor_tensor(hfp[:, :], zb[:, :],
                                           self.bc0[:, :], ALU.mult))
            e_h = VE.tag(nc.vector.tensor_tensor(hfp[:, :], hfp[:, :],
                                                 self.bc1[:, :], ALU.add))
            wt(nc.scalar, e_h, e_te_hi)
            e_hi = AE.tag(nc.scalar.copy(hhi[:, :], hfp[:, :]))
            wt(nc.gpsimd, e_hi, e_te_lo)
            e_lo = GE.tag(nc.gpsimd.tensor_sub(hlo[:, :], hfp[:, :],
                                               hhi[:, :]))
            e_te_hi, _ = self.emit_fm_store(hhi, self.hfm_hi, sc, e_hi)
            e_te_lo, _ = self.emit_fm_store(hlo, self.hfm_lo, sc, e_lo)
            e_hi_prev, e_lo_prev = e_hi, e_lo
        if self.want("h2fm"):
            t1 = self.dbg_dram("h2fm_hi", [128, 32768])
            t2 = self.dbg_dram("h2fm_lo", [128, 32768])
            wt(nc.gpsimd, VE.evt())
            self.dma(nc.gpsimd, "dbg", t1[:, :], self.hfm_hi[:, :])
            self.dma(nc.gpsimd, "dbg", t2[:, :], self.hfm_lo[:, :])

    def phase6(self):
        nc = self.nc
        TE, VE, AE, GE = self.TE, self.VE, self.AE, self.GE
        self.barrier()
        e_c1 = self.dma(nc.sync, "ncol", self.cols[:, 1:2],
                        bass.AP(self.srow.tensor, SROW_SC + SLOT_THR_UP,
                                [[0, 128], [1, 1]]))
        e_c2 = self.dma(nc.sync, "ncol", self.cols[:, 2:3],
                        bass.AP(self.srow.tensor, SROW_SC + SLOT_THR_GATE,
                                [[0, 128], [1, 1]]))
        e_c3 = self.dma(nc.sync, "ncol", self.cols[:, 3:4],
                        bass.AP(self.srow.tensor, SROW_SC + SLOT_SUP,
                                [[0, 128], [1, 1]]))
        wt(nc.vector, e_c3)
        e_neg = VE.tag(nc.vector.tensor_scalar(
            self.cols[:, 4:6], self.cols[:, 1:3], -1.0, None, ALU.mult))

        wus = [self.wview(F32, 2048, 0), self.wview(F32, 2048, 8192)]
        wgs = [self.wview(F32, 2048, 16384), self.wview(F32, 2048, 24576)]
        tu = [self.wview(BF16, 2048, 32768), self.wview(BF16, 2048, 36864)]
        tg = [self.wview(BF16, 2048, 40960), self.wview(BF16, 2048, 45056)]
        m2 = self.wview(BF16, 2048, 49152)
        silu = self.wview(F32, 512, 53248)
        hfp = self.wview(F32, 512, 55296)
        Db = self.arenaD.bitcast(BF16)
        hh = Db[:, 9728:10240]
        hl = Db[:, 10240:10752]

        def p6_load(ec, par):
            e1 = self.dma(nc.sync, f"wus{par}",
                          self.wap3(F32, par * 8192, [[128, 16], [1, 128]]),
                          bass.AP(self.wupT.tensor, ec * 128,
                                  [[DEX, 128], [DEX * 128, 16], [1, 128]]))
            e2 = self.dma(nc.sync, f"wgs{par}",
                          self.wap3(F32, 16384 + par * 8192,
                                    [[128, 16], [1, 128]]),
                          bass.AP(self.wgateT.tensor, ec * 128,
                                  [[DEX, 128], [DEX * 128, 16], [1, 128]]))
            return (e1, e2)

        def p6_quant(par, e_ld, e_teq):
            wt(nc.vector, e_ld[0], e_ld[1], e_neg, e_teq)
            VE.tag(nc.vector.tensor_scalar(
                m2[:, :], wus[par][:, :], self.cols[:, 4:5], None, ALU.is_le))
            VE.tag(nc.vector.scalar_tensor_tensor(
                tu[par][:, :], wus[par][:, :], self.cols[:, 1:2], m2[:, :],
                ALU.is_ge, ALU.subtract))
            VE.tag(nc.vector.tensor_scalar(
                m2[:, :], wgs[par][:, :], self.cols[:, 5:6], None, ALU.is_le))
            return VE.tag(nc.vector.scalar_tensor_tensor(
                tg[par][:, :], wgs[par][:, :], self.cols[:, 2:3], m2[:, :],
                ALU.is_ge, ALU.subtract))

        e_q = [None, None]
        e_tq = [None, None]
        e_hid = None
        e_spill = None
        e_hf_prev = None
        e_bank = [None, None, None, None]
        e_tq[0] = p6_quant(0, p6_load(0, 0), None)
        for ec in range(ECH):
            par = ec % 2
            if ec + 1 < ECH:
                np_ = 1 - par
                wt(nc.sync, e_tq[np_])
                e_ldn = p6_load(ec + 1, np_)
                e_tq[np_] = p6_quant(np_, e_ldn, e_q[np_])
            for st in range(4):
                bu = st % 2
                bg = 2 + st % 2
                wt(nc.tensor, e_tq[par], e_bank[bu], e_bank[bg])
                ins = None
                for d in range(16):
                    lt_ = tu[par][:, d * 128:(d + 1) * 128]
                    nc.tensor.matmul(
                        self.psb[bu][:, :], lt_,
                        self.hfm_hi[:, d * 2048 + st * 512:
                                    d * 2048 + st * 512 + 512],
                        start=(d == 0), stop=False)
                    ins = nc.tensor.matmul(
                        self.psb[bu][:, :], lt_,
                        self.hfm_lo[:, d * 2048 + st * 512:
                                    d * 2048 + st * 512 + 512],
                        start=False, stop=(d == 15))
                e_mu = TE.tag(ins)
                for d in range(16):
                    lt_ = tg[par][:, d * 128:(d + 1) * 128]
                    nc.tensor.matmul(
                        self.psb[bg][:, :], lt_,
                        self.hfm_hi[:, d * 2048 + st * 512:
                                    d * 2048 + st * 512 + 512],
                        start=(d == 0), stop=False)
                    ins = nc.tensor.matmul(
                        self.psb[bg][:, :], lt_,
                        self.hfm_lo[:, d * 2048 + st * 512:
                                    d * 2048 + st * 512 + 512],
                        start=False, stop=(d == 15))
                e_mg = TE.tag(ins)
                wt(nc.scalar, e_mu, e_hid, e_c3, e_hf_prev)
                e_si = AE.tag(nc.scalar.activation(
                    silu[:, :], self.psb[bu][:, :], AF.Silu,
                    scale=self.cols[:, 3:4]))
                e_bank[bu] = e_si
                wt(nc.vector, e_si, e_mg, e_hid)
                e_hf = VE.tag(nc.vector.tensor_tensor(
                    hfp[:, :], silu[:, :], self.psb[bg][:, :], ALU.mult))
                e_bank[bg] = e_hf
                e_hf_prev = e_hf
                wt(nc.scalar, e_hf, e_spill)
                e_hh = AE.tag(nc.scalar.copy(hh[:, :], hfp[:, :]))
                wt(nc.vector, e_hh)
                e_hid = VE.tag(nc.vector.tensor_sub(hl[:, :], hfp[:, :],
                                                    hh[:, :]))
                wt(nc.scalar, e_hid)
                self.dma(nc.scalar, "sph",
                         self.shid_hi[ec, :, st * 512:(st + 1) * 512],
                         hh[:, :])
                e_spill = self.dma(nc.scalar, "sph",
                                   self.shid_lo[ec, :,
                                                st * 512:(st + 1) * 512],
                                   hl[:, :])
            e_q[par] = TE.evt()
        self.e_p6 = e_spill

    def phase7(self):
        nc = self.nc
        TE, VE, AE, GE = self.TE, self.VE, self.AE, self.GE
        self.barrier()
        yA = self.arenaA.bitcast(F32)
        hb = self.arenaB.bitcast(BF16)
        e_g = self.dma(nc.sync, "bc0", self.bc0[:, :],
                       bass.AP(self.srow.tensor, SROW_MOD + 5 * E,
                               [[0, 128], [1, E]]))
        self.dma(nc.sync, "ncol", self.cols[:, 7:8],
                 bass.AP(self.srow.tensor, SROW_SC + SLOT_SGD,
                         [[0, 128], [1, 1]]))
        e_c4 = self.dma(nc.sync, "ncol", self.cols[:, 8:9],
                        bass.AP(self.srow.tensor, SROW_SC + SLOT_THR_DOWN,
                                [[0, 128], [1, 1]]))
        t1 = self.wview(F32, 2048, 0)
        t2 = self.wview(F32, 2048, 8192)
        wt(nc.vector, e_g, e_c4)
        VE.tag(nc.vector.tensor_scalar(t1[:, :], self.bc0[:, :], 0.25, None,
                                       ALU.max))
        VE.tag(nc.vector.tensor_scalar(t2[:, :], self.bc0[:, :], -0.25, None,
                                       ALU.min))
        VE.tag(nc.vector.tensor_scalar(self.bc0[:, :], self.bc0[:, :], 0.0,
                                       None, ALU.is_ge))
        VE.tag(nc.vector.tensor_tensor(t1[:, :], t1[:, :], t2[:, :],
                                       ALU.subtract))
        VE.tag(nc.vector.tensor_tensor(t1[:, :], self.bc0[:, :], t1[:, :],
                                       ALU.mult))
        VE.tag(nc.vector.tensor_tensor(self.bc0[:, :], t2[:, :], t1[:, :],
                                       ALU.add))
        VE.tag(nc.vector.tensor_scalar(self.cols[:, 9:10], self.cols[:, 8:9],
                                       -1.0, None, ALU.mult))
        e_ga = VE.tag(nc.vector.tensor_scalar(
            self.bc0[:, :], self.bc0[:, :], self.cols[:, 7:8], None,
            ALU.mult))

        tdc = self.wview(F32, 2048, 0)
        td = [[self.wview(BF16, 2048, 8192 + (p * 4 + i) * 4096)
               for i in range(4)] for p in range(2)]
        m2 = self.wview(BF16, 2048, 40960)
        ocb = self.wview(F32, 1024, 45056)
        x1h = [self.wview(F32, 1024, 49152), self.wview(F32, 1024, 53248)]
        NBLK = 16

        def p7_hid_load(half, blk, par):
            e = None
            for el in range(4):
                self.dma(nc.sync, f"hid{par}",
                         hb[:, par * 16384 + el * 2048:
                            par * 16384 + (el + 1) * 2048],
                         self.shid_hi[blk * 4 + el, :, :])
                e = self.dma(nc.sync, f"hid{par}",
                             hb[:, par * 16384 + 8192 + el * 2048:
                                par * 16384 + 8192 + (el + 1) * 2048],
                             self.shid_lo[blk * 4 + el, :, :])
            return e

        self._p7_tdq = None

        def p7_td_quant(blk, par, e_teq):
            e_tq = self._p7_tdq
            for el in range(4):
                wt(nc.sync, e_tq, e_teq if el == 0 else None, e_ga)
                e_lt = self.dma(nc.sync, "tdc", tdc[:, :],
                                self.wdownT[(blk * 4 + el) * 128:
                                            (blk * 4 + el + 1) * 128, :])
                wt(nc.vector, e_lt, e_ga)
                VE.tag(nc.vector.tensor_scalar(
                    m2[:, :], tdc[:, :], self.cols[:, 9:10], None,
                    ALU.is_le))
                e_tq = VE.tag(nc.vector.scalar_tensor_tensor(
                    td[par][el][:, :], tdc[:, :], self.cols[:, 8:9],
                    m2[:, :], ALU.is_ge, ALU.subtract))
            self._p7_tdq = e_tq
            return e_tq

        e_bank = [None, None, None, None]
        e_teB = [None, None]
        e_out = None
        e_y_rd = None
        e_oc = None
        for half in range(2):
            e_ms = VE.tag(nc.vector.memset(yA[:, :], 0.0))
            e_hb = [None, None]
            e_tq = [None, None]
            e_hb[0] = p7_hid_load(half, 0, 0)
            e_tq[0] = p7_td_quant(0, 0, e_teB[0])
            for blk in range(NBLK):
                par = blk % 2
                if blk + 1 < NBLK:
                    np_ = 1 - par
                    wt(nc.sync, e_teB[np_])
                    e_hb[np_] = p7_hid_load(half, blk + 1, np_)
                    e_tq[np_] = p7_td_quant(blk + 1, np_, e_teB[np_])
                for sc in range(NCH):
                    for dt_ in range(2):
                        bk = (sc * 2 + dt_) % 4
                        wt(nc.tensor, e_hb[par], e_tq[par], e_bank[bk])
                        ins = None
                        for el in range(4):
                            lh = hb[:, par * 16384 + el * 2048 + sc * 128:
                                    par * 16384 + el * 2048 + sc * 128 + 128]
                            ll = hb[:, par * 16384 + 8192 + el * 2048 +
                                    sc * 128:
                                    par * 16384 + 8192 + el * 2048 +
                                    sc * 128 + 128]
                            rt = td[par][el][:, half * 1024 + dt_ * 512:
                                             half * 1024 + dt_ * 512 + 512]
                            nc.tensor.matmul(self.psb[bk][:, :], lh, rt,
                                             start=(el == 0), stop=False)
                            ins = nc.tensor.matmul(self.psb[bk][:, :], ll,
                                                   rt, start=False,
                                                   stop=(el == 3))
                        e_mm = TE.tag(ins)
                        wt(nc.vector, e_mm, e_y_rd, e_ms)
                        e_acc = VE.tag(nc.vector.tensor_tensor(
                            yA[:, sc * 1024 + dt_ * 512:
                               sc * 1024 + dt_ * 512 + 512],
                            yA[:, sc * 1024 + dt_ * 512:
                               sc * 1024 + dt_ * 512 + 512],
                            self.psb[bk][:, :], ALU.add))
                        e_bank[bk] = e_acc
                        last_ev = e_acc
                e_teB[par] = TE.evt()
            e_x1u = [None, None]
            for sc in range(NCH):
                par = sc % 2
                wt(nc.sync, e_x1u[par])
                e_x1 = self.dma(nc.sync, f"x1h{par}", x1h[par][:, :],
                                bass.AP(self.sx1.tensor,
                                        sc * 128 * E + half * 1024,
                                        [[E, 128], [1, 1024]]))
                wt(nc.vector, e_x1, last_ev, e_out)
                VE.tag(nc.vector.tensor_tensor(
                    ocb[:, :],
                    yA[:, sc * 1024:sc * 1024 + 1024],
                    self.bc0[:, half * 1024:(half + 1) * 1024], ALU.mult))
                e_oc = VE.tag(nc.vector.tensor_tensor(
                    ocb[:, :], ocb[:, :], x1h[par][:, :], ALU.add))
                e_x1u[par] = e_oc
                wt(nc.scalar, e_oc)
                e_out = self.dma(
                    nc.scalar, "out",
                    bass.AP(self.out.tensor, sc * 128 * E + half * 1024,
                            [[E, 128], [1, 1024]]),
                    ocb[:, :])
            e_y_rd = e_oc

    def finish(self):
        nc = self.nc
        evs = [sl.evt() for sl in self._slots.values()]
        evs += [self.TE.evt(), self.VE.evt(), self.AE.evt()]
        wt(nc.gpsimd, *evs)


def build(dbg=(), upto=0):
    b = Builder(dbg=dbg, upto=upto)
    ncp = b.build()
    return b, ncp


from concourse.bass_utils import run_bass_kernel_spmd


_NC_CACHE = {}


def kernel(L0_attn_x, L0_attn_Wq, L0_attn_Wk, L0_attn_Wv, L0_attn_Wo,
           L0_attn_lambda_q1, L0_attn_lambda_k1, L0_attn_lambda_q2,
           L0_attn_lambda_k2, L0_attn_subln_weight, L0_ffn_W_up,
           L0_ffn_W_gate, L0_ffn_W_down, L0_c, L0_wmod):
    n = 8
    if "nc" not in _NC_CACHE:
        b = Builder(dbg=(), upto=7)
        _NC_CACHE["nc"] = b.build()
    ncp = _NC_CACHE["nc"]
    f32 = np.float32
    shared = {
        "wq": np.ascontiguousarray(L0_attn_Wq, dtype=f32),
        "wk": np.ascontiguousarray(L0_attn_Wk, dtype=f32),
        "wv": np.ascontiguousarray(L0_attn_Wv, dtype=f32),
        "wo": np.ascontiguousarray(L0_attn_Wo, dtype=f32),
        "wmod": np.ascontiguousarray(L0_wmod, dtype=f32),
        "wupT": np.ascontiguousarray(np.asarray(L0_ffn_W_up, dtype=f32).T),
        "wgateT": np.ascontiguousarray(np.asarray(L0_ffn_W_gate, dtype=f32).T),
        "wdownT": np.ascontiguousarray(np.asarray(L0_ffn_W_down, dtype=f32).T),
        "lq1": np.ascontiguousarray(L0_attn_lambda_q1, dtype=f32),
        "lk1": np.ascontiguousarray(L0_attn_lambda_k1, dtype=f32),
        "lq2": np.ascontiguousarray(L0_attn_lambda_q2, dtype=f32),
        "lk2": np.ascontiguousarray(L0_attn_lambda_k2, dtype=f32),
        "subw": np.ascontiguousarray(L0_attn_subln_weight, dtype=f32),
    }
    x = np.asarray(L0_attn_x, dtype=f32)
    c = np.asarray(L0_c, dtype=f32)
    in_maps = []
    for bb in range(n):
        m = dict(shared)
        m["xin"] = np.ascontiguousarray(x[bb])
        m["cvec"] = np.ascontiguousarray(c[bb])
        in_maps.append(m)
    res = run_bass_kernel_spmd(ncp, in_maps, core_ids=list(range(n)))
    return np.stack([np.asarray(r["out"]) for r in res.results], axis=0)


# version 07
# speedup vs baseline: 1.8440x; 1.1449x over previous
import contextlib
import numpy as np
import concourse.bass as bass
import concourse.mybir as mybir

F32 = mybir.dt.float32
BF16 = mybir.dt.bfloat16
U8 = mybir.dt.uint8
AF = mybir.ActivationFunctionType
ALU = mybir.AluOpType
AX = mybir.AxisListType

S = 2048
E = 2048
NCH = 16
NH = 8
HD = 128
DEX = 8192
ECH = 64
LAM_INIT = 0.8 - 0.6 * float(np.exp(-0.3 * 12))
SCALE = HD ** -0.5
EPS_LN = 1e-5
EPS_Q = 1e-8
MASKVAL = -1e30

SROW_MOD = 0
SROW_SC = 6 * E
SLOT_NEGLAM = 0
SLOT_SUP = 1
SLOT_SGD = 2
SLOT_THR_UP = 3
SLOT_THR_GATE = 4
SLOT_THR_DOWN = 5
NSLOT = 8


class Evt:
    __slots__ = ("sem", "v")

    def __init__(self, sem, v):
        self.sem = sem
        self.v = v


class ESem:

    def __init__(self, nc, es, name, eng, flush):
        self.sem = es.enter_context(nc.semaphore(name))
        self.n = 0
        self.eng = eng
        self.flush = flush

    def tag(self, ins):
        ins.then_inc(self.sem, 1)
        self.n += 1
        if self.flush:
            self.eng.wait_ge(self.sem, self.n)
        return Evt(self.sem, self.n)

    def bump(self):
        if self.n > 0:
            self.eng.wait_ge(self.sem, self.n)
        self.eng.nop().then_inc(self.sem, 1)
        self.n += 1
        return Evt(self.sem, self.n)

    def evt(self):
        return Evt(self.sem, self.n)


class Slot:

    def __init__(self, nc, es, name):
        self.sem = es.enter_context(nc.semaphore(name))
        self.n = 0

    def evt(self):
        return Evt(self.sem, self.n)


def wt(eng, *evts):
    for e in evts:
        if e is not None and e.v > 0:
            eng.wait_ge(e.sem, e.v)


class Builder:
    def __init__(self, dbg=(), upto=99):
        self.dbg_names = set(dbg)
        self.upto = upto
        nc = self.nc = bass.Bass("TRN2", target_bir_lowering=False)
        self.es = contextlib.ExitStack()

        def din(name, shape):
            return nc.dram_tensor(name, shape, F32, kind="ExternalInput").ap()

        self.xin = din("xin", [S, E])
        self.cvec = din("cvec", [E])
        self.wq = din("wq", [E, E])
        self.wk = din("wk", [E, E])
        self.wv = din("wv", [E, E])
        self.wo = din("wo", [E, E])
        self.wmod = din("wmod", [E, 6 * E])
        self.wupT = din("wupT", [E, DEX])
        self.wgateT = din("wgateT", [E, DEX])
        self.wdownT = din("wdownT", [DEX, E])
        self.lq1 = din("lq1", [HD])
        self.lk1 = din("lk1", [HD])
        self.lq2 = din("lq2", [HD])
        self.lk2 = din("lk2", [HD])
        self.subw = din("subw", [2 * HD])
        self.out = nc.dram_tensor("out", [S, E], F32, kind="ExternalOutput").ap()

        def scr(name, shape, dt=BF16):
            kind = "ExternalOutput" if ("spill" in self.dbg_names) else "Internal"
            return nc.dram_tensor(name, shape, dt, kind=kind).ap()

        self.srow = scr("srow", [SROW_SC + NSLOT], F32)
        self.sq_hi = scr("sq_hi", [NCH, 128, S])
        self.sq_lo = scr("sq_lo", [NCH, 128, S])
        self.sk_hi = scr("sk_hi", [NCH, 128, S])
        self.sk_lo = scr("sk_lo", [NCH, 128, S])
        self.sv_hi = scr("sv_hi", [NH, NCH, 128, 256])
        self.sv_lo = scr("sv_lo", [NH, NCH, 128, 256])
        self.sa_hi = scr("sa_hi", [NCH, NCH, 128, 128])
        self.sa_lo = scr("sa_lo", [NCH, NCH, 128, 128])
        self.shid_hi = scr("shid_hi", [ECH, 128, S])
        self.shid_lo = scr("shid_lo", [ECH, 128, S])
        self.sx1 = scr("sx1", [NCH, 128, E], F32)

        self.arenaA = nc.alloc_sbuf_tensor("arenaA", [128, 65536], U8)
        self.arenaB = nc.alloc_sbuf_tensor("arenaB", [128, 65536], U8)
        self.arenaW = nc.alloc_sbuf_tensor("arenaW", [128, 57344], U8)
        self.arenaD = nc.alloc_sbuf_tensor("arenaD", [128, 22528], U8)

        self.hfm_hi = self.arenaA.bitcast(BF16)
        self.hfm_lo = self.arenaB.bitcast(BF16)
        D = self.arenaD.bitcast(F32)
        Db = self.arenaD.bitcast(BF16)
        self.bc0 = D[:, 0:2048]
        self.bc1 = D[:, 2048:4096]
        self.sw2 = D[:, 4096:4352]
        self.mask = D[:, 4352:4480]
        self.ident = Db[:, 8960:9088]
        self.cols = D[:, 4544:4608]
        self.stats = D[:, 4608:4864]

        self.psb = [nc.alloc_psum_tensor(f"psb{i}", [128, 512], F32)
                    for i in range(4)]
        self.pst = [nc.alloc_psum_tensor(f"pst{i}", [128, 1024], BF16)
                    for i in range(2)]
        self.ps6 = nc.alloc_psum_tensor("ps6", [128, 512], F32)
        self.ps7 = nc.alloc_psum_tensor("ps7", [128, 512], F32)

        es = self.es
        self.TE = ESem(nc, es, "sTE", nc.tensor, flush=False)
        self.VE = ESem(nc, es, "sVE", nc.vector, flush=True)
        self.AE = ESem(nc, es, "sAE", nc.scalar, flush=True)
        self.GE = ESem(nc, es, "sGE", nc.gpsimd, flush=True)
        self.SE = ESem(nc, es, "sSE", nc.sync, flush=False)
        self.engines = [(nc.tensor, self.TE), (nc.vector, self.VE),
                        (nc.scalar, self.AE), (nc.gpsimd, self.GE),
                        (nc.sync, self.SE)]
        self._slots = {}
        self.dbg_tensors = {}

    def slot(self, name):
        if name not in self._slots:
            self._slots[name] = Slot(self.nc, self.es, "sl_" + name)
        return self._slots[name]

    def dma(self, eng, slotname, out, in_, **kw):
        sl = self.slot(slotname)
        eng.dma_start(out=out, in_=in_, **kw).then_inc(sl.sem, 16)
        sl.n += 16
        return Evt(sl.sem, sl.n)

    def barrier(self):
        for eng, es_ in self.engines:
            es_.bump()
        evs = [es_.evt() for _, es_ in self.engines]
        evs += [sl.evt() for sl in self._slots.values()]
        for eng, _ in self.engines:
            wt(eng, *evs)

    def want(self, name):
        return name in self.dbg_names

    def dbg_dram(self, name, shape, dt=F32):
        t = self.nc.dram_tensor("dbg_" + name, list(shape), dt,
                                kind="ExternalOutput").ap()
        self.dbg_tensors[name] = t
        return t

    def wview(self, dt, cols, off_bytes):
        a = self.arenaW.bitcast(dt)
        isz = {F32: 4, BF16: 2, U8: 1}[dt]
        assert off_bytes % isz == 0
        c0 = off_bytes // isz
        assert (c0 + cols) * isz <= 57344, (off_bytes, cols)
        return a[:, c0:c0 + cols]

    def wap3(self, dt, off_bytes, dims):
        h = self.arenaW.bitcast(dt)
        isz = {F32: 4, BF16: 2, U8: 1}[dt]
        width = 57344 // isz
        assert off_bytes % isz == 0
        return bass.AP(h, off_bytes // isz, [[width, 128]] + list(dims))

    def build(self):
        nc = self.nc
        with self.es, nc.Block():
            with nc.named_scope("p0mod"):
                self.phase0()
            if self.upto >= 1:
                with nc.named_scope("p1ln1"):
                    self.phase1()
            if self.upto >= 2:
                with nc.named_scope("p2qkv"):
                    self.phase2()
            if self.upto >= 3:
                with nc.named_scope("p3attn"):
                    self.phase3()
            if self.upto >= 4:
                with nc.named_scope("p4wo"):
                    self.phase4()
            if self.upto >= 5:
                with nc.named_scope("p5ln2"):
                    self.phase5()
            if self.upto >= 6:
                with nc.named_scope("p6upgate"):
                    self.phase6()
            if self.upto >= 7:
                with nc.named_scope("p7down"):
                    self.phase7()
            self.finish()
        return nc

    def phase0(self):
        nc = self.nc
        TE, VE, AE, GE = self.TE, self.VE, self.AE, self.GE

        cfm = self.wview(F32, 16, 0)
        lv = self.wview(F32, 512, 256)
        lt = self.wview(F32, 16, 2304)
        mrow = [self.wview(F32, 512, 3584), self.wview(F32, 512, 5632)]
        wmb = [self.wview(F32, 6144, 8192), self.wview(F32, 6144, 32768)]

        GE.tag(nc.gpsimd.memset(self.stats[:, :], 0.0))
        GE.tag(nc.gpsimd.memset(self.cols[:, :], 0.0))
        GE.tag(nc.gpsimd.memset(self.cols[:, 60:64], 1.0))
        GE.tag(nc.gpsimd.memset(self.ident[:, :], 0.0))
        GE.tag(nc.gpsimd.affine_select(
            out=self.ident[:, :], in_=self.ident[:, :],
            compare_op=ALU.not_equal, fill=1.0, base=0,
            pattern=[[-1, 128]], channel_multiplier=1))
        GE.tag(nc.gpsimd.memset(self.mask[:, :], 0.0))
        self.e_consts = GE.tag(nc.gpsimd.affine_select(
            out=self.mask[:, :], in_=self.mask[:, :],
            compare_op=ALU.is_ge, fill=MASKVAL, base=0,
            pattern=[[-1, 128]], channel_multiplier=1))

        self.dma(nc.sync, "lam", lv[0:1, 0:128], self.lq1[None, :])
        self.dma(nc.sync, "lam", lv[0:1, 128:256], self.lk1[None, :])
        self.dma(nc.sync, "lam", lv[0:1, 256:384], self.lq2[None, :])
        e_l = self.dma(nc.sync, "lam", lv[0:1, 384:512], self.lk2[None, :])
        e_c = self.dma(nc.sync, "cfm", cfm[:, 0:16],
                       bass.AP(self.cvec.tensor, 0, [[1, 128], [128, 16]]),
                       allow_slow_non_contiguous=True)
        e_swb = self.dma(nc.sync, "swb", self.sw2[:, :],
                         bass.AP(self.subw.tensor, 0, [[0, 128], [1, 256]]))
        wt(nc.vector, e_swb)
        self.e_sw2 = VE.tag(nc.vector.tensor_scalar(
            self.sw2[:, :], self.sw2[:, :], 1.0 - LAM_INIT, None, ALU.mult))

        wt(nc.scalar, e_c)
        e_sc = AE.tag(nc.scalar.activation(cfm[:, 0:16], cfm[:, 0:16], AF.Silu))

        wt(nc.vector, e_l)
        VE.tag(nc.vector.tensor_tensor(lv[0:1, 0:128], lv[0:1, 0:128],
                                       lv[0:1, 128:256], ALU.mult))
        VE.tag(nc.vector.tensor_tensor(lv[0:1, 256:384], lv[0:1, 256:384],
                                       lv[0:1, 384:512], ALU.mult))
        nc.vector.reduce_sum(lt[0:1, 0:1], lv[0:1, 0:128], axis=AX.X)
        e_vr = VE.tag(nc.vector.reduce_sum(lt[0:1, 1:2], lv[0:1, 256:384],
                                           axis=AX.X))
        wt(nc.scalar, e_vr)
        e_ae = AE.tag(nc.scalar.activation(lt[0:1, 2:4], lt[0:1, 0:2], AF.Exp))
        wt(nc.vector, e_ae)
        VE.tag(nc.vector.tensor_tensor(lt[0:1, 4:5], lt[0:1, 2:3],
                                       lt[0:1, 3:4], ALU.subtract))
        e_nl = VE.tag(nc.vector.tensor_scalar(
            lt[0:1, 5:6], lt[0:1, 4:5], LAM_INIT, -1.0, ALU.add, ALU.mult))
        wt(nc.gpsimd, e_nl)
        self.dma(nc.gpsimd, "srow_sc",
                 bass.AP(self.srow.tensor, SROW_SC + SLOT_NEGLAM,
                         [[1, 1], [1, 1]]),
                 lt[0:1, 5:6])

        Af = self.arenaA.bitcast(F32)
        xbs = [Af[:, 0:2048], Af[:, 2048:4096]]
        sdump = Af[:, 4096:6144]
        e_xs = [None, None]
        stt = self.stats

        def ln1_stats_chunk(sc):
            spar = sc % 2
            wt(nc.sync, e_xs[spar])
            e_x = self.dma(nc.sync, f"x{spar}", xbs[spar][:, :],
                           self.xin[sc * 128:(sc + 1) * 128, :])
            e_r = self.emit_stats(xbs[spar], stt[:, 18 + sc:19 + sc],
                                  stt[:, 34 + sc:35 + sc],
                                  [e_x, self.e_consts], par=spar,
                                  dump=sdump)
            e_xs[spar] = e_r

        wmt = self.wmod.tensor
        psm = [self.ps6, self.ps7]
        ev_te = [None, None]
        ev_cp = [None, None]
        ev_wr = [None, None]
        WMW = 384
        for t in range(32):
            if t % 2 == 0:
                ln1_stats_chunk(t // 2)
            par = t % 2
            wt(nc.sync, ev_te[par])
            w3 = self.wap3(F32, 8192 + par * 24576, [[WMW, 16], [1, WMW]])
            e_ld = self.dma(nc.sync, f"wm{par}", w3,
                            bass.AP(wmt, t * WMW,
                                    [[6 * E, 128], [6 * E * 128, 16],
                                     [1, WMW]]))
            wt(nc.tensor, e_ld, e_sc, ev_cp[par])
            ins = None
            for j in range(16):
                rj = wmb[par][:, j * WMW:(j + 1) * WMW]
                ins = nc.tensor.matmul(
                    psm[par][0:1, 0:WMW], cfm[:, j:j + 1], rj,
                    start=(j == 0), stop=(j == 15))
            ev_te[par] = TE.tag(ins)
            wt(nc.vector, ev_te[par], ev_wr[par])
            e_cp = VE.tag(nc.vector.tensor_copy(mrow[par][0:1, 0:WMW],
                                                psm[par][0:1, 0:WMW]))
            ev_cp[par] = e_cp
            wt(nc.gpsimd, e_cp)
            ev_wr[par] = self.dma(
                nc.gpsimd, f"mrow{par}",
                bass.AP(self.srow.tensor, SROW_MOD + t * WMW,
                        [[1, 1], [1, WMW]]),
                mrow[par][0:1, 0:WMW])
        self.e_modrow = [ev_wr[0], ev_wr[1]]

        if self.want("mod"):
            t = self.dbg_dram("mod", [6 * E])
            wt(nc.gpsimd, *self.e_modrow)
            self.dma(nc.gpsimd, "dbg", t[None, :],
                     bass.AP(self.srow.tensor, 0, [[1, 1], [1, 6 * E]]))

        wt(nc.sync, *self.e_modrow)
        e_b0 = self.dma(nc.sync, "bc0", self.bc0[:, :],
                        bass.AP(self.srow.tensor, SROW_MOD + E,
                                [[0, 128], [1, E]]))
        e_b1 = self.dma(nc.sync, "bc1", self.bc1[:, :],
                        bass.AP(self.srow.tensor, SROW_MOD + 0,
                                [[0, 128], [1, E]]))
        wt(nc.vector, e_b0)
        self.e_bc = VE.tag(nc.vector.tensor_scalar(
            self.bc0[:, :], self.bc0[:, :], 1.0, None, ALU.add))
        self.e_bc_ld = e_b1

    def emit_stats(self, xb, mcol, rcol, pre_evts, eps=EPS_LN, n=E, par=0,
                   dump=None):
        nc = self.nc
        VE, AE = self.VE, self.AE
        stt = self.stats
        if dump is None:
            dump = self.wview(F32, 2048, 16384)
        wt(nc.scalar, *pre_evts)
        e_sq = AE.tag(nc.scalar.activation(dump[:, 0:n], xb, AF.Square,
                                           accum_out=stt[:, 12 + par:13 + par]))
        wt(nc.vector, *pre_evts)
        VE.tag(nc.vector.reduce_sum(stt[:, 1:2], xb, axis=AX.X))
        VE.tag(nc.vector.tensor_scalar(mcol, stt[:, 1:2], 1.0 / n, None,
                                       ALU.mult))
        nc.vector.tensor_tensor(stt[:, 3:4], mcol, mcol, ALU.mult)
        wt(nc.vector, e_sq)
        VE.tag(nc.vector.tensor_scalar(stt[:, 4:5], stt[:, 12 + par:13 + par],
                                       1.0 / n, None, ALU.mult))
        VE.tag(nc.vector.tensor_tensor(stt[:, 5:6], stt[:, 4:5], stt[:, 3:4],
                                       ALU.subtract))
        e_u = VE.tag(nc.vector.tensor_scalar(stt[:, 5:6], stt[:, 5:6], eps,
                                             None, ALU.add))
        wt(nc.scalar, e_u)
        e_sq2 = AE.tag(nc.scalar.activation(stt[:, 6:7], stt[:, 5:6], AF.Sqrt))
        wt(nc.vector, e_sq2)
        VE.tag(nc.vector.reciprocal(stt[:, 7:8], stt[:, 6:7]))
        VE.tag(nc.vector.tensor_tensor(stt[:, 8:9], stt[:, 7:8], stt[:, 7:8],
                                       ALU.mult))
        VE.tag(nc.vector.tensor_tensor(stt[:, 9:10], stt[:, 8:9], stt[:, 5:6],
                                       ALU.mult))
        VE.tag(nc.vector.tensor_scalar(stt[:, 9:10], stt[:, 9:10], -0.5, 1.5,
                                       ALU.mult, ALU.add))
        e_r = VE.tag(nc.vector.tensor_tensor(rcol, stt[:, 7:8], stt[:, 9:10],
                                             ALU.mult))
        return e_r

    def emit_fm_store(self, src_bf16, dest_handle, sc, e_src, w=2048,
                      dest_width=32768, dest_col0=0, sc_stride=2048):
        nc = self.nc
        TE, VE = self.TE, self.VE
        nj = w // 128
        e_ev = None
        e_t = None
        for g in range(0, nj, 8):
            gb = min(8, nj - g)
            bi = (g // 8) % 2
            pbank = self.pst[bi]
            wt(nc.tensor, self.pst_guard[bi], e_src)
            ins = None
            for jj in range(gb):
                j = g + jj
                ins = nc.tensor.transpose(
                    pbank[:, jj * 128:(jj + 1) * 128],
                    src_bf16[:, j * 128:(j + 1) * 128], self.ident[:, :])
            e_t = TE.tag(ins)
            wt(nc.vector, e_t)
            outap = bass.AP(dest_handle,
                            dest_col0 + g * sc_stride + sc * 128,
                            [[dest_width, 128], [sc_stride, gb], [1, 128]])
            inap = bass.AP(pbank, 0, [[1024, 128], [128, gb], [1, 128]])
            e_ev = VE.tag(nc.vector.tensor_copy(outap, inap))
            self.pst_guard[bi] = e_ev
        return e_t, e_ev

    def emit_pass1(self, bufs):
        nc = self.nc
        AE = self.AE
        c = self.p1_count
        if c >= 192:
            return
        self.p1_count += 1
        mat = self.p1_mats[c // 64]
        sub = c % 64
        par = c % 2
        wt(nc.gpsimd, self.e_p1_act[par])
        e_ld = self.dma(nc.gpsimd, f"p1{par}", bufs[par][:, :],
                        bass.AP(mat.tensor, sub * 128 * 2048,
                                [[2048, 128], [1, 2048]]))
        wt(nc.scalar, e_ld)
        self.e_p1_act[par] = AE.tag(nc.scalar.activation(
            bufs[par][:, 0:2048], bufs[par][:, 0:2048], AF.Abs,
            accum_out=self.stats[:, 64 + c:65 + c]))

    def phase1(self):
        nc = self.nc
        VE, AE, GE = self.VE, self.AE, self.GE
        self.barrier()
        self.pst_guard = [None, None]
        xb = [self.wview(F32, 2048, 0), self.wview(F32, 2048, 8192)]
        zb = self.wview(F32, 2048, 24576)
        hfp = self.wview(F32, 2048, 32768)
        hhi = self.wview(BF16, 2048, 40960)
        hlo = self.wview(BF16, 2048, 45056)
        stt = self.stats
        e_used = [None, None]
        e_hi_prev = e_lo_prev = None
        e_te_hi = e_te_lo = None
        for sc in range(NCH):
            par = sc % 2
            wt(nc.sync, e_used[par])
            e_x = self.dma(nc.sync, f"x{par}", xb[par][:, :],
                           self.xin[sc * 128:(sc + 1) * 128, :])
            wt(nc.vector, e_x)
            e_z = VE.tag(nc.vector.tensor_scalar(
                zb[:, :], xb[par][:, :], stt[:, 18 + sc:19 + sc],
                stt[:, 34 + sc:35 + sc], ALU.subtract, ALU.mult))
            e_used[par] = e_z
            wt(nc.vector, self.e_bc, self.e_bc_ld, e_hi_prev, e_lo_prev)
            VE.tag(nc.vector.tensor_tensor(hfp[:, :], zb[:, :],
                                           self.bc0[:, :], ALU.mult))
            e_h = VE.tag(nc.vector.tensor_tensor(hfp[:, :], hfp[:, :],
                                                 self.bc1[:, :], ALU.add))
            wt(nc.scalar, e_h, e_te_hi)
            e_hi = AE.tag(nc.scalar.copy(hhi[:, :], hfp[:, :]))
            wt(nc.gpsimd, e_hi, e_te_lo)
            e_lo = GE.tag(nc.gpsimd.tensor_sub(hlo[:, :], hfp[:, :],
                                               hhi[:, :]))
            e_te_hi, _ = self.emit_fm_store(hhi, self.hfm_hi, sc, e_hi)
            e_te_lo, _ = self.emit_fm_store(hlo, self.hfm_lo, sc, e_lo)
            e_hi_prev, e_lo_prev = e_hi, e_lo
        self.e_h1 = VE.evt()
        if self.want("h1fm"):
            t1 = self.dbg_dram("h1fm_hi", [128, 32768])
            t2 = self.dbg_dram("h1fm_lo", [128, 32768])
            wt(nc.gpsimd, VE.evt())
            self.dma(nc.gpsimd, "dbg", t1[:, :], self.hfm_hi[:, :])
            self.dma(nc.gpsimd, "dbg", t2[:, :], self.hfm_lo[:, :])

    def phase2(self):
        nc = self.nc
        TE, VE, AE, GE = self.TE, self.VE, self.AE, self.GE
        self.barrier()
        stt = self.stats

        self.p1_count = 0
        self.p1_mats = [self.wupT, self.wgateT, self.wdownT]
        self.e_p1_act = [None, None]
        pbuf = [self.wview(F32, 2048, 36864), self.wview(F32, 2048, 36864)]
        self.p1_bufs = pbuf

        def pass1_chunk():
            self.emit_pass1(self.p1_bufs)

        wsf = [self.wview(F32, 2048, 0), self.wview(F32, 2048, 4096 + 4096)]
        whi = [self.wview(BF16, 2048, 16384), self.wview(BF16, 2048, 20480)]
        wlo = [self.wview(BF16, 2048, 24576 - 16384 + 12288) if False else
               self.wview(BF16, 2048, 24576), self.wview(BF16, 2048, 28672)]
        sthi = self.wview(BF16, 2048, 45056)
        stlo = self.wview(BF16, 2048, 49152)
        qfp = self.wview(F32, 512, 32768)
        mats = [(self.wq, self.sq_hi, self.sq_lo, SCALE),
                (self.wk, self.sk_hi, self.sk_lo, 1.0)]

        def qk_load(i):
            wmat = mats[i // NCH][0]
            m = i % NCH
            return self.dma(nc.sync, f"ws{i % 2}",
                            self.wap3(F32, [0, 8192][i % 2],
                                      [[128, 16], [1, 128]]),
                            bass.AP(wmat.tensor, m * 128,
                                    [[E, 128], [E * 128, 16], [1, 128]]))

        e_split = [None, None]
        e_te_hl = [None, None]
        e_bank = [None, None]
        e_spill = None
        e_stage_lo = None
        e_ld = [None, None]
        e_ld[0] = qk_load(0)
        for i in range(2 * NCH):
            par = i % 2
            mat_i, m = i // NCH, i % NCH
            dst_hi, dst_lo, qs = mats[mat_i][1], mats[mat_i][2], mats[mat_i][3]
            if i + 1 < 2 * NCH:
                wt(nc.sync, e_split[1 - par])
                e_ld[1 - par] = qk_load(i + 1)
                wt(nc.scalar, e_ld[1 - par], e_te_hl[1 - par])
                e_hi_n = AE.tag(nc.scalar.copy(whi[1 - par][:, :],
                                               wsf[1 - par][:, :]))
                wt(nc.vector, e_hi_n)
                e_split[1 - par] = VE.tag(nc.vector.tensor_sub(
                    wlo[1 - par][:, :], wsf[1 - par][:, :],
                    whi[1 - par][:, :]))
            if i == 0:
                wt(nc.scalar, e_ld[0])
                e_hi0 = AE.tag(nc.scalar.copy(whi[0][:, :], wsf[0][:, :]))
                wt(nc.vector, e_hi0)
                e_split[0] = VE.tag(nc.vector.tensor_sub(
                    wlo[0][:, :], wsf[0][:, :], whi[0][:, :]))
            for st in range(4):
                bk = st % 2
                wt(nc.tensor, e_split[par], e_bank[bk])
                ins = None
                for f in range(16):
                    lt_ = whi[par][:, f * 128:(f + 1) * 128]
                    nc.tensor.matmul(
                        self.psb[bk][:, :], lt_,
                        self.hfm_hi[:, f * 2048 + st * 512:
                                    f * 2048 + st * 512 + 512],
                        start=(f == 0), stop=False)
                    nc.tensor.matmul(
                        self.psb[bk][:, :], lt_,
                        self.hfm_lo[:, f * 2048 + st * 512:
                                    f * 2048 + st * 512 + 512],
                        start=False, stop=False)
                for f in range(16):
                    ins = nc.tensor.matmul(
                        self.psb[bk][:, :],
                        wlo[par][:, f * 128:(f + 1) * 128],
                        self.hfm_hi[:, f * 2048 + st * 512:
                                    f * 2048 + st * 512 + 512],
                        start=False, stop=(f == 15))
                e_mm = TE.tag(ins)
                wt(nc.vector, e_mm, e_stage_lo)
                e_q = VE.tag(nc.vector.tensor_scalar(
                    qfp[:, :], self.psb[bk][:, :], qs, None, ALU.mult))
                e_bank[bk] = e_q
                wt(nc.scalar, e_q, e_spill)
                e_sh = AE.tag(nc.scalar.copy(
                    sthi[:, st * 512:(st + 1) * 512], qfp[:, :]))
                wt(nc.gpsimd, e_sh)
                e_stage_lo = GE.tag(nc.gpsimd.tensor_sub(
                    stlo[:, st * 512:(st + 1) * 512], qfp[:, :],
                    sthi[:, st * 512:(st + 1) * 512]))
            e_te_hl[par] = TE.evt()
            wt(nc.scalar, e_stage_lo)
            self.dma(nc.scalar, "spq", dst_hi[m, :, :], sthi[:, :])
            e_spill = self.dma(nc.scalar, "spq", dst_lo[m, :, :],
                               stlo[:, :])
            pass1_chunk()

        self.barrier()
        e_bank = [None, None]
        e_spill = None
        e_stage_lo = None
        wvf = self.wview(F32, 4096, 0)
        vhi = self.wview(BF16, 4096, 16384)
        vlo = self.wview(BF16, 4096, 24576)
        vsh = self.wview(BF16, 256, 32768)
        vsl = self.wview(BF16, 256, 33280)
        vfp = self.wview(F32, 256, 33792)
        e_vsplit = None
        for h in range(NH):
            wt(nc.sync, e_vsplit, e_te_hl[0], e_te_hl[1])
            e_ld = self.dma(nc.sync, "ws",
                            self.wap3(F32, 0, [[256, 16], [1, 256]]),
                            bass.AP(self.wv.tensor, h * 256,
                                    [[E, 128], [E * 128, 16], [1, 256]]))
            wt(nc.scalar, e_ld)
            e_hi = AE.tag(nc.scalar.copy(vhi[:, :], wvf[:, :]))
            wt(nc.gpsimd, e_hi)
            e_lo = GE.tag(nc.gpsimd.tensor_sub(vlo[:, :], wvf[:, :],
                                               vhi[:, :]))
            e_vsplit = e_lo
            for sc in range(NCH):
                bk = sc % 2
                wt(nc.tensor, e_lo, e_bank[bk])
                ins = None
                for f in range(16):
                    lh = self.hfm_hi[:, f * 2048 + sc * 128:
                                     f * 2048 + sc * 128 + 128]
                    rh = vhi[:, f * 256:(f + 1) * 256]
                    rl = vlo[:, f * 256:(f + 1) * 256]
                    nc.tensor.matmul(self.psb[bk][:, 0:256], lh, rh,
                                     start=(f == 0), stop=False)
                    nc.tensor.matmul(self.psb[bk][:, 0:256], lh, rl,
                                     start=False, stop=False)
                for f in range(16):
                    ll = self.hfm_lo[:, f * 2048 + sc * 128:
                                     f * 2048 + sc * 128 + 128]
                    rh = vhi[:, f * 256:(f + 1) * 256]
                    ins = nc.tensor.matmul(self.psb[bk][:, 0:256], ll, rh,
                                           start=False, stop=(f == 15))
                e_mm = TE.tag(ins)
                wt(nc.vector, e_mm, e_stage_lo)
                e_q = VE.tag(nc.vector.tensor_copy(vfp[:, 0:256],
                                                   self.psb[bk][:, 0:256]))
                e_bank[bk] = e_q
                wt(nc.scalar, e_q, e_spill)
                e_sh = AE.tag(nc.scalar.copy(vsh[:, 0:256], vfp[:, 0:256]))
                wt(nc.gpsimd, e_sh)
                e_stage_lo = GE.tag(nc.gpsimd.tensor_sub(
                    vsl[:, 0:256], vfp[:, 0:256], vsh[:, 0:256]))
                wt(nc.scalar, e_stage_lo)
                self.dma(nc.scalar, "spv", self.sv_hi[h, sc, :, :],
                         vsh[:, 0:256])
                e_spill = self.dma(nc.scalar, "spv", self.sv_lo[h, sc, :, :],
                                   vsl[:, 0:256])
            e_te_hl = [TE.evt(), TE.evt()]

    def phase3(self):
        nc = self.nc
        TE, VE, AE, GE = self.TE, self.VE, self.AE, self.GE
        self.barrier()
        stt = self.stats
        e_nc = self.dma(nc.sync, "ncol", self.cols[:, 0:1],
                        bass.AP(self.srow.tensor, SROW_SC + SLOT_NEGLAM,
                                [[0, 128], [1, 1]]))
        AR = [self.hfm_hi, self.hfm_lo]

        def hv(par, c0, w):
            return AR[par][:, c0:c0 + w]

        pfp = [self.wview(F32, 2048, 0), self.wview(F32, 2048, 16384)]
        phi = [self.wview(BF16, 2048, 8192), self.wview(BF16, 2048, 40960)]
        plo = [self.wview(BF16, 2048, 12288), self.wview(BF16, 2048, 45056)]
        afp = self.wview(F32, 4096, 24576)
        abuf_lo = self.wview(BF16, 4096, 49152)
        Dv = self.arenaD.bitcast(F32)
        ct1 = Dv[:, 4864:5120]

        def load_head(h):
            par = h % 2
            evs = []
            for i, src in enumerate([self.sq_hi, self.sq_lo,
                                     self.sk_hi, self.sk_lo]):
                for br in range(2):
                    m = 2 * h + br
                    evs.append(self.dma(
                        nc.sync, f"qk{par}",
                        hv(par, (2 * i + br) * 2048, 2048),
                        src[m, :, :]))
            for i, src in enumerate([self.sv_hi, self.sv_lo]):
                evs.append(self.dma(
                    nc.sync, f"qk{par}",
                    hv(par, 16384 + i * 4096, 4096),
                    bass.AP(src.tensor, h * NCH * 128 * 256,
                            [[256, 128], [128 * 256, 16], [1, 256]])))
            return evs[-1]

        e_ld = [None, None]
        e_ld[0] = load_head(0)
        e_arena_free = [None, None]
        e_bsc = [None, None, None, None]
        e_pst_ev = [None, None]
        e_o_read = None
        e_spill_a = None
        e_pfp_free = [None, None]
        self.e_phi_free = [None, None]
        e_plo = None
        e_phi = None
        for h in range(NH):
            par = h % 2
            if h + 1 < NH:
                wt(nc.sync, e_arena_free[1 - par])
                e_ld[1 - par] = load_head(h + 1)
            q_of = [0, 2048]
            k_of = [8192, 10240]
            ptb_hi = hv(par, 24576, 2048)
            ptb_lo = hv(par, 26624, 2048)
            abuf_hi = hv(par, 28672, 4096)
            rr = 0
            for qc in range(NCH):
                nfull = qc // 4
                rem = qc % 4
                remw = (rem + 1) * 128
                kact = (qc + 1) * 128
                nkc = qc + 1
                e_exp = [None, None]
                for bi in range(2):
                    qhi = hv(par, q_of[bi], 2048)[:, qc * 128:(qc + 1) * 128]
                    qlo = hv(par, q_of[bi] + 4096, 2048)[:, qc * 128:
                                                         (qc + 1) * 128]
                    khi = hv(par, k_of[bi], 2048)
                    klo = hv(par, k_of[bi] + 4096, 2048)
                    wt(nc.tensor, e_ld[par])
                    for kt in range(nfull + 1):
                        w = 512 if kt < nfull else remw
                        bidx = rr % 4
                        rr += 1
                        bank = self.psb[bidx]
                        wt(nc.tensor, e_bsc[bidx])
                        c0 = kt * 512
                        nc.tensor.matmul(bank[:, 0:w], qhi,
                                         khi[:, c0:c0 + w],
                                         start=True, stop=False)
                        nc.tensor.matmul(bank[:, 0:w], qhi,
                                         klo[:, c0:c0 + w],
                                         start=False, stop=False)
                        ins = nc.tensor.matmul(bank[:, 0:w], qlo,
                                               khi[:, c0:c0 + w],
                                               start=False, stop=True)
                        e_mm = TE.tag(ins)
                        wt(nc.vector, e_mm, e_pfp_free[bi])
                        e_bsc[bidx] = VE.tag(nc.vector.tensor_copy(
                            pfp[bi][:, c0:c0 + w], bank[:, 0:w]))
                e_pv2 = None
                for bi in range(2):
                    sf = pfp[bi]
                    wt(nc.vector, self.e_consts)
                    VE.tag(nc.vector.tensor_tensor(
                        sf[:, kact - 128:kact], sf[:, kact - 128:kact],
                        self.mask[:, :], ALU.add))
                    e_mx = VE.tag(nc.vector.tensor_reduce(
                        stt[:, 5:6], sf[:, 0:kact], axis=AX.X,
                        op=ALU.max, negate=True))
                    wt(nc.scalar, e_mx)
                    e_ex = AE.tag(nc.scalar.activation(
                        sf[:, 0:kact], sf[:, 0:kact], AF.Exp,
                        bias=stt[:, 5:6],
                        accum_out=stt[:, 8 + bi:9 + bi]))
                    wt(nc.vector, e_ex)
                    e_zr = VE.tag(nc.vector.reciprocal(
                        stt[:, 14 + bi:15 + bi], stt[:, 8 + bi:9 + bi]))
                    wt(nc.vector, self.e_phi_free[bi])
                    e_phi = VE.tag(nc.vector.tensor_copy(phi[bi][:, 0:kact],
                                                         sf[:, 0:kact]))
                    wt(nc.gpsimd, e_phi)
                    e_plo = GE.tag(nc.gpsimd.tensor_sub(
                        plo[bi][:, 0:kact], sf[:, 0:kact], phi[bi][:, 0:kact]))
                    e_pfp_free[bi] = e_plo
                    for src, dstv, e_src in ((phi[bi], ptb_hi, e_phi),
                                             (plo[bi], ptb_lo, e_plo)):
                        for g in range(0, nkc, 8):
                            gb = min(8, nkc - g)
                            tb = (g // 8) % 2
                            wt(nc.tensor, e_pst_ev[tb], e_src, e_spill_a)
                            for jj in range(gb):
                                kc = g + jj
                                ins = nc.tensor.transpose(
                                    self.pst[tb][:, jj * 128:(jj + 1) * 128],
                                    src[:, kc * 128:(kc + 1) * 128],
                                    self.ident[:, :])
                            e_t = TE.tag(ins)
                            wt(nc.vector, e_t)
                            outap = bass.AP(
                                dstv.tensor, dstv.offset + g * 128,
                                [[32768, 128], [1, gb * 128]])
                            inap = bass.AP(self.pst[tb], 0,
                                           [[1024, 128], [1, gb * 128]])
                            e_pst_ev[tb] = VE.tag(
                                nc.vector.tensor_copy(outap, inap))
                    self.e_phi_free[bi] = e_pst_ev[(nkc - 1) // 8 % 2]
                    ops = self.ps6 if bi == 0 else self.ps7
                    wt(nc.tensor, e_pst_ev[0], e_pst_ev[1], e_o_read)
                    ins = None
                    for kc in range(nkc):
                        vh = hv(par, 16384, 4096)[:, kc * 256:(kc + 1) * 256]
                        vl = hv(par, 20480, 4096)[:, kc * 256:(kc + 1) * 256]
                        ph_ = ptb_hi[:, kc * 128:(kc + 1) * 128]
                        pl_ = ptb_lo[:, kc * 128:(kc + 1) * 128]
                        nc.tensor.matmul(ops[:, 0:256], ph_, vh,
                                         start=(kc == 0), stop=False)
                        nc.tensor.matmul(ops[:, 0:256], ph_, vl,
                                         start=False, stop=False)
                        ins = nc.tensor.matmul(ops[:, 0:256], pl_, vh,
                                               start=False,
                                               stop=(kc == nkc - 1))
                    e_pv = TE.tag(ins)
                    if bi == 1:
                        e_pv2 = e_pv
                wt(nc.vector, e_pv2, e_nc)
                e_nlz = VE.tag(nc.vector.tensor_tensor(
                    stt[:, 16:17], self.cols[:, 0:1], stt[:, 15:16],
                    ALU.mult))
                VE.tag(nc.vector.tensor_scalar(
                    ct1[:, :], self.ps6[:, 0:256], stt[:, 14:15], None,
                    ALU.mult))
                e_comb = VE.tag(nc.vector.scalar_tensor_tensor(
                    afp[:, qc * 256:(qc + 1) * 256], self.ps7[:, 0:256],
                    stt[:, 16:17], ct1[:, :], ALU.mult, ALU.add))
                e_o_read = e_comb
                wt(nc.scalar, e_comb)
                AE.tag(nc.scalar.activation(
                    ct1[:, :], afp[:, qc * 256:(qc + 1) * 256],
                    AF.Square, accum_out=stt[:, 160 + qc:161 + qc]))
            e_ss = self.AE.evt()
            wt(nc.vector, e_ss)
            VE.tag(nc.vector.tensor_scalar(stt[:, 176:192],
                                           stt[:, 160:176], 1.0 / 256,
                                           EPS_LN, ALU.mult, ALU.add))
            e_u = VE.tag(nc.vector.tensor_copy(stt[:, 144:160],
                                               stt[:, 176:192]))
            wt(nc.scalar, e_u)
            e_sq = AE.tag(nc.scalar.activation(stt[:, 176:192],
                                               stt[:, 144:160], AF.Sqrt))
            wt(nc.vector, e_sq)
            VE.tag(nc.vector.reciprocal(stt[:, 176:192], stt[:, 176:192]))
            VE.tag(nc.vector.tensor_tensor(stt[:, 128:144], stt[:, 176:192],
                                           stt[:, 176:192], ALU.mult))
            VE.tag(nc.vector.tensor_tensor(stt[:, 128:144], stt[:, 128:144],
                                           stt[:, 144:160], ALU.mult))
            VE.tag(nc.vector.tensor_scalar(stt[:, 128:144], stt[:, 128:144],
                                           -0.5, 1.5, ALU.mult, ALU.add))
            e_rst = VE.tag(nc.vector.tensor_tensor(
                stt[:, 176:192], stt[:, 176:192], stt[:, 128:144], ALU.mult))
            e_ab = [None, None]
            for qc in range(NCH):
                VE.tag(nc.vector.tensor_scalar(
                    pfp[0][:, 0:256], afp[:, qc * 256:(qc + 1) * 256],
                    stt[:, 176 + qc:177 + qc], None, ALU.mult))
                wt(nc.vector, self.e_sw2, e_ab[0], e_ab[1])
                e_af = VE.tag(nc.vector.tensor_tensor(
                    pfp[0][:, 1024:1280], pfp[0][:, 0:256], self.sw2[:, :],
                    ALU.mult))
                wt(nc.scalar, e_af)
                e_h_ = AE.tag(nc.scalar.copy(
                    abuf_hi[:, qc * 256:(qc + 1) * 256],
                    pfp[0][:, 1024:1280]))
                wt(nc.gpsimd, e_h_)
                e_l_ = GE.tag(nc.gpsimd.tensor_sub(
                    abuf_lo[:, qc * 256:(qc + 1) * 256],
                    pfp[0][:, 1024:1280],
                    abuf_hi[:, qc * 256:(qc + 1) * 256]))
                e_ab = [e_h_, e_l_]
                e_pfp_free[0] = e_l_
            for srcv, dstv, sh in ((abuf_hi, ptb_hi, self.sa_hi),
                                   (abuf_lo, ptb_lo, self.sa_lo)):
                e_src = e_ab[0] if srcv is abuf_hi else e_ab[1]
                for eb in range(2):
                    for g in range(0, NCH, 8):
                        tb = (g // 8) % 2
                        wt(nc.tensor, e_pst_ev[tb], e_src, e_spill_a)
                        for jj in range(8):
                            qc = g + jj
                            ins = nc.tensor.transpose(
                                self.pst[tb][:, jj * 128:(jj + 1) * 128],
                                srcv[:, qc * 256 + eb * 128:
                                     qc * 256 + eb * 128 + 128],
                                self.ident[:, :])
                        e_t = TE.tag(ins)
                        wt(nc.vector, e_t)
                        outap = bass.AP(dstv.tensor,
                                        dstv.offset + g * 128,
                                        [[32768, 128], [1, 1024]])
                        inap = bass.AP(self.pst[tb], 0,
                                       [[1024, 128], [1, 1024]])
                        e_pst_ev[tb] = VE.tag(
                            nc.vector.tensor_copy(outap, inap))
                    wt(nc.scalar, e_pst_ev[0], e_pst_ev[1])
                    e_spill_a = self.dma(
                        nc.scalar, "spa",
                        bass.AP(sh.tensor, (2 * h + eb) * 128 * 128,
                                [[128, 128], [NCH * 128 * 128, 16],
                                 [1, 128]]),
                        bass.AP(dstv.tensor, dstv.offset,
                                [[32768, 128], [128, 16], [1, 128]]))
            e_arena_free[par] = TE.evt()
        self.e_attn_done = e_spill_a
    def phase4(self):
        nc = self.nc
        TE, VE, AE, GE = self.TE, self.VE, self.AE, self.GE
        self.barrier()
        stt = self.stats

        e_g = self.dma(nc.sync, "bc0", self.bc0[:, :],
                       bass.AP(self.srow.tensor, SROW_MOD + 2 * E,
                               [[0, 128], [1, E]]))
        t1 = self.wview(F32, 2048, 24576)
        t2 = self.wview(F32, 2048, 32768)
        wt(nc.vector, e_g)
        VE.tag(nc.vector.tensor_scalar(t1[:, :], self.bc0[:, :], 0.25, None,
                                       ALU.max))
        VE.tag(nc.vector.tensor_scalar(t2[:, :], self.bc0[:, :], -0.25, None,
                                       ALU.min))
        VE.tag(nc.vector.tensor_scalar(self.bc0[:, :], self.bc0[:, :], 0.0,
                                       None, ALU.is_ge))
        VE.tag(nc.vector.tensor_tensor(t1[:, :], t1[:, :], t2[:, :],
                                       ALU.subtract))
        VE.tag(nc.vector.tensor_tensor(t1[:, :], self.bc0[:, :], t1[:, :],
                                       ALU.mult))
        e_ga = VE.tag(nc.vector.tensor_tensor(self.bc0[:, :], t2[:, :],
                                              t1[:, :], ALU.add))

        woc = self.wview(F32, 2048, 16384)
        e_wsp = [None, None]
        for wc in range(NCH):
            wt(nc.sync, e_wsp[0], e_wsp[1])
            e_ld = self.dma(nc.sync, "woc", woc[:, :],
                            self.wo[wc * 128:(wc + 1) * 128, :])
            wt(nc.scalar, e_ld)
            e_hi = AE.tag(nc.scalar.copy(
                self.hfm_hi[:, wc * 2048:(wc + 1) * 2048], woc[:, :]))
            wt(nc.gpsimd, e_hi)
            e_lo = GE.tag(nc.gpsimd.tensor_sub(
                self.hfm_lo[:, wc * 2048:(wc + 1) * 2048], woc[:, :],
                self.hfm_hi[:, wc * 2048:(wc + 1) * 2048]))
            e_wsp = [e_hi, e_lo]

        ab_hi = [self.wview(BF16, 2048, 0), self.wview(BF16, 2048, 8192)]
        ab_lo = [self.wview(BF16, 2048, 4096), self.wview(BF16, 2048, 12288)]
        xb = [self.wview(F32, 2048, 24576), self.wview(F32, 2048, 32768)]
        x1b = [self.wview(F32, 2048, 40960), self.wview(F32, 2048, 49152)]
        ot = self.wview(F32, 512, 16384)
        e_a_used = [None, None]
        e_x_used = [None, None]
        e_bank = [None, None]
        e_x1_done = [None, None]
        e_sx = [None, None]
        e_sq_prev = None
        for sc in range(NCH):
            par = sc % 2
            wt(nc.sync, e_a_used[par])
            self.dma(nc.sync, f"pa{par}",
                     bass.AP(ab_hi[par].tensor, ab_hi[par].offset,
                             [[28672, 128], [128, 16], [1, 128]]),
                     bass.AP(self.sa_hi.tensor, sc * 16 * 128 * 128,
                             [[128, 128], [128 * 128, 16], [1, 128]]))
            e_a = self.dma(nc.sync, f"pa{par}",
                           bass.AP(ab_lo[par].tensor, ab_lo[par].offset,
                                   [[28672, 128], [128, 16], [1, 128]]),
                           bass.AP(self.sa_lo.tensor, sc * 16 * 128 * 128,
                                   [[128, 128], [128 * 128, 16], [1, 128]]))
            wt(nc.sync, e_x_used[par], e_ga)
            e_x = self.dma(nc.sync, f"px{par}", xb[par][:, :],
                           self.xin[sc * 128:(sc + 1) * 128, :])
            wt(nc.vector, e_x1_done[par])
            for ft in range(4):
                bk = ft % 2
                wt(nc.tensor, e_a, e_wsp[0], e_wsp[1], e_bank[bk])
                ins = None
                for af in range(16):
                    lh = ab_hi[par][:, af * 128:(af + 1) * 128]
                    rh = self.hfm_hi[:, af * 2048 + ft * 512:
                                     af * 2048 + ft * 512 + 512]
                    rl = self.hfm_lo[:, af * 2048 + ft * 512:
                                     af * 2048 + ft * 512 + 512]
                    nc.tensor.matmul(self.psb[bk][:, :], lh, rh,
                                     start=(af == 0), stop=False)
                    nc.tensor.matmul(self.psb[bk][:, :], lh, rl,
                                     start=False, stop=False)
                for af in range(16):
                    ll = ab_lo[par][:, af * 128:(af + 1) * 128]
                    rh = self.hfm_hi[:, af * 2048 + ft * 512:
                                     af * 2048 + ft * 512 + 512]
                    ins = nc.tensor.matmul(self.psb[bk][:, :], ll, rh,
                                           start=False, stop=(af == 15))
                e_mm = TE.tag(ins)
                wt(nc.vector, e_mm, e_ga, e_x)
                if ft == 0:
                    wt(nc.vector, e_sx[par], e_sq_prev)
                VE.tag(nc.vector.tensor_tensor(
                    ot[:, :], self.psb[bk][:, :],
                    self.bc0[:, ft * 512:(ft + 1) * 512], ALU.mult))
                e_ev = VE.tag(nc.vector.tensor_tensor(
                    x1b[par][:, ft * 512:(ft + 1) * 512], ot[:, :],
                    xb[par][:, ft * 512:(ft + 1) * 512], ALU.add))
                e_bank[bk] = e_ev
            e_a_used[par] = TE.evt()
            e_x_used[par] = e_ev
            e_r = self.emit_stats(x1b[par][:, :], stt[:, 18 + sc:19 + sc],
                                  stt[:, 34 + sc:35 + sc], [e_ev], par=par)
            e_x1_done[par] = e_r
            e_sq_prev = self.AE.evt()
            wt(nc.scalar, e_ev)
            e_sx[par] = self.dma(nc.scalar, f"sx{par}", self.sx1[sc, :, :],
                                 x1b[par][:, :])
        self.e_p4 = VE.evt()

    def phase5(self):
        nc = self.nc
        VE, AE, GE = self.VE, self.AE, self.GE
        self.barrier()
        self.pst_guard = [None, None]
        stt = self.stats
        e_b0 = self.dma(nc.sync, "bc0", self.bc0[:, :],
                        bass.AP(self.srow.tensor, SROW_MOD + 4 * E,
                                [[0, 128], [1, E]]))
        e_b1 = self.dma(nc.sync, "bc1", self.bc1[:, :],
                        bass.AP(self.srow.tensor, SROW_MOD + 3 * E,
                                [[0, 128], [1, E]]))
        wt(nc.vector, e_b0)
        e_bc = VE.tag(nc.vector.tensor_scalar(
            self.bc0[:, :], self.bc0[:, :], 1.0, None, ALU.add))
        xb = [self.wview(F32, 2048, 0), self.wview(F32, 2048, 8192)]
        zb = self.wview(F32, 2048, 24576)
        hfp = self.wview(F32, 2048, 32768)
        hhi = self.wview(BF16, 2048, 40960)
        hlo = self.wview(BF16, 2048, 45056)
        p5buf = [self.wview(F32, 2048, 16384), self.wview(F32, 2048, 49152)]
        e_used = [None, None]
        e_hi_prev = e_lo_prev = None
        e_te_hi = e_te_lo = None
        for sc in range(NCH):
            par = sc % 2
            wt(nc.sync, e_used[par])
            e_x = self.dma(nc.sync, f"x{par}", xb[par][:, :],
                           self.sx1[sc, :, :])
            wt(nc.vector, e_x)
            e_z = VE.tag(nc.vector.tensor_scalar(
                zb[:, :], xb[par][:, :], stt[:, 18 + sc:19 + sc],
                stt[:, 34 + sc:35 + sc], ALU.subtract, ALU.mult))
            e_used[par] = e_z
            wt(nc.vector, e_bc, e_b1, e_hi_prev, e_lo_prev)
            VE.tag(nc.vector.tensor_tensor(hfp[:, :], zb[:, :],
                                           self.bc0[:, :], ALU.mult))
            e_h = VE.tag(nc.vector.tensor_tensor(hfp[:, :], hfp[:, :],
                                                 self.bc1[:, :], ALU.add))
            wt(nc.scalar, e_h, e_te_hi)
            e_hi = AE.tag(nc.scalar.copy(hhi[:, :], hfp[:, :]))
            wt(nc.gpsimd, e_hi, e_te_lo)
            e_lo = GE.tag(nc.gpsimd.tensor_sub(hlo[:, :], hfp[:, :],
                                               hhi[:, :]))
            e_te_hi, _ = self.emit_fm_store(hhi, self.hfm_hi, sc, e_hi)
            e_te_lo, _ = self.emit_fm_store(hlo, self.hfm_lo, sc, e_lo)
            e_hi_prev, e_lo_prev = e_hi, e_lo
            for _ in range(10):
                self.emit_pass1(p5buf)
        while self.p1_count < 192:
            self.emit_pass1(p5buf)
        if self.want("h2fm"):
            t1 = self.dbg_dram("h2fm_hi", [128, 32768])
            t2 = self.dbg_dram("h2fm_lo", [128, 32768])
            wt(nc.gpsimd, VE.evt())
            self.dma(nc.gpsimd, "dbg", t1[:, :], self.hfm_hi[:, :])
            self.dma(nc.gpsimd, "dbg", t2[:, :], self.hfm_lo[:, :])

    def phase6(self):
        nc = self.nc
        TE, VE, AE, GE = self.TE, self.VE, self.AE, self.GE
        self.barrier()
        stt = self.stats
        wt(nc.vector, *self.e_p1_act)
        for mi in range(3):
            c0 = 64 + mi * 64
            VE.tag(nc.vector.reduce_sum(stt[:, 50 + mi:51 + mi],
                                        stt[:, c0:c0 + 64], axis=AX.X))
        e_ps = VE.evt()
        wt(nc.tensor, e_ps)
        ins = None
        for mi in range(3):
            ins = nc.tensor.matmul(self.psb[0][0:1, mi:mi + 1],
                                   self.cols[:, 60:61],
                                   stt[:, 50 + mi:51 + mi],
                                   start=(mi == 0), stop=(mi == 2))
        e_sc3 = TE.tag(ins)
        wt(nc.vector, e_sc3)
        e_fin_ts = VE.tag(nc.vector.tensor_scalar(
            stt[0:1, 53:56], self.psb[0][0:1, 0:3],
            1.0 / (E * DEX), EPS_Q, ALU.mult, ALU.add))
        self._fin_psb = e_fin_ts
        VE.tag(nc.vector.tensor_scalar(stt[0:1, 56:59], stt[0:1, 53:56], 0.5,
                                       None, ALU.mult))
        e_fin = VE.tag(nc.vector.tensor_tensor(
            stt[0:1, 59:60], stt[0:1, 54:55], stt[0:1, 55:56], ALU.mult))
        self.e_bank2 = VE.evt()
        wt(nc.gpsimd, e_fin)
        for slot_i, col in [(SLOT_SUP, 53), (SLOT_SGD, 59), (SLOT_THR_UP, 56),
                            (SLOT_THR_GATE, 57), (SLOT_THR_DOWN, 58)]:
            self.e_srow_sc = self.dma(
                nc.gpsimd, "srow_sc",
                bass.AP(self.srow.tensor, SROW_SC + slot_i, [[1, 1], [1, 1]]),
                stt[0:1, col:col + 1])


        wt(nc.sync, self.e_srow_sc)
        e_c1 = self.dma(nc.sync, "ncol", self.cols[:, 1:2],
                        bass.AP(self.srow.tensor, SROW_SC + SLOT_THR_UP,
                                [[0, 128], [1, 1]]))
        e_c2 = self.dma(nc.sync, "ncol", self.cols[:, 2:3],
                        bass.AP(self.srow.tensor, SROW_SC + SLOT_THR_GATE,
                                [[0, 128], [1, 1]]))
        e_c3 = self.dma(nc.sync, "ncol", self.cols[:, 3:4],
                        bass.AP(self.srow.tensor, SROW_SC + SLOT_SUP,
                                [[0, 128], [1, 1]]))
        wt(nc.vector, e_c3)
        e_neg = VE.tag(nc.vector.tensor_scalar(
            self.cols[:, 4:6], self.cols[:, 1:3], -1.0, None, ALU.mult))

        wus = [self.wview(F32, 2048, 0), self.wview(F32, 2048, 8192)]
        wgs = [self.wview(F32, 2048, 16384), self.wview(F32, 2048, 24576)]
        tu = [self.wview(BF16, 2048, 32768), self.wview(BF16, 2048, 36864)]
        tg = [self.wview(BF16, 2048, 40960), self.wview(BF16, 2048, 45056)]
        m2 = self.wview(BF16, 2048, 49152)
        silu = self.wview(F32, 512, 53248)
        hfp = self.wview(F32, 512, 55296)
        Db = self.arenaD.bitcast(BF16)
        hh = Db[:, 9728:10240]
        hl = Db[:, 10240:10752]

        def p6_load(ec, par):
            e1 = self.dma(nc.sync, f"wus{par}",
                          self.wap3(F32, par * 8192, [[128, 16], [1, 128]]),
                          bass.AP(self.wupT.tensor, ec * 128,
                                  [[DEX, 128], [DEX * 128, 16], [1, 128]]))
            e2 = self.dma(nc.sync, f"wgs{par}",
                          self.wap3(F32, 16384 + par * 8192,
                                    [[128, 16], [1, 128]]),
                          bass.AP(self.wgateT.tensor, ec * 128,
                                  [[DEX, 128], [DEX * 128, 16], [1, 128]]))
            return (e1, e2)

        def p6_quant(par, e_ld, e_teq):
            wt(nc.vector, e_ld[0], e_ld[1], e_neg, e_teq)
            VE.tag(nc.vector.tensor_scalar(
                m2[:, :], wus[par][:, :], self.cols[:, 4:5], None, ALU.is_le))
            VE.tag(nc.vector.scalar_tensor_tensor(
                tu[par][:, :], wus[par][:, :], self.cols[:, 1:2], m2[:, :],
                ALU.is_ge, ALU.subtract))
            VE.tag(nc.vector.tensor_scalar(
                m2[:, :], wgs[par][:, :], self.cols[:, 5:6], None, ALU.is_le))
            return VE.tag(nc.vector.scalar_tensor_tensor(
                tg[par][:, :], wgs[par][:, :], self.cols[:, 2:3], m2[:, :],
                ALU.is_ge, ALU.subtract))

        e_q = [None, None]
        e_tq = [None, None]
        e_hid = None
        e_spill = None
        e_hf_prev = None
        e_bank = [self._fin_psb, None, None, None]
        e_tq[0] = p6_quant(0, p6_load(0, 0), None)
        for ec in range(ECH):
            par = ec % 2
            if ec + 1 < ECH:
                np_ = 1 - par
                wt(nc.sync, e_tq[np_])
                e_ldn = p6_load(ec + 1, np_)
                e_tq[np_] = p6_quant(np_, e_ldn, e_q[np_])
            for st in range(4):
                bu = st % 2
                bg = 2 + st % 2
                wt(nc.tensor, e_tq[par], e_bank[bu], e_bank[bg])
                ins = None
                for d in range(16):
                    lt_ = tu[par][:, d * 128:(d + 1) * 128]
                    nc.tensor.matmul(
                        self.psb[bu][:, :], lt_,
                        self.hfm_hi[:, d * 2048 + st * 512:
                                    d * 2048 + st * 512 + 512],
                        start=(d == 0), stop=False)
                    ins = nc.tensor.matmul(
                        self.psb[bu][:, :], lt_,
                        self.hfm_lo[:, d * 2048 + st * 512:
                                    d * 2048 + st * 512 + 512],
                        start=False, stop=(d == 15))
                e_mu = TE.tag(ins)
                for d in range(16):
                    lt_ = tg[par][:, d * 128:(d + 1) * 128]
                    nc.tensor.matmul(
                        self.psb[bg][:, :], lt_,
                        self.hfm_hi[:, d * 2048 + st * 512:
                                    d * 2048 + st * 512 + 512],
                        start=(d == 0), stop=False)
                    ins = nc.tensor.matmul(
                        self.psb[bg][:, :], lt_,
                        self.hfm_lo[:, d * 2048 + st * 512:
                                    d * 2048 + st * 512 + 512],
                        start=False, stop=(d == 15))
                e_mg = TE.tag(ins)
                wt(nc.scalar, e_mu, e_hid, e_c3, e_hf_prev)
                e_si = AE.tag(nc.scalar.activation(
                    silu[:, :], self.psb[bu][:, :], AF.Silu,
                    scale=self.cols[:, 3:4]))
                e_bank[bu] = e_si
                wt(nc.vector, e_si, e_mg, e_hid)
                e_hf = VE.tag(nc.vector.tensor_tensor(
                    hfp[:, :], silu[:, :], self.psb[bg][:, :], ALU.mult))
                e_bank[bg] = e_hf
                e_hf_prev = e_hf
                wt(nc.scalar, e_hf, e_spill)
                e_hh = AE.tag(nc.scalar.copy(hh[:, :], hfp[:, :]))
                wt(nc.vector, e_hh)
                e_hid = VE.tag(nc.vector.tensor_sub(hl[:, :], hfp[:, :],
                                                    hh[:, :]))
                wt(nc.scalar, e_hid)
                self.dma(nc.scalar, "sph",
                         self.shid_hi[ec, :, st * 512:(st + 1) * 512],
                         hh[:, :])
                e_spill = self.dma(nc.scalar, "sph",
                                   self.shid_lo[ec, :,
                                                st * 512:(st + 1) * 512],
                                   hl[:, :])
            e_q[par] = TE.evt()
        self.e_p6 = e_spill

    def phase7(self):
        nc = self.nc
        TE, VE, AE, GE = self.TE, self.VE, self.AE, self.GE
        self.barrier()
        yA = self.arenaA.bitcast(F32)
        hb = self.arenaB.bitcast(BF16)
        e_g = self.dma(nc.sync, "bc0", self.bc0[:, :],
                       bass.AP(self.srow.tensor, SROW_MOD + 5 * E,
                               [[0, 128], [1, E]]))
        self.dma(nc.sync, "ncol", self.cols[:, 7:8],
                 bass.AP(self.srow.tensor, SROW_SC + SLOT_SGD,
                         [[0, 128], [1, 1]]))
        e_c4 = self.dma(nc.sync, "ncol", self.cols[:, 8:9],
                        bass.AP(self.srow.tensor, SROW_SC + SLOT_THR_DOWN,
                                [[0, 128], [1, 1]]))
        t1 = self.wview(F32, 2048, 0)
        t2 = self.wview(F32, 2048, 8192)
        wt(nc.vector, e_g, e_c4)
        VE.tag(nc.vector.tensor_scalar(t1[:, :], self.bc0[:, :], 0.25, None,
                                       ALU.max))
        VE.tag(nc.vector.tensor_scalar(t2[:, :], self.bc0[:, :], -0.25, None,
                                       ALU.min))
        VE.tag(nc.vector.tensor_scalar(self.bc0[:, :], self.bc0[:, :], 0.0,
                                       None, ALU.is_ge))
        VE.tag(nc.vector.tensor_tensor(t1[:, :], t1[:, :], t2[:, :],
                                       ALU.subtract))
        VE.tag(nc.vector.tensor_tensor(t1[:, :], self.bc0[:, :], t1[:, :],
                                       ALU.mult))
        VE.tag(nc.vector.tensor_tensor(self.bc0[:, :], t2[:, :], t1[:, :],
                                       ALU.add))
        VE.tag(nc.vector.tensor_scalar(self.cols[:, 9:10], self.cols[:, 8:9],
                                       -1.0, None, ALU.mult))
        e_ga = VE.tag(nc.vector.tensor_scalar(
            self.bc0[:, :], self.bc0[:, :], self.cols[:, 7:8], None,
            ALU.mult))

        tdc = self.wview(F32, 2048, 0)
        td = [[self.wview(BF16, 2048, 8192 + (p * 4 + i) * 4096)
               for i in range(4)] for p in range(2)]
        m2 = self.wview(BF16, 2048, 40960)
        ocb = self.wview(F32, 1024, 45056)
        x1h = [self.wview(F32, 1024, 49152), self.wview(F32, 1024, 53248)]
        NBLK = 16

        def p7_hid_load(half, blk, par):
            e = None
            for el in range(4):
                self.dma(nc.sync, f"hid{par}",
                         hb[:, par * 16384 + el * 2048:
                            par * 16384 + (el + 1) * 2048],
                         self.shid_hi[blk * 4 + el, :, :])
                e = self.dma(nc.sync, f"hid{par}",
                             hb[:, par * 16384 + 8192 + el * 2048:
                                par * 16384 + 8192 + (el + 1) * 2048],
                             self.shid_lo[blk * 4 + el, :, :])
            return e

        self._p7_tdq = None

        def p7_td_quant(blk, par, e_teq):
            e_tq = self._p7_tdq
            for el in range(4):
                wt(nc.sync, e_tq, e_teq if el == 0 else None, e_ga)
                e_lt = self.dma(nc.sync, "tdc", tdc[:, :],
                                self.wdownT[(blk * 4 + el) * 128:
                                            (blk * 4 + el + 1) * 128, :])
                wt(nc.vector, e_lt, e_ga)
                VE.tag(nc.vector.tensor_scalar(
                    m2[:, :], tdc[:, :], self.cols[:, 9:10], None,
                    ALU.is_le))
                e_tq = VE.tag(nc.vector.scalar_tensor_tensor(
                    td[par][el][:, :], tdc[:, :], self.cols[:, 8:9],
                    m2[:, :], ALU.is_ge, ALU.subtract))
            self._p7_tdq = e_tq
            return e_tq

        e_bank = [None, None, None, None]
        e_teB = [None, None]
        e_out = None
        e_oc = None
        e_resid = [None] * NCH
        last_ev = None

        def p7_el_quant(blk, par, el, e_teq):
            e_tq = self._p7_tdq
            wt(nc.sync, e_tq, e_teq if el == 0 else None, e_ga)
            e_lt = self.dma(nc.sync, "tdc", tdc[:, :],
                            self.wdownT[(blk * 4 + el) * 128:
                                        (blk * 4 + el + 1) * 128, :])
            wt(nc.vector, e_lt, e_ga)
            VE.tag(nc.vector.tensor_scalar(
                m2[:, :], tdc[:, :], self.cols[:, 9:10], None, ALU.is_le))
            e_tq = VE.tag(nc.vector.scalar_tensor_tensor(
                td[par][el][:, :], tdc[:, :], self.cols[:, 8:9],
                m2[:, :], ALU.is_ge, ALU.subtract))
            self._p7_tdq = e_tq
            return e_tq

        def p7_residual_sc(half, sc, par):
            nonlocal e_out, e_oc
            wt(nc.sync, self._e_x1u[par])
            e_x1 = self.dma(nc.sync, f"x1h{par}", x1h[par][:, :],
                            bass.AP(self.sx1.tensor,
                                    sc * 128 * E + half * 1024,
                                    [[E, 128], [1, 1024]]))
            wt(nc.vector, e_x1, last_ev, e_out)
            VE.tag(nc.vector.tensor_tensor(
                ocb[:, :], yA[:, sc * 1024:sc * 1024 + 1024],
                self.bc0[:, half * 1024:(half + 1) * 1024], ALU.mult))
            e_oc = VE.tag(nc.vector.tensor_tensor(
                ocb[:, :], ocb[:, :], x1h[par][:, :], ALU.add))
            self._e_x1u[par] = e_oc
            e_resid[sc] = e_oc
            wt(nc.scalar, e_oc)
            e_out = self.dma(
                nc.scalar, "out",
                bass.AP(self.out.tensor, sc * 128 * E + half * 1024,
                        [[E, 128], [1, 1024]]),
                ocb[:, :])

        self._e_x1u = [None, None]
        for half in range(2):
            e_hb = [None, None]
            e_tq = [None, None]
            wt(nc.sync, e_teB[0])
            e_hb[0] = p7_hid_load(half, 0, 0)
            for el in range(4):
                e_tq[0] = p7_el_quant(0, 0, el, e_teB[0])
            for blk in range(NBLK):
                par = blk % 2
                np_ = 1 - par
                if blk + 1 < NBLK:
                    wt(nc.sync, e_teB[np_])
                    e_hb[np_] = p7_hid_load(half, blk + 1, np_)
                nel = 0
                for sc in range(NCH):
                    if half == 1 and blk == 0:
                        p7_residual_sc(0, sc, sc % 2)
                    if blk + 1 < NBLK and sc in (4, 7, 10, 13):
                        e_tq[np_] = p7_el_quant(blk + 1, np_, nel,
                                                e_teB[np_])
                        nel += 1
                    for dt_ in range(2):
                        bk = (sc * 2 + dt_) % 4
                        wt(nc.tensor, e_hb[par], e_tq[par], e_bank[bk])
                        ins = None
                        for el in range(4):
                            lh = hb[:, par * 16384 + el * 2048 + sc * 128:
                                    par * 16384 + el * 2048 + sc * 128 + 128]
                            ll = hb[:, par * 16384 + 8192 + el * 2048 +
                                    sc * 128:
                                    par * 16384 + 8192 + el * 2048 +
                                    sc * 128 + 128]
                            rt = td[par][el][:, half * 1024 + dt_ * 512:
                                             half * 1024 + dt_ * 512 + 512]
                            nc.tensor.matmul(self.psb[bk][:, :], lh, rt,
                                             start=(el == 0), stop=False)
                            ins = nc.tensor.matmul(self.psb[bk][:, :], ll,
                                                   rt, start=False,
                                                   stop=(el == 3))
                        e_mm = TE.tag(ins)
                        if blk == 0:
                            wt(nc.vector, e_mm, e_resid[sc])
                            e_acc = VE.tag(nc.vector.tensor_copy(
                                yA[:, sc * 1024 + dt_ * 512:
                                   sc * 1024 + dt_ * 512 + 512],
                                self.psb[bk][:, :]))
                        else:
                            wt(nc.vector, e_mm)
                            e_acc = VE.tag(nc.vector.tensor_tensor(
                                yA[:, sc * 1024 + dt_ * 512:
                                   sc * 1024 + dt_ * 512 + 512],
                                yA[:, sc * 1024 + dt_ * 512:
                                   sc * 1024 + dt_ * 512 + 512],
                                self.psb[bk][:, :], ALU.add))
                        e_bank[bk] = e_acc
                        last_ev = e_acc
                e_teB[par] = TE.evt()
            if half == 1:
                for sc in range(NCH):
                    p7_residual_sc(1, sc, sc % 2)

    def finish(self):
        nc = self.nc
        evs = [sl.evt() for sl in self._slots.values()]
        evs += [self.TE.evt(), self.VE.evt(), self.AE.evt()]
        wt(nc.gpsimd, *evs)


def build(dbg=(), upto=0):
    b = Builder(dbg=dbg, upto=upto)
    ncp = b.build()
    return b, ncp


from concourse.bass_utils import run_bass_kernel_spmd


_NC_CACHE = {}


def kernel(L0_attn_x, L0_attn_Wq, L0_attn_Wk, L0_attn_Wv, L0_attn_Wo,
           L0_attn_lambda_q1, L0_attn_lambda_k1, L0_attn_lambda_q2,
           L0_attn_lambda_k2, L0_attn_subln_weight, L0_ffn_W_up,
           L0_ffn_W_gate, L0_ffn_W_down, L0_c, L0_wmod):
    n = 8
    if "nc" not in _NC_CACHE:
        b = Builder(dbg=(), upto=7)
        _NC_CACHE["nc"] = b.build()
    ncp = _NC_CACHE["nc"]
    f32 = np.float32
    shared = {
        "wq": np.ascontiguousarray(L0_attn_Wq, dtype=f32),
        "wk": np.ascontiguousarray(L0_attn_Wk, dtype=f32),
        "wv": np.ascontiguousarray(L0_attn_Wv, dtype=f32),
        "wo": np.ascontiguousarray(L0_attn_Wo, dtype=f32),
        "wmod": np.ascontiguousarray(L0_wmod, dtype=f32),
        "wupT": np.ascontiguousarray(np.asarray(L0_ffn_W_up, dtype=f32).T),
        "wgateT": np.ascontiguousarray(np.asarray(L0_ffn_W_gate, dtype=f32).T),
        "wdownT": np.ascontiguousarray(np.asarray(L0_ffn_W_down, dtype=f32).T),
        "lq1": np.ascontiguousarray(L0_attn_lambda_q1, dtype=f32),
        "lk1": np.ascontiguousarray(L0_attn_lambda_k1, dtype=f32),
        "lq2": np.ascontiguousarray(L0_attn_lambda_q2, dtype=f32),
        "lk2": np.ascontiguousarray(L0_attn_lambda_k2, dtype=f32),
        "subw": np.ascontiguousarray(L0_attn_subln_weight, dtype=f32),
    }
    x = np.asarray(L0_attn_x, dtype=f32)
    c = np.asarray(L0_c, dtype=f32)
    in_maps = []
    for bb in range(n):
        m = dict(shared)
        m["xin"] = np.ascontiguousarray(x[bb])
        m["cvec"] = np.ascontiguousarray(c[bb])
        in_maps.append(m)
    res = run_bass_kernel_spmd(ncp, in_maps, core_ids=list(range(n)))
    return np.stack([np.asarray(r["out"]) for r in res.results], axis=0)
